# version 28
# baseline (speedup 1.0000x reference)
import contextlib
import math
import numpy as np
import ml_dtypes
import concourse.bass as bass
import concourse.mybir as mybir
from concourse.bass_utils import run_bass_kernel_spmd

F32 = mybir.dt.float32
BF16 = mybir.dt.bfloat16
AF = mybir.ActivationFunctionType
ALU = mybir.AluOpType
AX = mybir.AxisListType

NCORES = 8


def _rel_bucket(rel_pos, num_buckets=32, max_distance=128):
    rel_pos = np.asarray(rel_pos, dtype=np.int64)
    nb = num_buckets // 2
    buckets = (rel_pos > 0).astype(np.int64) * nb
    rp = np.abs(rel_pos)
    max_exact = nb // 2
    is_small = rp < max_exact
    rp_f = np.maximum(rp, 1).astype(np.float64)
    large = max_exact + (
        np.log(rp_f / max_exact) / math.log(max_distance / max_exact) * (nb - max_exact)
    ).astype(np.int64)
    large = np.minimum(large, nb - 1)
    return buckets + np.where(is_small, rp, large)


def _make_gt(rel_bias, S=2048):
    d = np.arange(-(S - 1), S)
    idx = _rel_bucket(d)
    table = rel_bias[idx, :]
    W = 2 * S - 128
    i = np.arange(128)[:, None]
    c = np.arange(W)[None, :]
    t_idx = (W - 1) + i - c
    GT = table[t_idx, :]
    return np.ascontiguousarray(GT.transpose(2, 0, 1)).astype(np.float32)


def _make_m01():
    deltas = [-256, -128, 0, 128, 256, 384]
    i = np.arange(128)[None, :, None]
    j = np.arange(512)[None, None, :]
    d = np.asarray(deltas)[:, None, None]
    qk = j - (d + i)
    return ((qk >= 0) & (qk <= 255)).astype(np.float32)


def _feat_tiles(w):
    K, N = w.shape
    return np.ascontiguousarray(w.reshape(K // 128, 128, N).transpose(1, 0, 2).reshape(128, -1))


def prep_host(inputs):
    f32 = np.float32
    x_all = np.ascontiguousarray(inputs["L0_attn_hidden_states"], dtype=f32)

    wqT = _feat_tiles(np.ascontiguousarray(inputs["L0_attn_wq"].T, dtype=f32))
    wkT = _feat_tiles(np.ascontiguousarray(inputs["L0_attn_wk"].T, dtype=f32))
    wvT = _feat_tiles(np.ascontiguousarray(inputs["L0_attn_wv"].T, dtype=f32))
    woT = _feat_tiles(np.ascontiguousarray(inputs["L0_attn_wo"].T, dtype=f32))
    gt = _make_gt(np.asarray(inputs["L0_attn_rel_bias"], dtype=f32))

    W1 = np.asarray(inputs["L0_ffn_W1"], dtype=f32)
    W2 = np.asarray(inputs["L0_ffn_W2"], dtype=f32)
    Ws1 = np.asarray(inputs["L0_ffn_Ws1"], dtype=f32)[None]
    Ws2 = np.asarray(inputs["L0_ffn_Ws2"], dtype=f32)[None]
    w1all = np.ascontiguousarray(np.concatenate([W1, Ws1], 0))
    w2all = np.ascontiguousarray(np.concatenate([W2, Ws2], 0))
    b1 = np.asarray(inputs["L0_ffn_b1"], dtype=f32)
    bs1 = np.asarray(inputs["L0_ffn_bs1"], dtype=f32)[None]
    b1all = np.concatenate([b1, bs1], 0)
    b1dev = np.ascontiguousarray(b1all.reshape(9, 32, 128).transpose(2, 1, 0).reshape(128, 32 * 9))
    b2 = np.asarray(inputs["L0_ffn_b2"], dtype=f32)
    bs2 = np.asarray(inputs["L0_ffn_bs2"], dtype=f32)[None]
    b2all = np.ascontiguousarray(np.concatenate([b2, bs2], 0))
    wg = np.asarray(inputs["L0_ffn_Wg"], dtype=f32)
    wgdev = _feat_tiles(wg)

    l1qT = _feat_tiles(np.ascontiguousarray(inputs["L1_attn_Wq"].T, dtype=f32))
    l1kT = _feat_tiles(np.ascontiguousarray(inputs["L1_attn_Wk"].T, dtype=f32))
    l1vT = _feat_tiles(np.ascontiguousarray(inputs["L1_attn_Wv"].T, dtype=f32))
    l1oT = _feat_tiles(np.ascontiguousarray(inputs["L1_attn_Wo"].T, dtype=f32))
    m01 = _make_m01()

    wgate = np.ascontiguousarray(inputs["L1_ffn_w_gate"], dtype=f32)
    wup = np.ascontiguousarray(inputs["L1_ffn_w_up"], dtype=f32)
    wdown = np.ascontiguousarray(inputs["L1_ffn_w_down"], dtype=f32)

    ident = np.eye(128, dtype=f32)

    shared = dict(
        wqT=wqT, wkT=wkT, wvT=wvT, woT=woT, gt=gt,
        w1all=w1all, w2all=w2all, b1dev=b1dev, b2all=b2all, wgdev=wgdev,
        l1qT=l1qT, l1kT=l1kT, l1vT=l1vT, l1oT=l1oT, m01=m01,
        wgate=wgate, wup=wup, wdown=wdown, ident=ident,
    )
    maps = []
    for b in range(NCORES):
        m = dict(shared)
        m["x"] = np.ascontiguousarray(x_all[b])
        maps.append(m)
    return maps



ENGS = ("tensor", "scalar", "vector", "gpsimd", "sync")


class Sem:
    __slots__ = ("h", "n", "name")

    def __init__(self, h, name):
        self.h = h
        self.n = 0
        self.name = name

    def tick(self, amt=1):
        self.n += amt
        return self.n


class Plan:
    def __init__(self, nc, ctx):
        self.nc = nc
        self.ctx = ctx
        self.items = {e: [] for e in ENGS}
        self.sems = {}
        self.watermark = {}
        self._deps = []
        self._deps_todo = set()

    def phase_barrier(self):
        self._deps = [(s, s.n) for s in self.sems.values() if s.n > 0]
        self._deps_todo = set(ENGS)

    def sem(self, name):
        h = self.ctx.enter_context(self.nc.semaphore(name))
        s = Sem(h, name)
        self.sems[name] = s
        return s

    def emit(self, eng, fn, wait=(), inc=(), is_dma=False):
        wait = list(wait)
        inc = list(inc)
        if eng in self._deps_todo:
            wait = self._deps + wait
            self._deps_todo.discard(eng)
        if eng in ("vector", "scalar") or (eng == "gpsimd" and not is_dma and "g" in self.sems):
            own = self.sems["v" if eng == "vector" else ("s" if eng == "scalar" else "g")]
            if own.n:
                wait.append((own, own.n))
            if not any(s is own for s, _ in inc):
                inc.append((own, 1))
        pruned = []
        for s, v in wait:
            assert v <= s.n, f"wait on {s.name}>={v} but only {s.n} planned (deadlock)"
            key = (eng, s.name)
            if self.watermark.get(key, -1) >= v:
                continue
            self.watermark[key] = v
            pruned.append((s, v))
        incs = []
        for s, amt in inc:
            s.tick(amt)
            incs.append((s, amt, s.n))
        self.items[eng].append((pruned, fn, incs))
        return [(s, n) for s, _a, n in incs]

    def run(self):
        block = self.ctx.enter_context(self.nc.Block())
        plans = self.items

        def mk(eng_name):
            def body(proxy):
                for waits, fn, incs in plans[eng_name]:
                    for s, v in waits:
                        proxy.wait_ge(s.h, v)
                    ins = fn(proxy)
                    if incs:
                        assert ins is not None, f"inc on None instr ({eng_name})"
                        for s, amt, _n in incs:
                            ins = ins.then_inc(s.h, amt)
            return body

        block.tensor(mk("tensor"))
        block.scalar(mk("scalar"))
        block.vector(mk("vector"))
        block.gpsimd(mk("gpsimd"))
        block.sync(mk("sync"))



class DmaStream:

    def __init__(self, P, name, eng):
        self.P = P
        self.eng = eng
        self.sem = P.sem(name)

    def load(self, dst, src, wait=(), transpose=False):
        self.P.emit(self.eng, lambda g, d=dst, s=src, tr=transpose: g.dma_start(d, s, transpose=tr),
                    wait=wait, inc=[(self.sem, 16)], is_dma=True)
        return self.sem.n

    def wv(self):
        return self.sem.n



def build_phase_A(P, hbm, bufs, cfg, fused_out=False):
    nc = P.nc
    T = cfg["T"]; NT = T // 128
    NDT = 8
    F = cfg["F"]; NFT = F // 128
    NG = cfg["NG"]; FLG = NFT // NG
    NE = cfg["NE"]
    TC = 256; NTC = T // TC
    ER = NE - 1
    GL = FLG * 128

    st = P.sems["t"]; ss = P.sems["s"]; sv = P.sems["v"]
    gc = DmaStream(P, "A_gc", "gpsimd")
    sx = [DmaStream(P, f"A_sx{i}", "sync") for i in range(2)]
    gw = [DmaStream(P, f"A_gw{i}", "gpsimd") for i in range(2)]
    so = [DmaStream(P, f"A_so{i}", "sync") for i in range(2)]

    if bufs.get("xnT") is None:
        bufs_xnT = P.ctx.enter_context(nc.sbuf_tensor("A_xnT", [128, 8 * T], BF16))
        bufs_gacc = P.ctx.enter_context(nc.sbuf_tensor("A_gacc", [128, NT * 1024], F32))
    else:
        bufs_xnT = bufs["xnT"]
        bufs_gacc = bufs["g_acc"]
    xnT = bufs_xnT
    g_acc = bufs_gacc
    g_tok = bufs.get("g_tok")
    ident = bufs["ident"]

    ctx = P.ctx
    b1dev = ctx.enter_context(nc.sbuf_tensor("A_b1dev", [128, (F // 128) * NE], F32))
    cw_sb = ctx.enter_context(nc.sbuf_tensor("A_cw", [128, NT * NE], F32))
    b2bf = ctx.enter_context(nc.sbuf_tensor("A_b2bf", [NE, 1024], BF16))
    wg_sb = ctx.enter_context(nc.sbuf_tensor("A_wg", [128, NDT * ER], F32))

    gc.load(b1dev[:, :], hbm["b1dev"][:, :])
    gc.load(b2bf[:, :], hbm["b2all"][:, :])
    gc.load(wg_sb[:, :], hbm["wgdev"][:, :])
    gc.load(ident[:, :], hbm["ident"][:, :])
    identf = P.ctx.enter_context(nc.sbuf_tensor("A_identf", [128, 128], F32))
    gc.load(identf[:, :], hbm["ident"][:, :])
    CW = gc.wv()
    P.emit("vector", lambda v: v.memset(cw_sb[:, :], 0.1))
    eps_col = P.ctx.enter_context(nc.sbuf_tensor("A_eps", [128, 1], F32))
    P.emit("vector", lambda v: v.memset(eps_col[:, :], 1e-5))
    eps_v = P.sems["v"].n

    lnctx = contextlib.ExitStack()
    ptr = [lnctx.enter_context(nc.psum_tensor(f"A_ptr{i}", [128, 128], F32)) for i in range(2)]
    p_r = lnctx.enter_context(nc.psum_tensor("A_pr", [128, ER], F32))
    p_cwT = lnctx.enter_context(nc.psum_tensor("A_pcwT", [NE, 128], BF16))
    p_b2 = lnctx.enter_context(nc.psum_tensor("A_pb2", [128, 1024], F32))
    x_t = [lnctx.enter_context(nc.sbuf_tensor(f"A_x{i}", [128, 1024], F32)) for i in range(2)]
    xsq = lnctx.enter_context(nc.sbuf_tensor("A_xsq", [128, 1024], F32))
    stat4 = lnctx.enter_context(nc.sbuf_tensor("A_stat", [128, 8], F32))
    xn_tok = lnctx.enter_context(nc.sbuf_tensor("A_xntok", [128, 1024], F32))
    xnf = [lnctx.enter_context(nc.sbuf_tensor(f"A_xnf{i}", [128, 128], F32)) for i in range(2)]
    cwT = lnctx.enter_context(nc.sbuf_tensor("A_cwT", [NE, NT * 128], BF16))
    r_sc = lnctx.enter_context(nc.sbuf_tensor("A_rsc", [128, 16], F32))
    probs = lnctx.enter_context(nc.sbuf_tensor("A_probs", [128, ER], F32))
    pm = lnctx.enter_context(nc.sbuf_tensor("A_pm", [128, ER], F32))
    msk = lnctx.enter_context(nc.sbuf_tensor("A_msk", [128, ER], F32))
    cwbf = lnctx.enter_context(nc.sbuf_tensor("A_cwbf", [128, NE], BF16))
    eps = 1e-5
    inv_d = 1.0 / 1024.0
    router_done = [0] * NT
    ginit_done = [0] * NT
    x_free = [0, 0]
    xsq_free = 0
    xn_tok_free = 0
    ptr_free = [0, 0]
    pr_free = 0
    xnf_free = [0, 0]
    pcwT_free = 0
    cwbf_free = 0
    pb2_free = 0

    for tt in range(NT):
        par = tt % 2
        wts = [(sv, x_free[par])] if x_free[par] else []
        xv = sx[par].load(x_t[par][:, :], hbm["x"][128 * tt:128 * (tt + 1), :], wait=wts)
        xw = (sx[par].sem, xv)
        P.emit("vector", lambda v, p=par: v.reduce_sum(stat4[:, 0:1], x_t[p][:, :], AX.X),
               wait=[xw])
        swts = [xw] + ([(sv, xsq_free)] if xsq_free else [])
        P.emit("scalar", lambda s, p=par: s.activation(xsq[:, :], x_t[p][:, :], AF.Square),
               wait=swts)
        sq_done = ss.n
        P.emit("vector", lambda v: v.reduce_sum(stat4[:, 1:2], xsq[:, :], AX.X),
               wait=[(ss, sq_done)])
        xsq_free = sv.n
        P.emit("vector", lambda v: v.tensor_scalar_mul(stat4[:, 2:3], stat4[:, 0:1], inv_d))
        P.emit("vector", lambda v: v.tensor_scalar_mul(stat4[:, 3:4], stat4[:, 1:2], inv_d))
        P.emit("vector", lambda v: v.scalar_tensor_tensor(
            stat4[:, 4:5], stat4[:, 2:3], -1.0, stat4[:, 2:3], ALU.mult, ALU.mult))
        P.emit("vector", lambda v: v.tensor_add(stat4[:, 5:6], stat4[:, 3:4], stat4[:, 4:5]))
        var_v = sv.n
        P.emit("scalar", lambda s: s.activation(stat4[:, 6:7], stat4[:, 5:6], AF.Sqrt, bias=eps_col[:, :]),
               wait=[(sv, var_v), (sv, eps_v)])
        sd_v = ss.n
        P.emit("vector", lambda v: v.reciprocal(stat4[:, 7:8], stat4[:, 6:7]),
               wait=[(ss, sd_v)])
        rstd_v = sv.n
        wts = [(sv, rstd_v)] + ([(st, xn_tok_free)] if xn_tok_free else [])
        P.emit("vector", lambda v, p=par: v.tensor_scalar(
            xn_tok[:, :], x_t[p][:, :], stat4[:, 2:3], stat4[:, 7:8], ALU.subtract, ALU.mult),
            wait=wts)
        xn_v = sv.n
        x_free[par] = xn_v
        for ft in range(NDT):
            tp = ft % 2
            wts = [(sv, xn_v), (gc.sem, CW)]
            if ptr_free[tp]:
                wts.append((sv, ptr_free[tp]))
            P.emit("tensor", lambda t, f=ft, p=tp: t.transpose(
                ptr[p][:, :], xn_tok[:, 128 * f:128 * (f + 1)], identf[:, :]),
                wait=wts, inc=[(st, 1)])
            tr_v = st.n
            P.emit("vector", lambda v, f=ft, p=tp, _tt=tt: v.tensor_copy(
                xnT[:, T * f + 128 * _tt: T * f + 128 * (_tt + 1)], ptr[p][:, :]),
                wait=[(st, tr_v)])
            wts = [(st, tr_v)]
            if xnf_free[tp]:
                wts.append((st, xnf_free[tp]))
            P.emit("vector", lambda v, p=tp: v.tensor_copy(xnf[p][:, :], ptr[p][:, :]),
                   wait=wts)
            ptr_free[tp] = sv.n
            xnf_v = sv.n
            P.emit("tensor", lambda t, f=ft, p=tp: t.matmul(
                p_r[:, :], xnf[p][:, :], wg_sb[:, ER * f: ER * (f + 1)],
                start=(f == 0), stop=(f == NDT - 1)),
                wait=[(sv, xnf_v)] + ([(ss, pr_free)] if (ft == 0 and pr_free) else []),
                inc=[(st, 1)])
            xnf_free[tp] = st.n
        xn_tok_free = st.n
        xn_copied = sv.n

        log_v = st.n
        P.emit("vector", lambda v: v.reduce_max(r_sc[:, 0:1], p_r[:, :], AX.X, negate=True),
               wait=[(st, log_v)])
        negm_v = sv.n
        P.emit("scalar", lambda s: s.activation(probs[:, :], p_r[:, :], AF.Exp, bias=r_sc[:, 0:1]),
               wait=[(sv, negm_v)])
        punn_v = ss.n
        pr_free = punn_v
        P.emit("vector", lambda v: v.reduce_sum(r_sc[:, 1:2], probs[:, :], AX.X),
               wait=[(ss, punn_v)])
        P.emit("vector", lambda v: v.reciprocal(r_sc[:, 2:3], r_sc[:, 1:2]))
        P.emit("vector", lambda v: v.tensor_scalar_mul(probs[:, :], probs[:, :], r_sc[:, 2:3]))
        P.emit("vector", lambda v: v.reduce_max(r_sc[:, 3:4], probs[:, :], AX.X))
        P.emit("vector", lambda v: v.tensor_scalar(msk[:, :], probs[:, :], r_sc[:, 3:4], None, ALU.is_ge))
        P.emit("vector", lambda v: v.tensor_mul(pm[:, :], probs[:, :], msk[:, :]))
        P.emit("vector", lambda v: v.tensor_sub(pm[:, :], probs[:, :], pm[:, :]))
        P.emit("vector", lambda v: v.reduce_max(r_sc[:, 4:5], pm[:, :], AX.X))
        P.emit("vector", lambda v: v.tensor_scalar(r_sc[:, 8:8 + ER], pm[:, :], r_sc[:, 4:5], None, ALU.is_ge))
        P.emit("vector", lambda v: v.tensor_add(msk[:, :], msk[:, :], r_sc[:, 8:8 + ER]))
        P.emit("vector", lambda v: v.tensor_mul(pm[:, :], probs[:, :], msk[:, :]))
        P.emit("vector", lambda v: v.reduce_sum(r_sc[:, 5:6], pm[:, :], AX.X))
        P.emit("vector", lambda v: v.tensor_scalar_add(r_sc[:, 5:6], r_sc[:, 5:6], 1e-9))
        P.emit("vector", lambda v: v.reciprocal(r_sc[:, 6:7], r_sc[:, 5:6]))
        P.emit("vector", lambda v, _tt=tt: v.tensor_scalar_mul(
            cw_sb[:, NE * _tt: NE * _tt + ER], pm[:, :], r_sc[:, 6:7]))
        router_done[tt] = sv.n
        wts = [(st, cwbf_free)] if cwbf_free else []
        P.emit("vector", lambda v, _tt=tt: v.tensor_copy(cwbf[:, :], cw_sb[:, NE * _tt: NE * (_tt + 1)]),
               wait=wts)
        cwbf_v = sv.n
        wts = [(sv, cwbf_v)]
        if pcwT_free:
            wts.append((sv, pcwT_free))
        P.emit("tensor", lambda t: t.transpose(p_cwT[:, :], cwbf[:, :], ident[:, :]),
               wait=wts, inc=[(st, 1)])
        cwt_v = st.n
        cwbf_free = cwt_v
        P.emit("vector", lambda v, _tt=tt: v.tensor_copy(cwT[:, 128 * _tt:128 * (_tt + 1)], p_cwT[:, :]),
               wait=[(st, cwt_v)])
        cwT_v = sv.n
        pcwT_free = cwT_v
        for hh in range(2):
            wts = [(sv, cwT_v), (gc.sem, CW)]
            if hh == 0 and pb2_free:
                wts.append((sv, pb2_free))
            P.emit("tensor", lambda t, _tt=tt, h=hh: t.matmul(
                p_b2[:, 512 * h:512 * (h + 1)], cwT[:, 128 * _tt:128 * (_tt + 1)],
                b2bf[:, 512 * h:512 * (h + 1)], start=True, stop=True),
                wait=wts, inc=[(st, 1)] if hh == 1 else [])
        b2_v = st.n
        P.emit("vector", lambda v, _tt=tt: v.tensor_copy(
            g_acc[:, 1024 * _tt:1024 * (_tt + 1)], p_b2[:, :]),
            wait=[(st, b2_v)])
        pb2_free = sv.n
        ginit_done[tt] = sv.n

    ln_all_v = sv.n
    lnctx.close()
    psumA = [ctx.enter_context(nc.psum_tensor(f"A_pA{i}", [128, 512], F32)) for i in range(2)]
    ypsum = [ctx.enter_context(nc.psum_tensor(f"A_yp{i}", [128, 1024], F32)) for i in range(2)]
    actTg = ctx.enter_context(nc.sbuf_tensor("A_actg", [128, FLG * 512], BF16))
    w1g = [ctx.enter_context(nc.sbuf_tensor(f"A_w1g{i}", [128, 8 * GL], BF16)) for i in range(2)]
    w2g = [ctx.enter_context(nc.sbuf_tensor(f"A_w2g{i}", [128, FLG * 1024], BF16)) for i in range(2)]

    TC2 = 512
    NTC2 = T // TC2
    w_ready = {}
    w_free = {0: 0, 1: 0}
    eg_list = [(e, g) for e in range(NE) for g in range(NG)]

    def emit_wload(i):
        e, g = eg_list[i]
        par = i % 2
        w1src = hbm["w1all"].rearrange("e (hc p) f -> e p hc f", p=128)[e, :, :, GL * g:GL * (g + 1)]
        w2src = hbm["w2all"].rearrange("e (ft p) h -> e p ft h", p=128)[e, :, FLG * g:FLG * (g + 1), :]
        wts = [(st, w_free[par])] if w_free[par] else []
        gw[par].load(w1g[par].rearrange("p (hc f) -> p hc f", hc=8), w1src, wait=wts)
        gw[par].load(w2g[par].rearrange("p (ft h) -> p ft h", ft=FLG), w2src)
        w_ready[i] = gw[par].wv()

    bar_eg = [(st, st.n), (ss, ss.n), (sv, sv.n)]
    w_free[0] = -1
    w_free[1] = -1
    _orig_emit_wload = emit_wload

    def emit_wload0(i):
        e, g = eg_list[i]
        par = i % 2
        w1src = hbm["w1all"].rearrange("e (hc p) f -> e p hc f", p=128)[e, :, :, GL * g:GL * (g + 1)]
        w2src = hbm["w2all"].rearrange("e (ft p) h -> e p ft h", p=128)[e, :, FLG * g:FLG * (g + 1), :]
        gw[par].load(w1g[par].rearrange("p (hc f) -> p hc f", hc=8), w1src, wait=list(bar_eg))
        gw[par].load(w2g[par].rearrange("p (ft h) -> p ft h", ft=FLG), w2src)
        w_ready[i] = gw[par].wv()

    emit_wload0(0)
    emit_wload0(1)
    w_free[0] = 0
    w_free[1] = 0

    psA_free = [0, 0]
    act_free = 0
    y_free = [0, 0]

    for i, (e, g) in enumerate(eg_list):
        par_w = i % 2
        wv = (gw[par_w].sem, w_ready[i])
        for tc in range(NTC2):
            silu_vals = [0] * FLG
            for fl in range(FLG):
                pa = fl % 2
                for hc in range(NDT):
                    wts = [wv]
                    if hc == 0:
                        wts.append((sv, ln_all_v))
                        if psA_free[pa]:
                            wts.append((ss, psA_free[pa]))
                    P.emit("tensor", lambda t, h=hc, f=fl, p=pa, w=par_w, _tc=tc: t.matmul(
                        psumA[p][:, :],
                        w1g[w][:, GL * h + 128 * f: GL * h + 128 * (f + 1)],
                        xnT[:, T * h + TC2 * _tc: T * h + TC2 * (_tc + 1)],
                        start=(h == 0), stop=(h == NDT - 1)),
                        wait=wts, inc=[(st, 1)] if hc == NDT - 1 else [])
                mmA_v = st.n
                ft_g = FLG * g + fl
                wts = [(st, mmA_v), (gc.sem, CW)]
                if act_free:
                    wts.append((st, act_free))
                P.emit("scalar", lambda s, p=pa, f=fl, c=(ft_g * NE + e): s.activation(
                    actTg[:, 512 * f:512 * (f + 1)], psumA[p][:, :], AF.Silu, bias=b1dev[:, c:c + 1]),
                    wait=wts)
                psA_free[pa] = ss.n
                silu_vals[fl] = ss.n
            for sub in range(2):
                for ts in range(2):
                    tt = (TC2 // 128) * tc + 2 * sub + ts
                    for fl in range(FLG):
                        wts = [(ss, silu_vals[fl])]
                        if sub == 0 and ts == 0 and fl == 0 and y_free[ts]:
                            wts += [(sv, y_free[0]), (sv, y_free[1])]
                        elif fl == 0 and y_free[ts]:
                            wts.append((sv, y_free[ts]))
                        for hh in range(2):
                            P.emit("tensor", lambda t, f=fl, _s=sub, _ts=ts, h=hh, w=par_w: t.matmul(
                                ypsum[_ts][:, 512 * h:512 * (h + 1)],
                                actTg[:, 512 * f + 256 * _s + 128 * _ts: 512 * f + 256 * _s + 128 * (_ts + 1)],
                                w2g[w][:, 1024 * f + 512 * h: 1024 * f + 512 * (h + 1)],
                                start=(f == 0), stop=(f == FLG - 1)),
                                wait=wts if hh == 0 else [],
                                inc=[(st, 1)] if (fl == FLG - 1 and hh == 1) else [])
                    mmB_v = st.n
                    wts = [(st, mmB_v), (sv, ginit_done[tt])]
                    if e == NE - 1:
                        fn = lambda v, _ts=ts, _tt=tt: v.scalar_tensor_tensor(
                            g_acc[:, 1024 * _tt:1024 * (_tt + 1)], ypsum[_ts][:, :], 0.1,
                            g_acc[:, 1024 * _tt:1024 * (_tt + 1)], ALU.mult, ALU.add)
                    else:
                        fn = lambda v, _ts=ts, _tt=tt, _e=e: v.scalar_tensor_tensor(
                            g_acc[:, 1024 * _tt:1024 * (_tt + 1)], ypsum[_ts][:, :],
                            cw_sb[:, NE * _tt + _e: NE * _tt + _e + 1],
                            g_acc[:, 1024 * _tt:1024 * (_tt + 1)], ALU.mult, ALU.add)
                    P.emit("vector", fn, wait=wts)
                    y_free[ts] = sv.n
            act_free = st.n
            fil = bufs.get("filler")
            if fil is not None:
                fil()
        if i + 2 < len(eg_list):
            w_free[par_w] = st.n
            emit_wload(i + 2)

    all_evac_v = sv.n
    g_done = []
    if fused_out:
        for tt in range(NT):
            P.emit("scalar", lambda s, _tt=tt: s.activation(
                g_tok[:, 1024 * _tt:1024 * (_tt + 1)], g_acc[:, 1024 * _tt:1024 * (_tt + 1)], AF.Silu),
                wait=[(sv, all_evac_v)])
            g_done.append(ss.n)
    else:
        gst0 = ctx.enter_context(nc.sbuf_tensor("A_gst0", [128, 1024], BF16))
        gfree = 0
        for tt in range(NT):
            wts = [(sv, all_evac_v)]
            if gfree:
                wts.append((so[0].sem, gfree))
            P.emit("scalar", lambda s, _tt=tt: s.activation(
                gst0[:, :], g_acc[:, 1024 * _tt:1024 * (_tt + 1)], AF.Silu),
                wait=wts)
            sd = ss.n
            gfree = so[0].load(hbm["g_out"][128 * tt:128 * (tt + 1), :],
                               gst0[:, :], wait=[(ss, sd)])
            g_done.append(sd)
        P.emit("sync", lambda s: None, wait=[(so[0].sem, so[0].wv())])
    return g_done




def emit_ln_to_xnT(P, hbm_x, xnT, identb, cfg, pfx, deps=()):
    nc = P.nc
    T = cfg["T"]; NT = T // 128; NDT = 8
    st = P.sems["t"]; ss = P.sems["s"]; sv = P.sems["v"]
    ctx = P.ctx
    sx = [DmaStream(P, f"{pfx}_sx{i}", "sync") for i in range(2)]
    x_t = [ctx.enter_context(nc.sbuf_tensor(f"{pfx}_x{i}", [128, 1024], F32)) for i in range(2)]
    xsq = ctx.enter_context(nc.sbuf_tensor(f"{pfx}_xsq", [128, 1024], F32))
    stat4 = ctx.enter_context(nc.sbuf_tensor(f"{pfx}_stat", [128, 8], F32))
    xn_tok = ctx.enter_context(nc.sbuf_tensor(f"{pfx}_xntok", [128, 1024], BF16))
    eps_col = ctx.enter_context(nc.sbuf_tensor(f"{pfx}_eps", [128, 1], F32))
    ptr = [ctx.enter_context(nc.psum_tensor(f"{pfx}_ptr{i}", [128, 128], BF16)) for i in range(2)]
    P.emit("vector", lambda v: v.memset(eps_col[:, :], 1e-5))
    eps_v = P.sems["v"].n
    x_free = [0, 0]
    xsq_free = 0
    xn_tok_free = 0
    ptr_free = [0, 0]
    for tt in range(NT):
        par = tt % 2
        wts = ([(sv, x_free[par])] if x_free[par] else []) + list(deps)
        xv = sx[par].load(x_t[par][:, :], hbm_x[128 * tt:128 * (tt + 1), :], wait=wts)
        xw = (sx[par].sem, xv)
        P.emit("vector", lambda v, p=par: v.reduce_sum(stat4[:, 0:1], x_t[p][:, :], AX.X),
               wait=[xw])
        swts = [xw] + ([(sv, xsq_free)] if xsq_free else [])
        P.emit("scalar", lambda s, p=par: s.activation(xsq[:, :], x_t[p][:, :], AF.Square),
               wait=swts)
        sq_done = ss.n
        P.emit("vector", lambda v: v.reduce_sum(stat4[:, 1:2], xsq[:, :], AX.X),
               wait=[(ss, sq_done)])
        xsq_free = sv.n
        P.emit("vector", lambda v: v.tensor_scalar_mul(stat4[:, 2:3], stat4[:, 0:1], 1.0 / 1024.0))
        P.emit("vector", lambda v: v.tensor_scalar_mul(stat4[:, 3:4], stat4[:, 1:2], 1.0 / 1024.0))
        P.emit("vector", lambda v: v.scalar_tensor_tensor(
            stat4[:, 4:5], stat4[:, 2:3], -1.0, stat4[:, 2:3], ALU.mult, ALU.mult))
        P.emit("vector", lambda v: v.tensor_add(stat4[:, 5:6], stat4[:, 3:4], stat4[:, 4:5]))
        var_v = sv.n
        P.emit("scalar", lambda s: s.activation(stat4[:, 6:7], stat4[:, 5:6], AF.Sqrt, bias=eps_col[:, :]),
               wait=[(sv, var_v), (sv, eps_v)])
        sd_v = ss.n
        P.emit("vector", lambda v: v.reciprocal(stat4[:, 7:8], stat4[:, 6:7]),
               wait=[(ss, sd_v)])
        rstd_v = sv.n
        wts = [(sv, rstd_v)] + ([(st, xn_tok_free)] if xn_tok_free else [])
        P.emit("vector", lambda v, p=par: v.tensor_scalar(
            xn_tok[:, :], x_t[p][:, :], stat4[:, 2:3], stat4[:, 7:8], ALU.subtract, ALU.mult),
            wait=wts)
        xn_v = sv.n
        x_free[par] = xn_v
        for ft in range(NDT):
            tp = ft % 2
            wts = [(sv, xn_v)]
            if ptr_free[tp]:
                wts.append((sv, ptr_free[tp]))
            P.emit("tensor", lambda t, f=ft, p=tp: t.transpose(
                ptr[p][:, :], xn_tok[:, 128 * f:128 * (f + 1)], identb[:, :]),
                wait=wts, inc=[(st, 1)])
            tr_v = st.n
            P.emit("vector", lambda v, f=ft, p=tp, _tt=tt: v.tensor_copy(
                xnT[:, T * f + 128 * _tt: T * f + 128 * (_tt + 1)], ptr[p][:, :]),
                wait=[(st, tr_v)])
            ptr_free[tp] = sv.n
        xn_tok_free = st.n
    return P.sems["v"].n


def build_phase_B(P, hbm, bufs, cfg):
    nc = P.nc
    T = cfg["T"]; NT = T // 128; NDT = 8
    NH = 16
    QC = 512; NQC = T // QC
    NKT = T // 128
    W_GT = 2 * T - 128
    C0 = T - 128
    st = P.sems["t"]; ss = P.sems["s"]; sv = P.sems["v"]
    ctx = P.ctx

    identb = bufs["ident"]
    gc = DmaStream(P, "B_gc", "gpsimd")
    gwq = [DmaStream(P, f"B_gw{i}", "gpsimd") for i in range(2)]
    ggt = [DmaStream(P, f"B_ggt{i}", "gpsimd") for i in range(2)]
    sxc = [DmaStream(P, f"B_sxc{i}", "sync") for i in range(2)]
    sg = [DmaStream(P, f"B_sg{i}", "sync") for i in range(2)]
    so = [DmaStream(P, f"B_so{i}", "gpsimd") for i in range(2)]

    QT = ctx.enter_context(nc.sbuf_tensor("B_QT", [128, 8 * T], BF16))
    KT = ctx.enter_context(nc.sbuf_tensor("B_KT", [128, 8 * T], BF16))
    VTM = ctx.enter_context(nc.sbuf_tensor("B_VTM", [128, NKT * (NH * 65)], BF16))
    g_sb = bufs.get("g_tok")
    ones1 = ctx.enter_context(nc.sbuf_tensor("B_ones1", [1, 64], F32))

    gc.load(identb[:, :], hbm["ident"][:, :])
    CW = gc.wv()
    P.emit("vector", lambda v: v.memset(VTM[:, :], 1.0))
    vtm_init = sv.n
    P.emit("vector", lambda v: v.memset(ones1[:, :], 1.0))
    ones_v = sv.n

    with contextlib.ExitStack() as lctx:
        Pctx_saved = P.ctx
        P.ctx = lctx
        xnT = lctx.enter_context(nc.sbuf_tensor("B_xnT", [128, 8 * T], BF16))
        wbuf = [lctx.enter_context(nc.sbuf_tensor(f"B_w{i}", [128, 8 * 1024], BF16)) for i in range(2)]
        xn_done = emit_ln_to_xnT(P, hbm["x"], xnT, identb, cfg, "B_ln", deps=[(gc.sem, CW)])
        P.ctx = Pctx_saved

        names = ["wqT", "wkT", "wvT"]
        wv_ld = [0, 0, 0]
        wproj_done = [0, 0, 0]
        wv_ld[0] = gwq[0].load(wbuf[0][:, :], hbm[names[0]][:, :])
        wv_ld[1] = gwq[1].load(wbuf[1][:, :], hbm[names[1]][:, :])
        with contextlib.ExitStack() as pctx:
            pp = [pctx.enter_context(nc.psum_tensor(f"B_pp{i}", [128, 512], F32)) for i in range(2)]
            ptr2 = [pctx.enter_context(nc.psum_tensor(f"B_ptr2{i}", [128, 128], BF16)) for i in range(2)]
            vstage = pctx.enter_context(nc.sbuf_tensor("B_vstage", [128, 512], BF16))
            pp_free = [0, 0]
            ptr2_free = [0, 0]
            vstage_free = 0
            for wi, nm in enumerate(names):
                par = wi % 2
                if wi >= 2:
                    wv_ld[wi] = gwq[par].load(wbuf[par][:, :], hbm[names[wi]][:, :],
                                              wait=[(st, wproj_done[wi - 2])])
                for et in range(8):
                    for tcq in range(T // 512):
                        pa = (et * (T // 512) + tcq) % 2
                        for dc in range(8):
                            wts = [(gwq[par].sem, wv_ld[wi]), (sv, xn_done)]
                            if dc == 0 and pp_free[pa]:
                                wts.append((sv, pp_free[pa]))
                            P.emit("tensor", lambda t, d=dc, e=et, q=tcq, p=pa, w=par: t.matmul(
                                pp[p][:, :],
                                wbuf[w][:, 1024 * d + 128 * e: 1024 * d + 128 * (e + 1)],
                                xnT[:, T * d + 512 * q: T * d + 512 * (q + 1)],
                                start=(d == 0), stop=(d == 7)),
                                wait=wts, inc=[(st, 1)] if dc == 7 else [])
                        mm_v = st.n
                        if nm == "wqT":
                            P.emit("vector", lambda v, e=et, q=tcq, p=pa: v.tensor_copy(
                                QT[:, T * e + 512 * q: T * e + 512 * (q + 1)], pp[p][:, :]),
                                wait=[(st, mm_v)])
                            pp_free[pa] = sv.n
                        elif nm == "wkT":
                            P.emit("vector", lambda v, e=et, q=tcq, p=pa: v.tensor_copy(
                                KT[:, T * e + 512 * q: T * e + 512 * (q + 1)], pp[p][:, :]),
                                wait=[(st, mm_v)])
                            pp_free[pa] = sv.n
                        else:
                            wts = [(st, mm_v)]
                            if vstage_free:
                                wts.append((st, vstage_free))
                            P.emit("vector", lambda v, p=pa: v.tensor_copy(vstage[:, :], pp[p][:, :]),
                                   wait=wts)
                            pp_free[pa] = sv.n
                            vst_v = sv.n
                            for j in range(4):
                                tp2 = j % 2
                                wts = [(sv, vst_v), (gc.sem, CW)]
                                if ptr2_free[tp2]:
                                    wts.append((sv, ptr2_free[tp2]))
                                P.emit("tensor", lambda t, _j=j, p=tp2: t.transpose(
                                    ptr2[p][:, :], vstage[:, 128 * _j:128 * (_j + 1)], identb[:, :]),
                                    wait=wts, inc=[(st, 1)])
                                tr_v = st.n
                                kt_idx = tcq * 4 + j
                                for half in range(2):
                                    h_idx = 2 * et + half
                                    P.emit("vector", lambda v, k=kt_idx, h=h_idx, p=tp2, hf=half: v.tensor_copy(
                                        VTM[:, 65 * NH * k + 65 * h: 65 * NH * k + 65 * h + 64],
                                        ptr2[p][:, 64 * hf:64 * (hf + 1)]),
                                        wait=[(st, tr_v), (sv, vtm_init)])
                                ptr2_free[tp2] = sv.n
                            vstage_free = st.n
                wproj_done[wi] = st.n
        proj_evac_v = P.sems["v"].n

    aoT = ctx.enter_context(nc.sbuf_tensor("B_aoT", [128, 8 * T], BF16))

    with contextlib.ExitStack() as actx:
        gtst = [actx.enter_context(nc.sbuf_tensor(f"B_gt{i}", [128, W_GT], BF16)) for i in range(2)]
        expg = [actx.enter_context(nc.sbuf_tensor(f"B_eg{i}", [128, W_GT], BF16)) for i in range(2)]
        pstage = [actx.enter_context(nc.sbuf_tensor(f"B_pst{i}", [128, 1024], BF16)) for i in range(2)]
        pm_sb = [actx.enter_context(nc.sbuf_tensor(f"B_pm{i}", [128, 1024], BF16)) for i in range(4)]
        recip_sb = actx.enter_context(nc.sbuf_tensor("B_rcp", [1, 512], F32))
        ao_st = actx.enter_context(nc.sbuf_tensor("B_aost", [64, 512], BF16))
        S2 = [actx.enter_context(nc.psum_tensor(f"B_S2{i}", [128, 1024], F32)) for i in range(2)]
        pv = [actx.enter_context(nc.psum_tensor(f"B_pv{i}", [65, 512], F32)) for i in range(2)]
        rb = actx.enter_context(nc.psum_tensor("B_rb", [64, 512], F32))

        gt_ready = [0, 0]
        gt_free = [0, 0]
        bar_att = [(st, st.n), (ss, ss.n), (sv, sv.n)]

        def load_gt(h):
            par = h % 2
            wts = list(bar_att) + ([(ss, gt_free[par])] if gt_free[par] else [])
            ggt[par].load(gtst[par][:, :], hbm["gt"][h, :, :], wait=wts)
            gt_ready[par] = ggt[par].wv()

        load_gt(0)
        load_gt(1)

        expg_free = [0, 0]
        S2_free = [0, 0]
        pst_free = [0, 0]

        pm_free = [0, 0, 0, 0]
        pv_free = 0
        pv_freed = [0, 0]
        rcp_free = 0
        ao_free = 0
        rb_free = 0

        for h in range(NH):
            par = h % 2
            wts = [(ggt[par].sem, gt_ready[par])]
            if expg_free[par]:
                wts.append((sv, expg_free[par]))
            P.emit("scalar", lambda s, p=par: s.activation(expg[p][:, :], gtst[p][:, :], AF.Exp),
                   wait=wts)
            gt_free[par] = ss.n
            expg_v = ss.n
            if h + 2 < NH:
                load_gt(h + 2)
            hb = 64 * (h % 2)
            et = h // 2
            for qc in range(NQC):
                pvp = qc % 2
                mult_vals = [0] * (NKT // 2)

                def emit_pv(ktp, _h=h, _pvp=None):
                    p_ = pvp
                    for u in range(2):
                        kt = 2 * ktp + u
                        wts = [(sv, mult_vals[ktp])]
                        if kt == 0:
                            wts.append((sv, vtm_init))
                            if pv_freed[p_]:
                                wts.append((sv, pv_freed[p_]))
                        P.emit("tensor", lambda t, k=kt, _u=u, p=p_, __h=_h, mp=ktp % 4: t.matmul(
                            pv[p][:, :],
                            VTM[:, 65 * NH * k + 65 * __h: 65 * NH * k + 65 * __h + 65],
                            pm_sb[mp][:, 512 * _u:512 * (_u + 1)],
                            start=(k == 0), stop=(k == NKT - 1)),
                            wait=wts, inc=[(st, 1)] if kt == NKT - 1 else [])
                        pm_free[ktp % 4] = st.n

                for ktp in range(NKT // 2):
                    sp = ktp % 2
                    for u in range(2):
                        kt = 2 * ktp + u
                        wts = []
                        if u == 0 and S2_free[sp]:
                            wts.append((ss, S2_free[sp]))
                        if h == 0 and qc == 0 and ktp == 0 and u == 0:
                            wts.append((st, wproj_done[2]))
                            wts.append((sv, proj_evac_v))
                        P.emit("tensor", lambda t, k=kt, _u=u, p=sp, _qc=qc, _hb=hb, _et=et: t.matmul(
                            S2[p][:, 512 * _u:512 * (_u + 1)],
                            KT[_hb:_hb + 64, T * _et + 128 * k: T * _et + 128 * (k + 1)],
                            QT[_hb:_hb + 64, T * _et + 512 * _qc: T * _et + 512 * (_qc + 1)],
                            start=True, stop=True),
                            wait=wts, inc=[(st, 1)] if u == 1 else [])
                    sc_v = st.n
                    mp = ktp % 4
                    wts = [(st, sc_v), (ss, expg_v)]
                    if pst_free[mp % 2]:
                        wts.append((sv, pst_free[mp % 2]))
                    P.emit("scalar", lambda s, p=sp, _mp=mp % 2: s.activation(
                        pstage[_mp][:, :], S2[p][:, :], AF.Exp),
                        wait=wts)
                    S2_free[sp] = ss.n
                    exp_v = ss.n
                    for u in range(2):
                        kt = 2 * ktp + u
                        off = C0 + 512 * qc - 128 * kt
                        wts = [(ss, exp_v)]
                        if pm_free[mp]:
                            wts.append((st, pm_free[mp]))
                        P.emit("vector", lambda v, _u=u, p=par, o=off, _mp=mp, _ps=mp % 2: v.tensor_mul(
                            pm_sb[_mp][:, 512 * _u:512 * (_u + 1)],
                            pstage[_ps][:, 512 * _u:512 * (_u + 1)],
                            expg[p][:, o:o + 512]),
                            wait=wts)
                    pst_free[mp % 2] = sv.n
                    mult_vals[ktp] = sv.n
                    if ktp >= 3:
                        emit_pv(ktp - 3)
                for _k in range(min(3, NKT // 2), 0, -1):
                    emit_pv(NKT // 2 - _k)
                pv_v = st.n
                pvp = qc % 2
                wts = [(st, pv_v)]
                if rcp_free:
                    wts.append((st, rcp_free))
                P.emit("vector", lambda v, p=pvp: v.reciprocal(recip_sb[:, :], pv[p][64:65, :]),
                       wait=wts)
                rcp_v = sv.n
                wts = [(st, pv_v)]
                if ao_free:
                    wts.append((sv, ao_free))
                P.emit("scalar", lambda s, p=pvp: s.copy(ao_st[:, :], pv[p][0:64, :]),
                       wait=wts)
                ao_v = ss.n
                wts = [(sv, rcp_v), (sv, ones_v)]
                if rb_free:
                    wts.append((sv, rb_free))
                P.emit("tensor", lambda t: t.matmul(
                    rb[:, :], ones1[:, :], recip_sb[:, :], start=True, stop=True),
                    wait=wts, inc=[(st, 1)])
                rcp_free = st.n
                rb_v = st.n
                P.emit("vector", lambda v, _qc=qc, _hb=hb, _et=et: v.tensor_mul(
                    aoT[_hb:_hb + 64, T * _et + 512 * _qc: T * _et + 512 * (_qc + 1)],
                    ao_st[:, :], rb[:, :]),
                    wait=[(st, rb_v), (ss, ao_v)])
                ao_free = sv.n
                rb_free = sv.n
                pv_freed[pvp] = sv.n
            expg_free[par] = sv.n
            fil = bufs.get("filler")
            if fil is not None:
                fil()

    with contextlib.ExitStack() as octx:
        po = [octx.enter_context(nc.psum_tensor(f"B_po{i}", [128, 512], F32)) for i in range(2)]
        ptr3 = [octx.enter_context(nc.psum_tensor(f"B_ptr3{i}", [128, 128], BF16)) for i in range(2)]
        ostage = octx.enter_context(nc.sbuf_tensor("B_ost", [128, 512], BF16))
        stok = octx.enter_context(nc.sbuf_tensor("B_stok", [128, 4 * 1024], BF16))
        x_in = [octx.enter_context(nc.sbuf_tensor(f"B_xi{i}", [128, 1024], F32)) for i in range(2)]
        if g_sb is None:
            g_st = [octx.enter_context(nc.sbuf_tensor(f"B_gst{i}", [128, 1024], BF16)) for i in range(2)]
        else:
            g_st = None
        x1st = [octx.enter_context(nc.sbuf_tensor(f"B_x1{i}", [128, 1024], F32)) for i in range(2)]
        wobuf = octx.enter_context(nc.sbuf_tensor("B_wo", [128, 8 * 1024], BF16))
        bar_o = [(st, st.n), (ss, ss.n), (sv, sv.n)]
        wo_v = gwq[0].load(wobuf[:, :], hbm["woT"][:, :],
                           wait=list(bar_o))
        ao_all = P.sems["v"].n
        po_free = [0, 0]
        ptr3_free = [0, 0]
        ost_free = 0
        stok_free = 0
        xi_free = [0, 0]
        x1_free = [0, 0]
        for qc in range(NQC):
            for et in range(8):
                pa = et % 2
                for dc in range(8):
                    wts = [(gwq[0].sem, wo_v), (sv, ao_all), (ss, bar_o[1][1])]
                    if dc == 0 and po_free[pa]:
                        wts.append((ss, po_free[pa]))
                    P.emit("tensor", lambda t, d=dc, e=et, p=pa, _qc=qc: t.matmul(
                        po[p][:, :],
                        wobuf[:, 1024 * d + 128 * e: 1024 * d + 128 * (e + 1)],
                        aoT[:, T * d + 512 * _qc: T * d + 512 * (_qc + 1)],
                        start=(d == 0), stop=(d == 7)),
                        wait=wts, inc=[(st, 1)] if dc == 7 else [])
                mm_v = st.n
                wts = [(st, mm_v)]
                if ost_free:
                    wts.append((st, ost_free))
                P.emit("scalar", lambda s, p=pa: s.copy(ostage[:, :], po[p][:, :]),
                       wait=wts)
                po_free[pa] = ss.n
                ost_v = ss.n
                for j in range(4):
                    tp3 = j % 2
                    wts = [(ss, ost_v), (gc.sem, CW)]
                    if ptr3_free[tp3]:
                        wts.append((sv, ptr3_free[tp3]))
                    P.emit("tensor", lambda t, _j=j, p=tp3: t.transpose(
                        ptr3[p][:, :], ostage[:, 128 * _j:128 * (_j + 1)], identb[:, :]),
                        wait=wts, inc=[(st, 1)])
                    tr_v = st.n
                    wts = [(st, tr_v)]
                    P.emit("vector", lambda v, _j=j, e=et, p=tp3: v.tensor_copy(
                        stok[:, 1024 * _j + 128 * e: 1024 * _j + 128 * (e + 1)], ptr3[p][:, :]),
                        wait=wts)
                    ptr3_free[tp3] = sv.n
                ost_free = st.n
            stok_v = sv.n
            for j in range(4):
                tt = 4 * qc + j
                par = j % 2
                wts = list(bar_o) + ([(sv, xi_free[par])] if xi_free[par] else [])
                xv = sxc[par].load(x_in[par][:, :], hbm["x"][128 * tt:128 * (tt + 1), :], wait=wts)
                if g_st is not None:
                    gv = sg[par].load(g_st[par][:, :], hbm["g_in"][128 * tt:128 * (tt + 1), :],
                                      wait=wts)
                    g_dep = [(sg[par].sem, gv)]
                    g_ap = g_st[par]
                    g_lo = 0
                else:
                    g_dep = [(P.sems["s"], bufs["g_ready"])]
                    g_ap = g_sb
                    g_lo = 1024 * tt
                wts = [(sv, stok_v), (sxc[par].sem, xv)] + g_dep
                if x1_free[par]:
                    wts.append((so[par].sem, x1_free[par]))
                P.emit("vector", lambda v, _j=j, p=par, ga=g_ap, lo=g_lo: v.tensor_mul(
                    x1st[p][:, :], stok[:, 1024 * _j:1024 * (_j + 1)],
                    ga[:, lo:lo + 1024]),
                    wait=wts)
                P.emit("vector", lambda v, p=par: v.tensor_add(x1st[p][:, :], x1st[p][:, :], x_in[p][:, :]))
                x1_v = sv.n
                xi_free[par] = x1_v
                x1w = so[par].load(hbm["x1_out"][128 * tt:128 * (tt + 1), :], x1st[par][:, :],
                                   wait=[(sv, x1_v)])
                x1_free[par] = x1w
            stok_free = sv.n
        P.emit("sync", lambda s: None, wait=[(so[0].sem, so[0].wv()), (so[1].sem, so[1].wv())])
    return


def build_B(cfg):
    nc = bass.Bass("TRN2", target_bir_lowering=False)
    T = cfg["T"]
    hbm = {}
    hbm["x"] = nc.dram_tensor("x", [T, 1024], F32, kind="ExternalInput").ap()
    hbm["g_in"] = nc.dram_tensor("g_in", [T, 1024], BF16, kind="ExternalInput").ap()
    for nm in ("wqT", "wkT", "wvT", "woT"):
        hbm[nm] = nc.dram_tensor(nm, [128, 8 * 1024], F32, kind="ExternalInput").ap()
    hbm["gt"] = nc.dram_tensor("gt", [16, 128, 2 * T - 128], F32, kind="ExternalInput").ap()
    hbm["ident"] = nc.dram_tensor("ident", [128, 128], F32, kind="ExternalInput").ap()
    hbm["x1_out"] = nc.dram_tensor("x1_out", [T, 1024], F32, kind="ExternalOutput").ap()
    with contextlib.ExitStack() as ctx:
        P = Plan(nc, ctx)
        for s in ("t", "s", "v", "g"):
            P.sem(s)
        bufs = {}
        bufs["ident"] = ctx.enter_context(nc.sbuf_tensor("identb", [128, 128], BF16))
        build_phase_B(P, hbm, bufs, cfg)
        P.run()
    return nc




def build_phase_C(P, hbm, bufs, cfg):
    nc = P.nc
    T = cfg["T"]; NT = T // 128; NDT = 8
    NH = 16
    QC = 512; NQC = T // QC
    NKT = T // 128
    WIN_DELTAS = [-256, -128, 0, 128, 256, 384]
    st = P.sems["t"]; ss = P.sems["s"]; sv = P.sems["v"]
    ctx = P.ctx

    identb = bufs["ident"]
    gc = DmaStream(P, "C_gc", "gpsimd")
    gwq = [DmaStream(P, f"C_gw{i}", "gpsimd") for i in range(2)]
    so = [DmaStream(P, f"C_so{i}", "gpsimd") for i in range(2)]

    QT = ctx.enter_context(nc.sbuf_tensor("C_QT", [128, 8 * T], BF16))
    KT = ctx.enter_context(nc.sbuf_tensor("C_KT", [128, 8 * T], BF16))
    VTM = ctx.enter_context(nc.sbuf_tensor("C_VTM", [128, NKT * (NH * 65)], BF16))
    ones1 = ctx.enter_context(nc.sbuf_tensor("C_ones1", [1, 64], F32))

    gc.load(identb[:, :], hbm["ident"][:, :])
    CW = gc.wv()
    P.emit("vector", lambda v: v.memset(VTM[:, :], 1.0))
    vtm_init = sv.n
    P.emit("vector", lambda v: v.memset(ones1[:, :], 1.0))
    ones_v = sv.n

    with contextlib.ExitStack() as lctx:
        Pctx_saved = P.ctx
        P.ctx = lctx
        xnT = lctx.enter_context(nc.sbuf_tensor("C_xnT", [128, 8 * T], BF16))
        wbuf = [lctx.enter_context(nc.sbuf_tensor(f"C_w{i}", [128, 8 * 1024], BF16)) for i in range(2)]
        xn_done = emit_ln_to_xnT(P, hbm["x1"], xnT, identb, cfg, "C_ln", deps=[(gc.sem, CW)])
        P.ctx = Pctx_saved

        names = ["l1qT", "l1kT", "l1vT"]
        wv_ld = [0, 0, 0]
        wproj_done = [0, 0, 0]
        wv_ld[0] = gwq[0].load(wbuf[0][:, :], hbm[names[0]][:, :])
        wv_ld[1] = gwq[1].load(wbuf[1][:, :], hbm[names[1]][:, :])
        with contextlib.ExitStack() as pctx:
            pp = [pctx.enter_context(nc.psum_tensor(f"C_pp{i}", [128, 512], F32)) for i in range(2)]
            ptr2 = [pctx.enter_context(nc.psum_tensor(f"C_ptr2{i}", [128, 128], BF16)) for i in range(2)]
            vstage = pctx.enter_context(nc.sbuf_tensor("C_vstage", [128, 512], BF16))
            pp_free = [0, 0]
            ptr2_free = [0, 0]
            vstage_free = 0
            for wi, nm in enumerate(names):
                par = wi % 2
                if wi >= 2:
                    wv_ld[wi] = gwq[par].load(wbuf[par][:, :], hbm[names[wi]][:, :],
                                              wait=[(st, wproj_done[wi - 2])])
                for et in range(8):
                    for tcq in range(T // 512):
                        pa = (et * (T // 512) + tcq) % 2
                        for dc in range(8):
                            wts = [(gwq[par].sem, wv_ld[wi]), (sv, xn_done)]
                            if dc == 0 and pp_free[pa]:
                                wts.append((sv, pp_free[pa]))
                            P.emit("tensor", lambda t, d=dc, e=et, q=tcq, p=pa, w=par: t.matmul(
                                pp[p][:, :],
                                wbuf[w][:, 1024 * d + 128 * e: 1024 * d + 128 * (e + 1)],
                                xnT[:, T * d + 512 * q: T * d + 512 * (q + 1)],
                                start=(d == 0), stop=(d == 7)),
                                wait=wts, inc=[(st, 1)] if dc == 7 else [])
                        mm_v = st.n
                        if nm == "l1qT":
                            P.emit("vector", lambda v, e=et, q=tcq, p=pa: v.tensor_copy(
                                QT[:, T * e + 512 * q: T * e + 512 * (q + 1)], pp[p][:, :]),
                                wait=[(st, mm_v)])
                            pp_free[pa] = sv.n
                        elif nm == "l1kT":
                            P.emit("vector", lambda v, e=et, q=tcq, p=pa: v.tensor_copy(
                                KT[:, T * e + 512 * q: T * e + 512 * (q + 1)], pp[p][:, :]),
                                wait=[(st, mm_v)])
                            pp_free[pa] = sv.n
                        else:
                            wts = [(st, mm_v)]
                            if vstage_free:
                                wts.append((st, vstage_free))
                            P.emit("vector", lambda v, p=pa: v.tensor_copy(vstage[:, :], pp[p][:, :]),
                                   wait=wts)
                            pp_free[pa] = sv.n
                            vst_v = sv.n
                            for j in range(4):
                                tp2 = j % 2
                                wts = [(sv, vst_v), (gc.sem, CW)]
                                if ptr2_free[tp2]:
                                    wts.append((sv, ptr2_free[tp2]))
                                P.emit("tensor", lambda t, _j=j, p=tp2: t.transpose(
                                    ptr2[p][:, :], vstage[:, 128 * _j:128 * (_j + 1)], identb[:, :]),
                                    wait=wts, inc=[(st, 1)])
                                tr_v = st.n
                                kt_idx = tcq * 4 + j
                                for half in range(2):
                                    h_idx = 2 * et + half
                                    P.emit("vector", lambda v, k=kt_idx, h=h_idx, p=tp2, hf=half: v.tensor_copy(
                                        VTM[:, 65 * NH * k + 65 * h: 65 * NH * k + 65 * h + 64],
                                        ptr2[p][:, 64 * hf:64 * (hf + 1)]),
                                        wait=[(st, tr_v), (sv, vtm_init)])
                                ptr2_free[tp2] = sv.n
                            vstage_free = st.n
                wproj_done[wi] = st.n
        proj_evac_v = P.sems["v"].n

    aoT = ctx.enter_context(nc.sbuf_tensor("C_aoT", [128, 8 * T], BF16))

    with contextlib.ExitStack() as actx:
        m01 = actx.enter_context(nc.sbuf_tensor("C_m01", [128, 6 * 512], BF16))
        gc.load(m01.rearrange("p (d n) -> p d n", d=6), hbm["m01"][:, :, :].rearrange("d p n -> p d n"),
                wait=[(st, st.n), (ss, ss.n), (sv, sv.n)])
        CWM = gc.wv()
        pstage = [actx.enter_context(nc.sbuf_tensor(f"C_pst{i}", [128, 1024], BF16)) for i in range(2)]
        pm_sb = [actx.enter_context(nc.sbuf_tensor(f"C_pm{i}", [128, 1024], BF16)) for i in range(4)]
        recip_sb = actx.enter_context(nc.sbuf_tensor("C_rcp", [1, 512], F32))
        ao_st = actx.enter_context(nc.sbuf_tensor("C_aost", [64, 512], BF16))
        S2 = [actx.enter_context(nc.psum_tensor(f"C_S2{i}", [128, 1024], F32)) for i in range(2)]
        pv = [actx.enter_context(nc.psum_tensor(f"C_pv{i}", [65, 512], F32)) for i in range(2)]
        rb = actx.enter_context(nc.psum_tensor("C_rb", [64, 512], F32))
        bar_att = [(st, st.n), (ss, ss.n), (sv, sv.n)]

        S2_free = [0, 0]
        pst_free = [0, 0]
        pm_free = [0, 0, 0, 0]
        pv_freed = [0, 0]
        rcp_free = 0
        ao_free = 0
        rb_free = 0

        for h in range(NH):
            hb = 64 * (h % 2)
            et = h // 2
            for qc in range(NQC):
                kts = [(di, (512 * qc + d) // 128) for di, d in enumerate(WIN_DELTAS)
                       if 0 <= 512 * qc + d < T]
                npair = (len(kts) + 1) // 2
                pvp = qc % 2
                mult_vals = [0] * npair

                def emit_pv(pi, pairs=None):
                    pair = kts[2 * pi: 2 * pi + 2]
                    for u, (di, kt) in enumerate(pair):
                        first = (pi == 0 and u == 0)
                        last = (pi == npair - 1 and u == len(pair) - 1)
                        wts = [(sv, mult_vals[pi])]
                        if first:
                            wts.append((sv, vtm_init))
                            if pv_freed[pvp]:
                                wts.append((sv, pv_freed[pvp]))
                        P.emit("tensor", lambda t, k=kt, _u=u, p=pvp, _h=h, f=first, l2=last, mp=pi % 4: t.matmul(
                            pv[p][:, :],
                            VTM[:, 65 * NH * k + 65 * _h: 65 * NH * k + 65 * _h + 65],
                            pm_sb[mp][:, 512 * _u:512 * (_u + 1)],
                            start=f, stop=l2),
                            wait=wts, inc=[(st, 1)] if last else [])
                        pm_free[pi % 4] = st.n

                for pi in range(npair):
                    pair = kts[2 * pi: 2 * pi + 2]
                    sp = pi % 2
                    for u, (di, kt) in enumerate(pair):
                        wts = []
                        if u == 0 and S2_free[sp]:
                            wts.append((ss, S2_free[sp]))
                        if h == 0 and qc == 0 and pi == 0 and u == 0:
                            wts.append((st, wproj_done[2]))
                            wts.append((sv, proj_evac_v))
                        P.emit("tensor", lambda t, k=kt, _u=u, p=sp, _qc=qc, _hb=hb, _et=et: t.matmul(
                            S2[p][:, 512 * _u:512 * (_u + 1)],
                            KT[_hb:_hb + 64, T * _et + 128 * k: T * _et + 128 * (k + 1)],
                            QT[_hb:_hb + 64, T * _et + 512 * _qc: T * _et + 512 * (_qc + 1)],
                            start=True, stop=True),
                            wait=wts, inc=[(st, 1)] if u == len(pair) - 1 else [])
                    sc_v = st.n
                    mp = pi % 4
                    wts = [(st, sc_v), (gc.sem, CW)]
                    if pst_free[mp % 2]:
                        wts.append((sv, pst_free[mp % 2]))
                    wdt = 512 * len(pair)
                    P.emit("scalar", lambda s, p=sp, w=wdt, _mp=mp % 2: s.activation(
                        pstage[_mp][:, 0:w], S2[p][:, 0:w], AF.Exp, scale=0.125),
                        wait=wts)
                    S2_free[sp] = ss.n
                    exp_v = ss.n
                    for u, (di, kt) in enumerate(pair):
                        wts = [(ss, exp_v), (gc.sem, CWM)]
                        if pm_free[mp]:
                            wts.append((st, pm_free[mp]))
                        P.emit("vector", lambda v, _u=u, _di=di, _mp=mp, _ps=mp % 2: v.tensor_mul(
                            pm_sb[_mp][:, 512 * _u:512 * (_u + 1)],
                            pstage[_ps][:, 512 * _u:512 * (_u + 1)],
                            m01[:, 512 * _di:512 * (_di + 1)]),
                            wait=wts)
                    pst_free[mp % 2] = sv.n
                    mult_vals[pi] = sv.n
                for pi in range(npair):
                    emit_pv(pi)
                pv_v = st.n
                pvp = qc % 2
                wts = [(st, pv_v)]
                if rcp_free:
                    wts.append((st, rcp_free))
                P.emit("vector", lambda v, p=pvp: v.reciprocal(recip_sb[:, :], pv[p][64:65, :]),
                       wait=wts)
                rcp_v = sv.n
                wts = [(st, pv_v)]
                if ao_free:
                    wts.append((sv, ao_free))
                P.emit("scalar", lambda s, p=pvp: s.copy(ao_st[:, :], pv[p][0:64, :]),
                       wait=wts)
                ao_v = ss.n
                wts = [(sv, rcp_v), (sv, ones_v)]
                if rb_free:
                    wts.append((sv, rb_free))
                P.emit("tensor", lambda t: t.matmul(
                    rb[:, :], ones1[:, :], recip_sb[:, :], start=True, stop=True),
                    wait=wts, inc=[(st, 1)])
                rcp_free = st.n
                rb_v = st.n
                P.emit("vector", lambda v, _qc=qc, _hb=hb, _et=et: v.tensor_mul(
                    aoT[_hb:_hb + 64, T * _et + 512 * _qc: T * _et + 512 * (_qc + 1)],
                    ao_st[:, :], rb[:, :]),
                    wait=[(st, rb_v), (ss, ao_v)])
                ao_free = sv.n
                rb_free = sv.n
                pv_freed[pvp] = sv.n
            fil = bufs.get("filler")
            if fil is not None:
                fil()

    with contextlib.ExitStack() as octx:
        po = [octx.enter_context(nc.psum_tensor(f"C_po{i}", [128, 512], F32)) for i in range(2)]
        ptr3 = [octx.enter_context(nc.psum_tensor(f"C_ptr3{i}", [128, 128], BF16)) for i in range(2)]
        ostage = octx.enter_context(nc.sbuf_tensor("C_ost", [128, 512], BF16))
        stok = octx.enter_context(nc.sbuf_tensor("C_stok", [128, 4 * 1024], BF16))
        wobuf = octx.enter_context(nc.sbuf_tensor("C_wo", [128, 8 * 1024], BF16))
        bar_o = [(st, st.n), (ss, ss.n), (sv, sv.n)]
        wo_v = gwq[0].load(wobuf[:, :], hbm["l1oT"][:, :], wait=list(bar_o))
        ao_all = P.sems["v"].n
        po_free = [0, 0]
        ptr3_free = [0, 0]
        ost_free = 0
        stok_free = [0, 0]
        for qc in range(NQC):
            for et in range(8):
                pa = et % 2
                for dc in range(8):
                    wts = [(gwq[0].sem, wo_v), (sv, ao_all), (ss, bar_o[1][1])]
                    if dc == 0 and po_free[pa]:
                        wts.append((ss, po_free[pa]))
                    P.emit("tensor", lambda t, d=dc, e=et, p=pa, _qc=qc: t.matmul(
                        po[p][:, :],
                        wobuf[:, 1024 * d + 128 * e: 1024 * d + 128 * (e + 1)],
                        aoT[:, T * d + 512 * _qc: T * d + 512 * (_qc + 1)],
                        start=(d == 0), stop=(d == 7)),
                        wait=wts, inc=[(st, 1)] if dc == 7 else [])
                mm_v = st.n
                wts = [(st, mm_v)]
                if ost_free:
                    wts.append((st, ost_free))
                P.emit("scalar", lambda s, p=pa: s.copy(ostage[:, :], po[p][:, :]),
                       wait=wts)
                po_free[pa] = ss.n
                ost_v = ss.n
                for j in range(4):
                    tp3 = j % 2
                    wts = [(ss, ost_v), (gc.sem, CW)]
                    if ptr3_free[tp3]:
                        wts.append((sv, ptr3_free[tp3]))
                    P.emit("tensor", lambda t, _j=j, p=tp3: t.transpose(
                        ptr3[p][:, :], ostage[:, 128 * _j:128 * (_j + 1)], identb[:, :]),
                        wait=wts, inc=[(st, 1)])
                    tr_v = st.n
                    wts = [(st, tr_v)]
                    par_t = j % 2
                    if et == 0 and stok_free[par_t]:
                        wts.append((so[par_t].sem, stok_free[par_t]))
                    P.emit("vector", lambda v, _j=j, e=et, p=tp3: v.tensor_copy(
                        stok[:, 1024 * _j + 128 * e: 1024 * _j + 128 * (e + 1)], ptr3[p][:, :]),
                        wait=wts)
                    ptr3_free[tp3] = sv.n
                ost_free = st.n
            stok_v = sv.n
            for j in range(4):
                tt = 4 * qc + j
                par = j % 2
                stok_free[par] = so[par].load(
                    hbm["s1_out"][128 * tt:128 * (tt + 1), :],
                    stok[:, 1024 * j:1024 * (j + 1)], wait=[(sv, stok_v)])
        P.emit("sync", lambda s: None, wait=[(so[0].sem, so[0].wv()), (so[1].sem, so[1].wv())])
    return


def build_C(cfg):
    nc = bass.Bass("TRN2", target_bir_lowering=False)
    T = cfg["T"]
    hbm = {}
    hbm["x1"] = nc.dram_tensor("x1", [T, 1024], F32, kind="ExternalInput").ap()
    for nm in ("l1qT", "l1kT", "l1vT", "l1oT"):
        hbm[nm] = nc.dram_tensor(nm, [128, 8 * 1024], F32, kind="ExternalInput").ap()
    hbm["m01"] = nc.dram_tensor("m01", [6, 128, 512], F32, kind="ExternalInput").ap()
    hbm["ident"] = nc.dram_tensor("ident", [128, 128], F32, kind="ExternalInput").ap()
    hbm["s1_out"] = nc.dram_tensor("s1_out", [T, 1024], BF16, kind="ExternalOutput").ap()
    with contextlib.ExitStack() as ctx:
        P = Plan(nc, ctx)
        for s in ("t", "s", "v", "g"):
            P.sem(s)
        bufs = {}
        bufs["ident"] = ctx.enter_context(nc.sbuf_tensor("identb", [128, 128], BF16))
        build_phase_C(P, hbm, bufs, cfg)
        P.run()
    return nc




class WeightQuantEmitter:

    def __init__(self, P, hbm, cfg, identb):
        self.P = P
        self.bufs_ident = identb
        nc = P.nc
        I = cfg.get("I", 2816)
        ctx = P.ctx
        self.gin = [DmaStream(P, f"WQ_in{i}", "sync") for i in range(2)]
        self.sout = [DmaStream(P, f"WQ_out{i}", "sync") for i in range(2)]
        self.wtile = [ctx.enter_context(nc.sbuf_tensor(f"WQ_wt{i}", [128, 512], F32, side="right"))
                      for i in range(4)]
        WMX = max(1024, I)
        self.wq_t = [ctx.enter_context(nc.sbuf_tensor(f"WQ_q{i}", [128, WMX], BF16, side="right"))
                     for i in range(2)]

        qs = {}
        qs["t_s"] = ctx.enter_context(nc.sbuf_tensor("WQ_ts", [128, 32], F32, side="right"))
        qs["rcp"] = ctx.enter_context(nc.sbuf_tensor("WQ_rcp", [128, 32], F32, side="right"))
        for nm in ("absx", "y", "accA", "accC", "msf", "c2"):
            qs[nm] = ctx.enter_context(nc.sbuf_tensor(f"WQ_{nm}", [128, 512], F32, side="right"))
        qs["u"] = qs["absx"]
        self.qs = qs
        self._pctx = contextlib.ExitStack()
        self.jobs = []
        for nt in range(I // 128):
            self.jobs.append(("wgate", "wgq_nat", nt, 1024))
        for nt in range(I // 128):
            self.jobs.append(("wup", "wuq_nat", nt, 1024))
        for nt in range(8):
            self.jobs.append(("wdown", "wdq_nat", nt, I))
        self.hbm = hbm
        self.qcur = 0
        self.ccur = 0
        self.loaded = {}
        self.qvals = {}
        self.wt_free = [(0, 0)] * 4
        self.wq_free = [0, 0]
        self.done_dep = None

    def _load_job(self, j):
        if j >= len(self.jobs) or j in self.loaded:
            return
        src_name, dst, nt, width = self.jobs[j]
        P = self.P
        ss = P.sems["s"]; sv = P.sems["v"]
        nch = (width + 511) // 512
        vals = []
        for c in range(nch):
            w2 = min(512, width - 512 * c)
            bi = (2 * j + c) % 4
            buf = self.wtile[bi]
            fs, fv = self.wt_free[bi]
            wts = ([(ss, fs)] if fs else []) + ([(sv, fv)] if fv else [])
            lv = self.gin[j % 2].load(buf[:, 0:w2],
                                      self.hbm[src_name][128 * nt:128 * (nt + 1), 512 * c:512 * c + w2],
                                      wait=wts)
            vals.append(lv)
        self.loaded[j] = vals

    def emit_chunk(self):
        j = self.qcur
        if j >= len(self.jobs):
            return False
        P = self.P
        src_name, dst, nt, width = self.jobs[j]
        self._load_job(j)
        self._load_job(j + 1)
        nch = (width + 511) // 512
        c = self.ccur
        qp = j % 2
        w2 = min(512, width - 512 * c)
        bi = (2 * j + c) % 4
        buf = self.wtile[bi]
        wts = [(self.gin[j % 2].sem, self.loaded[j][-1])]
        if self.wq_free[qp] and c == 0:
            wts.append((self.sout[qp].sem, self.wq_free[qp]))
        qv = emit_quant(P, self.wq_t[qp][:, 512 * c:512 * c + w2], buf[:, 0:w2], w2, self.qs,
                        src_ready=wts)
        self.wt_free[bi] = (self.qs["last_scalar"], self.qs["src_free_v"])
        self.ccur += 1
        if self.ccur >= nch:
            self.wq_free[qp] = self.sout[qp].load(
                self.hbm[dst][128 * nt:128 * (nt + 1), 0:width],
                self.wq_t[qp][:, 0:width], wait=[qv])
            self.qcur += 1
            self.ccur = 0
            if self.qcur >= len(self.jobs):
                self._pctx.close()
                self.done_dep = [(self.sout[0].sem, self.sout[0].wv()),
                                 (self.sout[1].sem, self.sout[1].wv())]
        return True

    def emit_job(self):
        j = self.qcur
        if j >= len(self.jobs):
            return False
        while self.qcur == j:
            self.emit_chunk()
        return True

    def finish(self):
        while self.emit_job():
            pass
        return self.done_dep



def _hap(h, pairs):
    return bass.AP(h, 0, [[h.shape[1], 128]] + pairs)


def emit_quant(P, dst, src, width, qs, src_ready=(), extra_dst_wait=()):
    nc = P.nc
    ss = P.sems["s"]; sv = P.sems["v"]
    nb = width // 16
    src3 = bass.AP(src.tensor, src.offset,
                   [list(src.ap[0]), [16, nb], [1, 16]])
    ts = qs["t_s"]; rcp = qs["rcp"]
    P.emit("vector", lambda v: v.reduce_max(ts[:, 0:nb], src3, AX.X,
                                            apply_absolute_value=True),
           wait=list(src_ready))
    qs["src_free_v"] = sv.n
    P.emit("vector", lambda v: v.tensor_scalar(ts[:, 0:nb], ts[:, 0:nb],
                                               1.0 / 6.0, 1e-12, ALU.mult, ALU.max))
    P.emit("vector", lambda v: v.reciprocal(rcp[:, 0:nb], ts[:, 0:nb]))
    rcp_v = sv.n
    P.emit("scalar", lambda s: s.activation(qs["absx"][:, 0:width], src, AF.Abs),
           wait=list(src_ready) + ([(sv, qs["absx_free"])] if qs.get("absx_free") else []))
    absx_v = ss.n
    rcp_b = _hap(rcp, [[1, nb], [0, 16]])
    y3 = _hap(qs["y"], [[16, nb], [1, 16]])
    absx3 = _hap(qs["absx"], [[16, nb], [1, 16]])
    P.emit("vector", lambda v: v.tensor_mul(y3, absx3, rcp_b),
           wait=[(ss, absx_v)])
    qs["absx_free"] = sv.n
    y = qs["y"]; accA = qs["accA"]; accC = qs["accC"]
    u = qs["u"]; c2 = qs["c2"]
    W = width
    P.emit("vector", lambda v: v.tensor_single_scalar(
        accA[:, 0:W], y[:, 0:W], 0.25, ALU.is_gt))
    for c in (0.75, 1.25, 1.75):
        P.emit("vector", lambda v, _c=c: v.scalar_tensor_tensor(
            accA[:, 0:W], y[:, 0:W], _c, accA[:, 0:W], ALU.is_gt, ALU.add))
    P.emit("vector", lambda v: v.tensor_single_scalar(
        accC[:, 0:W], y[:, 0:W], 2.5, ALU.is_gt))
    P.emit("vector", lambda v: v.scalar_tensor_tensor(
        accC[:, 0:W], y[:, 0:W], 3.5, accC[:, 0:W], ALU.is_gt, ALU.add))
    P.emit("vector", lambda v: v.tensor_single_scalar(
        c2[:, 0:W], y[:, 0:W], 5.0, ALU.is_gt))
    P.emit("scalar", lambda s: s.activation(qs["msf"][:, 0:width], src, AF.Sign),
           wait=list(src_ready) + ([(sv, qs["msf_free"])] if qs.get("msf_free") else []))
    sgn_v = ss.n
    qs["last_scalar"] = sgn_v
    ts_b = _hap(ts, [[1, nb], [0, 16]])
    msf3 = _hap(qs["msf"], [[16, nb], [1, 16]])
    P.emit("vector", lambda v: v.tensor_mul(msf3, msf3, ts_b), wait=[(ss, sgn_v)])
    P.emit("vector", lambda v: v.scalar_tensor_tensor(
        u[:, 0:W], accA[:, 0:W], 0.5, accC[:, 0:W], ALU.mult, ALU.add))
    P.emit("vector", lambda v: v.scalar_tensor_tensor(
        u[:, 0:W], c2[:, 0:W], 2.0, u[:, 0:W], ALU.mult, ALU.add))
    P.emit("vector", lambda v: v.tensor_mul(dst, u[:, 0:W], qs["msf"][:, 0:width]),
           wait=list(extra_dst_wait))
    qs["msf_free"] = sv.n
    qs["absx_free"] = sv.n
    return (sv, sv.n)


_QS_N = [0]


def mk_quant_scratch(P, width):
    ctx = P.ctx
    nc = P.nc
    _QS_N[0] += 1
    tag = _QS_N[0]
    qs = {}
    qs["t_s"] = ctx.enter_context(nc.sbuf_tensor(f"q{tag}_ts", [128, width // 16], F32))
    qs["rcp"] = ctx.enter_context(nc.sbuf_tensor(f"q{tag}_rcp", [128, width // 16], F32))
    for nm in ("absx", "y", "accA", "accC", "msf", "c2"):
        qs[nm] = ctx.enter_context(nc.sbuf_tensor(f"q{tag}_{nm}", [128, width], F32))
    qs["u"] = qs["absx"]
    return qs


def emit_sharded_wq(P, hbm, bufs, cfg, ident_dep=None):
    nc = P.nc
    I = cfg.get("I", 2816)
    NI = I // 128
    st = P.sems["t"]; ss = P.sems["s"]; sv = P.sems["v"]
    ctx = P.ctx
    identb = bufs["ident"]
    with contextlib.ExitStack() as wctx:
        saved = P.ctx
        P.ctx = wctx
        gin = [DmaStream(P, f"SW_in{i}", "sync") for i in range(2)]
        sosc = [DmaStream(P, f"SW_out{i}", "sync") for i in range(2)]
        srs = DmaStream(P, "SW_rs", "sync")
        ccs = P.sem("SW_cc")
        WMX = max(1024, I)
        wtile = [wctx.enter_context(nc.sbuf_tensor(f"SW_wt{i}", [128, WMX], F32)) for i in range(2)]
        wq_t = wctx.enter_context(nc.sbuf_tensor("SW_q", [128, WMX], BF16))
        stage = [wctx.enter_context(nc.sbuf_tensor(f"SW_st{i}", [128, max(8, NI) * 128], BF16)) for i in range(2)]
        qs = mk_quant_scratch(P, 1024)
        ptrw = [wctx.enter_context(nc.psum_tensor(f"SW_ptr{i}", [128, 128], BF16)) for i in range(2)]
        bar_w = [(st, st.n), (ss, ss.n), (sv, sv.n)]
        wt_free = [(0, 0), (0, 0)]
        wq_free = 0
        ptrw_free = [0, 0]
        stage_free = [0, 0]

        jobs = []
        for s in range(6):
            jobs.append((hbm["wjob_a"][s, :, :], 1024 * s, 1024))
        jobs.append((hbm["wjob_b"][:, :], 6144, I))

        for ji, (srcap, orow, width) in enumerate(jobs):
            par = ji % 2
            nkt = width // 128
            fs, fv = wt_free[par]
            wts = list(bar_w) + ([(ss, fs)] if fs else []) + ([(sv, fv)] if fv else [])
            lv = gin[par].load(wtile[par][:, 0:width], srcap, wait=wts)
            for qoff in range(0, width, 1024):
                w2 = min(1024, width - qoff)
                qv = emit_quant(P, wq_t[:, qoff:qoff + w2],
                                wtile[par][:, qoff:qoff + w2], w2, qs,
                                src_ready=[(gin[par].sem, lv)],
                                extra_dst_wait=([(st, wq_free)] if (wq_free and qoff == 0) else []))
            wt_free[par] = (qs["last_scalar"], qs["src_free_v"])
            spar = ji % 2
            for kt in range(nkt):
                tp = kt % 2
                wts = [qv] + ([ident_dep] if ident_dep else [])
                if ptrw_free[tp]:
                    wts.append((sv, ptrw_free[tp]))
                P.emit("tensor", lambda t, k=kt, p=tp, _pa=par: t.transpose(
                    ptrw[p][:, :], wq_t[:, 128 * k:128 * (k + 1)], identb[:, :]),
                    wait=wts, inc=[(st, 1)])
                tr_v = st.n
                wts = [(st, tr_v)]
                if kt == 0 and stage_free[spar]:
                    wts.append((sosc[spar].sem, stage_free[spar]))
                P.emit("vector", lambda v, k=kt, p=tp, sp=spar: v.tensor_copy(
                    stage[sp][:, 128 * k:128 * (k + 1)], ptrw[p][:, :]),
                    wait=wts)
                ptrw_free[tp] = sv.n
            wq_free = st.n
            stg_v = sv.n
            dst = hbm["wq_outc"].rearrange("(r p) c -> p r c", p=128)[:, orow // 128: orow // 128 + nkt, :]
            stage_free[spar] = sosc[spar].load(
                dst, stage[spar].rearrange("p (k n) -> p k n", n=128)[:, 0:nkt, :],
                wait=[(sv, stg_v)])

        ag_wait = [(sosc[0].sem, sosc[0].wv()), (sosc[1].sem, sosc[1].wv())]
        P.emit("gpsimd", lambda g: g.collective_compute(
            "AllGather", ALU.bypass, replica_groups=[list(range(NCORES))],
            ins=[hbm["wq_outc"]], outs=[hbm["wq_gath"]]),
            wait=ag_wait, inc=[(ccs, 16)], is_dma=True)
        agv = ccs.n

        blk = (6 * 1024 + I) // 128
        gath = hbm["wq_gath"].rearrange("(r p) c -> r p c", p=128)
        for ja in range(2 * NI):
            c, s = divmod(ja, 6)
            r0 = c * blk + s * 8
            dstname = "wgq_scr" if ja < NI else "wuq_scr"
            nt = ja % NI
            srs.load(hbm[dstname][:, :, 128 * nt:128 * (nt + 1)],
                     gath[r0:r0 + 8, :, :], wait=[(ccs, agv)])
        for c in range(8):
            r0 = c * blk + 48
            srs.load(hbm["wdq_scr"][:, :, 128 * c:128 * (c + 1)],
                     gath[r0:r0 + NI, :, :], wait=[(ccs, agv)])
        dep = [(srs.sem, srs.wv())]
        P.ctx = saved
    return dep


def build_phase_D(P, hbm, bufs, cfg):
    nc = P.nc
    T = cfg["T"]; NT = T // 128
    I = cfg.get("I", 2816)
    NI = I // 128
    st = P.sems["t"]; ss = P.sems["s"]; sv = P.sems["v"]
    ctx = P.ctx
    identb = bufs["ident"]

    gcD = DmaStream(P, "D_gc", "gpsimd")
    gcD.load(identb[:, :], hbm["ident"][:, :])
    CW = gcD.wv()

    xqT = ctx.enter_context(nc.sbuf_tensor("D_xqT", [128, 8 * T], BF16))
    out_acc = ctx.enter_context(nc.sbuf_tensor("D_oacc", [128, NT * 1024], F32))

    with contextlib.ExitStack() as lctx:
        Ps = P.ctx; P.ctx = lctx
        sx = [DmaStream(P, f"D_sx{i}", "sync") for i in range(2)]
        x_t = [lctx.enter_context(nc.sbuf_tensor(f"D_x{i}", [128, 1024], F32)) for i in range(2)]
        xsq = lctx.enter_context(nc.sbuf_tensor("D_xsq", [128, 1024], F32))
        stat4 = lctx.enter_context(nc.sbuf_tensor("D_stat", [128, 8], F32))
        xn_tok = lctx.enter_context(nc.sbuf_tensor("D_xntok", [128, 1024], F32))
        xq_tok = lctx.enter_context(nc.sbuf_tensor("D_xqtok", [128, 1024], BF16))
        eps_col = lctx.enter_context(nc.sbuf_tensor("D_eps", [128, 1], F32))
        qs = mk_quant_scratch(P, 1024)
        ptr = [lctx.enter_context(nc.psum_tensor(f"D_ptr{i}", [128, 128], BF16)) for i in range(2)]
        P.emit("vector", lambda v: v.memset(eps_col[:, :], 1e-5))
        eps_v = sv.n
        x_free = [0, 0]
        xsq_free = 0
        xn_free = 0
        xq_free = 0
        ptr_free = [0, 0]
        for tt in range(NT):
            par = tt % 2
            wts = ([(sv, x_free[par])] if x_free[par] else [])
            xv = sx[par].load(x_t[par][:, :], hbm["x1"][128 * tt:128 * (tt + 1), :], wait=wts)
            xw = (sx[par].sem, xv)
            P.emit("vector", lambda v, p=par: v.reduce_sum(stat4[:, 0:1], x_t[p][:, :], AX.X),
                   wait=[xw])
            swts = [xw] + ([(sv, xsq_free)] if xsq_free else [])
            P.emit("scalar", lambda s, p=par: s.activation(xsq[:, :], x_t[p][:, :], AF.Square),
                   wait=swts)
            sq_done = ss.n
            P.emit("vector", lambda v: v.reduce_sum(stat4[:, 1:2], xsq[:, :], AX.X),
                   wait=[(ss, sq_done)])
            xsq_free = sv.n
            P.emit("vector", lambda v: v.tensor_scalar_mul(stat4[:, 2:3], stat4[:, 0:1], 1.0 / 1024.0))
            P.emit("vector", lambda v: v.tensor_scalar_mul(stat4[:, 3:4], stat4[:, 1:2], 1.0 / 1024.0))
            P.emit("vector", lambda v: v.scalar_tensor_tensor(
                stat4[:, 4:5], stat4[:, 2:3], -1.0, stat4[:, 2:3], ALU.mult, ALU.mult))
            P.emit("vector", lambda v: v.tensor_add(stat4[:, 5:6], stat4[:, 3:4], stat4[:, 4:5]))
            var_v = sv.n
            P.emit("scalar", lambda s: s.activation(stat4[:, 6:7], stat4[:, 5:6], AF.Sqrt, bias=eps_col[:, :]),
                   wait=[(sv, var_v), (sv, eps_v)])
            sd_v = ss.n
            P.emit("vector", lambda v: v.reciprocal(stat4[:, 7:8], stat4[:, 6:7]),
                   wait=[(ss, sd_v)])
            rstd_v = sv.n
            wts = [(sv, rstd_v)] + ([(ss, xn_free)] if xn_free else [])
            P.emit("vector", lambda v, p=par: v.tensor_scalar(
                xn_tok[:, :], x_t[p][:, :], stat4[:, 2:3], stat4[:, 7:8], ALU.subtract, ALU.mult),
                wait=wts)
            xn_v = sv.n
            x_free[par] = xn_v
            qv = emit_quant(P, xq_tok[:, :], xn_tok[:, :], 1024, qs,
                            src_ready=[(sv, xn_v)],
                            extra_dst_wait=([(st, xq_free)] if xq_free else []))
            xn_free = qs["last_scalar"]
            for ft in range(8):
                tp = ft % 2
                wts = [qv]
                if ptr_free[tp]:
                    wts.append((sv, ptr_free[tp]))
                wts.append((gcD.sem, CW))
                P.emit("tensor", lambda t, f=ft, p=tp: t.transpose(
                    ptr[p][:, :], xq_tok[:, 128 * f:128 * (f + 1)], identb[:, :]),
                    wait=wts, inc=[(st, 1)])
                tr_v = st.n
                P.emit("vector", lambda v, f=ft, p=tp, _tt=tt: v.tensor_copy(
                    xqT[:, T * f + 128 * _tt: T * f + 128 * (_tt + 1)], ptr[p][:, :]),
                    wait=[(st, tr_v)])
                ptr_free[tp] = sv.n
            xq_free = st.n
        P.ctx = Ps
        xq_done_v = sv.n

    if bufs.get("wq_done_dep") is not None:
        wq_all_done = bufs["wq_done_dep"]
    else:
        _wq = WeightQuantEmitter(P, hbm, cfg, identb)
        wq_all_done = _wq.finish()

    chunks = []
    off = 0
    while off < I:
        w = min(512, I - off)
        chunks.append((off, w))
        off += w
    with contextlib.ExitStack() as mctx:
        Ps = P.ctx; P.ctx = mctx
        gwd = [DmaStream(P, f"D_gwd{i}", "sync") for i in range(2)]
        wgq_sb = [mctx.enter_context(nc.sbuf_tensor(f"D_wgq{i}", [128, 8 * 512], BF16)) for i in range(2)]
        wuq_sb = [mctx.enter_context(nc.sbuf_tensor(f"D_wuq{i}", [128, 8 * 512], BF16)) for i in range(2)]
        wdq_sb = [mctx.enter_context(nc.sbuf_tensor(f"D_wdq{i}", [128, 4 * 1024], BF16)) for i in range(2)]
        sg_st = mctx.enter_context(nc.sbuf_tensor("D_sg", [128, 512], F32))
        h_st = mctx.enter_context(nc.sbuf_tensor("D_h", [128, 512], F32))
        hq_st = [mctx.enter_context(nc.sbuf_tensor(f"D_hq{i}", [128, 512], BF16)) for i in range(2)]
        hqT_st = mctx.enter_context(nc.sbuf_tensor("D_hqT", [128, 4 * 128], BF16))
        qs3 = mk_quant_scratch(P, 512)
        pg = mctx.enter_context(nc.psum_tensor("D_pg", [128, 512], F32))
        pu = mctx.enter_context(nc.psum_tensor("D_pu", [128, 512], F32))
        pd = [mctx.enter_context(nc.psum_tensor(f"D_pd{i}", [128, 1024], F32)) for i in range(2)]
        ptrh = [mctx.enter_context(nc.psum_tensor(f"D_ptrh{i}", [128, 128], BF16)) for i in range(2)]
        bar_m = [(st, st.n), (ss, ss.n), (sv, sv.n)]

        w_ready = {}
        w_free = {0: 0, 1: 0}

        def load_wchunk(ci):
            offc, wc = chunks[ci]
            par = ci % 2
            nt0 = offc // 128
            wnt = wc // 128
            wts = list(bar_m) + list(wq_all_done) + ([(st, w_free[par])] if w_free[par] else [])
            for kt in range(8):
                gwd[par].load(wgq_sb[par][:, 512 * kt:512 * kt + wc],
                              hbm["wgq_nat"][offc:offc + wc, 128 * kt:128 * (kt + 1)],
                              wait=wts, transpose=True)
                gwd[par].load(wuq_sb[par][:, 512 * kt:512 * kt + wc],
                              hbm["wuq_nat"][offc:offc + wc, 128 * kt:128 * (kt + 1)],
                              transpose=True)
            for nl in range(wnt):
                gwd[par].load(wdq_sb[par][:, 1024 * nl:1024 * (nl + 1)],
                              hbm["wdq_nat"][:, 128 * (nt0 + nl):128 * (nt0 + nl + 1)],
                              transpose=True)
            w_ready[ci] = gwd[par].wv()

        load_wchunk(0)
        if len(chunks) > 1:
            load_wchunk(1)

        pg_free = 0
        pu_free = 0
        pd_free = [0, 0]
        sg_free = 0
        h_free = 0
        hq_free = [0, 0]
        hqT_free = 0
        ptrh_free = [0, 0]
        oacc_ready = [0] * NT

        for ci, (offc, wc) in enumerate(chunks):
            par = ci % 2
            wnt = wc // 128
            wv_ = (gwd[par].sem, w_ready[ci])
            gu_vals = [0] * NT
            dmm_vals = [0] * NT

            def emit_head(tt):
                nonlocal pg_free, pu_free
                for k in range(8):
                    wts = [wv_, (sv, xq_done_v)]
                    if k == 0 and pg_free:
                        wts.append((sv, pg_free))
                    P.emit("tensor", lambda t, _k=k, _tt=tt, p=par, w=wc: t.matmul(
                        pg[:, 0:w],
                        xqT[:, T * _k + 128 * _tt: T * _k + 128 * (_tt + 1)],
                        wgq_sb[p][:, 512 * _k: 512 * _k + w],
                        start=(_k == 0), stop=(_k == 7)),
                        wait=wts, inc=[(st, 1)] if k == 7 else [])
                    wts2 = []
                    if k == 0 and pu_free:
                        wts2.append((sv, pu_free))
                    P.emit("tensor", lambda t, _k=k, _tt=tt, p=par, w=wc: t.matmul(
                        pu[:, 0:w],
                        xqT[:, T * _k + 128 * _tt: T * _k + 128 * (_tt + 1)],
                        wuq_sb[p][:, 512 * _k: 512 * _k + w],
                        start=(_k == 0), stop=(_k == 7)),
                        wait=wts2, inc=[(st, 1)] if k == 7 else [])
                gu_vals[tt] = st.n

            def emit_mid(tt):
                nonlocal sg_free, h_free, pg_free, pu_free
                hp = tt % 2
                wts = [(st, gu_vals[tt])]
                if sg_free:
                    wts.append((sv, sg_free))
                P.emit("scalar", lambda s, w=wc: s.activation(sg_st[:, 0:w], pg[:, 0:w], AF.Silu),
                       wait=wts)
                pg_free_s = ss.n
                sgv = ss.n
                wts = [(ss, sgv)]
                if h_free:
                    wts.append((ss, h_free))
                P.emit("vector", lambda v, w=wc: v.tensor_mul(h_st[:, 0:w], sg_st[:, 0:w], pu[:, 0:w]),
                       wait=wts)
                pu_free = sv.n
                pg_free = sv.n
                sg_free = sv.n
                h_v = sv.n
                qv = emit_quant(P, hq_st[hp][:, 0:wc], h_st[:, 0:wc], wc, qs3,
                                src_ready=[(sv, h_v)],
                                extra_dst_wait=([(st, hq_free[hp])] if hq_free[hp] else []))
                h_free = qs3["last_scalar"]
                return qv

            def emit_tmid(tt, qv):
                nonlocal hqT_free, hq_free
                hp = tt % 2
                pdp = tt % 2
                for nt in range(wnt):
                    tp = nt % 2
                    wts = [qv, (gcD.sem, CW)]
                    if ptrh_free[tp]:
                        wts.append((ss, ptrh_free[tp]))
                    P.emit("tensor", lambda t, _n=nt, p=tp, q=hp: t.transpose(
                        ptrh[p][:, :], hq_st[q][:, 128 * _n:128 * (_n + 1)], identb[:, :]),
                        wait=wts, inc=[(st, 1)])
                    tr_v = st.n
                    wts = [(st, tr_v)]
                    if nt == 0 and hqT_free:
                        wts.append((st, hqT_free))
                    P.emit("scalar", lambda s, _n=nt, p=tp: s.copy(
                        hqT_st[:, 128 * _n:128 * (_n + 1)], ptrh[p][:, :]),
                        wait=wts)
                    ptrh_free[tp] = ss.n
                hq_free[hp] = st.n
                hqT_v = ss.n
                for nt in range(wnt):
                    for hh in range(2):
                        wts = [(ss, hqT_v), wv_]
                        if nt == 0 and hh == 0 and pd_free[pdp]:
                            wts.append((sv, pd_free[pdp]))
                        P.emit("tensor", lambda t, _n=nt, _h=hh, p=par, q=pdp: t.matmul(
                            pd[q][:, 512 * _h:512 * (_h + 1)],
                            hqT_st[:, 128 * _n:128 * (_n + 1)],
                            wdq_sb[p][:, 1024 * _n + 512 * _h: 1024 * _n + 512 * (_h + 1)],
                            start=(_n == 0), stop=(_n == wnt - 1)),
                            wait=wts, inc=[(st, 1)] if (nt == wnt - 1 and hh == 1) else [])
                hqT_free = st.n
                dmm_vals[tt] = st.n

            def emit_evac(tt):
                nonlocal pd_free
                pdp = tt % 2
                if ci == 0:
                    P.emit("vector", lambda v, _tt=tt, q=pdp: v.tensor_copy(
                        out_acc[:, 1024 * _tt:1024 * (_tt + 1)], pd[q][:, :]),
                        wait=[(st, dmm_vals[tt])])
                else:
                    P.emit("vector", lambda v, _tt=tt, q=pdp: v.tensor_add(
                        out_acc[:, 1024 * _tt:1024 * (_tt + 1)],
                        out_acc[:, 1024 * _tt:1024 * (_tt + 1)], pd[q][:, :]),
                        wait=[(st, dmm_vals[tt])])
                pd_free[pdp] = sv.n
                if ci == len(chunks) - 1:
                    oacc_ready[tt] = sv.n

            emit_head(0)
            qv_cur = emit_mid(0)
            for tt in range(NT):
                if tt + 1 < NT:
                    emit_head(tt + 1)
                emit_tmid(tt, qv_cur)
                if tt + 1 < NT:
                    qv_cur = emit_mid(tt + 1)
                if tt >= 1:
                    emit_evac(tt - 1)
            emit_evac(NT - 1)
            if ci + 2 < len(chunks):
                w_free[par] = st.n
                load_wchunk(ci + 2)

        P.ctx = Ps

    with contextlib.ExitStack() as fctx:
        Ps = P.ctx; P.ctx = fctx
        sxf = [DmaStream(P, f"D_sxf{i}", "sync") for i in range(2)]
        ssf = [DmaStream(P, f"D_ssf{i}", "sync") for i in range(2)]
        sof = [DmaStream(P, f"D_sof{i}", "sync") for i in range(2)]
        x_f = [fctx.enter_context(nc.sbuf_tensor(f"D_xf{i}", [128, 1024], F32)) for i in range(2)]
        s_f = [fctx.enter_context(nc.sbuf_tensor(f"D_sf{i}", [128, 1024], BF16)) for i in range(2)]
        g1_f = fctx.enter_context(nc.sbuf_tensor("D_g1", [128, 1024], BF16))
        o_f = [fctx.enter_context(nc.sbuf_tensor(f"D_of{i}", [128, 1024], F32)) for i in range(2)]
        bar_f = [(st, st.n), (ss, ss.n), (sv, sv.n)]
        xf_free = [0, 0]
        of_free = [0, 0]
        g1_free = 0
        for tt in range(NT):
            par = tt % 2
            wts = list(bar_f) + ([(sv, xf_free[par])] if xf_free[par] else [])
            xv = sxf[par].load(x_f[par][:, :], hbm["x1"][128 * tt:128 * (tt + 1), :], wait=wts)
            sv_ld = ssf[par].load(s_f[par][:, :], hbm["s1"][128 * tt:128 * (tt + 1), :], wait=wts)
            wts = [(sv, oacc_ready[tt])]
            if g1_free:
                wts.append((sv, g1_free))
            P.emit("scalar", lambda s, _tt=tt: s.activation(
                g1_f[:, :], out_acc[:, 1024 * _tt:1024 * (_tt + 1)], AF.Silu),
                wait=wts)
            g1_v = ss.n
            wts = [(ss, g1_v), (ssf[par].sem, sv_ld)]
            if of_free[par]:
                wts.append((sof[par].sem, of_free[par]))
            P.emit("vector", lambda v, p=par: v.tensor_mul(o_f[p][:, :], g1_f[:, :], s_f[p][:, :]),
                   wait=wts)
            g1_free = sv.n
            P.emit("vector", lambda v, p=par: v.tensor_add(o_f[p][:, :], o_f[p][:, :], x_f[p][:, :]),
                   wait=[(sxf[par].sem, xv)])
            o_v = sv.n
            xf_free[par] = o_v
            of_free[par] = sof[par].load(hbm["out"][128 * tt:128 * (tt + 1), :], o_f[par][:, :],
                                         wait=[(sv, o_v)])
        P.emit("sync", lambda s: None, wait=[(sof[0].sem, sof[0].wv()), (sof[1].sem, sof[1].wv())])
        P.ctx = Ps
    return


def build_D(cfg):
    nc = bass.Bass("TRN2", target_bir_lowering=False)
    T = cfg["T"]
    I = cfg.get("I", 2816)
    hbm = {}
    hbm["x1"] = nc.dram_tensor("x1", [T, 1024], F32, kind="ExternalInput").ap()
    hbm["s1"] = nc.dram_tensor("s1", [T, 1024], BF16, kind="ExternalInput").ap()
    hbm["wgate"] = nc.dram_tensor("wgate", [I, 1024], F32, kind="ExternalInput").ap()
    hbm["wup"] = nc.dram_tensor("wup", [I, 1024], F32, kind="ExternalInput").ap()
    hbm["wdown"] = nc.dram_tensor("wdown", [1024, I], F32, kind="ExternalInput").ap()
    hbm["ident"] = nc.dram_tensor("ident", [128, 128], F32, kind="ExternalInput").ap()
    hbm["wgq_nat"] = nc.dram_tensor("wgq_nat", [I, 1024], BF16).ap()
    hbm["wuq_nat"] = nc.dram_tensor("wuq_nat", [I, 1024], BF16).ap()
    hbm["wdq_nat"] = nc.dram_tensor("wdq_nat", [1024, I], BF16).ap()
    if cfg.get("shard_wq"):
        hbm["wjob_a"] = nc.dram_tensor("wjob_a", [6, 128, 1024], F32, kind="ExternalInput").ap()
        hbm["wjob_b"] = nc.dram_tensor("wjob_b", [128, I], F32, kind="ExternalInput").ap()
        nrow = 6 * 1024 + I
        hbm["wq_outc"] = nc.dram_tensor("wq_outc", [nrow, 128], BF16).ap()
        hbm["wq_gath"] = nc.dram_tensor("wq_gath", [8 * nrow, 128], BF16, addr_space="Shared").ap()
    hbm["out"] = nc.dram_tensor("out", [T, 1024], F32, kind="ExternalOutput").ap()
    with contextlib.ExitStack() as ctx:
        P = Plan(nc, ctx)
        for s in ("t", "s", "v", "g"):
            P.sem(s)
        bufs = {}
        bufs["ident"] = ctx.enter_context(nc.sbuf_tensor("identb", [128, 128], BF16))
        build_phase_D(P, hbm, bufs, cfg)
        P.run()
    return nc



def build_A(cfg, fused=False):
    nc = bass.Bass("TRN2", target_bir_lowering=False)
    T = cfg["T"]
    hbm = {}
    hbm["x"] = nc.dram_tensor("x", [T, 1024], F32, kind="ExternalInput").ap()
    hbm["w1all"] = nc.dram_tensor("w1all", [cfg["NE"], 1024, cfg["F"]], F32, kind="ExternalInput").ap()
    hbm["w2all"] = nc.dram_tensor("w2all", [cfg["NE"], cfg["F"], 1024], F32, kind="ExternalInput").ap()
    hbm["b1dev"] = nc.dram_tensor("b1dev", [128, (cfg["F"] // 128) * cfg["NE"]], F32, kind="ExternalInput").ap()
    hbm["b2all"] = nc.dram_tensor("b2all", [cfg["NE"], 1024], F32, kind="ExternalInput").ap()
    hbm["wgdev"] = nc.dram_tensor("wgdev", [128, 8 * (cfg["NE"] - 1)], F32, kind="ExternalInput").ap()
    hbm["ident"] = nc.dram_tensor("ident", [128, 128], F32, kind="ExternalInput").ap()
    hbm["g_out"] = nc.dram_tensor("g_out", [T, 1024], BF16, kind="ExternalOutput").ap()

    with contextlib.ExitStack() as ctx:
        P = Plan(nc, ctx)
        for s in ("sq", "gq", "t", "s", "v"):
            P.sem(s)
        bufs = {}
        bufs["xnT"] = ctx.enter_context(nc.sbuf_tensor("xnT", [128, 8 * T], BF16))
        bufs["g_acc"] = ctx.enter_context(nc.sbuf_tensor("g_acc", [128, (T // 128) * 1024], F32))
        bufs["g_tok"] = None
        bufs["ident"] = ctx.enter_context(nc.sbuf_tensor("identb", [128, 128], BF16))
        build_phase_A(P, hbm, bufs, cfg)
        P.run()
    return nc


def build_full(cfg):
    nc = bass.Bass("TRN2", target_bir_lowering=False)
    T = cfg["T"]
    I = cfg.get("I", 2816)
    h = {}
    h["x"] = nc.dram_tensor("x", [T, 1024], F32, kind="ExternalInput").ap()
    h["w1all"] = nc.dram_tensor("w1all", [cfg["NE"], 1024, cfg["F"]], F32, kind="ExternalInput").ap()
    h["w2all"] = nc.dram_tensor("w2all", [cfg["NE"], cfg["F"], 1024], F32, kind="ExternalInput").ap()
    h["b1dev"] = nc.dram_tensor("b1dev", [128, (cfg["F"] // 128) * cfg["NE"]], F32, kind="ExternalInput").ap()
    h["b2all"] = nc.dram_tensor("b2all", [cfg["NE"], 1024], F32, kind="ExternalInput").ap()
    h["wgdev"] = nc.dram_tensor("wgdev", [128, 8 * (cfg["NE"] - 1)], F32, kind="ExternalInput").ap()
    h["ident"] = nc.dram_tensor("ident", [128, 128], F32, kind="ExternalInput").ap()
    for nm in ("wqT", "wkT", "wvT", "woT", "l1qT", "l1kT", "l1vT", "l1oT"):
        h[nm] = nc.dram_tensor(nm, [128, 8 * 1024], F32, kind="ExternalInput").ap()
    h["gt"] = nc.dram_tensor("gt", [16, 128, 2 * T - 128], F32, kind="ExternalInput").ap()
    h["m01"] = nc.dram_tensor("m01", [6, 128, 512], F32, kind="ExternalInput").ap()
    h["wgate"] = nc.dram_tensor("wgate", [I, 1024], F32, kind="ExternalInput").ap()
    h["wup"] = nc.dram_tensor("wup", [I, 1024], F32, kind="ExternalInput").ap()
    h["wdown"] = nc.dram_tensor("wdown", [1024, I], F32, kind="ExternalInput").ap()
    h["g_out"] = h["g_in"] = nc.dram_tensor("g_h", [T, 1024], BF16).ap()
    h["x1_out"] = h["x1"] = nc.dram_tensor("x1_h", [T, 1024], F32).ap()
    h["s1_out"] = h["s1"] = nc.dram_tensor("s1_h", [T, 1024], BF16).ap()
    h["wgq_nat"] = nc.dram_tensor("wgq_nat", [I, 1024], BF16).ap()
    h["wuq_nat"] = nc.dram_tensor("wuq_nat", [I, 1024], BF16).ap()
    h["wdq_nat"] = nc.dram_tensor("wdq_nat", [1024, I], BF16).ap()
    if cfg.get("shard_wq"):
        h["wjob_a"] = nc.dram_tensor("wjob_a", [6, 128, 1024], F32, kind="ExternalInput").ap()
        h["wjob_b"] = nc.dram_tensor("wjob_b", [128, I], F32, kind="ExternalInput").ap()
        nrow = 6 * 1024 + I
        h["wq_outc"] = nc.dram_tensor("wq_outc", [nrow, 128], BF16).ap()
        h["wq_gath"] = nc.dram_tensor("wq_gath", [8 * nrow, 128], BF16, addr_space="Shared").ap()
    h["out"] = nc.dram_tensor("out", [T, 1024], F32, kind="ExternalOutput").ap()

    with contextlib.ExitStack() as ctx:
        P = Plan(nc, ctx)
        for s in ("t", "s", "v", "g"):
            P.sem(s)
        bufs = {}
        bufs["ident"] = ctx.enter_context(nc.sbuf_tensor("identb", [128, 128], BF16))
        bufs["g_tok"] = None
        wq = WeightQuantEmitter(P, h, cfg, bufs["ident"])
        bufs["filler"] = lambda: wq.emit_chunk()
        for builder in (build_phase_A, build_phase_B, build_phase_C, build_phase_D):
            if builder is build_phase_B:
                bufs["filler"] = None
            if builder is build_phase_D:
                bufs["wq_done_dep"] = wq.finish()
            with contextlib.ExitStack() as pctx:
                saved = P.ctx
                P.ctx = pctx
                builder(P, h, bufs, cfg)
                P.ctx = saved
            P.phase_barrier()
        P.run()
    return nc


CFG_FULL = dict(T=2048, F=4096, NG=4, NE=9, I=2816)


def kernel(**inputs):
    maps = prep_host(inputs)
    cfg = CFG_FULL
    cores = list(range(NCORES))
    ncF = build_full(cfg)
    resF = run_bass_kernel_spmd(ncF, maps, core_ids=cores)
    return np.stack([np.asarray(r["out"], dtype=np.float32) for r in resF.results], axis=0)


def kernel_multi(**inputs):
    maps = prep_host(inputs)
    cfg = CFG_FULL
    cores = list(range(NCORES))

    ncA = build_A(cfg)
    inA = [dict(x=m["x"], w1all=m["w1all"], w2all=m["w2all"], b1dev=m["b1dev"],
                b2all=m["b2all"], wgdev=m["wgdev"], ident=m["ident"]) for m in maps]
    resA = run_bass_kernel_spmd(ncA, inA, core_ids=cores)
    g_all = [r["g_out"] for r in resA.results]

    ncB = build_B(cfg)
    inB = [dict(x=m["x"], g_in=g_all[b], wqT=m["wqT"], wkT=m["wkT"], wvT=m["wvT"],
                woT=m["woT"], gt=m["gt"], ident=m["ident"]) for b, m in enumerate(maps)]
    resB = run_bass_kernel_spmd(ncB, inB, core_ids=cores)
    x1_all = [r["x1_out"] for r in resB.results]

    ncC = build_C(cfg)
    inC = [dict(x1=x1_all[b], l1qT=m["l1qT"], l1kT=m["l1kT"], l1vT=m["l1vT"],
                l1oT=m["l1oT"], m01=m["m01"], ident=m["ident"]) for b, m in enumerate(maps)]
    resC = run_bass_kernel_spmd(ncC, inC, core_ids=cores)
    s1_all = [r["s1_out"] for r in resC.results]

    ncD = build_D(cfg)
    inD = [dict(x1=x1_all[b], s1=s1_all[b], wgate=m["wgate"], wup=m["wup"],
                wdown=m["wdown"], ident=m["ident"]) for b, m in enumerate(maps)]
    resD = run_bass_kernel_spmd(ncD, inD, core_ids=cores)
    out = np.stack([np.asarray(r["out"], dtype=np.float32) for r in resD.results], axis=0)
    return out


# version 29
# speedup vs baseline: 1.1951x; 1.1951x over previous
import contextlib
import math
import numpy as np
import ml_dtypes
import concourse.bass as bass
import concourse.mybir as mybir
from concourse.bass_utils import run_bass_kernel_spmd

F32 = mybir.dt.float32
BF16 = mybir.dt.bfloat16
AF = mybir.ActivationFunctionType
ALU = mybir.AluOpType
AX = mybir.AxisListType

NCORES = 8


def _rel_bucket(rel_pos, num_buckets=32, max_distance=128):
    rel_pos = np.asarray(rel_pos, dtype=np.int64)
    nb = num_buckets // 2
    buckets = (rel_pos > 0).astype(np.int64) * nb
    rp = np.abs(rel_pos)
    max_exact = nb // 2
    is_small = rp < max_exact
    rp_f = np.maximum(rp, 1).astype(np.float64)
    large = max_exact + (
        np.log(rp_f / max_exact) / math.log(max_distance / max_exact) * (nb - max_exact)
    ).astype(np.int64)
    large = np.minimum(large, nb - 1)
    return buckets + np.where(is_small, rp, large)


def _make_gt(rel_bias, S=2048):
    d = np.arange(-(S - 1), S)
    idx = _rel_bucket(d)
    table = rel_bias[idx, :]
    W = 2 * S - 128
    i = np.arange(128)[:, None]
    c = np.arange(W)[None, :]
    t_idx = (W - 1) + i - c
    GT = table[t_idx, :]
    return np.ascontiguousarray(GT.transpose(2, 0, 1)).astype(np.float32)


def _make_m01():
    deltas = [-256, -128, 0, 128, 256, 384]
    i = np.arange(128)[None, :, None]
    j = np.arange(512)[None, None, :]
    d = np.asarray(deltas)[:, None, None]
    qk = j - (d + i)
    return ((qk >= 0) & (qk <= 255)).astype(np.float32)


def _feat_tiles(w):
    K, N = w.shape
    return np.ascontiguousarray(w.reshape(K // 128, 128, N).transpose(1, 0, 2).reshape(128, -1))


def prep_host(inputs):
    f32 = np.float32
    x_all = np.ascontiguousarray(inputs["L0_attn_hidden_states"], dtype=f32)

    wqT = _feat_tiles(np.ascontiguousarray(inputs["L0_attn_wq"].T, dtype=f32))
    wkT = _feat_tiles(np.ascontiguousarray(inputs["L0_attn_wk"].T, dtype=f32))
    wvT = _feat_tiles(np.ascontiguousarray(inputs["L0_attn_wv"].T, dtype=f32))
    woT = _feat_tiles(np.ascontiguousarray(inputs["L0_attn_wo"].T, dtype=f32))
    gt = _make_gt(np.asarray(inputs["L0_attn_rel_bias"], dtype=f32))

    W1 = np.asarray(inputs["L0_ffn_W1"], dtype=f32)
    W2 = np.asarray(inputs["L0_ffn_W2"], dtype=f32)
    Ws1 = np.asarray(inputs["L0_ffn_Ws1"], dtype=f32)[None]
    Ws2 = np.asarray(inputs["L0_ffn_Ws2"], dtype=f32)[None]
    w1all = np.ascontiguousarray(np.concatenate([W1, Ws1], 0))
    w2all = np.ascontiguousarray(np.concatenate([W2, Ws2], 0))
    b1 = np.asarray(inputs["L0_ffn_b1"], dtype=f32)
    bs1 = np.asarray(inputs["L0_ffn_bs1"], dtype=f32)[None]
    b1all = np.concatenate([b1, bs1], 0)
    b1dev = np.ascontiguousarray(b1all.reshape(9, 32, 128).transpose(2, 1, 0).reshape(128, 32 * 9))
    b2 = np.asarray(inputs["L0_ffn_b2"], dtype=f32)
    bs2 = np.asarray(inputs["L0_ffn_bs2"], dtype=f32)[None]
    b2all = np.ascontiguousarray(np.concatenate([b2, bs2], 0))
    wg = np.asarray(inputs["L0_ffn_Wg"], dtype=f32)
    wgdev = _feat_tiles(wg)

    l1qT = _feat_tiles(np.ascontiguousarray(inputs["L1_attn_Wq"].T, dtype=f32))
    l1kT = _feat_tiles(np.ascontiguousarray(inputs["L1_attn_Wk"].T, dtype=f32))
    l1vT = _feat_tiles(np.ascontiguousarray(inputs["L1_attn_Wv"].T, dtype=f32))
    l1oT = _feat_tiles(np.ascontiguousarray(inputs["L1_attn_Wo"].T, dtype=f32))
    m01 = _make_m01()

    wgate = np.ascontiguousarray(inputs["L1_ffn_w_gate"], dtype=f32)
    wup = np.ascontiguousarray(inputs["L1_ffn_w_up"], dtype=f32)
    wdown = np.ascontiguousarray(inputs["L1_ffn_w_down"], dtype=f32)

    ident = np.eye(128, dtype=f32)

    shared = dict(
        wqT=wqT, wkT=wkT, wvT=wvT, woT=woT, gt=gt,
        w1all=w1all, w2all=w2all, b1dev=b1dev, b2all=b2all, wgdev=wgdev,
        l1qT=l1qT, l1kT=l1kT, l1vT=l1vT, l1oT=l1oT, m01=m01,
        wgate=wgate, wup=wup, wdown=wdown, ident=ident,
    )
    maps = []
    for b in range(NCORES):
        m = dict(shared)
        m["x"] = np.ascontiguousarray(x_all[b])
        maps.append(m)
    return maps



ENGS = ("tensor", "scalar", "vector", "gpsimd", "sync")


class Sem:
    __slots__ = ("h", "n", "name")

    def __init__(self, h, name):
        self.h = h
        self.n = 0
        self.name = name

    def tick(self, amt=1):
        self.n += amt
        return self.n


class Plan:
    def __init__(self, nc, ctx):
        self.nc = nc
        self.ctx = ctx
        self.items = {e: [] for e in ENGS}
        self.sems = {}
        self.watermark = {}
        self._deps = []
        self._deps_todo = set()

    def phase_barrier(self):
        self._deps = [(s, s.n) for s in self.sems.values() if s.n > 0]
        self._deps_todo = set(ENGS)

    def sem(self, name):
        h = self.ctx.enter_context(self.nc.semaphore(name))
        s = Sem(h, name)
        self.sems[name] = s
        return s

    def emit(self, eng, fn, wait=(), inc=(), is_dma=False):
        wait = list(wait)
        inc = list(inc)
        if eng in self._deps_todo:
            wait = self._deps + wait
            self._deps_todo.discard(eng)
        if eng in ("vector", "scalar") or (eng == "gpsimd" and not is_dma and "g" in self.sems):
            own = self.sems["v" if eng == "vector" else ("s" if eng == "scalar" else "g")]
            if own.n:
                wait.append((own, own.n))
            if not any(s is own for s, _ in inc):
                inc.append((own, 1))
        pruned = []
        for s, v in wait:
            assert v <= s.n, f"wait on {s.name}>={v} but only {s.n} planned (deadlock)"
            key = (eng, s.name)
            if self.watermark.get(key, -1) >= v:
                continue
            self.watermark[key] = v
            pruned.append((s, v))
        incs = []
        for s, amt in inc:
            s.tick(amt)
            incs.append((s, amt, s.n))
        self.items[eng].append((pruned, fn, incs))
        return [(s, n) for s, _a, n in incs]

    def run(self):
        block = self.ctx.enter_context(self.nc.Block())
        plans = self.items

        def mk(eng_name):
            def body(proxy):
                for waits, fn, incs in plans[eng_name]:
                    for s, v in waits:
                        proxy.wait_ge(s.h, v)
                    ins = fn(proxy)
                    if incs:
                        assert ins is not None, f"inc on None instr ({eng_name})"
                        for s, amt, _n in incs:
                            ins = ins.then_inc(s.h, amt)
            return body

        block.tensor(mk("tensor"))
        block.scalar(mk("scalar"))
        block.vector(mk("vector"))
        block.gpsimd(mk("gpsimd"))
        block.sync(mk("sync"))



class DmaStream:

    def __init__(self, P, name, eng):
        self.P = P
        self.eng = eng
        self.sem = P.sem(name)

    def load(self, dst, src, wait=(), transpose=False):
        self.P.emit(self.eng, lambda g, d=dst, s=src, tr=transpose: g.dma_start(d, s, transpose=tr),
                    wait=wait, inc=[(self.sem, 16)], is_dma=True)
        return self.sem.n

    def wv(self):
        return self.sem.n



def build_phase_A(P, hbm, bufs, cfg, fused_out=False):
    nc = P.nc
    T = cfg["T"]; NT = T // 128
    NDT = 8
    F = cfg["F"]; NFT = F // 128
    NG = cfg["NG"]; FLG = NFT // NG
    NE = cfg["NE"]
    TC = 256; NTC = T // TC
    ER = NE - 1
    GL = FLG * 128

    st = P.sems["t"]; ss = P.sems["s"]; sv = P.sems["v"]
    gc = DmaStream(P, "A_gc", "gpsimd")
    sx = [DmaStream(P, f"A_sx{i}", "sync") for i in range(2)]
    gw = [DmaStream(P, f"A_gw{i}", "gpsimd") for i in range(2)]
    so = [DmaStream(P, f"A_so{i}", "sync") for i in range(2)]

    if bufs.get("xnT") is None:
        bufs_xnT = P.ctx.enter_context(nc.sbuf_tensor("A_xnT", [128, 8 * T], BF16))
        bufs_gacc = P.ctx.enter_context(nc.sbuf_tensor("A_gacc", [128, NT * 1024], F32))
    else:
        bufs_xnT = bufs["xnT"]
        bufs_gacc = bufs["g_acc"]
    xnT = bufs_xnT
    g_acc = bufs_gacc
    g_tok = bufs.get("g_tok")
    ident = bufs["ident"]

    ctx = P.ctx
    b1dev = ctx.enter_context(nc.sbuf_tensor("A_b1dev", [128, (F // 128) * NE], F32))
    cw_sb = ctx.enter_context(nc.sbuf_tensor("A_cw", [128, NT * NE], F32))
    b2bf = ctx.enter_context(nc.sbuf_tensor("A_b2bf", [NE, 1024], BF16))
    wg_sb = ctx.enter_context(nc.sbuf_tensor("A_wg", [128, NDT * ER], F32))

    gc.load(b1dev[:, :], hbm["b1dev"][:, :])
    gc.load(b2bf[:, :], hbm["b2all"][:, :])
    gc.load(wg_sb[:, :], hbm["wgdev"][:, :])
    gc.load(ident[:, :], hbm["ident"][:, :])
    identf = P.ctx.enter_context(nc.sbuf_tensor("A_identf", [128, 128], F32))
    gc.load(identf[:, :], hbm["ident"][:, :])
    CW = gc.wv()
    P.emit("vector", lambda v: v.memset(cw_sb[:, :], 0.1))
    eps_col = P.ctx.enter_context(nc.sbuf_tensor("A_eps", [128, 1], F32))
    P.emit("vector", lambda v: v.memset(eps_col[:, :], 1e-5))
    eps_v = P.sems["v"].n

    lnctx = contextlib.ExitStack()
    ptr = [lnctx.enter_context(nc.psum_tensor(f"A_ptr{i}", [128, 128], F32)) for i in range(2)]
    p_r = lnctx.enter_context(nc.psum_tensor("A_pr", [128, ER], F32))
    p_cwT = lnctx.enter_context(nc.psum_tensor("A_pcwT", [NE, 128], BF16))
    p_b2 = lnctx.enter_context(nc.psum_tensor("A_pb2", [128, 1024], F32))
    x_t = [lnctx.enter_context(nc.sbuf_tensor(f"A_x{i}", [128, 1024], F32)) for i in range(2)]
    xsq = lnctx.enter_context(nc.sbuf_tensor("A_xsq", [128, 1024], F32))
    stat4 = lnctx.enter_context(nc.sbuf_tensor("A_stat", [128, 8], F32))
    xn_tok = lnctx.enter_context(nc.sbuf_tensor("A_xntok", [128, 1024], F32))
    xnf = [lnctx.enter_context(nc.sbuf_tensor(f"A_xnf{i}", [128, 128], F32)) for i in range(2)]
    cwT = lnctx.enter_context(nc.sbuf_tensor("A_cwT", [NE, NT * 128], BF16))
    r_sc = lnctx.enter_context(nc.sbuf_tensor("A_rsc", [128, 16], F32))
    probs = lnctx.enter_context(nc.sbuf_tensor("A_probs", [128, ER], F32))
    pm = lnctx.enter_context(nc.sbuf_tensor("A_pm", [128, ER], F32))
    msk = lnctx.enter_context(nc.sbuf_tensor("A_msk", [128, ER], F32))
    cwbf = lnctx.enter_context(nc.sbuf_tensor("A_cwbf", [128, NE], BF16))
    eps = 1e-5
    inv_d = 1.0 / 1024.0
    router_done = [0] * NT
    ginit_done = [0] * NT
    x_free = [0, 0]
    xsq_free = 0
    xn_tok_free = 0
    ptr_free = [0, 0]
    pr_free = 0
    xnf_free = [0, 0]
    pcwT_free = 0
    cwbf_free = 0
    pb2_free = 0

    for tt in range(NT):
        par = tt % 2
        wts = [(sv, x_free[par])] if x_free[par] else []
        xv = sx[par].load(x_t[par][:, :], hbm["x"][128 * tt:128 * (tt + 1), :], wait=wts)
        xw = (sx[par].sem, xv)
        P.emit("vector", lambda v, p=par: v.reduce_sum(stat4[:, 0:1], x_t[p][:, :], AX.X),
               wait=[xw])
        swts = [xw] + ([(sv, xsq_free)] if xsq_free else [])
        P.emit("scalar", lambda s, p=par: s.activation(xsq[:, :], x_t[p][:, :], AF.Square),
               wait=swts)
        sq_done = ss.n
        P.emit("vector", lambda v: v.reduce_sum(stat4[:, 1:2], xsq[:, :], AX.X),
               wait=[(ss, sq_done)])
        xsq_free = sv.n
        P.emit("vector", lambda v: v.tensor_scalar_mul(stat4[:, 2:3], stat4[:, 0:1], inv_d))
        P.emit("vector", lambda v: v.tensor_scalar_mul(stat4[:, 3:4], stat4[:, 1:2], inv_d))
        P.emit("vector", lambda v: v.scalar_tensor_tensor(
            stat4[:, 4:5], stat4[:, 2:3], -1.0, stat4[:, 2:3], ALU.mult, ALU.mult))
        P.emit("vector", lambda v: v.tensor_add(stat4[:, 5:6], stat4[:, 3:4], stat4[:, 4:5]))
        var_v = sv.n
        P.emit("scalar", lambda s: s.activation(stat4[:, 6:7], stat4[:, 5:6], AF.Sqrt, bias=eps_col[:, :]),
               wait=[(sv, var_v), (sv, eps_v)])
        sd_v = ss.n
        P.emit("vector", lambda v: v.reciprocal(stat4[:, 7:8], stat4[:, 6:7]),
               wait=[(ss, sd_v)])
        rstd_v = sv.n
        wts = [(sv, rstd_v)] + ([(st, xn_tok_free)] if xn_tok_free else [])
        P.emit("vector", lambda v, p=par: v.tensor_scalar(
            xn_tok[:, :], x_t[p][:, :], stat4[:, 2:3], stat4[:, 7:8], ALU.subtract, ALU.mult),
            wait=wts)
        xn_v = sv.n
        x_free[par] = xn_v
        for ft in range(NDT):
            tp = ft % 2
            wts = [(sv, xn_v), (gc.sem, CW)]
            if ptr_free[tp]:
                wts.append((sv, ptr_free[tp]))
            P.emit("tensor", lambda t, f=ft, p=tp: t.transpose(
                ptr[p][:, :], xn_tok[:, 128 * f:128 * (f + 1)], identf[:, :]),
                wait=wts, inc=[(st, 1)])
            tr_v = st.n
            P.emit("vector", lambda v, f=ft, p=tp, _tt=tt: v.tensor_copy(
                xnT[:, T * f + 128 * _tt: T * f + 128 * (_tt + 1)], ptr[p][:, :]),
                wait=[(st, tr_v)])
            wts = [(st, tr_v)]
            if xnf_free[tp]:
                wts.append((st, xnf_free[tp]))
            P.emit("vector", lambda v, p=tp: v.tensor_copy(xnf[p][:, :], ptr[p][:, :]),
                   wait=wts)
            ptr_free[tp] = sv.n
            xnf_v = sv.n
            P.emit("tensor", lambda t, f=ft, p=tp: t.matmul(
                p_r[:, :], xnf[p][:, :], wg_sb[:, ER * f: ER * (f + 1)],
                start=(f == 0), stop=(f == NDT - 1)),
                wait=[(sv, xnf_v)] + ([(ss, pr_free)] if (ft == 0 and pr_free) else []),
                inc=[(st, 1)])
            xnf_free[tp] = st.n
        xn_tok_free = st.n
        xn_copied = sv.n

        log_v = st.n
        P.emit("vector", lambda v: v.reduce_max(r_sc[:, 0:1], p_r[:, :], AX.X, negate=True),
               wait=[(st, log_v)])
        negm_v = sv.n
        P.emit("scalar", lambda s: s.activation(probs[:, :], p_r[:, :], AF.Exp, bias=r_sc[:, 0:1]),
               wait=[(sv, negm_v)])
        punn_v = ss.n
        pr_free = punn_v
        P.emit("vector", lambda v: v.reduce_sum(r_sc[:, 1:2], probs[:, :], AX.X),
               wait=[(ss, punn_v)])
        P.emit("vector", lambda v: v.reciprocal(r_sc[:, 2:3], r_sc[:, 1:2]))
        P.emit("vector", lambda v: v.tensor_scalar_mul(probs[:, :], probs[:, :], r_sc[:, 2:3]))
        P.emit("vector", lambda v: v.reduce_max(r_sc[:, 3:4], probs[:, :], AX.X))
        P.emit("vector", lambda v: v.tensor_scalar(msk[:, :], probs[:, :], r_sc[:, 3:4], None, ALU.is_ge))
        P.emit("vector", lambda v: v.tensor_mul(pm[:, :], probs[:, :], msk[:, :]))
        P.emit("vector", lambda v: v.tensor_sub(pm[:, :], probs[:, :], pm[:, :]))
        P.emit("vector", lambda v: v.reduce_max(r_sc[:, 4:5], pm[:, :], AX.X))
        P.emit("vector", lambda v: v.tensor_scalar(r_sc[:, 8:8 + ER], pm[:, :], r_sc[:, 4:5], None, ALU.is_ge))
        P.emit("vector", lambda v: v.tensor_add(msk[:, :], msk[:, :], r_sc[:, 8:8 + ER]))
        P.emit("vector", lambda v: v.tensor_mul(pm[:, :], probs[:, :], msk[:, :]))
        P.emit("vector", lambda v: v.reduce_sum(r_sc[:, 5:6], pm[:, :], AX.X))
        P.emit("vector", lambda v: v.tensor_scalar_add(r_sc[:, 5:6], r_sc[:, 5:6], 1e-9))
        P.emit("vector", lambda v: v.reciprocal(r_sc[:, 6:7], r_sc[:, 5:6]))
        P.emit("vector", lambda v, _tt=tt: v.tensor_scalar_mul(
            cw_sb[:, NE * _tt: NE * _tt + ER], pm[:, :], r_sc[:, 6:7]))
        router_done[tt] = sv.n
        wts = [(st, cwbf_free)] if cwbf_free else []
        P.emit("vector", lambda v, _tt=tt: v.tensor_copy(cwbf[:, :], cw_sb[:, NE * _tt: NE * (_tt + 1)]),
               wait=wts)
        cwbf_v = sv.n
        wts = [(sv, cwbf_v)]
        if pcwT_free:
            wts.append((sv, pcwT_free))
        P.emit("tensor", lambda t: t.transpose(p_cwT[:, :], cwbf[:, :], ident[:, :]),
               wait=wts, inc=[(st, 1)])
        cwt_v = st.n
        cwbf_free = cwt_v
        P.emit("vector", lambda v, _tt=tt: v.tensor_copy(cwT[:, 128 * _tt:128 * (_tt + 1)], p_cwT[:, :]),
               wait=[(st, cwt_v)])
        cwT_v = sv.n
        pcwT_free = cwT_v
        for hh in range(2):
            wts = [(sv, cwT_v), (gc.sem, CW)]
            if hh == 0 and pb2_free:
                wts.append((sv, pb2_free))
            P.emit("tensor", lambda t, _tt=tt, h=hh: t.matmul(
                p_b2[:, 512 * h:512 * (h + 1)], cwT[:, 128 * _tt:128 * (_tt + 1)],
                b2bf[:, 512 * h:512 * (h + 1)], start=True, stop=True),
                wait=wts, inc=[(st, 1)] if hh == 1 else [])
        b2_v = st.n
        P.emit("vector", lambda v, _tt=tt: v.tensor_copy(
            g_acc[:, 1024 * _tt:1024 * (_tt + 1)], p_b2[:, :]),
            wait=[(st, b2_v)])
        pb2_free = sv.n
        ginit_done[tt] = sv.n

    ln_all_v = sv.n
    lnctx.close()
    psumA = [ctx.enter_context(nc.psum_tensor(f"A_pA{i}", [128, 512], F32)) for i in range(2)]
    ypsum = [ctx.enter_context(nc.psum_tensor(f"A_yp{i}", [128, 1024], F32)) for i in range(2)]
    actTg = ctx.enter_context(nc.sbuf_tensor("A_actg", [128, FLG * 512], BF16))
    w1g = [ctx.enter_context(nc.sbuf_tensor(f"A_w1g{i}", [128, 8 * GL], BF16)) for i in range(2)]
    w2g = [ctx.enter_context(nc.sbuf_tensor(f"A_w2g{i}", [128, FLG * 1024], BF16)) for i in range(2)]

    TC2 = 512
    NTC2 = T // TC2
    w_ready = {}
    w_free = {0: 0, 1: 0}
    eg_list = [(e, g) for e in range(NE) for g in range(NG)]

    def emit_wload(i):
        e, g = eg_list[i]
        par = i % 2
        w1src = hbm["w1all"].rearrange("e (hc p) f -> e p hc f", p=128)[e, :, :, GL * g:GL * (g + 1)]
        w2src = hbm["w2all"].rearrange("e (ft p) h -> e p ft h", p=128)[e, :, FLG * g:FLG * (g + 1), :]
        wts = [(st, w_free[par])] if w_free[par] else []
        gw[par].load(w1g[par].rearrange("p (hc f) -> p hc f", hc=8), w1src, wait=wts)
        gw[par].load(w2g[par].rearrange("p (ft h) -> p ft h", ft=FLG), w2src)
        w_ready[i] = gw[par].wv()

    bar_eg = [(st, st.n), (ss, ss.n), (sv, sv.n)]
    w_free[0] = -1
    w_free[1] = -1
    _orig_emit_wload = emit_wload

    def emit_wload0(i):
        e, g = eg_list[i]
        par = i % 2
        w1src = hbm["w1all"].rearrange("e (hc p) f -> e p hc f", p=128)[e, :, :, GL * g:GL * (g + 1)]
        w2src = hbm["w2all"].rearrange("e (ft p) h -> e p ft h", p=128)[e, :, FLG * g:FLG * (g + 1), :]
        gw[par].load(w1g[par].rearrange("p (hc f) -> p hc f", hc=8), w1src, wait=list(bar_eg))
        gw[par].load(w2g[par].rearrange("p (ft h) -> p ft h", ft=FLG), w2src)
        w_ready[i] = gw[par].wv()

    emit_wload0(0)
    emit_wload0(1)
    w_free[0] = 0
    w_free[1] = 0

    psA_free = [0, 0]
    act_free = 0
    y_free = [0, 0]

    for i, (e, g) in enumerate(eg_list):
        par_w = i % 2
        wv = (gw[par_w].sem, w_ready[i])
        for tc in range(NTC2):
            silu_vals = [0] * FLG
            for fl in range(FLG):
                pa = fl % 2
                for hc in range(NDT):
                    wts = [wv]
                    if hc == 0:
                        wts.append((sv, ln_all_v))
                        if psA_free[pa]:
                            wts.append((ss, psA_free[pa]))
                    P.emit("tensor", lambda t, h=hc, f=fl, p=pa, w=par_w, _tc=tc: t.matmul(
                        psumA[p][:, :],
                        w1g[w][:, GL * h + 128 * f: GL * h + 128 * (f + 1)],
                        xnT[:, T * h + TC2 * _tc: T * h + TC2 * (_tc + 1)],
                        start=(h == 0), stop=(h == NDT - 1)),
                        wait=wts, inc=[(st, 1)] if hc == NDT - 1 else [])
                mmA_v = st.n
                ft_g = FLG * g + fl
                wts = [(st, mmA_v), (gc.sem, CW)]
                if act_free:
                    wts.append((st, act_free))
                P.emit("scalar", lambda s, p=pa, f=fl, c=(ft_g * NE + e): s.activation(
                    actTg[:, 512 * f:512 * (f + 1)], psumA[p][:, :], AF.Silu, bias=b1dev[:, c:c + 1]),
                    wait=wts)
                psA_free[pa] = ss.n
                silu_vals[fl] = ss.n
            for sub in range(2):
                for ts in range(2):
                    tt = (TC2 // 128) * tc + 2 * sub + ts
                    for fl in range(FLG):
                        wts = [(ss, silu_vals[fl])]
                        if sub == 0 and ts == 0 and fl == 0 and y_free[ts]:
                            wts += [(sv, y_free[0]), (sv, y_free[1])]
                        elif fl == 0 and y_free[ts]:
                            wts.append((sv, y_free[ts]))
                        for hh in range(2):
                            P.emit("tensor", lambda t, f=fl, _s=sub, _ts=ts, h=hh, w=par_w: t.matmul(
                                ypsum[_ts][:, 512 * h:512 * (h + 1)],
                                actTg[:, 512 * f + 256 * _s + 128 * _ts: 512 * f + 256 * _s + 128 * (_ts + 1)],
                                w2g[w][:, 1024 * f + 512 * h: 1024 * f + 512 * (h + 1)],
                                start=(f == 0), stop=(f == FLG - 1)),
                                wait=wts if hh == 0 else [],
                                inc=[(st, 1)] if (fl == FLG - 1 and hh == 1) else [])
                    mmB_v = st.n
                    wts = [(st, mmB_v), (sv, ginit_done[tt])]
                    if e == NE - 1:
                        fn = lambda v, _ts=ts, _tt=tt: v.scalar_tensor_tensor(
                            g_acc[:, 1024 * _tt:1024 * (_tt + 1)], ypsum[_ts][:, :], 0.1,
                            g_acc[:, 1024 * _tt:1024 * (_tt + 1)], ALU.mult, ALU.add)
                    else:
                        fn = lambda v, _ts=ts, _tt=tt, _e=e: v.scalar_tensor_tensor(
                            g_acc[:, 1024 * _tt:1024 * (_tt + 1)], ypsum[_ts][:, :],
                            cw_sb[:, NE * _tt + _e: NE * _tt + _e + 1],
                            g_acc[:, 1024 * _tt:1024 * (_tt + 1)], ALU.mult, ALU.add)
                    P.emit("vector", fn, wait=wts)
                    y_free[ts] = sv.n
            act_free = st.n
            fil = bufs.get("filler")
            if fil is not None:
                fil()
        if i + 2 < len(eg_list):
            w_free[par_w] = st.n
            emit_wload(i + 2)

    all_evac_v = sv.n
    g_done = []
    if fused_out:
        for tt in range(NT):
            P.emit("scalar", lambda s, _tt=tt: s.activation(
                g_tok[:, 1024 * _tt:1024 * (_tt + 1)], g_acc[:, 1024 * _tt:1024 * (_tt + 1)], AF.Silu),
                wait=[(sv, all_evac_v)])
            g_done.append(ss.n)
    else:
        gst0 = ctx.enter_context(nc.sbuf_tensor("A_gst0", [128, 1024], BF16))
        gfree = 0
        for tt in range(NT):
            wts = [(sv, all_evac_v)]
            if gfree:
                wts.append((so[0].sem, gfree))
            P.emit("scalar", lambda s, _tt=tt: s.activation(
                gst0[:, :], g_acc[:, 1024 * _tt:1024 * (_tt + 1)], AF.Silu),
                wait=wts)
            sd = ss.n
            gfree = so[0].load(hbm["g_out"][128 * tt:128 * (tt + 1), :],
                               gst0[:, :], wait=[(ss, sd)])
            g_done.append(sd)
        P.emit("sync", lambda s: None, wait=[(so[0].sem, so[0].wv())])
    return g_done




def emit_ln_to_xnT(P, hbm_x, xnT, identb, cfg, pfx, deps=()):
    nc = P.nc
    T = cfg["T"]; NT = T // 128; NDT = 8
    st = P.sems["t"]; ss = P.sems["s"]; sv = P.sems["v"]
    ctx = P.ctx
    sx = [DmaStream(P, f"{pfx}_sx{i}", "sync") for i in range(2)]
    x_t = [ctx.enter_context(nc.sbuf_tensor(f"{pfx}_x{i}", [128, 1024], F32)) for i in range(2)]
    xsq = ctx.enter_context(nc.sbuf_tensor(f"{pfx}_xsq", [128, 1024], F32))
    stat4 = ctx.enter_context(nc.sbuf_tensor(f"{pfx}_stat", [128, 8], F32))
    xn_tok = ctx.enter_context(nc.sbuf_tensor(f"{pfx}_xntok", [128, 1024], BF16))
    eps_col = ctx.enter_context(nc.sbuf_tensor(f"{pfx}_eps", [128, 1], F32))
    ptr = [ctx.enter_context(nc.psum_tensor(f"{pfx}_ptr{i}", [128, 128], BF16)) for i in range(2)]
    P.emit("vector", lambda v: v.memset(eps_col[:, :], 1e-5))
    eps_v = P.sems["v"].n
    x_free = [0, 0]
    xsq_free = 0
    xn_tok_free = 0
    ptr_free = [0, 0]
    for tt in range(NT):
        par = tt % 2
        wts = ([(sv, x_free[par])] if x_free[par] else []) + list(deps)
        xv = sx[par].load(x_t[par][:, :], hbm_x[128 * tt:128 * (tt + 1), :], wait=wts)
        xw = (sx[par].sem, xv)
        P.emit("vector", lambda v, p=par: v.reduce_sum(stat4[:, 0:1], x_t[p][:, :], AX.X),
               wait=[xw])
        swts = [xw] + ([(sv, xsq_free)] if xsq_free else [])
        P.emit("scalar", lambda s, p=par: s.activation(xsq[:, :], x_t[p][:, :], AF.Square),
               wait=swts)
        sq_done = ss.n
        P.emit("vector", lambda v: v.reduce_sum(stat4[:, 1:2], xsq[:, :], AX.X),
               wait=[(ss, sq_done)])
        xsq_free = sv.n
        P.emit("vector", lambda v: v.tensor_scalar_mul(stat4[:, 2:3], stat4[:, 0:1], 1.0 / 1024.0))
        P.emit("vector", lambda v: v.tensor_scalar_mul(stat4[:, 3:4], stat4[:, 1:2], 1.0 / 1024.0))
        P.emit("vector", lambda v: v.scalar_tensor_tensor(
            stat4[:, 4:5], stat4[:, 2:3], -1.0, stat4[:, 2:3], ALU.mult, ALU.mult))
        P.emit("vector", lambda v: v.tensor_add(stat4[:, 5:6], stat4[:, 3:4], stat4[:, 4:5]))
        var_v = sv.n
        P.emit("scalar", lambda s: s.activation(stat4[:, 6:7], stat4[:, 5:6], AF.Sqrt, bias=eps_col[:, :]),
               wait=[(sv, var_v), (sv, eps_v)])
        sd_v = ss.n
        P.emit("vector", lambda v: v.reciprocal(stat4[:, 7:8], stat4[:, 6:7]),
               wait=[(ss, sd_v)])
        rstd_v = sv.n
        wts = [(sv, rstd_v)] + ([(st, xn_tok_free)] if xn_tok_free else [])
        P.emit("vector", lambda v, p=par: v.tensor_scalar(
            xn_tok[:, :], x_t[p][:, :], stat4[:, 2:3], stat4[:, 7:8], ALU.subtract, ALU.mult),
            wait=wts)
        xn_v = sv.n
        x_free[par] = xn_v
        for ft in range(NDT):
            tp = ft % 2
            wts = [(sv, xn_v)]
            if ptr_free[tp]:
                wts.append((sv, ptr_free[tp]))
            P.emit("tensor", lambda t, f=ft, p=tp: t.transpose(
                ptr[p][:, :], xn_tok[:, 128 * f:128 * (f + 1)], identb[:, :]),
                wait=wts, inc=[(st, 1)])
            tr_v = st.n
            P.emit("vector", lambda v, f=ft, p=tp, _tt=tt: v.tensor_copy(
                xnT[:, T * f + 128 * _tt: T * f + 128 * (_tt + 1)], ptr[p][:, :]),
                wait=[(st, tr_v)])
            ptr_free[tp] = sv.n
        xn_tok_free = st.n
    return P.sems["v"].n


def build_phase_B(P, hbm, bufs, cfg):
    nc = P.nc
    T = cfg["T"]; NT = T // 128; NDT = 8
    NH = 16
    QC = 512; NQC = T // QC
    NKT = T // 128
    W_GT = 2 * T - 128
    C0 = T - 128
    st = P.sems["t"]; ss = P.sems["s"]; sv = P.sems["v"]
    ctx = P.ctx

    identb = bufs["ident"]
    gc = DmaStream(P, "B_gc", "gpsimd")
    gwq = [DmaStream(P, f"B_gw{i}", "gpsimd") for i in range(2)]
    ggt = [DmaStream(P, f"B_ggt{i}", "gpsimd") for i in range(2)]
    sxc = [DmaStream(P, f"B_sxc{i}", "sync") for i in range(2)]
    sg = [DmaStream(P, f"B_sg{i}", "sync") for i in range(2)]
    so = [DmaStream(P, f"B_so{i}", "gpsimd") for i in range(2)]

    QT = ctx.enter_context(nc.sbuf_tensor("B_QT", [128, 8 * T], BF16))
    KT = ctx.enter_context(nc.sbuf_tensor("B_KT", [128, 8 * T], BF16))
    VTM = ctx.enter_context(nc.sbuf_tensor("B_VTM", [128, NKT * (NH * 65)], BF16))
    g_sb = bufs.get("g_tok")
    ones1 = ctx.enter_context(nc.sbuf_tensor("B_ones1", [1, 64], F32))

    gc.load(identb[:, :], hbm["ident"][:, :])
    CW = gc.wv()
    P.emit("vector", lambda v: v.memset(VTM[:, :], 1.0))
    vtm_init = sv.n
    P.emit("vector", lambda v: v.memset(ones1[:, :], 1.0))
    ones_v = sv.n

    with contextlib.ExitStack() as lctx:
        Pctx_saved = P.ctx
        P.ctx = lctx
        xnT = lctx.enter_context(nc.sbuf_tensor("B_xnT", [128, 8 * T], BF16))
        wbuf = [lctx.enter_context(nc.sbuf_tensor(f"B_w{i}", [128, 8 * 1024], BF16)) for i in range(2)]
        xn_done = emit_ln_to_xnT(P, hbm["x"], xnT, identb, cfg, "B_ln", deps=[(gc.sem, CW)])
        P.ctx = Pctx_saved

        names = ["wqT", "wkT", "wvT"]
        wv_ld = [0, 0, 0]
        wproj_done = [0, 0, 0]
        wv_ld[0] = gwq[0].load(wbuf[0][:, :], hbm[names[0]][:, :])
        wv_ld[1] = gwq[1].load(wbuf[1][:, :], hbm[names[1]][:, :])
        with contextlib.ExitStack() as pctx:
            pp = [pctx.enter_context(nc.psum_tensor(f"B_pp{i}", [128, 512], F32)) for i in range(2)]
            ptr2 = [pctx.enter_context(nc.psum_tensor(f"B_ptr2{i}", [128, 128], BF16)) for i in range(2)]
            vstage = pctx.enter_context(nc.sbuf_tensor("B_vstage", [128, 512], BF16))
            pp_free = [0, 0]
            ptr2_free = [0, 0]
            vstage_free = 0
            for wi, nm in enumerate(names):
                par = wi % 2
                if wi >= 2:
                    wv_ld[wi] = gwq[par].load(wbuf[par][:, :], hbm[names[wi]][:, :],
                                              wait=[(st, wproj_done[wi - 2])])
                for et in range(8):
                    for tcq in range(T // 512):
                        pa = (et * (T // 512) + tcq) % 2
                        for dc in range(8):
                            wts = [(gwq[par].sem, wv_ld[wi]), (sv, xn_done)]
                            if dc == 0 and pp_free[pa]:
                                wts.append((sv, pp_free[pa]))
                            P.emit("tensor", lambda t, d=dc, e=et, q=tcq, p=pa, w=par: t.matmul(
                                pp[p][:, :],
                                wbuf[w][:, 1024 * d + 128 * e: 1024 * d + 128 * (e + 1)],
                                xnT[:, T * d + 512 * q: T * d + 512 * (q + 1)],
                                start=(d == 0), stop=(d == 7)),
                                wait=wts, inc=[(st, 1)] if dc == 7 else [])
                        mm_v = st.n
                        if nm == "wqT":
                            P.emit("vector", lambda v, e=et, q=tcq, p=pa: v.tensor_copy(
                                QT[:, T * e + 512 * q: T * e + 512 * (q + 1)], pp[p][:, :]),
                                wait=[(st, mm_v)])
                            pp_free[pa] = sv.n
                        elif nm == "wkT":
                            P.emit("vector", lambda v, e=et, q=tcq, p=pa: v.tensor_copy(
                                KT[:, T * e + 512 * q: T * e + 512 * (q + 1)], pp[p][:, :]),
                                wait=[(st, mm_v)])
                            pp_free[pa] = sv.n
                        else:
                            wts = [(st, mm_v)]
                            if vstage_free:
                                wts.append((st, vstage_free))
                            P.emit("vector", lambda v, p=pa: v.tensor_copy(vstage[:, :], pp[p][:, :]),
                                   wait=wts)
                            pp_free[pa] = sv.n
                            vst_v = sv.n
                            for j in range(4):
                                tp2 = j % 2
                                wts = [(sv, vst_v), (gc.sem, CW)]
                                if ptr2_free[tp2]:
                                    wts.append((sv, ptr2_free[tp2]))
                                P.emit("tensor", lambda t, _j=j, p=tp2: t.transpose(
                                    ptr2[p][:, :], vstage[:, 128 * _j:128 * (_j + 1)], identb[:, :]),
                                    wait=wts, inc=[(st, 1)])
                                tr_v = st.n
                                kt_idx = tcq * 4 + j
                                for half in range(2):
                                    h_idx = 2 * et + half
                                    P.emit("vector", lambda v, k=kt_idx, h=h_idx, p=tp2, hf=half: v.tensor_copy(
                                        VTM[:, 65 * NH * k + 65 * h: 65 * NH * k + 65 * h + 64],
                                        ptr2[p][:, 64 * hf:64 * (hf + 1)]),
                                        wait=[(st, tr_v), (sv, vtm_init)])
                                ptr2_free[tp2] = sv.n
                            vstage_free = st.n
                wproj_done[wi] = st.n
        proj_evac_v = P.sems["v"].n

    aoT = ctx.enter_context(nc.sbuf_tensor("B_aoT", [128, 8 * T], BF16))

    with contextlib.ExitStack() as actx:
        gtst = [actx.enter_context(nc.sbuf_tensor(f"B_gt{i}", [128, W_GT], BF16)) for i in range(2)]
        expg = [actx.enter_context(nc.sbuf_tensor(f"B_eg{i}", [128, W_GT], BF16)) for i in range(2)]
        pstage = [actx.enter_context(nc.sbuf_tensor(f"B_pst{i}", [128, 1024], BF16)) for i in range(2)]
        pm_sb = [actx.enter_context(nc.sbuf_tensor(f"B_pm{i}", [128, 1024], BF16)) for i in range(4)]
        recip_sb = actx.enter_context(nc.sbuf_tensor("B_rcp", [1, 512], F32))
        ao_st = actx.enter_context(nc.sbuf_tensor("B_aost", [64, 512], BF16))
        S2 = [actx.enter_context(nc.psum_tensor(f"B_S2{i}", [128, 1024], F32)) for i in range(2)]
        pv = [actx.enter_context(nc.psum_tensor(f"B_pv{i}", [65, 512], F32)) for i in range(2)]
        rb = actx.enter_context(nc.psum_tensor("B_rb", [64, 512], F32))

        gt_ready = [0, 0]
        gt_free = [0, 0]
        bar_att = [(st, st.n), (ss, ss.n), (sv, sv.n)]

        def load_gt(h):
            par = h % 2
            wts = list(bar_att) + ([(ss, gt_free[par])] if gt_free[par] else [])
            ggt[par].load(gtst[par][:, :], hbm["gt"][h, :, :], wait=wts)
            gt_ready[par] = ggt[par].wv()

        load_gt(0)
        load_gt(1)

        expg_free = [0, 0]
        S2_free = [0, 0]
        pst_free = [0, 0]

        pm_free = [0, 0, 0, 0]
        pv_free = 0
        pv_freed = [0, 0]
        rcp_free = 0
        ao_free = 0
        rb_free = 0

        for h in range(NH):
            par = h % 2
            wts = [(ggt[par].sem, gt_ready[par])]
            if expg_free[par]:
                wts.append((sv, expg_free[par]))
            P.emit("scalar", lambda s, p=par: s.activation(expg[p][:, :], gtst[p][:, :], AF.Exp),
                   wait=wts)
            gt_free[par] = ss.n
            expg_v = ss.n
            if h + 2 < NH:
                load_gt(h + 2)
            hb = 64 * (h % 2)
            et = h // 2
            for qc in range(NQC):
                pvp = qc % 2
                mult_vals = [0] * (NKT // 2)

                def emit_pv(ktp, _h=h, _pvp=None):
                    p_ = pvp
                    for u in range(2):
                        kt = 2 * ktp + u
                        wts = [(sv, mult_vals[ktp])]
                        if kt == 0:
                            wts.append((sv, vtm_init))
                            if pv_freed[p_]:
                                wts.append((sv, pv_freed[p_]))
                        P.emit("tensor", lambda t, k=kt, _u=u, p=p_, __h=_h, mp=ktp % 4: t.matmul(
                            pv[p][:, :],
                            VTM[:, 65 * NH * k + 65 * __h: 65 * NH * k + 65 * __h + 65],
                            pm_sb[mp][:, 512 * _u:512 * (_u + 1)],
                            start=(k == 0), stop=(k == NKT - 1)),
                            wait=wts, inc=[(st, 1)] if kt == NKT - 1 else [])
                        pm_free[ktp % 4] = st.n

                for ktp in range(NKT // 2):
                    sp = ktp % 2
                    for u in range(2):
                        kt = 2 * ktp + u
                        wts = []
                        if u == 0 and S2_free[sp]:
                            wts.append((ss, S2_free[sp]))
                        if h == 0 and qc == 0 and ktp == 0 and u == 0:
                            wts.append((st, wproj_done[2]))
                            wts.append((sv, proj_evac_v))
                        P.emit("tensor", lambda t, k=kt, _u=u, p=sp, _qc=qc, _hb=hb, _et=et: t.matmul(
                            S2[p][:, 512 * _u:512 * (_u + 1)],
                            KT[_hb:_hb + 64, T * _et + 128 * k: T * _et + 128 * (k + 1)],
                            QT[_hb:_hb + 64, T * _et + 512 * _qc: T * _et + 512 * (_qc + 1)],
                            start=True, stop=True),
                            wait=wts, inc=[(st, 1)] if u == 1 else [])
                    sc_v = st.n
                    mp = ktp % 4
                    wts = [(st, sc_v), (ss, expg_v)]
                    if pst_free[mp % 2]:
                        wts.append((sv, pst_free[mp % 2]))
                    P.emit("scalar", lambda s, p=sp, _mp=mp % 2: s.activation(
                        pstage[_mp][:, :], S2[p][:, :], AF.Exp),
                        wait=wts)
                    S2_free[sp] = ss.n
                    exp_v = ss.n
                    for u in range(2):
                        kt = 2 * ktp + u
                        off = C0 + 512 * qc - 128 * kt
                        wts = [(ss, exp_v)]
                        if pm_free[mp]:
                            wts.append((st, pm_free[mp]))
                        P.emit("vector", lambda v, _u=u, p=par, o=off, _mp=mp, _ps=mp % 2: v.tensor_mul(
                            pm_sb[_mp][:, 512 * _u:512 * (_u + 1)],
                            pstage[_ps][:, 512 * _u:512 * (_u + 1)],
                            expg[p][:, o:o + 512]),
                            wait=wts)
                    pst_free[mp % 2] = sv.n
                    mult_vals[ktp] = sv.n
                    if ktp >= 3:
                        emit_pv(ktp - 3)
                for _k in range(min(3, NKT // 2), 0, -1):
                    emit_pv(NKT // 2 - _k)
                pv_v = st.n
                pvp = qc % 2
                wts = [(st, pv_v)]
                if rcp_free:
                    wts.append((st, rcp_free))
                P.emit("vector", lambda v, p=pvp: v.reciprocal(recip_sb[:, :], pv[p][64:65, :]),
                       wait=wts)
                rcp_v = sv.n
                wts = [(st, pv_v)]
                if ao_free:
                    wts.append((sv, ao_free))
                P.emit("scalar", lambda s, p=pvp: s.copy(ao_st[:, :], pv[p][0:64, :]),
                       wait=wts)
                ao_v = ss.n
                wts = [(sv, rcp_v), (sv, ones_v)]
                if rb_free:
                    wts.append((sv, rb_free))
                P.emit("tensor", lambda t: t.matmul(
                    rb[:, :], ones1[:, :], recip_sb[:, :], start=True, stop=True),
                    wait=wts, inc=[(st, 1)])
                rcp_free = st.n
                rb_v = st.n
                P.emit("vector", lambda v, _qc=qc, _hb=hb, _et=et: v.tensor_mul(
                    aoT[_hb:_hb + 64, T * _et + 512 * _qc: T * _et + 512 * (_qc + 1)],
                    ao_st[:, :], rb[:, :]),
                    wait=[(st, rb_v), (ss, ao_v)])
                ao_free = sv.n
                rb_free = sv.n
                pv_freed[pvp] = sv.n
            expg_free[par] = sv.n
            fil = bufs.get("filler")
            if fil is not None:
                fil()

    with contextlib.ExitStack() as octx:
        po = [octx.enter_context(nc.psum_tensor(f"B_po{i}", [128, 512], F32)) for i in range(2)]
        ptr3 = [octx.enter_context(nc.psum_tensor(f"B_ptr3{i}", [128, 128], BF16)) for i in range(2)]
        ostage = octx.enter_context(nc.sbuf_tensor("B_ost", [128, 512], BF16))
        stok = octx.enter_context(nc.sbuf_tensor("B_stok", [128, 4 * 1024], BF16))
        x_in = [octx.enter_context(nc.sbuf_tensor(f"B_xi{i}", [128, 1024], F32)) for i in range(2)]
        if g_sb is None:
            g_st = [octx.enter_context(nc.sbuf_tensor(f"B_gst{i}", [128, 1024], BF16)) for i in range(2)]
        else:
            g_st = None
        x1st = [octx.enter_context(nc.sbuf_tensor(f"B_x1{i}", [128, 1024], F32)) for i in range(2)]
        wobuf = octx.enter_context(nc.sbuf_tensor("B_wo", [128, 8 * 1024], BF16))
        bar_o = [(st, st.n), (ss, ss.n), (sv, sv.n)]
        wo_v = gwq[0].load(wobuf[:, :], hbm["woT"][:, :],
                           wait=list(bar_o))
        ao_all = P.sems["v"].n
        po_free = [0, 0]
        ptr3_free = [0, 0]
        ost_free = 0
        stok_free = 0
        xi_free = [0, 0]
        x1_free = [0, 0]
        for qc in range(NQC):
            for et in range(8):
                pa = et % 2
                for dc in range(8):
                    wts = [(gwq[0].sem, wo_v), (sv, ao_all), (ss, bar_o[1][1])]
                    if dc == 0 and po_free[pa]:
                        wts.append((ss, po_free[pa]))
                    P.emit("tensor", lambda t, d=dc, e=et, p=pa, _qc=qc: t.matmul(
                        po[p][:, :],
                        wobuf[:, 1024 * d + 128 * e: 1024 * d + 128 * (e + 1)],
                        aoT[:, T * d + 512 * _qc: T * d + 512 * (_qc + 1)],
                        start=(d == 0), stop=(d == 7)),
                        wait=wts, inc=[(st, 1)] if dc == 7 else [])
                mm_v = st.n
                wts = [(st, mm_v)]
                if ost_free:
                    wts.append((st, ost_free))
                P.emit("scalar", lambda s, p=pa: s.copy(ostage[:, :], po[p][:, :]),
                       wait=wts)
                po_free[pa] = ss.n
                ost_v = ss.n
                for j in range(4):
                    tp3 = j % 2
                    wts = [(ss, ost_v), (gc.sem, CW)]
                    if ptr3_free[tp3]:
                        wts.append((sv, ptr3_free[tp3]))
                    P.emit("tensor", lambda t, _j=j, p=tp3: t.transpose(
                        ptr3[p][:, :], ostage[:, 128 * _j:128 * (_j + 1)], identb[:, :]),
                        wait=wts, inc=[(st, 1)])
                    tr_v = st.n
                    wts = [(st, tr_v)]
                    P.emit("vector", lambda v, _j=j, e=et, p=tp3: v.tensor_copy(
                        stok[:, 1024 * _j + 128 * e: 1024 * _j + 128 * (e + 1)], ptr3[p][:, :]),
                        wait=wts)
                    ptr3_free[tp3] = sv.n
                ost_free = st.n
            stok_v = sv.n
            for j in range(4):
                tt = 4 * qc + j
                par = j % 2
                wts = list(bar_o) + ([(sv, xi_free[par])] if xi_free[par] else [])
                xv = sxc[par].load(x_in[par][:, :], hbm["x"][128 * tt:128 * (tt + 1), :], wait=wts)
                if g_st is not None:
                    gv = sg[par].load(g_st[par][:, :], hbm["g_in"][128 * tt:128 * (tt + 1), :],
                                      wait=wts)
                    g_dep = [(sg[par].sem, gv)]
                    g_ap = g_st[par]
                    g_lo = 0
                else:
                    g_dep = [(P.sems["s"], bufs["g_ready"])]
                    g_ap = g_sb
                    g_lo = 1024 * tt
                wts = [(sv, stok_v), (sxc[par].sem, xv)] + g_dep
                if x1_free[par]:
                    wts.append((so[par].sem, x1_free[par]))
                P.emit("vector", lambda v, _j=j, p=par, ga=g_ap, lo=g_lo: v.tensor_mul(
                    x1st[p][:, :], stok[:, 1024 * _j:1024 * (_j + 1)],
                    ga[:, lo:lo + 1024]),
                    wait=wts)
                P.emit("vector", lambda v, p=par: v.tensor_add(x1st[p][:, :], x1st[p][:, :], x_in[p][:, :]))
                x1_v = sv.n
                xi_free[par] = x1_v
                x1w = so[par].load(hbm["x1_out"][128 * tt:128 * (tt + 1), :], x1st[par][:, :],
                                   wait=[(sv, x1_v)])
                x1_free[par] = x1w
            stok_free = sv.n
        P.emit("sync", lambda s: None, wait=[(so[0].sem, so[0].wv()), (so[1].sem, so[1].wv())])
    return


def build_B(cfg):
    nc = bass.Bass("TRN2", target_bir_lowering=False)
    T = cfg["T"]
    hbm = {}
    hbm["x"] = nc.dram_tensor("x", [T, 1024], F32, kind="ExternalInput").ap()
    hbm["g_in"] = nc.dram_tensor("g_in", [T, 1024], BF16, kind="ExternalInput").ap()
    for nm in ("wqT", "wkT", "wvT", "woT"):
        hbm[nm] = nc.dram_tensor(nm, [128, 8 * 1024], F32, kind="ExternalInput").ap()
    hbm["gt"] = nc.dram_tensor("gt", [16, 128, 2 * T - 128], F32, kind="ExternalInput").ap()
    hbm["ident"] = nc.dram_tensor("ident", [128, 128], F32, kind="ExternalInput").ap()
    hbm["x1_out"] = nc.dram_tensor("x1_out", [T, 1024], F32, kind="ExternalOutput").ap()
    with contextlib.ExitStack() as ctx:
        P = Plan(nc, ctx)
        for s in ("t", "s", "v", "g"):
            P.sem(s)
        bufs = {}
        bufs["ident"] = ctx.enter_context(nc.sbuf_tensor("identb", [128, 128], BF16))
        build_phase_B(P, hbm, bufs, cfg)
        P.run()
    return nc




def build_phase_C(P, hbm, bufs, cfg):
    nc = P.nc
    T = cfg["T"]; NT = T // 128; NDT = 8
    NH = 16
    QC = 512; NQC = T // QC
    NKT = T // 128
    WIN_DELTAS = [-256, -128, 0, 128, 256, 384]
    st = P.sems["t"]; ss = P.sems["s"]; sv = P.sems["v"]
    ctx = P.ctx

    identb = bufs["ident"]
    gc = DmaStream(P, "C_gc", "gpsimd")
    gwq = [DmaStream(P, f"C_gw{i}", "gpsimd") for i in range(2)]
    so = [DmaStream(P, f"C_so{i}", "gpsimd") for i in range(2)]

    QT = ctx.enter_context(nc.sbuf_tensor("C_QT", [128, 8 * T], BF16))
    KT = ctx.enter_context(nc.sbuf_tensor("C_KT", [128, 8 * T], BF16))
    VTM = ctx.enter_context(nc.sbuf_tensor("C_VTM", [128, NKT * (NH * 65)], BF16))
    ones1 = ctx.enter_context(nc.sbuf_tensor("C_ones1", [1, 64], F32))

    gc.load(identb[:, :], hbm["ident"][:, :])
    CW = gc.wv()
    P.emit("vector", lambda v: v.memset(VTM[:, :], 1.0))
    vtm_init = sv.n
    P.emit("vector", lambda v: v.memset(ones1[:, :], 1.0))
    ones_v = sv.n

    with contextlib.ExitStack() as lctx:
        Pctx_saved = P.ctx
        P.ctx = lctx
        xnT = lctx.enter_context(nc.sbuf_tensor("C_xnT", [128, 8 * T], BF16))
        wbuf = [lctx.enter_context(nc.sbuf_tensor(f"C_w{i}", [128, 8 * 1024], BF16)) for i in range(2)]
        xn_done = emit_ln_to_xnT(P, hbm["x1"], xnT, identb, cfg, "C_ln", deps=[(gc.sem, CW)])
        P.ctx = Pctx_saved

        names = ["l1qT", "l1kT", "l1vT"]
        wv_ld = [0, 0, 0]
        wproj_done = [0, 0, 0]
        wv_ld[0] = gwq[0].load(wbuf[0][:, :], hbm[names[0]][:, :])
        wv_ld[1] = gwq[1].load(wbuf[1][:, :], hbm[names[1]][:, :])
        with contextlib.ExitStack() as pctx:
            pp = [pctx.enter_context(nc.psum_tensor(f"C_pp{i}", [128, 512], F32)) for i in range(2)]
            ptr2 = [pctx.enter_context(nc.psum_tensor(f"C_ptr2{i}", [128, 128], BF16)) for i in range(2)]
            vstage = pctx.enter_context(nc.sbuf_tensor("C_vstage", [128, 512], BF16))
            pp_free = [0, 0]
            ptr2_free = [0, 0]
            vstage_free = 0
            for wi, nm in enumerate(names):
                par = wi % 2
                if wi >= 2:
                    wv_ld[wi] = gwq[par].load(wbuf[par][:, :], hbm[names[wi]][:, :],
                                              wait=[(st, wproj_done[wi - 2])])
                for et in range(8):
                    for tcq in range(T // 512):
                        pa = (et * (T // 512) + tcq) % 2
                        for dc in range(8):
                            wts = [(gwq[par].sem, wv_ld[wi]), (sv, xn_done)]
                            if dc == 0 and pp_free[pa]:
                                wts.append((sv, pp_free[pa]))
                            P.emit("tensor", lambda t, d=dc, e=et, q=tcq, p=pa, w=par: t.matmul(
                                pp[p][:, :],
                                wbuf[w][:, 1024 * d + 128 * e: 1024 * d + 128 * (e + 1)],
                                xnT[:, T * d + 512 * q: T * d + 512 * (q + 1)],
                                start=(d == 0), stop=(d == 7)),
                                wait=wts, inc=[(st, 1)] if dc == 7 else [])
                        mm_v = st.n
                        if nm == "l1qT":
                            P.emit("vector", lambda v, e=et, q=tcq, p=pa: v.tensor_copy(
                                QT[:, T * e + 512 * q: T * e + 512 * (q + 1)], pp[p][:, :]),
                                wait=[(st, mm_v)])
                            pp_free[pa] = sv.n
                        elif nm == "l1kT":
                            P.emit("vector", lambda v, e=et, q=tcq, p=pa: v.tensor_copy(
                                KT[:, T * e + 512 * q: T * e + 512 * (q + 1)], pp[p][:, :]),
                                wait=[(st, mm_v)])
                            pp_free[pa] = sv.n
                        else:
                            wts = [(st, mm_v)]
                            if vstage_free:
                                wts.append((st, vstage_free))
                            P.emit("vector", lambda v, p=pa: v.tensor_copy(vstage[:, :], pp[p][:, :]),
                                   wait=wts)
                            pp_free[pa] = sv.n
                            vst_v = sv.n
                            for j in range(4):
                                tp2 = j % 2
                                wts = [(sv, vst_v), (gc.sem, CW)]
                                if ptr2_free[tp2]:
                                    wts.append((sv, ptr2_free[tp2]))
                                P.emit("tensor", lambda t, _j=j, p=tp2: t.transpose(
                                    ptr2[p][:, :], vstage[:, 128 * _j:128 * (_j + 1)], identb[:, :]),
                                    wait=wts, inc=[(st, 1)])
                                tr_v = st.n
                                kt_idx = tcq * 4 + j
                                for half in range(2):
                                    h_idx = 2 * et + half
                                    P.emit("vector", lambda v, k=kt_idx, h=h_idx, p=tp2, hf=half: v.tensor_copy(
                                        VTM[:, 65 * NH * k + 65 * h: 65 * NH * k + 65 * h + 64],
                                        ptr2[p][:, 64 * hf:64 * (hf + 1)]),
                                        wait=[(st, tr_v), (sv, vtm_init)])
                                ptr2_free[tp2] = sv.n
                            vstage_free = st.n
                wproj_done[wi] = st.n
        proj_evac_v = P.sems["v"].n

    aoT = ctx.enter_context(nc.sbuf_tensor("C_aoT", [128, 8 * T], BF16))

    with contextlib.ExitStack() as actx:
        m01 = actx.enter_context(nc.sbuf_tensor("C_m01", [128, 6 * 512], BF16))
        gc.load(m01.rearrange("p (d n) -> p d n", d=6), hbm["m01"][:, :, :].rearrange("d p n -> p d n"),
                wait=[(st, st.n), (ss, ss.n), (sv, sv.n)])
        CWM = gc.wv()
        pstage = [actx.enter_context(nc.sbuf_tensor(f"C_pst{i}", [128, 1024], BF16)) for i in range(2)]
        pm_sb = [actx.enter_context(nc.sbuf_tensor(f"C_pm{i}", [128, 1024], BF16)) for i in range(4)]
        recip_sb = actx.enter_context(nc.sbuf_tensor("C_rcp", [1, 512], F32))
        ao_st = actx.enter_context(nc.sbuf_tensor("C_aost", [64, 512], BF16))
        S2 = [actx.enter_context(nc.psum_tensor(f"C_S2{i}", [128, 1024], F32)) for i in range(2)]
        pv = [actx.enter_context(nc.psum_tensor(f"C_pv{i}", [65, 512], F32)) for i in range(2)]
        rb = actx.enter_context(nc.psum_tensor("C_rb", [64, 512], F32))
        bar_att = [(st, st.n), (ss, ss.n), (sv, sv.n)]

        S2_free = [0, 0]
        pst_free = [0, 0]
        pm_free = [0, 0, 0, 0]
        pv_freed = [0, 0]
        rcp_free = 0
        ao_free = 0
        rb_free = 0

        for h in range(NH):
            hb = 64 * (h % 2)
            et = h // 2
            for qc in range(NQC):
                kts = [(di, (512 * qc + d) // 128) for di, d in enumerate(WIN_DELTAS)
                       if 0 <= 512 * qc + d < T]
                npair = (len(kts) + 1) // 2
                pvp = qc % 2
                mult_vals = [0] * npair

                def emit_pv(pi, pairs=None):
                    pair = kts[2 * pi: 2 * pi + 2]
                    for u, (di, kt) in enumerate(pair):
                        first = (pi == 0 and u == 0)
                        last = (pi == npair - 1 and u == len(pair) - 1)
                        wts = [(sv, mult_vals[pi])]
                        if first:
                            wts.append((sv, vtm_init))
                            if pv_freed[pvp]:
                                wts.append((sv, pv_freed[pvp]))
                        P.emit("tensor", lambda t, k=kt, _u=u, p=pvp, _h=h, f=first, l2=last, mp=pi % 4: t.matmul(
                            pv[p][:, :],
                            VTM[:, 65 * NH * k + 65 * _h: 65 * NH * k + 65 * _h + 65],
                            pm_sb[mp][:, 512 * _u:512 * (_u + 1)],
                            start=f, stop=l2),
                            wait=wts, inc=[(st, 1)] if last else [])
                        pm_free[pi % 4] = st.n

                for pi in range(npair):
                    pair = kts[2 * pi: 2 * pi + 2]
                    sp = pi % 2
                    for u, (di, kt) in enumerate(pair):
                        wts = []
                        if u == 0 and S2_free[sp]:
                            wts.append((ss, S2_free[sp]))
                        if h == 0 and qc == 0 and pi == 0 and u == 0:
                            wts.append((st, wproj_done[2]))
                            wts.append((sv, proj_evac_v))
                        P.emit("tensor", lambda t, k=kt, _u=u, p=sp, _qc=qc, _hb=hb, _et=et: t.matmul(
                            S2[p][:, 512 * _u:512 * (_u + 1)],
                            KT[_hb:_hb + 64, T * _et + 128 * k: T * _et + 128 * (k + 1)],
                            QT[_hb:_hb + 64, T * _et + 512 * _qc: T * _et + 512 * (_qc + 1)],
                            start=True, stop=True),
                            wait=wts, inc=[(st, 1)] if u == len(pair) - 1 else [])
                    sc_v = st.n
                    mp = pi % 4
                    wts = [(st, sc_v), (gc.sem, CW)]
                    if pst_free[mp % 2]:
                        wts.append((sv, pst_free[mp % 2]))
                    wdt = 512 * len(pair)
                    P.emit("scalar", lambda s, p=sp, w=wdt, _mp=mp % 2: s.activation(
                        pstage[_mp][:, 0:w], S2[p][:, 0:w], AF.Exp, scale=0.125),
                        wait=wts)
                    S2_free[sp] = ss.n
                    exp_v = ss.n
                    for u, (di, kt) in enumerate(pair):
                        wts = [(ss, exp_v), (gc.sem, CWM)]
                        if pm_free[mp]:
                            wts.append((st, pm_free[mp]))
                        P.emit("vector", lambda v, _u=u, _di=di, _mp=mp, _ps=mp % 2: v.tensor_mul(
                            pm_sb[_mp][:, 512 * _u:512 * (_u + 1)],
                            pstage[_ps][:, 512 * _u:512 * (_u + 1)],
                            m01[:, 512 * _di:512 * (_di + 1)]),
                            wait=wts)
                    pst_free[mp % 2] = sv.n
                    mult_vals[pi] = sv.n
                for pi in range(npair):
                    emit_pv(pi)
                pv_v = st.n
                pvp = qc % 2
                wts = [(st, pv_v)]
                if rcp_free:
                    wts.append((st, rcp_free))
                P.emit("vector", lambda v, p=pvp: v.reciprocal(recip_sb[:, :], pv[p][64:65, :]),
                       wait=wts)
                rcp_v = sv.n
                wts = [(st, pv_v)]
                if ao_free:
                    wts.append((sv, ao_free))
                P.emit("scalar", lambda s, p=pvp: s.copy(ao_st[:, :], pv[p][0:64, :]),
                       wait=wts)
                ao_v = ss.n
                wts = [(sv, rcp_v), (sv, ones_v)]
                if rb_free:
                    wts.append((sv, rb_free))
                P.emit("tensor", lambda t: t.matmul(
                    rb[:, :], ones1[:, :], recip_sb[:, :], start=True, stop=True),
                    wait=wts, inc=[(st, 1)])
                rcp_free = st.n
                rb_v = st.n
                P.emit("vector", lambda v, _qc=qc, _hb=hb, _et=et: v.tensor_mul(
                    aoT[_hb:_hb + 64, T * _et + 512 * _qc: T * _et + 512 * (_qc + 1)],
                    ao_st[:, :], rb[:, :]),
                    wait=[(st, rb_v), (ss, ao_v)])
                ao_free = sv.n
                rb_free = sv.n
                pv_freed[pvp] = sv.n
            fil = bufs.get("filler")
            if fil is not None:
                fil()

    with contextlib.ExitStack() as octx:
        po = [octx.enter_context(nc.psum_tensor(f"C_po{i}", [128, 512], F32)) for i in range(2)]
        ptr3 = [octx.enter_context(nc.psum_tensor(f"C_ptr3{i}", [128, 128], BF16)) for i in range(2)]
        ostage = octx.enter_context(nc.sbuf_tensor("C_ost", [128, 512], BF16))
        stok = octx.enter_context(nc.sbuf_tensor("C_stok", [128, 4 * 1024], BF16))
        wobuf = octx.enter_context(nc.sbuf_tensor("C_wo", [128, 8 * 1024], BF16))
        bar_o = [(st, st.n), (ss, ss.n), (sv, sv.n)]
        wo_v = gwq[0].load(wobuf[:, :], hbm["l1oT"][:, :], wait=list(bar_o))
        ao_all = P.sems["v"].n
        po_free = [0, 0]
        ptr3_free = [0, 0]
        ost_free = 0
        stok_free = [0, 0]
        for qc in range(NQC):
            for et in range(8):
                pa = et % 2
                for dc in range(8):
                    wts = [(gwq[0].sem, wo_v), (sv, ao_all), (ss, bar_o[1][1])]
                    if dc == 0 and po_free[pa]:
                        wts.append((ss, po_free[pa]))
                    P.emit("tensor", lambda t, d=dc, e=et, p=pa, _qc=qc: t.matmul(
                        po[p][:, :],
                        wobuf[:, 1024 * d + 128 * e: 1024 * d + 128 * (e + 1)],
                        aoT[:, T * d + 512 * _qc: T * d + 512 * (_qc + 1)],
                        start=(d == 0), stop=(d == 7)),
                        wait=wts, inc=[(st, 1)] if dc == 7 else [])
                mm_v = st.n
                wts = [(st, mm_v)]
                if ost_free:
                    wts.append((st, ost_free))
                P.emit("scalar", lambda s, p=pa: s.copy(ostage[:, :], po[p][:, :]),
                       wait=wts)
                po_free[pa] = ss.n
                ost_v = ss.n
                for j in range(4):
                    tp3 = j % 2
                    wts = [(ss, ost_v), (gc.sem, CW)]
                    if ptr3_free[tp3]:
                        wts.append((sv, ptr3_free[tp3]))
                    P.emit("tensor", lambda t, _j=j, p=tp3: t.transpose(
                        ptr3[p][:, :], ostage[:, 128 * _j:128 * (_j + 1)], identb[:, :]),
                        wait=wts, inc=[(st, 1)])
                    tr_v = st.n
                    wts = [(st, tr_v)]
                    par_t = j % 2
                    if et == 0 and stok_free[par_t]:
                        wts.append((so[par_t].sem, stok_free[par_t]))
                    P.emit("vector", lambda v, _j=j, e=et, p=tp3: v.tensor_copy(
                        stok[:, 1024 * _j + 128 * e: 1024 * _j + 128 * (e + 1)], ptr3[p][:, :]),
                        wait=wts)
                    ptr3_free[tp3] = sv.n
                ost_free = st.n
            stok_v = sv.n
            for j in range(4):
                tt = 4 * qc + j
                par = j % 2
                stok_free[par] = so[par].load(
                    hbm["s1_out"][128 * tt:128 * (tt + 1), :],
                    stok[:, 1024 * j:1024 * (j + 1)], wait=[(sv, stok_v)])
        P.emit("sync", lambda s: None, wait=[(so[0].sem, so[0].wv()), (so[1].sem, so[1].wv())])
    return


def build_C(cfg):
    nc = bass.Bass("TRN2", target_bir_lowering=False)
    T = cfg["T"]
    hbm = {}
    hbm["x1"] = nc.dram_tensor("x1", [T, 1024], F32, kind="ExternalInput").ap()
    for nm in ("l1qT", "l1kT", "l1vT", "l1oT"):
        hbm[nm] = nc.dram_tensor(nm, [128, 8 * 1024], F32, kind="ExternalInput").ap()
    hbm["m01"] = nc.dram_tensor("m01", [6, 128, 512], F32, kind="ExternalInput").ap()
    hbm["ident"] = nc.dram_tensor("ident", [128, 128], F32, kind="ExternalInput").ap()
    hbm["s1_out"] = nc.dram_tensor("s1_out", [T, 1024], BF16, kind="ExternalOutput").ap()
    with contextlib.ExitStack() as ctx:
        P = Plan(nc, ctx)
        for s in ("t", "s", "v", "g"):
            P.sem(s)
        bufs = {}
        bufs["ident"] = ctx.enter_context(nc.sbuf_tensor("identb", [128, 128], BF16))
        build_phase_C(P, hbm, bufs, cfg)
        P.run()
    return nc




class WeightQuantEmitter:

    def __init__(self, P, hbm, cfg, identb):
        self.P = P
        self.bufs_ident = identb
        nc = P.nc
        I = cfg.get("I", 2816)
        ctx = P.ctx
        self.gin = [DmaStream(P, f"WQ_in{i}", "sync") for i in range(2)]
        self.sout = [DmaStream(P, f"WQ_out{i}", "sync") for i in range(2)]
        self.wtile = [ctx.enter_context(nc.sbuf_tensor(f"WQ_wt{i}", [128, 512], F32, side="right"))
                      for i in range(4)]
        WMX = max(1024, I)
        self.wq_t = [ctx.enter_context(nc.sbuf_tensor(f"WQ_q{i}", [128, WMX], BF16, side="right"))
                     for i in range(2)]

        qs = {}
        qs["t_s"] = ctx.enter_context(nc.sbuf_tensor("WQ_ts", [128, 32], F32, side="right"))
        qs["rcp"] = ctx.enter_context(nc.sbuf_tensor("WQ_rcp", [128, 32], F32, side="right"))
        for nm in ("absx", "y", "accA", "accC", "msf", "c2"):
            qs[nm] = ctx.enter_context(nc.sbuf_tensor(f"WQ_{nm}", [128, 512], F32, side="right"))
        qs["u"] = qs["absx"]
        self.qs = qs
        self._pctx = contextlib.ExitStack()
        self.jobs = []
        for nt in range(I // 128):
            self.jobs.append(("wgate", "wgq_nat", nt, 1024))
        for nt in range(I // 128):
            self.jobs.append(("wup", "wuq_nat", nt, 1024))
        for nt in range(8):
            self.jobs.append(("wdown", "wdq_nat", nt, I))
        self.hbm = hbm
        self.flat = []
        for j, (src_name, dst, nt, width) in enumerate(self.jobs):
            nch = (width + 511) // 512
            for c in range(nch):
                self.flat.append((j, c, min(512, width - 512 * c), c == nch - 1))
        self.fcur = 0
        self.nloaded = 0
        self.loadv = {}
        self.wt_free = [(0, 0)] * 4
        self.wq_free = [0, 0]
        self.done_dep = None

    def _load_upto(self, i):
        P = self.P
        ss = P.sems["s"]; sv = P.sems["v"]
        while self.nloaded < min(i, len(self.flat)):
            pos = self.nloaded
            j, c, w2, last = self.flat[pos]
            src_name, dst, nt, width = self.jobs[j]
            bi = pos % 4
            fs, fv = self.wt_free[bi]
            wts = ([(ss, fs)] if fs else []) + ([(sv, fv)] if fv else [])
            self.loadv[pos] = self.gin[j % 2].load(
                self.wtile[bi][:, 0:w2],
                self.hbm[src_name][128 * nt:128 * (nt + 1), 512 * c:512 * c + w2],
                wait=wts)
            self.nloaded += 1

    def emit_chunk(self):
        pos = self.fcur
        if pos >= len(self.flat):
            return False
        P = self.P
        j, c, w2, last = self.flat[pos]
        src_name, dst, nt, width = self.jobs[j]
        self._load_upto(pos + 4)
        bi = pos % 4
        qp = j % 2
        wts = [(self.gin[j % 2].sem, self.gin[j % 2].wv())]
        if self.wq_free[qp] and c == 0:
            wts.append((self.sout[qp].sem, self.wq_free[qp]))
        qv = emit_quant(P, self.wq_t[qp][:, 512 * c:512 * c + w2],
                        self.wtile[bi][:, 0:w2], w2, self.qs,
                        src_ready=wts)
        self.wt_free[bi] = (self.qs["last_scalar"], self.qs["src_free_v"])
        self.fcur += 1
        if last:
            self.wq_free[qp] = self.sout[qp].load(
                self.hbm[dst][128 * nt:128 * (nt + 1), 0:width],
                self.wq_t[qp][:, 0:width], wait=[qv])
            if self.fcur >= len(self.flat):
                self._pctx.close()
                self.done_dep = [(self.sout[0].sem, self.sout[0].wv()),
                                 (self.sout[1].sem, self.sout[1].wv())]
        return True

    def finish(self):
        while self.emit_chunk():
            pass
        return self.done_dep



def _hap(h, pairs):
    return bass.AP(h, 0, [[h.shape[1], 128]] + pairs)


def emit_quant(P, dst, src, width, qs, src_ready=(), extra_dst_wait=()):
    nc = P.nc
    ss = P.sems["s"]; sv = P.sems["v"]
    nb = width // 16
    src3 = bass.AP(src.tensor, src.offset,
                   [list(src.ap[0]), [16, nb], [1, 16]])
    ts = qs["t_s"]; rcp = qs["rcp"]
    P.emit("vector", lambda v: v.reduce_max(ts[:, 0:nb], src3, AX.X,
                                            apply_absolute_value=True),
           wait=list(src_ready))
    qs["src_free_v"] = sv.n
    P.emit("vector", lambda v: v.tensor_scalar(ts[:, 0:nb], ts[:, 0:nb],
                                               1.0 / 6.0, 1e-12, ALU.mult, ALU.max))
    P.emit("vector", lambda v: v.reciprocal(rcp[:, 0:nb], ts[:, 0:nb]))
    rcp_v = sv.n
    P.emit("scalar", lambda s: s.activation(qs["absx"][:, 0:width], src, AF.Abs),
           wait=list(src_ready) + ([(sv, qs["absx_free"])] if qs.get("absx_free") else []))
    absx_v = ss.n
    rcp_b = _hap(rcp, [[1, nb], [0, 16]])
    y3 = _hap(qs["y"], [[16, nb], [1, 16]])
    absx3 = _hap(qs["absx"], [[16, nb], [1, 16]])
    P.emit("vector", lambda v: v.tensor_mul(y3, absx3, rcp_b),
           wait=[(ss, absx_v)])
    qs["absx_free"] = sv.n
    y = qs["y"]; accA = qs["accA"]; accC = qs["accC"]
    u = qs["u"]; c2 = qs["c2"]
    W = width
    P.emit("vector", lambda v: v.tensor_single_scalar(
        accA[:, 0:W], y[:, 0:W], 0.25, ALU.is_gt))
    for c in (0.75, 1.25, 1.75):
        P.emit("vector", lambda v, _c=c: v.scalar_tensor_tensor(
            accA[:, 0:W], y[:, 0:W], _c, accA[:, 0:W], ALU.is_gt, ALU.add))
    P.emit("vector", lambda v: v.tensor_single_scalar(
        accC[:, 0:W], y[:, 0:W], 2.5, ALU.is_gt))
    P.emit("vector", lambda v: v.scalar_tensor_tensor(
        accC[:, 0:W], y[:, 0:W], 3.5, accC[:, 0:W], ALU.is_gt, ALU.add))
    P.emit("vector", lambda v: v.tensor_single_scalar(
        c2[:, 0:W], y[:, 0:W], 5.0, ALU.is_gt))
    P.emit("scalar", lambda s: s.activation(qs["msf"][:, 0:width], src, AF.Sign),
           wait=list(src_ready) + ([(sv, qs["msf_free"])] if qs.get("msf_free") else []))
    sgn_v = ss.n
    qs["last_scalar"] = sgn_v
    ts_b = _hap(ts, [[1, nb], [0, 16]])
    msf3 = _hap(qs["msf"], [[16, nb], [1, 16]])
    P.emit("vector", lambda v: v.tensor_mul(msf3, msf3, ts_b), wait=[(ss, sgn_v)])
    P.emit("vector", lambda v: v.scalar_tensor_tensor(
        u[:, 0:W], accA[:, 0:W], 0.5, accC[:, 0:W], ALU.mult, ALU.add))
    P.emit("vector", lambda v: v.scalar_tensor_tensor(
        u[:, 0:W], c2[:, 0:W], 2.0, u[:, 0:W], ALU.mult, ALU.add))
    P.emit("vector", lambda v: v.tensor_mul(dst, u[:, 0:W], qs["msf"][:, 0:width]),
           wait=list(extra_dst_wait))
    qs["msf_free"] = sv.n
    qs["absx_free"] = sv.n
    return (sv, sv.n)


_QS_N = [0]


def mk_quant_scratch(P, width):
    ctx = P.ctx
    nc = P.nc
    _QS_N[0] += 1
    tag = _QS_N[0]
    qs = {}
    qs["t_s"] = ctx.enter_context(nc.sbuf_tensor(f"q{tag}_ts", [128, width // 16], F32))
    qs["rcp"] = ctx.enter_context(nc.sbuf_tensor(f"q{tag}_rcp", [128, width // 16], F32))
    for nm in ("absx", "y", "accA", "accC", "msf", "c2"):
        qs[nm] = ctx.enter_context(nc.sbuf_tensor(f"q{tag}_{nm}", [128, width], F32))
    qs["u"] = qs["absx"]
    return qs


def emit_sharded_wq(P, hbm, bufs, cfg, ident_dep=None):
    nc = P.nc
    I = cfg.get("I", 2816)
    NI = I // 128
    st = P.sems["t"]; ss = P.sems["s"]; sv = P.sems["v"]
    ctx = P.ctx
    identb = bufs["ident"]
    with contextlib.ExitStack() as wctx:
        saved = P.ctx
        P.ctx = wctx
        gin = [DmaStream(P, f"SW_in{i}", "sync") for i in range(2)]
        sosc = [DmaStream(P, f"SW_out{i}", "sync") for i in range(2)]
        srs = DmaStream(P, "SW_rs", "sync")
        ccs = P.sem("SW_cc")
        WMX = max(1024, I)
        wtile = [wctx.enter_context(nc.sbuf_tensor(f"SW_wt{i}", [128, WMX], F32)) for i in range(2)]
        wq_t = wctx.enter_context(nc.sbuf_tensor("SW_q", [128, WMX], BF16))
        stage = [wctx.enter_context(nc.sbuf_tensor(f"SW_st{i}", [128, max(8, NI) * 128], BF16)) for i in range(2)]
        qs = mk_quant_scratch(P, 1024)
        ptrw = [wctx.enter_context(nc.psum_tensor(f"SW_ptr{i}", [128, 128], BF16)) for i in range(2)]
        bar_w = [(st, st.n), (ss, ss.n), (sv, sv.n)]
        wt_free = [(0, 0), (0, 0)]
        wq_free = 0
        ptrw_free = [0, 0]
        stage_free = [0, 0]

        jobs = []
        for s in range(6):
            jobs.append((hbm["wjob_a"][s, :, :], 1024 * s, 1024))
        jobs.append((hbm["wjob_b"][:, :], 6144, I))

        for ji, (srcap, orow, width) in enumerate(jobs):
            par = ji % 2
            nkt = width // 128
            fs, fv = wt_free[par]
            wts = list(bar_w) + ([(ss, fs)] if fs else []) + ([(sv, fv)] if fv else [])
            lv = gin[par].load(wtile[par][:, 0:width], srcap, wait=wts)
            for qoff in range(0, width, 1024):
                w2 = min(1024, width - qoff)
                qv = emit_quant(P, wq_t[:, qoff:qoff + w2],
                                wtile[par][:, qoff:qoff + w2], w2, qs,
                                src_ready=[(gin[par].sem, lv)],
                                extra_dst_wait=([(st, wq_free)] if (wq_free and qoff == 0) else []))
            wt_free[par] = (qs["last_scalar"], qs["src_free_v"])
            spar = ji % 2
            for kt in range(nkt):
                tp = kt % 2
                wts = [qv] + ([ident_dep] if ident_dep else [])
                if ptrw_free[tp]:
                    wts.append((sv, ptrw_free[tp]))
                P.emit("tensor", lambda t, k=kt, p=tp, _pa=par: t.transpose(
                    ptrw[p][:, :], wq_t[:, 128 * k:128 * (k + 1)], identb[:, :]),
                    wait=wts, inc=[(st, 1)])
                tr_v = st.n
                wts = [(st, tr_v)]
                if kt == 0 and stage_free[spar]:
                    wts.append((sosc[spar].sem, stage_free[spar]))
                P.emit("vector", lambda v, k=kt, p=tp, sp=spar: v.tensor_copy(
                    stage[sp][:, 128 * k:128 * (k + 1)], ptrw[p][:, :]),
                    wait=wts)
                ptrw_free[tp] = sv.n
            wq_free = st.n
            stg_v = sv.n
            dst = hbm["wq_outc"].rearrange("(r p) c -> p r c", p=128)[:, orow // 128: orow // 128 + nkt, :]
            stage_free[spar] = sosc[spar].load(
                dst, stage[spar].rearrange("p (k n) -> p k n", n=128)[:, 0:nkt, :],
                wait=[(sv, stg_v)])

        ag_wait = [(sosc[0].sem, sosc[0].wv()), (sosc[1].sem, sosc[1].wv())]
        P.emit("gpsimd", lambda g: g.collective_compute(
            "AllGather", ALU.bypass, replica_groups=[list(range(NCORES))],
            ins=[hbm["wq_outc"]], outs=[hbm["wq_gath"]]),
            wait=ag_wait, inc=[(ccs, 16)], is_dma=True)
        agv = ccs.n

        blk = (6 * 1024 + I) // 128
        gath = hbm["wq_gath"].rearrange("(r p) c -> r p c", p=128)
        for ja in range(2 * NI):
            c, s = divmod(ja, 6)
            r0 = c * blk + s * 8
            dstname = "wgq_scr" if ja < NI else "wuq_scr"
            nt = ja % NI
            srs.load(hbm[dstname][:, :, 128 * nt:128 * (nt + 1)],
                     gath[r0:r0 + 8, :, :], wait=[(ccs, agv)])
        for c in range(8):
            r0 = c * blk + 48
            srs.load(hbm["wdq_scr"][:, :, 128 * c:128 * (c + 1)],
                     gath[r0:r0 + NI, :, :], wait=[(ccs, agv)])
        dep = [(srs.sem, srs.wv())]
        P.ctx = saved
    return dep


def build_phase_D(P, hbm, bufs, cfg):
    nc = P.nc
    T = cfg["T"]; NT = T // 128
    I = cfg.get("I", 2816)
    NI = I // 128
    st = P.sems["t"]; ss = P.sems["s"]; sv = P.sems["v"]
    ctx = P.ctx
    identb = bufs["ident"]

    gcD = DmaStream(P, "D_gc", "gpsimd")
    gcD.load(identb[:, :], hbm["ident"][:, :])
    CW = gcD.wv()

    xqT = ctx.enter_context(nc.sbuf_tensor("D_xqT", [128, 8 * T], BF16))
    out_acc = ctx.enter_context(nc.sbuf_tensor("D_oacc", [128, NT * 1024], F32))

    with contextlib.ExitStack() as lctx:
        Ps = P.ctx; P.ctx = lctx
        sx = [DmaStream(P, f"D_sx{i}", "sync") for i in range(2)]
        x_t = [lctx.enter_context(nc.sbuf_tensor(f"D_x{i}", [128, 1024], F32)) for i in range(2)]
        xsq = lctx.enter_context(nc.sbuf_tensor("D_xsq", [128, 1024], F32))
        stat4 = lctx.enter_context(nc.sbuf_tensor("D_stat", [128, 8], F32))
        xn_tok = lctx.enter_context(nc.sbuf_tensor("D_xntok", [128, 1024], F32))
        xq_tok = lctx.enter_context(nc.sbuf_tensor("D_xqtok", [128, 1024], BF16))
        eps_col = lctx.enter_context(nc.sbuf_tensor("D_eps", [128, 1], F32))
        qs = mk_quant_scratch(P, 1024)
        ptr = [lctx.enter_context(nc.psum_tensor(f"D_ptr{i}", [128, 128], BF16)) for i in range(2)]
        P.emit("vector", lambda v: v.memset(eps_col[:, :], 1e-5))
        eps_v = sv.n
        x_free = [0, 0]
        xsq_free = 0
        xn_free = 0
        xq_free = 0
        ptr_free = [0, 0]
        for tt in range(NT):
            par = tt % 2
            wts = ([(sv, x_free[par])] if x_free[par] else [])
            xv = sx[par].load(x_t[par][:, :], hbm["x1"][128 * tt:128 * (tt + 1), :], wait=wts)
            xw = (sx[par].sem, xv)
            P.emit("vector", lambda v, p=par: v.reduce_sum(stat4[:, 0:1], x_t[p][:, :], AX.X),
                   wait=[xw])
            swts = [xw] + ([(sv, xsq_free)] if xsq_free else [])
            P.emit("scalar", lambda s, p=par: s.activation(xsq[:, :], x_t[p][:, :], AF.Square),
                   wait=swts)
            sq_done = ss.n
            P.emit("vector", lambda v: v.reduce_sum(stat4[:, 1:2], xsq[:, :], AX.X),
                   wait=[(ss, sq_done)])
            xsq_free = sv.n
            P.emit("vector", lambda v: v.tensor_scalar_mul(stat4[:, 2:3], stat4[:, 0:1], 1.0 / 1024.0))
            P.emit("vector", lambda v: v.tensor_scalar_mul(stat4[:, 3:4], stat4[:, 1:2], 1.0 / 1024.0))
            P.emit("vector", lambda v: v.scalar_tensor_tensor(
                stat4[:, 4:5], stat4[:, 2:3], -1.0, stat4[:, 2:3], ALU.mult, ALU.mult))
            P.emit("vector", lambda v: v.tensor_add(stat4[:, 5:6], stat4[:, 3:4], stat4[:, 4:5]))
            var_v = sv.n
            P.emit("scalar", lambda s: s.activation(stat4[:, 6:7], stat4[:, 5:6], AF.Sqrt, bias=eps_col[:, :]),
                   wait=[(sv, var_v), (sv, eps_v)])
            sd_v = ss.n
            P.emit("vector", lambda v: v.reciprocal(stat4[:, 7:8], stat4[:, 6:7]),
                   wait=[(ss, sd_v)])
            rstd_v = sv.n
            wts = [(sv, rstd_v)] + ([(ss, xn_free)] if xn_free else [])
            P.emit("vector", lambda v, p=par: v.tensor_scalar(
                xn_tok[:, :], x_t[p][:, :], stat4[:, 2:3], stat4[:, 7:8], ALU.subtract, ALU.mult),
                wait=wts)
            xn_v = sv.n
            x_free[par] = xn_v
            qv = emit_quant(P, xq_tok[:, :], xn_tok[:, :], 1024, qs,
                            src_ready=[(sv, xn_v)],
                            extra_dst_wait=([(st, xq_free)] if xq_free else []))
            xn_free = qs["last_scalar"]
            for ft in range(8):
                tp = ft % 2
                wts = [qv]
                if ptr_free[tp]:
                    wts.append((sv, ptr_free[tp]))
                wts.append((gcD.sem, CW))
                P.emit("tensor", lambda t, f=ft, p=tp: t.transpose(
                    ptr[p][:, :], xq_tok[:, 128 * f:128 * (f + 1)], identb[:, :]),
                    wait=wts, inc=[(st, 1)])
                tr_v = st.n
                P.emit("vector", lambda v, f=ft, p=tp, _tt=tt: v.tensor_copy(
                    xqT[:, T * f + 128 * _tt: T * f + 128 * (_tt + 1)], ptr[p][:, :]),
                    wait=[(st, tr_v)])
                ptr_free[tp] = sv.n
            xq_free = st.n
        P.ctx = Ps
        xq_done_v = sv.n

    if bufs.get("wq_done_dep") is not None:
        wq_all_done = bufs["wq_done_dep"]
    else:
        _wq = WeightQuantEmitter(P, hbm, cfg, identb)
        wq_all_done = _wq.finish()

    chunks = []
    off = 0
    while off < I:
        w = min(512, I - off)
        chunks.append((off, w))
        off += w
    with contextlib.ExitStack() as mctx:
        Ps = P.ctx; P.ctx = mctx
        gwd = [DmaStream(P, f"D_gwd{i}", "sync") for i in range(2)]
        wgq_sb = [mctx.enter_context(nc.sbuf_tensor(f"D_wgq{i}", [128, 8 * 512], BF16)) for i in range(2)]
        wuq_sb = [mctx.enter_context(nc.sbuf_tensor(f"D_wuq{i}", [128, 8 * 512], BF16)) for i in range(2)]
        wdq_sb = [mctx.enter_context(nc.sbuf_tensor(f"D_wdq{i}", [128, 4 * 1024], BF16)) for i in range(2)]
        sg_st = mctx.enter_context(nc.sbuf_tensor("D_sg", [128, 512], F32))
        h_st = mctx.enter_context(nc.sbuf_tensor("D_h", [128, 512], F32))
        hq_st = [mctx.enter_context(nc.sbuf_tensor(f"D_hq{i}", [128, 512], BF16)) for i in range(2)]
        hqT_st = mctx.enter_context(nc.sbuf_tensor("D_hqT", [128, 4 * 128], BF16))
        qs3 = mk_quant_scratch(P, 512)
        pg = mctx.enter_context(nc.psum_tensor("D_pg", [128, 512], F32))
        pu = mctx.enter_context(nc.psum_tensor("D_pu", [128, 512], F32))
        pd = [mctx.enter_context(nc.psum_tensor(f"D_pd{i}", [128, 1024], F32)) for i in range(2)]
        ptrh = [mctx.enter_context(nc.psum_tensor(f"D_ptrh{i}", [128, 128], BF16)) for i in range(2)]
        bar_m = [(st, st.n), (ss, ss.n), (sv, sv.n)]

        w_ready = {}
        w_free = {0: 0, 1: 0}

        def load_wchunk(ci):
            offc, wc = chunks[ci]
            par = ci % 2
            nt0 = offc // 128
            wnt = wc // 128
            wts = list(bar_m) + list(wq_all_done) + ([(st, w_free[par])] if w_free[par] else [])
            for kt in range(8):
                gwd[par].load(wgq_sb[par][:, 512 * kt:512 * kt + wc],
                              hbm["wgq_nat"][offc:offc + wc, 128 * kt:128 * (kt + 1)],
                              wait=wts, transpose=True)
                gwd[par].load(wuq_sb[par][:, 512 * kt:512 * kt + wc],
                              hbm["wuq_nat"][offc:offc + wc, 128 * kt:128 * (kt + 1)],
                              transpose=True)
            for nl in range(wnt):
                gwd[par].load(wdq_sb[par][:, 1024 * nl:1024 * (nl + 1)],
                              hbm["wdq_nat"][:, 128 * (nt0 + nl):128 * (nt0 + nl + 1)],
                              transpose=True)
            w_ready[ci] = gwd[par].wv()

        load_wchunk(0)
        if len(chunks) > 1:
            load_wchunk(1)

        pg_free = 0
        pu_free = 0
        pd_free = [0, 0]
        sg_free = 0
        h_free = 0
        hq_free = [0, 0]
        hqT_free = 0
        ptrh_free = [0, 0]
        oacc_ready = [0] * NT

        for ci, (offc, wc) in enumerate(chunks):
            par = ci % 2
            wnt = wc // 128
            wv_ = (gwd[par].sem, w_ready[ci])
            gu_vals = [0] * NT
            dmm_vals = [0] * NT

            def emit_head(tt):
                nonlocal pg_free, pu_free
                for k in range(8):
                    wts = [wv_, (sv, xq_done_v)]
                    if k == 0 and pg_free:
                        wts.append((sv, pg_free))
                    P.emit("tensor", lambda t, _k=k, _tt=tt, p=par, w=wc: t.matmul(
                        pg[:, 0:w],
                        xqT[:, T * _k + 128 * _tt: T * _k + 128 * (_tt + 1)],
                        wgq_sb[p][:, 512 * _k: 512 * _k + w],
                        start=(_k == 0), stop=(_k == 7)),
                        wait=wts, inc=[(st, 1)] if k == 7 else [])
                    wts2 = []
                    if k == 0 and pu_free:
                        wts2.append((sv, pu_free))
                    P.emit("tensor", lambda t, _k=k, _tt=tt, p=par, w=wc: t.matmul(
                        pu[:, 0:w],
                        xqT[:, T * _k + 128 * _tt: T * _k + 128 * (_tt + 1)],
                        wuq_sb[p][:, 512 * _k: 512 * _k + w],
                        start=(_k == 0), stop=(_k == 7)),
                        wait=wts2, inc=[(st, 1)] if k == 7 else [])
                gu_vals[tt] = st.n

            def emit_mid(tt):
                nonlocal sg_free, h_free, pg_free, pu_free
                hp = tt % 2
                wts = [(st, gu_vals[tt])]
                if sg_free:
                    wts.append((sv, sg_free))
                P.emit("scalar", lambda s, w=wc: s.activation(sg_st[:, 0:w], pg[:, 0:w], AF.Silu),
                       wait=wts)
                pg_free_s = ss.n
                sgv = ss.n
                wts = [(ss, sgv)]
                if h_free:
                    wts.append((ss, h_free))
                P.emit("vector", lambda v, w=wc: v.tensor_mul(h_st[:, 0:w], sg_st[:, 0:w], pu[:, 0:w]),
                       wait=wts)
                pu_free = sv.n
                pg_free = sv.n
                sg_free = sv.n
                h_v = sv.n
                qv = emit_quant(P, hq_st[hp][:, 0:wc], h_st[:, 0:wc], wc, qs3,
                                src_ready=[(sv, h_v)],
                                extra_dst_wait=([(st, hq_free[hp])] if hq_free[hp] else []))
                h_free = qs3["last_scalar"]
                return qv

            def emit_tmid(tt, qv):
                nonlocal hqT_free, hq_free
                hp = tt % 2
                pdp = tt % 2
                for nt in range(wnt):
                    tp = nt % 2
                    wts = [qv, (gcD.sem, CW)]
                    if ptrh_free[tp]:
                        wts.append((ss, ptrh_free[tp]))
                    P.emit("tensor", lambda t, _n=nt, p=tp, q=hp: t.transpose(
                        ptrh[p][:, :], hq_st[q][:, 128 * _n:128 * (_n + 1)], identb[:, :]),
                        wait=wts, inc=[(st, 1)])
                    tr_v = st.n
                    wts = [(st, tr_v)]
                    if nt == 0 and hqT_free:
                        wts.append((st, hqT_free))
                    P.emit("scalar", lambda s, _n=nt, p=tp: s.copy(
                        hqT_st[:, 128 * _n:128 * (_n + 1)], ptrh[p][:, :]),
                        wait=wts)
                    ptrh_free[tp] = ss.n
                hq_free[hp] = st.n
                hqT_v = ss.n
                for nt in range(wnt):
                    for hh in range(2):
                        wts = [(ss, hqT_v), wv_]
                        if nt == 0 and hh == 0 and pd_free[pdp]:
                            wts.append((sv, pd_free[pdp]))
                        P.emit("tensor", lambda t, _n=nt, _h=hh, p=par, q=pdp: t.matmul(
                            pd[q][:, 512 * _h:512 * (_h + 1)],
                            hqT_st[:, 128 * _n:128 * (_n + 1)],
                            wdq_sb[p][:, 1024 * _n + 512 * _h: 1024 * _n + 512 * (_h + 1)],
                            start=(_n == 0), stop=(_n == wnt - 1)),
                            wait=wts, inc=[(st, 1)] if (nt == wnt - 1 and hh == 1) else [])
                hqT_free = st.n
                dmm_vals[tt] = st.n

            def emit_evac(tt):
                nonlocal pd_free
                pdp = tt % 2
                if ci == 0:
                    P.emit("vector", lambda v, _tt=tt, q=pdp: v.tensor_copy(
                        out_acc[:, 1024 * _tt:1024 * (_tt + 1)], pd[q][:, :]),
                        wait=[(st, dmm_vals[tt])])
                else:
                    P.emit("vector", lambda v, _tt=tt, q=pdp: v.tensor_add(
                        out_acc[:, 1024 * _tt:1024 * (_tt + 1)],
                        out_acc[:, 1024 * _tt:1024 * (_tt + 1)], pd[q][:, :]),
                        wait=[(st, dmm_vals[tt])])
                pd_free[pdp] = sv.n
                if ci == len(chunks) - 1:
                    oacc_ready[tt] = sv.n

            emit_head(0)
            qv_cur = emit_mid(0)
            for tt in range(NT):
                if tt + 1 < NT:
                    emit_head(tt + 1)
                emit_tmid(tt, qv_cur)
                if tt + 1 < NT:
                    qv_cur = emit_mid(tt + 1)
                if tt >= 1:
                    emit_evac(tt - 1)
            emit_evac(NT - 1)
            if ci + 2 < len(chunks):
                w_free[par] = st.n
                load_wchunk(ci + 2)

        P.ctx = Ps

    with contextlib.ExitStack() as fctx:
        Ps = P.ctx; P.ctx = fctx
        sxf = [DmaStream(P, f"D_sxf{i}", "sync") for i in range(2)]
        ssf = [DmaStream(P, f"D_ssf{i}", "sync") for i in range(2)]
        sof = [DmaStream(P, f"D_sof{i}", "sync") for i in range(2)]
        x_f = [fctx.enter_context(nc.sbuf_tensor(f"D_xf{i}", [128, 1024], F32)) for i in range(2)]
        s_f = [fctx.enter_context(nc.sbuf_tensor(f"D_sf{i}", [128, 1024], BF16)) for i in range(2)]
        g1_f = fctx.enter_context(nc.sbuf_tensor("D_g1", [128, 1024], BF16))
        o_f = [fctx.enter_context(nc.sbuf_tensor(f"D_of{i}", [128, 1024], F32)) for i in range(2)]
        bar_f = [(st, st.n), (ss, ss.n), (sv, sv.n)]
        xf_free = [0, 0]
        of_free = [0, 0]
        g1_free = 0
        for tt in range(NT):
            par = tt % 2
            wts = list(bar_f) + ([(sv, xf_free[par])] if xf_free[par] else [])
            xv = sxf[par].load(x_f[par][:, :], hbm["x1"][128 * tt:128 * (tt + 1), :], wait=wts)
            sv_ld = ssf[par].load(s_f[par][:, :], hbm["s1"][128 * tt:128 * (tt + 1), :], wait=wts)
            wts = [(sv, oacc_ready[tt])]
            if g1_free:
                wts.append((sv, g1_free))
            P.emit("scalar", lambda s, _tt=tt: s.activation(
                g1_f[:, :], out_acc[:, 1024 * _tt:1024 * (_tt + 1)], AF.Silu),
                wait=wts)
            g1_v = ss.n
            wts = [(ss, g1_v), (ssf[par].sem, sv_ld)]
            if of_free[par]:
                wts.append((sof[par].sem, of_free[par]))
            P.emit("vector", lambda v, p=par: v.tensor_mul(o_f[p][:, :], g1_f[:, :], s_f[p][:, :]),
                   wait=wts)
            g1_free = sv.n
            P.emit("vector", lambda v, p=par: v.tensor_add(o_f[p][:, :], o_f[p][:, :], x_f[p][:, :]),
                   wait=[(sxf[par].sem, xv)])
            o_v = sv.n
            xf_free[par] = o_v
            of_free[par] = sof[par].load(hbm["out"][128 * tt:128 * (tt + 1), :], o_f[par][:, :],
                                         wait=[(sv, o_v)])
        P.emit("sync", lambda s: None, wait=[(sof[0].sem, sof[0].wv()), (sof[1].sem, sof[1].wv())])
        P.ctx = Ps
    return


def build_D(cfg):
    nc = bass.Bass("TRN2", target_bir_lowering=False)
    T = cfg["T"]
    I = cfg.get("I", 2816)
    hbm = {}
    hbm["x1"] = nc.dram_tensor("x1", [T, 1024], F32, kind="ExternalInput").ap()
    hbm["s1"] = nc.dram_tensor("s1", [T, 1024], BF16, kind="ExternalInput").ap()
    hbm["wgate"] = nc.dram_tensor("wgate", [I, 1024], F32, kind="ExternalInput").ap()
    hbm["wup"] = nc.dram_tensor("wup", [I, 1024], F32, kind="ExternalInput").ap()
    hbm["wdown"] = nc.dram_tensor("wdown", [1024, I], F32, kind="ExternalInput").ap()
    hbm["ident"] = nc.dram_tensor("ident", [128, 128], F32, kind="ExternalInput").ap()
    hbm["wgq_nat"] = nc.dram_tensor("wgq_nat", [I, 1024], BF16).ap()
    hbm["wuq_nat"] = nc.dram_tensor("wuq_nat", [I, 1024], BF16).ap()
    hbm["wdq_nat"] = nc.dram_tensor("wdq_nat", [1024, I], BF16).ap()
    if cfg.get("shard_wq"):
        hbm["wjob_a"] = nc.dram_tensor("wjob_a", [6, 128, 1024], F32, kind="ExternalInput").ap()
        hbm["wjob_b"] = nc.dram_tensor("wjob_b", [128, I], F32, kind="ExternalInput").ap()
        nrow = 6 * 1024 + I
        hbm["wq_outc"] = nc.dram_tensor("wq_outc", [nrow, 128], BF16).ap()
        hbm["wq_gath"] = nc.dram_tensor("wq_gath", [8 * nrow, 128], BF16, addr_space="Shared").ap()
    hbm["out"] = nc.dram_tensor("out", [T, 1024], F32, kind="ExternalOutput").ap()
    with contextlib.ExitStack() as ctx:
        P = Plan(nc, ctx)
        for s in ("t", "s", "v", "g"):
            P.sem(s)
        bufs = {}
        bufs["ident"] = ctx.enter_context(nc.sbuf_tensor("identb", [128, 128], BF16))
        build_phase_D(P, hbm, bufs, cfg)
        P.run()
    return nc



def build_A(cfg, fused=False):
    nc = bass.Bass("TRN2", target_bir_lowering=False)
    T = cfg["T"]
    hbm = {}
    hbm["x"] = nc.dram_tensor("x", [T, 1024], F32, kind="ExternalInput").ap()
    hbm["w1all"] = nc.dram_tensor("w1all", [cfg["NE"], 1024, cfg["F"]], F32, kind="ExternalInput").ap()
    hbm["w2all"] = nc.dram_tensor("w2all", [cfg["NE"], cfg["F"], 1024], F32, kind="ExternalInput").ap()
    hbm["b1dev"] = nc.dram_tensor("b1dev", [128, (cfg["F"] // 128) * cfg["NE"]], F32, kind="ExternalInput").ap()
    hbm["b2all"] = nc.dram_tensor("b2all", [cfg["NE"], 1024], F32, kind="ExternalInput").ap()
    hbm["wgdev"] = nc.dram_tensor("wgdev", [128, 8 * (cfg["NE"] - 1)], F32, kind="ExternalInput").ap()
    hbm["ident"] = nc.dram_tensor("ident", [128, 128], F32, kind="ExternalInput").ap()
    hbm["g_out"] = nc.dram_tensor("g_out", [T, 1024], BF16, kind="ExternalOutput").ap()

    with contextlib.ExitStack() as ctx:
        P = Plan(nc, ctx)
        for s in ("sq", "gq", "t", "s", "v"):
            P.sem(s)
        bufs = {}
        bufs["xnT"] = ctx.enter_context(nc.sbuf_tensor("xnT", [128, 8 * T], BF16))
        bufs["g_acc"] = ctx.enter_context(nc.sbuf_tensor("g_acc", [128, (T // 128) * 1024], F32))
        bufs["g_tok"] = None
        bufs["ident"] = ctx.enter_context(nc.sbuf_tensor("identb", [128, 128], BF16))
        build_phase_A(P, hbm, bufs, cfg)
        P.run()
    return nc


def build_full(cfg):
    nc = bass.Bass("TRN2", target_bir_lowering=False)
    T = cfg["T"]
    I = cfg.get("I", 2816)
    h = {}
    h["x"] = nc.dram_tensor("x", [T, 1024], F32, kind="ExternalInput").ap()
    h["w1all"] = nc.dram_tensor("w1all", [cfg["NE"], 1024, cfg["F"]], F32, kind="ExternalInput").ap()
    h["w2all"] = nc.dram_tensor("w2all", [cfg["NE"], cfg["F"], 1024], F32, kind="ExternalInput").ap()
    h["b1dev"] = nc.dram_tensor("b1dev", [128, (cfg["F"] // 128) * cfg["NE"]], F32, kind="ExternalInput").ap()
    h["b2all"] = nc.dram_tensor("b2all", [cfg["NE"], 1024], F32, kind="ExternalInput").ap()
    h["wgdev"] = nc.dram_tensor("wgdev", [128, 8 * (cfg["NE"] - 1)], F32, kind="ExternalInput").ap()
    h["ident"] = nc.dram_tensor("ident", [128, 128], F32, kind="ExternalInput").ap()
    for nm in ("wqT", "wkT", "wvT", "woT", "l1qT", "l1kT", "l1vT", "l1oT"):
        h[nm] = nc.dram_tensor(nm, [128, 8 * 1024], F32, kind="ExternalInput").ap()
    h["gt"] = nc.dram_tensor("gt", [16, 128, 2 * T - 128], F32, kind="ExternalInput").ap()
    h["m01"] = nc.dram_tensor("m01", [6, 128, 512], F32, kind="ExternalInput").ap()
    h["wgate"] = nc.dram_tensor("wgate", [I, 1024], F32, kind="ExternalInput").ap()
    h["wup"] = nc.dram_tensor("wup", [I, 1024], F32, kind="ExternalInput").ap()
    h["wdown"] = nc.dram_tensor("wdown", [1024, I], F32, kind="ExternalInput").ap()
    h["g_out"] = h["g_in"] = nc.dram_tensor("g_h", [T, 1024], BF16).ap()
    h["x1_out"] = h["x1"] = nc.dram_tensor("x1_h", [T, 1024], F32).ap()
    h["s1_out"] = h["s1"] = nc.dram_tensor("s1_h", [T, 1024], BF16).ap()
    h["wgq_nat"] = nc.dram_tensor("wgq_nat", [I, 1024], BF16).ap()
    h["wuq_nat"] = nc.dram_tensor("wuq_nat", [I, 1024], BF16).ap()
    h["wdq_nat"] = nc.dram_tensor("wdq_nat", [1024, I], BF16).ap()
    if cfg.get("shard_wq"):
        h["wjob_a"] = nc.dram_tensor("wjob_a", [6, 128, 1024], F32, kind="ExternalInput").ap()
        h["wjob_b"] = nc.dram_tensor("wjob_b", [128, I], F32, kind="ExternalInput").ap()
        nrow = 6 * 1024 + I
        h["wq_outc"] = nc.dram_tensor("wq_outc", [nrow, 128], BF16).ap()
        h["wq_gath"] = nc.dram_tensor("wq_gath", [8 * nrow, 128], BF16, addr_space="Shared").ap()
    h["out"] = nc.dram_tensor("out", [T, 1024], F32, kind="ExternalOutput").ap()

    with contextlib.ExitStack() as ctx:
        P = Plan(nc, ctx)
        for s in ("t", "s", "v", "g"):
            P.sem(s)
        bufs = {}
        bufs["ident"] = ctx.enter_context(nc.sbuf_tensor("identb", [128, 128], BF16))
        bufs["g_tok"] = None
        wq = WeightQuantEmitter(P, h, cfg, bufs["ident"])
        bufs["filler"] = lambda: wq.emit_chunk()
        for builder in (build_phase_A, build_phase_B, build_phase_C, build_phase_D):
            if builder is build_phase_B:
                bufs["filler"] = None
            if builder is build_phase_D:
                bufs["wq_done_dep"] = wq.finish()
            with contextlib.ExitStack() as pctx:
                saved = P.ctx
                P.ctx = pctx
                builder(P, h, bufs, cfg)
                P.ctx = saved
            P.phase_barrier()
        P.run()
    return nc


CFG_FULL = dict(T=2048, F=4096, NG=4, NE=9, I=2816)


def kernel(**inputs):
    maps = prep_host(inputs)
    cfg = CFG_FULL
    cores = list(range(NCORES))
    ncF = build_full(cfg)
    resF = run_bass_kernel_spmd(ncF, maps, core_ids=cores)
    return np.stack([np.asarray(r["out"], dtype=np.float32) for r in resF.results], axis=0)


def kernel_multi(**inputs):
    maps = prep_host(inputs)
    cfg = CFG_FULL
    cores = list(range(NCORES))

    ncA = build_A(cfg)
    inA = [dict(x=m["x"], w1all=m["w1all"], w2all=m["w2all"], b1dev=m["b1dev"],
                b2all=m["b2all"], wgdev=m["wgdev"], ident=m["ident"]) for m in maps]
    resA = run_bass_kernel_spmd(ncA, inA, core_ids=cores)
    g_all = [r["g_out"] for r in resA.results]

    ncB = build_B(cfg)
    inB = [dict(x=m["x"], g_in=g_all[b], wqT=m["wqT"], wkT=m["wkT"], wvT=m["wvT"],
                woT=m["woT"], gt=m["gt"], ident=m["ident"]) for b, m in enumerate(maps)]
    resB = run_bass_kernel_spmd(ncB, inB, core_ids=cores)
    x1_all = [r["x1_out"] for r in resB.results]

    ncC = build_C(cfg)
    inC = [dict(x1=x1_all[b], l1qT=m["l1qT"], l1kT=m["l1kT"], l1vT=m["l1vT"],
                l1oT=m["l1oT"], m01=m["m01"], ident=m["ident"]) for b, m in enumerate(maps)]
    resC = run_bass_kernel_spmd(ncC, inC, core_ids=cores)
    s1_all = [r["s1_out"] for r in resC.results]

    ncD = build_D(cfg)
    inD = [dict(x1=x1_all[b], s1=s1_all[b], wgate=m["wgate"], wup=m["wup"],
                wdown=m["wdown"], ident=m["ident"]) for b, m in enumerate(maps)]
    resD = run_bass_kernel_spmd(ncD, inD, core_ids=cores)
    out = np.stack([np.asarray(r["out"], dtype=np.float32) for r in resD.results], axis=0)
    return out


# version 30
# speedup vs baseline: 1.1955x; 1.0003x over previous
import contextlib
import math
import numpy as np
import ml_dtypes
import concourse.bass as bass
import concourse.mybir as mybir
from concourse.bass_utils import run_bass_kernel_spmd

F32 = mybir.dt.float32
BF16 = mybir.dt.bfloat16
AF = mybir.ActivationFunctionType
ALU = mybir.AluOpType
AX = mybir.AxisListType

NCORES = 8


def _rel_bucket(rel_pos, num_buckets=32, max_distance=128):
    rel_pos = np.asarray(rel_pos, dtype=np.int64)
    nb = num_buckets // 2
    buckets = (rel_pos > 0).astype(np.int64) * nb
    rp = np.abs(rel_pos)
    max_exact = nb // 2
    is_small = rp < max_exact
    rp_f = np.maximum(rp, 1).astype(np.float64)
    large = max_exact + (
        np.log(rp_f / max_exact) / math.log(max_distance / max_exact) * (nb - max_exact)
    ).astype(np.int64)
    large = np.minimum(large, nb - 1)
    return buckets + np.where(is_small, rp, large)


def _make_gt(rel_bias, S=2048):
    d = np.arange(-(S - 1), S)
    idx = _rel_bucket(d)
    table = rel_bias[idx, :]
    W = 2 * S - 128
    i = np.arange(128)[:, None]
    c = np.arange(W)[None, :]
    t_idx = (W - 1) + i - c
    GT = table[t_idx, :]
    return np.ascontiguousarray(GT.transpose(2, 0, 1)).astype(np.float32)


def _make_m01():
    deltas = [-256, -128, 0, 128, 256, 384]
    i = np.arange(128)[None, :, None]
    j = np.arange(512)[None, None, :]
    d = np.asarray(deltas)[:, None, None]
    qk = j - (d + i)
    return ((qk >= 0) & (qk <= 255)).astype(np.float32)


def _feat_tiles(w):
    K, N = w.shape
    return np.ascontiguousarray(w.reshape(K // 128, 128, N).transpose(1, 0, 2).reshape(128, -1))


def prep_host(inputs):
    f32 = np.float32
    x_all = np.ascontiguousarray(inputs["L0_attn_hidden_states"], dtype=f32)

    wqT = _feat_tiles(np.ascontiguousarray(inputs["L0_attn_wq"].T, dtype=f32))
    wkT = _feat_tiles(np.ascontiguousarray(inputs["L0_attn_wk"].T, dtype=f32))
    wvT = _feat_tiles(np.ascontiguousarray(inputs["L0_attn_wv"].T, dtype=f32))
    woT = _feat_tiles(np.ascontiguousarray(inputs["L0_attn_wo"].T, dtype=f32))
    gt = _make_gt(np.asarray(inputs["L0_attn_rel_bias"], dtype=f32))

    W1 = np.asarray(inputs["L0_ffn_W1"], dtype=f32)
    W2 = np.asarray(inputs["L0_ffn_W2"], dtype=f32)
    Ws1 = np.asarray(inputs["L0_ffn_Ws1"], dtype=f32)[None]
    Ws2 = np.asarray(inputs["L0_ffn_Ws2"], dtype=f32)[None]
    w1all = np.ascontiguousarray(np.concatenate([W1, Ws1], 0))
    w2all = np.ascontiguousarray(np.concatenate([W2, Ws2], 0))
    b1 = np.asarray(inputs["L0_ffn_b1"], dtype=f32)
    bs1 = np.asarray(inputs["L0_ffn_bs1"], dtype=f32)[None]
    b1all = np.concatenate([b1, bs1], 0)
    b1dev = np.ascontiguousarray(b1all.reshape(9, 32, 128).transpose(2, 1, 0).reshape(128, 32 * 9))
    b2 = np.asarray(inputs["L0_ffn_b2"], dtype=f32)
    bs2 = np.asarray(inputs["L0_ffn_bs2"], dtype=f32)[None]
    b2all = np.ascontiguousarray(np.concatenate([b2, bs2], 0))
    wg = np.asarray(inputs["L0_ffn_Wg"], dtype=f32)
    wgdev = _feat_tiles(wg)

    l1qT = _feat_tiles(np.ascontiguousarray(inputs["L1_attn_Wq"].T, dtype=f32))
    l1kT = _feat_tiles(np.ascontiguousarray(inputs["L1_attn_Wk"].T, dtype=f32))
    l1vT = _feat_tiles(np.ascontiguousarray(inputs["L1_attn_Wv"].T, dtype=f32))
    l1oT = _feat_tiles(np.ascontiguousarray(inputs["L1_attn_Wo"].T, dtype=f32))
    m01 = _make_m01()

    wgate = np.ascontiguousarray(inputs["L1_ffn_w_gate"], dtype=f32)
    wup = np.ascontiguousarray(inputs["L1_ffn_w_up"], dtype=f32)
    wdown = np.ascontiguousarray(inputs["L1_ffn_w_down"], dtype=f32)

    ident = np.eye(128, dtype=f32)

    shared = dict(
        wqT=wqT, wkT=wkT, wvT=wvT, woT=woT, gt=gt,
        w1all=w1all, w2all=w2all, b1dev=b1dev, b2all=b2all, wgdev=wgdev,
        l1qT=l1qT, l1kT=l1kT, l1vT=l1vT, l1oT=l1oT, m01=m01,
        wgate=wgate, wup=wup, wdown=wdown, ident=ident,
    )
    maps = []
    for b in range(NCORES):
        m = dict(shared)
        m["x"] = np.ascontiguousarray(x_all[b])
        maps.append(m)
    return maps



ENGS = ("tensor", "scalar", "vector", "gpsimd", "sync")


class Sem:
    __slots__ = ("h", "n", "name")

    def __init__(self, h, name):
        self.h = h
        self.n = 0
        self.name = name

    def tick(self, amt=1):
        self.n += amt
        return self.n


class Plan:
    def __init__(self, nc, ctx):
        self.nc = nc
        self.ctx = ctx
        self.items = {e: [] for e in ENGS}
        self.sems = {}
        self.watermark = {}
        self._deps = []
        self._deps_todo = set()

    def phase_barrier(self):
        self._deps = [(s, s.n) for s in self.sems.values() if s.n > 0]
        self._deps_todo = set(ENGS)

    def sem(self, name):
        h = self.ctx.enter_context(self.nc.semaphore(name))
        s = Sem(h, name)
        self.sems[name] = s
        return s

    def emit(self, eng, fn, wait=(), inc=(), is_dma=False):
        wait = list(wait)
        inc = list(inc)
        if eng in self._deps_todo:
            wait = self._deps + wait
            self._deps_todo.discard(eng)
        if eng in ("vector", "scalar") or (eng == "gpsimd" and not is_dma and "g" in self.sems):
            own = self.sems["v" if eng == "vector" else ("s" if eng == "scalar" else "g")]
            if own.n:
                wait.append((own, own.n))
            if not any(s is own for s, _ in inc):
                inc.append((own, 1))
        pruned = []
        for s, v in wait:
            assert v <= s.n, f"wait on {s.name}>={v} but only {s.n} planned (deadlock)"
            key = (eng, s.name)
            if self.watermark.get(key, -1) >= v:
                continue
            self.watermark[key] = v
            pruned.append((s, v))
        incs = []
        for s, amt in inc:
            s.tick(amt)
            incs.append((s, amt, s.n))
        self.items[eng].append((pruned, fn, incs))
        return [(s, n) for s, _a, n in incs]

    def run(self):
        block = self.ctx.enter_context(self.nc.Block())
        plans = self.items

        def mk(eng_name):
            def body(proxy):
                for waits, fn, incs in plans[eng_name]:
                    for s, v in waits:
                        proxy.wait_ge(s.h, v)
                    ins = fn(proxy)
                    if incs:
                        assert ins is not None, f"inc on None instr ({eng_name})"
                        for s, amt, _n in incs:
                            ins = ins.then_inc(s.h, amt)
            return body

        block.tensor(mk("tensor"))
        block.scalar(mk("scalar"))
        block.vector(mk("vector"))
        block.gpsimd(mk("gpsimd"))
        block.sync(mk("sync"))



class DmaStream:

    def __init__(self, P, name, eng):
        self.P = P
        self.eng = eng
        self.sem = P.sem(name)

    def load(self, dst, src, wait=(), transpose=False):
        self.P.emit(self.eng, lambda g, d=dst, s=src, tr=transpose: g.dma_start(d, s, transpose=tr),
                    wait=wait, inc=[(self.sem, 16)], is_dma=True)
        return self.sem.n

    def wv(self):
        return self.sem.n



def build_phase_A(P, hbm, bufs, cfg, fused_out=False):
    nc = P.nc
    T = cfg["T"]; NT = T // 128
    NDT = 8
    F = cfg["F"]; NFT = F // 128
    NG = cfg["NG"]; FLG = NFT // NG
    NE = cfg["NE"]
    TC = 256; NTC = T // TC
    ER = NE - 1
    GL = FLG * 128

    st = P.sems["t"]; ss = P.sems["s"]; sv = P.sems["v"]
    gc = DmaStream(P, "A_gc", "gpsimd")
    sx = [DmaStream(P, f"A_sx{i}", "sync") for i in range(2)]
    gw = [DmaStream(P, f"A_gw{i}", "gpsimd") for i in range(2)]
    so = [DmaStream(P, f"A_so{i}", "sync") for i in range(2)]

    if bufs.get("xnT") is None:
        bufs_xnT = P.ctx.enter_context(nc.sbuf_tensor("A_xnT", [128, 8 * T], BF16))
        bufs_gacc = P.ctx.enter_context(nc.sbuf_tensor("A_gacc", [128, NT * 1024], F32))
    else:
        bufs_xnT = bufs["xnT"]
        bufs_gacc = bufs["g_acc"]
    xnT = bufs_xnT
    g_acc = bufs_gacc
    g_tok = bufs.get("g_tok")
    ident = bufs["ident"]

    ctx = P.ctx
    b1dev = ctx.enter_context(nc.sbuf_tensor("A_b1dev", [128, (F // 128) * NE], F32))
    cw_sb = ctx.enter_context(nc.sbuf_tensor("A_cw", [128, NT * NE], F32))
    b2bf = ctx.enter_context(nc.sbuf_tensor("A_b2bf", [NE, 1024], BF16))
    wg_sb = ctx.enter_context(nc.sbuf_tensor("A_wg", [128, NDT * ER], F32))

    gc.load(b1dev[:, :], hbm["b1dev"][:, :])
    gc.load(b2bf[:, :], hbm["b2all"][:, :])
    gc.load(wg_sb[:, :], hbm["wgdev"][:, :])
    gc.load(ident[:, :], hbm["ident"][:, :])
    identf = P.ctx.enter_context(nc.sbuf_tensor("A_identf", [128, 128], F32))
    gc.load(identf[:, :], hbm["ident"][:, :])
    CW = gc.wv()
    P.emit("vector", lambda v: v.memset(cw_sb[:, :], 0.1))
    eps_col = P.ctx.enter_context(nc.sbuf_tensor("A_eps", [128, 1], F32))
    P.emit("vector", lambda v: v.memset(eps_col[:, :], 1e-5))
    eps_v = P.sems["v"].n

    lnctx = contextlib.ExitStack()
    ptr = [lnctx.enter_context(nc.psum_tensor(f"A_ptr{i}", [128, 128], F32)) for i in range(2)]
    p_r = lnctx.enter_context(nc.psum_tensor("A_pr", [128, ER], F32))
    p_cwT = lnctx.enter_context(nc.psum_tensor("A_pcwT", [NE, 128], BF16))
    p_b2 = lnctx.enter_context(nc.psum_tensor("A_pb2", [128, 1024], F32))
    x_t = [lnctx.enter_context(nc.sbuf_tensor(f"A_x{i}", [128, 1024], F32)) for i in range(2)]
    xsq = lnctx.enter_context(nc.sbuf_tensor("A_xsq", [128, 1024], F32))
    stat4 = lnctx.enter_context(nc.sbuf_tensor("A_stat", [128, 8], F32))
    xn_tok = lnctx.enter_context(nc.sbuf_tensor("A_xntok", [128, 1024], F32))
    xnf = [lnctx.enter_context(nc.sbuf_tensor(f"A_xnf{i}", [128, 128], F32)) for i in range(2)]
    cwT = lnctx.enter_context(nc.sbuf_tensor("A_cwT", [NE, NT * 128], BF16))
    r_sc = lnctx.enter_context(nc.sbuf_tensor("A_rsc", [128, 16], F32))
    probs = lnctx.enter_context(nc.sbuf_tensor("A_probs", [128, ER], F32))
    pm = lnctx.enter_context(nc.sbuf_tensor("A_pm", [128, ER], F32))
    msk = lnctx.enter_context(nc.sbuf_tensor("A_msk", [128, ER], F32))
    cwbf = lnctx.enter_context(nc.sbuf_tensor("A_cwbf", [128, NE], BF16))
    eps = 1e-5
    inv_d = 1.0 / 1024.0
    router_done = [0] * NT
    ginit_done = [0] * NT
    x_free = [0, 0]
    xsq_free = 0
    xn_tok_free = 0
    ptr_free = [0, 0]
    pr_free = 0
    xnf_free = [0, 0]
    pcwT_free = 0
    cwbf_free = 0
    pb2_free = 0

    for tt in range(NT):
        par = tt % 2
        wts = [(sv, x_free[par])] if x_free[par] else []
        xv = sx[par].load(x_t[par][:, :], hbm["x"][128 * tt:128 * (tt + 1), :], wait=wts)
        xw = (sx[par].sem, xv)
        P.emit("vector", lambda v, p=par: v.reduce_sum(stat4[:, 0:1], x_t[p][:, :], AX.X),
               wait=[xw])
        swts = [xw] + ([(sv, xsq_free)] if xsq_free else [])
        P.emit("scalar", lambda s, p=par: s.activation(xsq[:, :], x_t[p][:, :], AF.Square),
               wait=swts)
        sq_done = ss.n
        P.emit("vector", lambda v: v.reduce_sum(stat4[:, 1:2], xsq[:, :], AX.X),
               wait=[(ss, sq_done)])
        xsq_free = sv.n
        P.emit("vector", lambda v: v.tensor_scalar_mul(stat4[:, 2:3], stat4[:, 0:1], inv_d))
        P.emit("vector", lambda v: v.tensor_scalar_mul(stat4[:, 3:4], stat4[:, 1:2], inv_d))
        P.emit("vector", lambda v: v.scalar_tensor_tensor(
            stat4[:, 4:5], stat4[:, 2:3], -1.0, stat4[:, 2:3], ALU.mult, ALU.mult))
        P.emit("vector", lambda v: v.tensor_add(stat4[:, 5:6], stat4[:, 3:4], stat4[:, 4:5]))
        var_v = sv.n
        P.emit("scalar", lambda s: s.activation(stat4[:, 6:7], stat4[:, 5:6], AF.Sqrt, bias=eps_col[:, :]),
               wait=[(sv, var_v), (sv, eps_v)])
        sd_v = ss.n
        P.emit("vector", lambda v: v.reciprocal(stat4[:, 7:8], stat4[:, 6:7]),
               wait=[(ss, sd_v)])
        rstd_v = sv.n
        wts = [(sv, rstd_v)] + ([(st, xn_tok_free)] if xn_tok_free else [])
        P.emit("vector", lambda v, p=par: v.tensor_scalar(
            xn_tok[:, :], x_t[p][:, :], stat4[:, 2:3], stat4[:, 7:8], ALU.subtract, ALU.mult),
            wait=wts)
        xn_v = sv.n
        x_free[par] = xn_v
        for ft in range(NDT):
            tp = ft % 2
            wts = [(sv, xn_v), (gc.sem, CW)]
            if ptr_free[tp]:
                wts.append((sv, ptr_free[tp]))
            P.emit("tensor", lambda t, f=ft, p=tp: t.transpose(
                ptr[p][:, :], xn_tok[:, 128 * f:128 * (f + 1)], identf[:, :]),
                wait=wts, inc=[(st, 1)])
            tr_v = st.n
            P.emit("vector", lambda v, f=ft, p=tp, _tt=tt: v.tensor_copy(
                xnT[:, T * f + 128 * _tt: T * f + 128 * (_tt + 1)], ptr[p][:, :]),
                wait=[(st, tr_v)])
            wts = [(st, tr_v)]
            if xnf_free[tp]:
                wts.append((st, xnf_free[tp]))
            P.emit("vector", lambda v, p=tp: v.tensor_copy(xnf[p][:, :], ptr[p][:, :]),
                   wait=wts)
            ptr_free[tp] = sv.n
            xnf_v = sv.n
            P.emit("tensor", lambda t, f=ft, p=tp: t.matmul(
                p_r[:, :], xnf[p][:, :], wg_sb[:, ER * f: ER * (f + 1)],
                start=(f == 0), stop=(f == NDT - 1)),
                wait=[(sv, xnf_v)] + ([(ss, pr_free)] if (ft == 0 and pr_free) else []),
                inc=[(st, 1)])
            xnf_free[tp] = st.n
        xn_tok_free = st.n
        xn_copied = sv.n

        log_v = st.n
        P.emit("vector", lambda v: v.reduce_max(r_sc[:, 0:1], p_r[:, :], AX.X, negate=True),
               wait=[(st, log_v)])
        negm_v = sv.n
        P.emit("scalar", lambda s: s.activation(probs[:, :], p_r[:, :], AF.Exp, bias=r_sc[:, 0:1]),
               wait=[(sv, negm_v)])
        punn_v = ss.n
        pr_free = punn_v
        P.emit("vector", lambda v: v.reduce_sum(r_sc[:, 1:2], probs[:, :], AX.X),
               wait=[(ss, punn_v)])
        P.emit("vector", lambda v: v.reciprocal(r_sc[:, 2:3], r_sc[:, 1:2]))
        P.emit("vector", lambda v: v.tensor_scalar_mul(probs[:, :], probs[:, :], r_sc[:, 2:3]))
        P.emit("vector", lambda v: v.reduce_max(r_sc[:, 3:4], probs[:, :], AX.X))
        P.emit("vector", lambda v: v.tensor_scalar(msk[:, :], probs[:, :], r_sc[:, 3:4], None, ALU.is_ge))
        P.emit("vector", lambda v: v.tensor_mul(pm[:, :], probs[:, :], msk[:, :]))
        P.emit("vector", lambda v: v.tensor_sub(pm[:, :], probs[:, :], pm[:, :]))
        P.emit("vector", lambda v: v.reduce_max(r_sc[:, 4:5], pm[:, :], AX.X))
        P.emit("vector", lambda v: v.tensor_scalar(r_sc[:, 8:8 + ER], pm[:, :], r_sc[:, 4:5], None, ALU.is_ge))
        P.emit("vector", lambda v: v.tensor_add(msk[:, :], msk[:, :], r_sc[:, 8:8 + ER]))
        P.emit("vector", lambda v: v.tensor_mul(pm[:, :], probs[:, :], msk[:, :]))
        P.emit("vector", lambda v: v.reduce_sum(r_sc[:, 5:6], pm[:, :], AX.X))
        P.emit("vector", lambda v: v.tensor_scalar_add(r_sc[:, 5:6], r_sc[:, 5:6], 1e-9))
        P.emit("vector", lambda v: v.reciprocal(r_sc[:, 6:7], r_sc[:, 5:6]))
        P.emit("vector", lambda v, _tt=tt: v.tensor_scalar_mul(
            cw_sb[:, NE * _tt: NE * _tt + ER], pm[:, :], r_sc[:, 6:7]))
        router_done[tt] = sv.n
        wts = [(st, cwbf_free)] if cwbf_free else []
        P.emit("vector", lambda v, _tt=tt: v.tensor_copy(cwbf[:, :], cw_sb[:, NE * _tt: NE * (_tt + 1)]),
               wait=wts)
        cwbf_v = sv.n
        wts = [(sv, cwbf_v)]
        if pcwT_free:
            wts.append((sv, pcwT_free))
        P.emit("tensor", lambda t: t.transpose(p_cwT[:, :], cwbf[:, :], ident[:, :]),
               wait=wts, inc=[(st, 1)])
        cwt_v = st.n
        cwbf_free = cwt_v
        P.emit("vector", lambda v, _tt=tt: v.tensor_copy(cwT[:, 128 * _tt:128 * (_tt + 1)], p_cwT[:, :]),
               wait=[(st, cwt_v)])
        cwT_v = sv.n
        pcwT_free = cwT_v
        for hh in range(2):
            wts = [(sv, cwT_v), (gc.sem, CW)]
            if hh == 0 and pb2_free:
                wts.append((sv, pb2_free))
            P.emit("tensor", lambda t, _tt=tt, h=hh: t.matmul(
                p_b2[:, 512 * h:512 * (h + 1)], cwT[:, 128 * _tt:128 * (_tt + 1)],
                b2bf[:, 512 * h:512 * (h + 1)], start=True, stop=True),
                wait=wts, inc=[(st, 1)] if hh == 1 else [])
        b2_v = st.n
        P.emit("vector", lambda v, _tt=tt: v.tensor_copy(
            g_acc[:, 1024 * _tt:1024 * (_tt + 1)], p_b2[:, :]),
            wait=[(st, b2_v)])
        pb2_free = sv.n
        ginit_done[tt] = sv.n

    ln_all_v = sv.n
    lnctx.close()
    psumA = [ctx.enter_context(nc.psum_tensor(f"A_pA{i}", [128, 512], F32)) for i in range(2)]
    ypsum = [ctx.enter_context(nc.psum_tensor(f"A_yp{i}", [128, 1024], F32)) for i in range(2)]
    actTg = ctx.enter_context(nc.sbuf_tensor("A_actg", [128, FLG * 512], BF16))
    w1g = [ctx.enter_context(nc.sbuf_tensor(f"A_w1g{i}", [128, 8 * GL], BF16)) for i in range(2)]
    w2g = [ctx.enter_context(nc.sbuf_tensor(f"A_w2g{i}", [128, FLG * 1024], BF16)) for i in range(2)]

    TC2 = 512
    NTC2 = T // TC2
    w_ready = {}
    w_free = {0: 0, 1: 0}
    eg_list = [(e, g) for e in range(NE) for g in range(NG)]

    def emit_wload(i):
        e, g = eg_list[i]
        par = i % 2
        w1src = hbm["w1all"].rearrange("e (hc p) f -> e p hc f", p=128)[e, :, :, GL * g:GL * (g + 1)]
        w2src = hbm["w2all"].rearrange("e (ft p) h -> e p ft h", p=128)[e, :, FLG * g:FLG * (g + 1), :]
        wts = [(st, w_free[par])] if w_free[par] else []
        gw[par].load(w1g[par].rearrange("p (hc f) -> p hc f", hc=8), w1src, wait=wts)
        gw[par].load(w2g[par].rearrange("p (ft h) -> p ft h", ft=FLG), w2src)
        w_ready[i] = gw[par].wv()

    bar_eg = [(st, st.n), (ss, ss.n), (sv, sv.n)]
    w_free[0] = -1
    w_free[1] = -1
    _orig_emit_wload = emit_wload

    def emit_wload0(i):
        e, g = eg_list[i]
        par = i % 2
        w1src = hbm["w1all"].rearrange("e (hc p) f -> e p hc f", p=128)[e, :, :, GL * g:GL * (g + 1)]
        w2src = hbm["w2all"].rearrange("e (ft p) h -> e p ft h", p=128)[e, :, FLG * g:FLG * (g + 1), :]
        gw[par].load(w1g[par].rearrange("p (hc f) -> p hc f", hc=8), w1src, wait=list(bar_eg))
        gw[par].load(w2g[par].rearrange("p (ft h) -> p ft h", ft=FLG), w2src)
        w_ready[i] = gw[par].wv()

    emit_wload0(0)
    emit_wload0(1)
    w_free[0] = 0
    w_free[1] = 0

    psA_free = [0, 0]
    act_free = 0
    y_free = [0, 0]

    for i, (e, g) in enumerate(eg_list):
        par_w = i % 2
        wv = (gw[par_w].sem, w_ready[i])
        for tc in range(NTC2):
            silu_vals = [0] * FLG
            for fl in range(FLG):
                pa = fl % 2
                for hc in range(NDT):
                    wts = [wv]
                    if hc == 0:
                        wts.append((sv, ln_all_v))
                        if psA_free[pa]:
                            wts.append((ss, psA_free[pa]))
                    P.emit("tensor", lambda t, h=hc, f=fl, p=pa, w=par_w, _tc=tc: t.matmul(
                        psumA[p][:, :],
                        w1g[w][:, GL * h + 128 * f: GL * h + 128 * (f + 1)],
                        xnT[:, T * h + TC2 * _tc: T * h + TC2 * (_tc + 1)],
                        start=(h == 0), stop=(h == NDT - 1)),
                        wait=wts, inc=[(st, 1)] if hc == NDT - 1 else [])
                mmA_v = st.n
                ft_g = FLG * g + fl
                wts = [(st, mmA_v), (gc.sem, CW)]
                if act_free:
                    wts.append((st, act_free))
                P.emit("scalar", lambda s, p=pa, f=fl, c=(ft_g * NE + e): s.activation(
                    actTg[:, 512 * f:512 * (f + 1)], psumA[p][:, :], AF.Silu, bias=b1dev[:, c:c + 1]),
                    wait=wts)
                psA_free[pa] = ss.n
                silu_vals[fl] = ss.n
            for sub in range(2):
                for ts in range(2):
                    tt = (TC2 // 128) * tc + 2 * sub + ts
                    for fl in range(FLG):
                        wts = [(ss, silu_vals[fl])]
                        if sub == 0 and ts == 0 and fl == 0 and y_free[ts]:
                            wts += [(sv, y_free[0]), (sv, y_free[1])]
                        elif fl == 0 and y_free[ts]:
                            wts.append((sv, y_free[ts]))
                        for hh in range(2):
                            P.emit("tensor", lambda t, f=fl, _s=sub, _ts=ts, h=hh, w=par_w: t.matmul(
                                ypsum[_ts][:, 512 * h:512 * (h + 1)],
                                actTg[:, 512 * f + 256 * _s + 128 * _ts: 512 * f + 256 * _s + 128 * (_ts + 1)],
                                w2g[w][:, 1024 * f + 512 * h: 1024 * f + 512 * (h + 1)],
                                start=(f == 0), stop=(f == FLG - 1)),
                                wait=wts if hh == 0 else [],
                                inc=[(st, 1)] if (fl == FLG - 1 and hh == 1) else [])
                    mmB_v = st.n
                    wts = [(st, mmB_v), (sv, ginit_done[tt])]
                    if e == NE - 1:
                        fn = lambda v, _ts=ts, _tt=tt: v.scalar_tensor_tensor(
                            g_acc[:, 1024 * _tt:1024 * (_tt + 1)], ypsum[_ts][:, :], 0.1,
                            g_acc[:, 1024 * _tt:1024 * (_tt + 1)], ALU.mult, ALU.add)
                    else:
                        fn = lambda v, _ts=ts, _tt=tt, _e=e: v.scalar_tensor_tensor(
                            g_acc[:, 1024 * _tt:1024 * (_tt + 1)], ypsum[_ts][:, :],
                            cw_sb[:, NE * _tt + _e: NE * _tt + _e + 1],
                            g_acc[:, 1024 * _tt:1024 * (_tt + 1)], ALU.mult, ALU.add)
                    P.emit("vector", fn, wait=wts)
                    y_free[ts] = sv.n
            act_free = st.n
            fil = bufs.get("filler")
            if fil is not None:
                fil()
        if i + 2 < len(eg_list):
            w_free[par_w] = st.n
            emit_wload(i + 2)

    all_evac_v = sv.n
    g_done = []
    if fused_out:
        for tt in range(NT):
            P.emit("scalar", lambda s, _tt=tt: s.activation(
                g_tok[:, 1024 * _tt:1024 * (_tt + 1)], g_acc[:, 1024 * _tt:1024 * (_tt + 1)], AF.Silu),
                wait=[(sv, all_evac_v)])
            g_done.append(ss.n)
    else:
        gst0 = ctx.enter_context(nc.sbuf_tensor("A_gst0", [128, 1024], BF16))
        gfree = 0
        for tt in range(NT):
            wts = [(sv, all_evac_v)]
            if gfree:
                wts.append((so[0].sem, gfree))
            P.emit("scalar", lambda s, _tt=tt: s.activation(
                gst0[:, :], g_acc[:, 1024 * _tt:1024 * (_tt + 1)], AF.Silu),
                wait=wts)
            sd = ss.n
            gfree = so[0].load(hbm["g_out"][128 * tt:128 * (tt + 1), :],
                               gst0[:, :], wait=[(ss, sd)])
            g_done.append(sd)
        P.emit("sync", lambda s: None, wait=[(so[0].sem, so[0].wv())])
    return g_done




def emit_ln_to_xnT(P, hbm_x, xnT, identb, cfg, pfx, deps=()):
    nc = P.nc
    T = cfg["T"]; NT = T // 128; NDT = 8
    st = P.sems["t"]; ss = P.sems["s"]; sv = P.sems["v"]
    ctx = P.ctx
    sx = [DmaStream(P, f"{pfx}_sx{i}", "sync") for i in range(2)]
    x_t = [ctx.enter_context(nc.sbuf_tensor(f"{pfx}_x{i}", [128, 1024], F32)) for i in range(2)]
    xsq = ctx.enter_context(nc.sbuf_tensor(f"{pfx}_xsq", [128, 1024], F32))
    stat4 = ctx.enter_context(nc.sbuf_tensor(f"{pfx}_stat", [128, 8], F32))
    xn_tok = ctx.enter_context(nc.sbuf_tensor(f"{pfx}_xntok", [128, 1024], BF16))
    eps_col = ctx.enter_context(nc.sbuf_tensor(f"{pfx}_eps", [128, 1], F32))
    ptr = [ctx.enter_context(nc.psum_tensor(f"{pfx}_ptr{i}", [128, 128], BF16)) for i in range(2)]
    P.emit("vector", lambda v: v.memset(eps_col[:, :], 1e-5))
    eps_v = P.sems["v"].n
    x_free = [0, 0]
    xsq_free = 0
    xn_tok_free = 0
    ptr_free = [0, 0]
    for tt in range(NT):
        par = tt % 2
        wts = ([(sv, x_free[par])] if x_free[par] else []) + list(deps)
        xv = sx[par].load(x_t[par][:, :], hbm_x[128 * tt:128 * (tt + 1), :], wait=wts)
        xw = (sx[par].sem, xv)
        P.emit("vector", lambda v, p=par: v.reduce_sum(stat4[:, 0:1], x_t[p][:, :], AX.X),
               wait=[xw])
        swts = [xw] + ([(sv, xsq_free)] if xsq_free else [])
        P.emit("scalar", lambda s, p=par: s.activation(xsq[:, :], x_t[p][:, :], AF.Square),
               wait=swts)
        sq_done = ss.n
        P.emit("vector", lambda v: v.reduce_sum(stat4[:, 1:2], xsq[:, :], AX.X),
               wait=[(ss, sq_done)])
        xsq_free = sv.n
        P.emit("vector", lambda v: v.tensor_scalar_mul(stat4[:, 2:3], stat4[:, 0:1], 1.0 / 1024.0))
        P.emit("vector", lambda v: v.tensor_scalar_mul(stat4[:, 3:4], stat4[:, 1:2], 1.0 / 1024.0))
        P.emit("vector", lambda v: v.scalar_tensor_tensor(
            stat4[:, 4:5], stat4[:, 2:3], -1.0, stat4[:, 2:3], ALU.mult, ALU.mult))
        P.emit("vector", lambda v: v.tensor_add(stat4[:, 5:6], stat4[:, 3:4], stat4[:, 4:5]))
        var_v = sv.n
        P.emit("scalar", lambda s: s.activation(stat4[:, 6:7], stat4[:, 5:6], AF.Sqrt, bias=eps_col[:, :]),
               wait=[(sv, var_v), (sv, eps_v)])
        sd_v = ss.n
        P.emit("vector", lambda v: v.reciprocal(stat4[:, 7:8], stat4[:, 6:7]),
               wait=[(ss, sd_v)])
        rstd_v = sv.n
        wts = [(sv, rstd_v)] + ([(st, xn_tok_free)] if xn_tok_free else [])
        P.emit("vector", lambda v, p=par: v.tensor_scalar(
            xn_tok[:, :], x_t[p][:, :], stat4[:, 2:3], stat4[:, 7:8], ALU.subtract, ALU.mult),
            wait=wts)
        xn_v = sv.n
        x_free[par] = xn_v
        for ft in range(NDT):
            tp = ft % 2
            wts = [(sv, xn_v)]
            if ptr_free[tp]:
                wts.append((sv, ptr_free[tp]))
            P.emit("tensor", lambda t, f=ft, p=tp: t.transpose(
                ptr[p][:, :], xn_tok[:, 128 * f:128 * (f + 1)], identb[:, :]),
                wait=wts, inc=[(st, 1)])
            tr_v = st.n
            P.emit("vector", lambda v, f=ft, p=tp, _tt=tt: v.tensor_copy(
                xnT[:, T * f + 128 * _tt: T * f + 128 * (_tt + 1)], ptr[p][:, :]),
                wait=[(st, tr_v)])
            ptr_free[tp] = sv.n
        xn_tok_free = st.n
    return P.sems["v"].n


def build_phase_B(P, hbm, bufs, cfg):
    nc = P.nc
    T = cfg["T"]; NT = T // 128; NDT = 8
    NH = 16
    QC = 512; NQC = T // QC
    NKT = T // 128
    W_GT = 2 * T - 128
    C0 = T - 128
    st = P.sems["t"]; ss = P.sems["s"]; sv = P.sems["v"]
    ctx = P.ctx

    identb = bufs["ident"]
    gc = DmaStream(P, "B_gc", "gpsimd")
    gwq = [DmaStream(P, f"B_gw{i}", "gpsimd") for i in range(2)]
    ggt = [DmaStream(P, f"B_ggt{i}", "gpsimd") for i in range(2)]
    sxc = [DmaStream(P, f"B_sxc{i}", "sync") for i in range(2)]
    sg = [DmaStream(P, f"B_sg{i}", "sync") for i in range(2)]
    so = [DmaStream(P, f"B_so{i}", "gpsimd") for i in range(2)]

    QT = ctx.enter_context(nc.sbuf_tensor("B_QT", [128, 8 * T], BF16))
    KT = ctx.enter_context(nc.sbuf_tensor("B_KT", [128, 8 * T], BF16))
    VTM = ctx.enter_context(nc.sbuf_tensor("B_VTM", [128, NKT * (NH * 65)], BF16))
    g_sb = bufs.get("g_tok")
    ones1 = ctx.enter_context(nc.sbuf_tensor("B_ones1", [1, 64], F32))

    gc.load(identb[:, :], hbm["ident"][:, :])
    CW = gc.wv()
    P.emit("vector", lambda v: v.memset(VTM[:, :], 1.0))
    vtm_init = sv.n
    P.emit("vector", lambda v: v.memset(ones1[:, :], 1.0))
    ones_v = sv.n

    with contextlib.ExitStack() as lctx:
        Pctx_saved = P.ctx
        P.ctx = lctx
        xnT = lctx.enter_context(nc.sbuf_tensor("B_xnT", [128, 8 * T], BF16))
        wbuf = [lctx.enter_context(nc.sbuf_tensor(f"B_w{i}", [128, 8 * 1024], BF16)) for i in range(2)]
        xn_done = emit_ln_to_xnT(P, hbm["x"], xnT, identb, cfg, "B_ln", deps=[(gc.sem, CW)])
        P.ctx = Pctx_saved

        names = ["wqT", "wkT", "wvT"]
        wv_ld = [0, 0, 0]
        wproj_done = [0, 0, 0]
        wv_ld[0] = gwq[0].load(wbuf[0][:, :], hbm[names[0]][:, :])
        wv_ld[1] = gwq[1].load(wbuf[1][:, :], hbm[names[1]][:, :])
        with contextlib.ExitStack() as pctx:
            pp = [pctx.enter_context(nc.psum_tensor(f"B_pp{i}", [128, 512], F32)) for i in range(2)]
            ptr2 = [pctx.enter_context(nc.psum_tensor(f"B_ptr2{i}", [128, 128], BF16)) for i in range(2)]
            vstage = pctx.enter_context(nc.sbuf_tensor("B_vstage", [128, 512], BF16))
            pp_free = [0, 0]
            ptr2_free = [0, 0]
            vstage_free = 0
            for wi, nm in enumerate(names):
                par = wi % 2
                if wi >= 2:
                    wv_ld[wi] = gwq[par].load(wbuf[par][:, :], hbm[names[wi]][:, :],
                                              wait=[(st, wproj_done[wi - 2])])
                for et in range(8):
                    for tcq in range(T // 512):
                        pa = (et * (T // 512) + tcq) % 2
                        for dc in range(8):
                            wts = [(gwq[par].sem, wv_ld[wi]), (sv, xn_done)]
                            if dc == 0 and pp_free[pa]:
                                wts.append((sv, pp_free[pa]))
                            P.emit("tensor", lambda t, d=dc, e=et, q=tcq, p=pa, w=par: t.matmul(
                                pp[p][:, :],
                                wbuf[w][:, 1024 * d + 128 * e: 1024 * d + 128 * (e + 1)],
                                xnT[:, T * d + 512 * q: T * d + 512 * (q + 1)],
                                start=(d == 0), stop=(d == 7)),
                                wait=wts, inc=[(st, 1)] if dc == 7 else [])
                        mm_v = st.n
                        if nm == "wqT":
                            P.emit("vector", lambda v, e=et, q=tcq, p=pa: v.tensor_copy(
                                QT[:, T * e + 512 * q: T * e + 512 * (q + 1)], pp[p][:, :]),
                                wait=[(st, mm_v)])
                            pp_free[pa] = sv.n
                        elif nm == "wkT":
                            P.emit("vector", lambda v, e=et, q=tcq, p=pa: v.tensor_copy(
                                KT[:, T * e + 512 * q: T * e + 512 * (q + 1)], pp[p][:, :]),
                                wait=[(st, mm_v)])
                            pp_free[pa] = sv.n
                        else:
                            wts = [(st, mm_v)]
                            if vstage_free:
                                wts.append((st, vstage_free))
                            P.emit("vector", lambda v, p=pa: v.tensor_copy(vstage[:, :], pp[p][:, :]),
                                   wait=wts)
                            pp_free[pa] = sv.n
                            vst_v = sv.n
                            for j in range(4):
                                tp2 = j % 2
                                wts = [(sv, vst_v), (gc.sem, CW)]
                                if ptr2_free[tp2]:
                                    wts.append((sv, ptr2_free[tp2]))
                                P.emit("tensor", lambda t, _j=j, p=tp2: t.transpose(
                                    ptr2[p][:, :], vstage[:, 128 * _j:128 * (_j + 1)], identb[:, :]),
                                    wait=wts, inc=[(st, 1)])
                                tr_v = st.n
                                kt_idx = tcq * 4 + j
                                for half in range(2):
                                    h_idx = 2 * et + half
                                    P.emit("vector", lambda v, k=kt_idx, h=h_idx, p=tp2, hf=half: v.tensor_copy(
                                        VTM[:, 65 * NH * k + 65 * h: 65 * NH * k + 65 * h + 64],
                                        ptr2[p][:, 64 * hf:64 * (hf + 1)]),
                                        wait=[(st, tr_v), (sv, vtm_init)])
                                ptr2_free[tp2] = sv.n
                            vstage_free = st.n
                wproj_done[wi] = st.n
        proj_evac_v = P.sems["v"].n

    aoT = ctx.enter_context(nc.sbuf_tensor("B_aoT", [128, 8 * T], BF16))

    with contextlib.ExitStack() as actx:
        gtst = [actx.enter_context(nc.sbuf_tensor(f"B_gt{i}", [128, W_GT], BF16)) for i in range(2)]
        expg = [actx.enter_context(nc.sbuf_tensor(f"B_eg{i}", [128, W_GT], BF16)) for i in range(2)]
        pstage = [actx.enter_context(nc.sbuf_tensor(f"B_pst{i}", [128, 1024], BF16)) for i in range(2)]
        pm_sb = [actx.enter_context(nc.sbuf_tensor(f"B_pm{i}", [128, 1024], BF16)) for i in range(4)]
        recip_sb = actx.enter_context(nc.sbuf_tensor("B_rcp", [1, 512], F32))
        ao_st = actx.enter_context(nc.sbuf_tensor("B_aost", [64, 512], BF16))
        S2 = [actx.enter_context(nc.psum_tensor(f"B_S2{i}", [128, 1024], F32)) for i in range(2)]
        pv = [actx.enter_context(nc.psum_tensor(f"B_pv{i}", [65, 512], F32)) for i in range(2)]
        rb = actx.enter_context(nc.psum_tensor("B_rb", [64, 512], F32))

        gt_ready = [0, 0]
        gt_free = [0, 0]
        bar_att = [(st, st.n), (ss, ss.n), (sv, sv.n)]

        def load_gt(h):
            par = h % 2
            wts = list(bar_att) + ([(ss, gt_free[par])] if gt_free[par] else [])
            ggt[par].load(gtst[par][:, :], hbm["gt"][h, :, :], wait=wts)
            gt_ready[par] = ggt[par].wv()

        load_gt(0)
        load_gt(1)

        expg_free = [0, 0]
        S2_free = [0, 0]
        pst_free = [0, 0]

        pm_free = [0, 0, 0, 0]
        pv_free = 0
        pv_freed = [0, 0]
        rcp_free = 0
        ao_free = 0
        rb_free = 0

        for h in range(NH):
            par = h % 2
            wts = [(ggt[par].sem, gt_ready[par])]
            if expg_free[par]:
                wts.append((sv, expg_free[par]))
            P.emit("scalar", lambda s, p=par: s.activation(expg[p][:, :], gtst[p][:, :], AF.Exp),
                   wait=wts)
            gt_free[par] = ss.n
            expg_v = ss.n
            if h + 2 < NH:
                load_gt(h + 2)
            hb = 64 * (h % 2)
            et = h // 2
            for qc in range(NQC):
                pvp = qc % 2
                mult_vals = [0] * (NKT // 2)

                def emit_pv(ktp, _h=h, _pvp=None):
                    p_ = pvp
                    for u in range(2):
                        kt = 2 * ktp + u
                        wts = [(sv, mult_vals[ktp])]
                        if kt == 0:
                            wts.append((sv, vtm_init))
                            if pv_freed[p_]:
                                wts.append((sv, pv_freed[p_]))
                        P.emit("tensor", lambda t, k=kt, _u=u, p=p_, __h=_h, mp=ktp % 4: t.matmul(
                            pv[p][:, :],
                            VTM[:, 65 * NH * k + 65 * __h: 65 * NH * k + 65 * __h + 65],
                            pm_sb[mp][:, 512 * _u:512 * (_u + 1)],
                            start=(k == 0), stop=(k == NKT - 1)),
                            wait=wts, inc=[(st, 1)] if kt == NKT - 1 else [])
                        pm_free[ktp % 4] = st.n

                for ktp in range(NKT // 2):
                    sp = ktp % 2
                    for u in range(2):
                        kt = 2 * ktp + u
                        wts = []
                        if u == 0 and S2_free[sp]:
                            wts.append((ss, S2_free[sp]))
                        if h == 0 and qc == 0 and ktp == 0 and u == 0:
                            wts.append((st, wproj_done[2]))
                            wts.append((sv, proj_evac_v))
                        P.emit("tensor", lambda t, k=kt, _u=u, p=sp, _qc=qc, _hb=hb, _et=et: t.matmul(
                            S2[p][:, 512 * _u:512 * (_u + 1)],
                            KT[_hb:_hb + 64, T * _et + 128 * k: T * _et + 128 * (k + 1)],
                            QT[_hb:_hb + 64, T * _et + 512 * _qc: T * _et + 512 * (_qc + 1)],
                            start=True, stop=True),
                            wait=wts, inc=[(st, 1)] if u == 1 else [])
                    sc_v = st.n
                    mp = ktp % 4
                    wts = [(st, sc_v), (ss, expg_v)]
                    if pst_free[mp % 2]:
                        wts.append((sv, pst_free[mp % 2]))
                    P.emit("scalar", lambda s, p=sp, _mp=mp % 2: s.activation(
                        pstage[_mp][:, :], S2[p][:, :], AF.Exp),
                        wait=wts)
                    S2_free[sp] = ss.n
                    exp_v = ss.n
                    for u in range(2):
                        kt = 2 * ktp + u
                        off = C0 + 512 * qc - 128 * kt
                        wts = [(ss, exp_v)]
                        if pm_free[mp]:
                            wts.append((st, pm_free[mp]))
                        P.emit("vector", lambda v, _u=u, p=par, o=off, _mp=mp, _ps=mp % 2: v.tensor_mul(
                            pm_sb[_mp][:, 512 * _u:512 * (_u + 1)],
                            pstage[_ps][:, 512 * _u:512 * (_u + 1)],
                            expg[p][:, o:o + 512]),
                            wait=wts)
                    pst_free[mp % 2] = sv.n
                    mult_vals[ktp] = sv.n
                    if ktp >= 3:
                        emit_pv(ktp - 3)
                for _k in range(min(3, NKT // 2), 0, -1):
                    emit_pv(NKT // 2 - _k)
                pv_v = st.n
                pvp = qc % 2
                wts = [(st, pv_v)]
                if rcp_free:
                    wts.append((st, rcp_free))
                P.emit("vector", lambda v, p=pvp: v.reciprocal(recip_sb[:, :], pv[p][64:65, :]),
                       wait=wts)
                rcp_v = sv.n
                wts = [(st, pv_v)]
                if ao_free:
                    wts.append((sv, ao_free))
                P.emit("scalar", lambda s, p=pvp: s.copy(ao_st[:, :], pv[p][0:64, :]),
                       wait=wts)
                ao_v = ss.n
                wts = [(sv, rcp_v), (sv, ones_v)]
                if rb_free:
                    wts.append((sv, rb_free))
                P.emit("tensor", lambda t: t.matmul(
                    rb[:, :], ones1[:, :], recip_sb[:, :], start=True, stop=True),
                    wait=wts, inc=[(st, 1)])
                rcp_free = st.n
                rb_v = st.n
                P.emit("vector", lambda v, _qc=qc, _hb=hb, _et=et: v.tensor_mul(
                    aoT[_hb:_hb + 64, T * _et + 512 * _qc: T * _et + 512 * (_qc + 1)],
                    ao_st[:, :], rb[:, :]),
                    wait=[(st, rb_v), (ss, ao_v)])
                ao_free = sv.n
                rb_free = sv.n
                pv_freed[pvp] = sv.n
            expg_free[par] = sv.n
            fil = bufs.get("filler")
            if fil is not None:
                fil()

    with contextlib.ExitStack() as octx:
        po = [octx.enter_context(nc.psum_tensor(f"B_po{i}", [128, 512], F32)) for i in range(2)]
        ptr3 = [octx.enter_context(nc.psum_tensor(f"B_ptr3{i}", [128, 128], BF16)) for i in range(2)]
        ostage = octx.enter_context(nc.sbuf_tensor("B_ost", [128, 512], BF16))
        stok = octx.enter_context(nc.sbuf_tensor("B_stok", [128, 4 * 1024], BF16))
        x_in = [octx.enter_context(nc.sbuf_tensor(f"B_xi{i}", [128, 1024], F32)) for i in range(2)]
        if g_sb is None:
            g_st = [octx.enter_context(nc.sbuf_tensor(f"B_gst{i}", [128, 1024], BF16)) for i in range(2)]
        else:
            g_st = None
        x1st = [octx.enter_context(nc.sbuf_tensor(f"B_x1{i}", [128, 1024], F32)) for i in range(2)]
        wobuf = octx.enter_context(nc.sbuf_tensor("B_wo", [128, 8 * 1024], BF16))
        bar_o = [(st, st.n), (ss, ss.n), (sv, sv.n)]
        wo_v = gwq[0].load(wobuf[:, :], hbm["woT"][:, :],
                           wait=list(bar_o))
        ao_all = P.sems["v"].n
        po_free = [0, 0]
        ptr3_free = [0, 0]
        ost_free = 0
        stok_free = 0
        xi_free = [0, 0]
        x1_free = [0, 0]
        for qc in range(NQC):
            for et in range(8):
                pa = et % 2
                for dc in range(8):
                    wts = [(gwq[0].sem, wo_v), (sv, ao_all), (ss, bar_o[1][1])]
                    if dc == 0 and po_free[pa]:
                        wts.append((ss, po_free[pa]))
                    P.emit("tensor", lambda t, d=dc, e=et, p=pa, _qc=qc: t.matmul(
                        po[p][:, :],
                        wobuf[:, 1024 * d + 128 * e: 1024 * d + 128 * (e + 1)],
                        aoT[:, T * d + 512 * _qc: T * d + 512 * (_qc + 1)],
                        start=(d == 0), stop=(d == 7)),
                        wait=wts, inc=[(st, 1)] if dc == 7 else [])
                mm_v = st.n
                wts = [(st, mm_v)]
                if ost_free:
                    wts.append((st, ost_free))
                P.emit("scalar", lambda s, p=pa: s.copy(ostage[:, :], po[p][:, :]),
                       wait=wts)
                po_free[pa] = ss.n
                ost_v = ss.n
                for j in range(4):
                    tp3 = j % 2
                    wts = [(ss, ost_v), (gc.sem, CW)]
                    if ptr3_free[tp3]:
                        wts.append((sv, ptr3_free[tp3]))
                    P.emit("tensor", lambda t, _j=j, p=tp3: t.transpose(
                        ptr3[p][:, :], ostage[:, 128 * _j:128 * (_j + 1)], identb[:, :]),
                        wait=wts, inc=[(st, 1)])
                    tr_v = st.n
                    wts = [(st, tr_v)]
                    P.emit("vector", lambda v, _j=j, e=et, p=tp3: v.tensor_copy(
                        stok[:, 1024 * _j + 128 * e: 1024 * _j + 128 * (e + 1)], ptr3[p][:, :]),
                        wait=wts)
                    ptr3_free[tp3] = sv.n
                ost_free = st.n
            stok_v = sv.n
            for j in range(4):
                tt = 4 * qc + j
                par = j % 2
                wts = list(bar_o) + ([(sv, xi_free[par])] if xi_free[par] else [])
                xv = sxc[par].load(x_in[par][:, :], hbm["x"][128 * tt:128 * (tt + 1), :], wait=wts)
                if g_st is not None:
                    gv = sg[par].load(g_st[par][:, :], hbm["g_in"][128 * tt:128 * (tt + 1), :],
                                      wait=wts)
                    g_dep = [(sg[par].sem, gv)]
                    g_ap = g_st[par]
                    g_lo = 0
                else:
                    g_dep = [(P.sems["s"], bufs["g_ready"])]
                    g_ap = g_sb
                    g_lo = 1024 * tt
                wts = [(sv, stok_v), (sxc[par].sem, xv)] + g_dep
                if x1_free[par]:
                    wts.append((so[par].sem, x1_free[par]))
                P.emit("vector", lambda v, _j=j, p=par, ga=g_ap, lo=g_lo: v.tensor_mul(
                    x1st[p][:, :], stok[:, 1024 * _j:1024 * (_j + 1)],
                    ga[:, lo:lo + 1024]),
                    wait=wts)
                P.emit("vector", lambda v, p=par: v.tensor_add(x1st[p][:, :], x1st[p][:, :], x_in[p][:, :]))
                x1_v = sv.n
                xi_free[par] = x1_v
                x1w = so[par].load(hbm["x1_out"][128 * tt:128 * (tt + 1), :], x1st[par][:, :],
                                   wait=[(sv, x1_v)])
                x1_free[par] = x1w
            stok_free = sv.n
        P.emit("sync", lambda s: None, wait=[(so[0].sem, so[0].wv()), (so[1].sem, so[1].wv())])
    return


def build_B(cfg):
    nc = bass.Bass("TRN2", target_bir_lowering=False)
    T = cfg["T"]
    hbm = {}
    hbm["x"] = nc.dram_tensor("x", [T, 1024], F32, kind="ExternalInput").ap()
    hbm["g_in"] = nc.dram_tensor("g_in", [T, 1024], BF16, kind="ExternalInput").ap()
    for nm in ("wqT", "wkT", "wvT", "woT"):
        hbm[nm] = nc.dram_tensor(nm, [128, 8 * 1024], F32, kind="ExternalInput").ap()
    hbm["gt"] = nc.dram_tensor("gt", [16, 128, 2 * T - 128], F32, kind="ExternalInput").ap()
    hbm["ident"] = nc.dram_tensor("ident", [128, 128], F32, kind="ExternalInput").ap()
    hbm["x1_out"] = nc.dram_tensor("x1_out", [T, 1024], F32, kind="ExternalOutput").ap()
    with contextlib.ExitStack() as ctx:
        P = Plan(nc, ctx)
        for s in ("t", "s", "v", "g"):
            P.sem(s)
        bufs = {}
        bufs["ident"] = ctx.enter_context(nc.sbuf_tensor("identb", [128, 128], BF16))
        build_phase_B(P, hbm, bufs, cfg)
        P.run()
    return nc




def build_phase_C(P, hbm, bufs, cfg):
    nc = P.nc
    T = cfg["T"]; NT = T // 128; NDT = 8
    NH = 16
    QC = 512; NQC = T // QC
    NKT = T // 128
    WIN_DELTAS = [-256, -128, 0, 128, 256, 384]
    st = P.sems["t"]; ss = P.sems["s"]; sv = P.sems["v"]
    ctx = P.ctx

    identb = bufs["ident"]
    gc = DmaStream(P, "C_gc", "gpsimd")
    gwq = [DmaStream(P, f"C_gw{i}", "gpsimd") for i in range(2)]
    so = [DmaStream(P, f"C_so{i}", "gpsimd") for i in range(2)]

    QT = ctx.enter_context(nc.sbuf_tensor("C_QT", [128, 8 * T], BF16))
    KT = ctx.enter_context(nc.sbuf_tensor("C_KT", [128, 8 * T], BF16))
    VTM = ctx.enter_context(nc.sbuf_tensor("C_VTM", [128, NKT * (NH * 65)], BF16))
    ones1 = ctx.enter_context(nc.sbuf_tensor("C_ones1", [1, 64], F32))

    gc.load(identb[:, :], hbm["ident"][:, :])
    CW = gc.wv()
    P.emit("vector", lambda v: v.memset(VTM[:, :], 1.0))
    vtm_init = sv.n
    P.emit("vector", lambda v: v.memset(ones1[:, :], 1.0))
    ones_v = sv.n

    with contextlib.ExitStack() as lctx:
        Pctx_saved = P.ctx
        P.ctx = lctx
        xnT = lctx.enter_context(nc.sbuf_tensor("C_xnT", [128, 8 * T], BF16))
        wbuf = [lctx.enter_context(nc.sbuf_tensor(f"C_w{i}", [128, 8 * 1024], BF16)) for i in range(2)]
        xn_done = emit_ln_to_xnT(P, hbm["x1"], xnT, identb, cfg, "C_ln", deps=[(gc.sem, CW)])
        P.ctx = Pctx_saved

        names = ["l1qT", "l1kT", "l1vT"]
        wv_ld = [0, 0, 0]
        wproj_done = [0, 0, 0]
        wv_ld[0] = gwq[0].load(wbuf[0][:, :], hbm[names[0]][:, :])
        wv_ld[1] = gwq[1].load(wbuf[1][:, :], hbm[names[1]][:, :])
        with contextlib.ExitStack() as pctx:
            pp = [pctx.enter_context(nc.psum_tensor(f"C_pp{i}", [128, 512], F32)) for i in range(2)]
            ptr2 = [pctx.enter_context(nc.psum_tensor(f"C_ptr2{i}", [128, 128], BF16)) for i in range(2)]
            vstage = pctx.enter_context(nc.sbuf_tensor("C_vstage", [128, 512], BF16))
            pp_free = [0, 0]
            ptr2_free = [0, 0]
            vstage_free = 0
            for wi, nm in enumerate(names):
                par = wi % 2
                if wi >= 2:
                    wv_ld[wi] = gwq[par].load(wbuf[par][:, :], hbm[names[wi]][:, :],
                                              wait=[(st, wproj_done[wi - 2])])
                for et in range(8):
                    for tcq in range(T // 512):
                        pa = (et * (T // 512) + tcq) % 2
                        for dc in range(8):
                            wts = [(gwq[par].sem, wv_ld[wi]), (sv, xn_done)]
                            if dc == 0 and pp_free[pa]:
                                wts.append((sv, pp_free[pa]))
                            P.emit("tensor", lambda t, d=dc, e=et, q=tcq, p=pa, w=par: t.matmul(
                                pp[p][:, :],
                                wbuf[w][:, 1024 * d + 128 * e: 1024 * d + 128 * (e + 1)],
                                xnT[:, T * d + 512 * q: T * d + 512 * (q + 1)],
                                start=(d == 0), stop=(d == 7)),
                                wait=wts, inc=[(st, 1)] if dc == 7 else [])
                        mm_v = st.n
                        if nm == "l1qT":
                            P.emit("vector", lambda v, e=et, q=tcq, p=pa: v.tensor_copy(
                                QT[:, T * e + 512 * q: T * e + 512 * (q + 1)], pp[p][:, :]),
                                wait=[(st, mm_v)])
                            pp_free[pa] = sv.n
                        elif nm == "l1kT":
                            P.emit("vector", lambda v, e=et, q=tcq, p=pa: v.tensor_copy(
                                KT[:, T * e + 512 * q: T * e + 512 * (q + 1)], pp[p][:, :]),
                                wait=[(st, mm_v)])
                            pp_free[pa] = sv.n
                        else:
                            wts = [(st, mm_v)]
                            if vstage_free:
                                wts.append((st, vstage_free))
                            P.emit("vector", lambda v, p=pa: v.tensor_copy(vstage[:, :], pp[p][:, :]),
                                   wait=wts)
                            pp_free[pa] = sv.n
                            vst_v = sv.n
                            for j in range(4):
                                tp2 = j % 2
                                wts = [(sv, vst_v), (gc.sem, CW)]
                                if ptr2_free[tp2]:
                                    wts.append((sv, ptr2_free[tp2]))
                                P.emit("tensor", lambda t, _j=j, p=tp2: t.transpose(
                                    ptr2[p][:, :], vstage[:, 128 * _j:128 * (_j + 1)], identb[:, :]),
                                    wait=wts, inc=[(st, 1)])
                                tr_v = st.n
                                kt_idx = tcq * 4 + j
                                for half in range(2):
                                    h_idx = 2 * et + half
                                    P.emit("vector", lambda v, k=kt_idx, h=h_idx, p=tp2, hf=half: v.tensor_copy(
                                        VTM[:, 65 * NH * k + 65 * h: 65 * NH * k + 65 * h + 64],
                                        ptr2[p][:, 64 * hf:64 * (hf + 1)]),
                                        wait=[(st, tr_v), (sv, vtm_init)])
                                ptr2_free[tp2] = sv.n
                            vstage_free = st.n
                wproj_done[wi] = st.n
        proj_evac_v = P.sems["v"].n

    aoT = ctx.enter_context(nc.sbuf_tensor("C_aoT", [128, 8 * T], BF16))

    with contextlib.ExitStack() as actx:
        m01 = actx.enter_context(nc.sbuf_tensor("C_m01", [128, 6 * 512], BF16))
        gc.load(m01.rearrange("p (d n) -> p d n", d=6), hbm["m01"][:, :, :].rearrange("d p n -> p d n"),
                wait=[(st, st.n), (ss, ss.n), (sv, sv.n)])
        CWM = gc.wv()
        pstage = [actx.enter_context(nc.sbuf_tensor(f"C_pst{i}", [128, 1024], BF16)) for i in range(2)]
        pm_sb = [actx.enter_context(nc.sbuf_tensor(f"C_pm{i}", [128, 1024], BF16)) for i in range(4)]
        recip_sb = actx.enter_context(nc.sbuf_tensor("C_rcp", [1, 512], F32))
        ao_st = actx.enter_context(nc.sbuf_tensor("C_aost", [64, 512], BF16))
        S2 = [actx.enter_context(nc.psum_tensor(f"C_S2{i}", [128, 1024], F32)) for i in range(2)]
        pv = [actx.enter_context(nc.psum_tensor(f"C_pv{i}", [65, 512], F32)) for i in range(2)]
        rb = actx.enter_context(nc.psum_tensor("C_rb", [64, 512], F32))
        bar_att = [(st, st.n), (ss, ss.n), (sv, sv.n)]

        S2_free = [0, 0]
        pst_free = [0, 0]
        pm_free = [0, 0, 0, 0]
        pv_freed = [0, 0]
        rcp_free = 0
        ao_free = 0
        rb_free = 0

        for h in range(NH):
            hb = 64 * (h % 2)
            et = h // 2
            for qc in range(NQC):
                kts = [(di, (512 * qc + d) // 128) for di, d in enumerate(WIN_DELTAS)
                       if 0 <= 512 * qc + d < T]
                npair = (len(kts) + 1) // 2
                pvp = qc % 2
                mult_vals = [0] * npair

                def emit_pv(pi, pairs=None):
                    pair = kts[2 * pi: 2 * pi + 2]
                    for u, (di, kt) in enumerate(pair):
                        first = (pi == 0 and u == 0)
                        last = (pi == npair - 1 and u == len(pair) - 1)
                        wts = [(sv, mult_vals[pi])]
                        if first:
                            wts.append((sv, vtm_init))
                            if pv_freed[pvp]:
                                wts.append((sv, pv_freed[pvp]))
                        P.emit("tensor", lambda t, k=kt, _u=u, p=pvp, _h=h, f=first, l2=last, mp=pi % 4: t.matmul(
                            pv[p][:, :],
                            VTM[:, 65 * NH * k + 65 * _h: 65 * NH * k + 65 * _h + 65],
                            pm_sb[mp][:, 512 * _u:512 * (_u + 1)],
                            start=f, stop=l2),
                            wait=wts, inc=[(st, 1)] if last else [])
                        pm_free[pi % 4] = st.n

                for pi in range(npair):
                    pair = kts[2 * pi: 2 * pi + 2]
                    sp = pi % 2
                    for u, (di, kt) in enumerate(pair):
                        wts = []
                        if u == 0 and S2_free[sp]:
                            wts.append((ss, S2_free[sp]))
                        if h == 0 and qc == 0 and pi == 0 and u == 0:
                            wts.append((st, wproj_done[2]))
                            wts.append((sv, proj_evac_v))
                        P.emit("tensor", lambda t, k=kt, _u=u, p=sp, _qc=qc, _hb=hb, _et=et: t.matmul(
                            S2[p][:, 512 * _u:512 * (_u + 1)],
                            KT[_hb:_hb + 64, T * _et + 128 * k: T * _et + 128 * (k + 1)],
                            QT[_hb:_hb + 64, T * _et + 512 * _qc: T * _et + 512 * (_qc + 1)],
                            start=True, stop=True),
                            wait=wts, inc=[(st, 1)] if u == len(pair) - 1 else [])
                    sc_v = st.n
                    mp = pi % 4
                    wts = [(st, sc_v), (gc.sem, CW)]
                    if pst_free[mp % 2]:
                        wts.append((sv, pst_free[mp % 2]))
                    wdt = 512 * len(pair)
                    P.emit("scalar", lambda s, p=sp, w=wdt, _mp=mp % 2: s.activation(
                        pstage[_mp][:, 0:w], S2[p][:, 0:w], AF.Exp, scale=0.125),
                        wait=wts)
                    S2_free[sp] = ss.n
                    exp_v = ss.n
                    for u, (di, kt) in enumerate(pair):
                        wts = [(ss, exp_v), (gc.sem, CWM)]
                        if pm_free[mp]:
                            wts.append((st, pm_free[mp]))
                        P.emit("vector", lambda v, _u=u, _di=di, _mp=mp, _ps=mp % 2: v.tensor_mul(
                            pm_sb[_mp][:, 512 * _u:512 * (_u + 1)],
                            pstage[_ps][:, 512 * _u:512 * (_u + 1)],
                            m01[:, 512 * _di:512 * (_di + 1)]),
                            wait=wts)
                    pst_free[mp % 2] = sv.n
                    mult_vals[pi] = sv.n
                for pi in range(npair):
                    emit_pv(pi)
                pv_v = st.n
                pvp = qc % 2
                wts = [(st, pv_v)]
                if rcp_free:
                    wts.append((st, rcp_free))
                P.emit("vector", lambda v, p=pvp: v.reciprocal(recip_sb[:, :], pv[p][64:65, :]),
                       wait=wts)
                rcp_v = sv.n
                wts = [(st, pv_v)]
                if ao_free:
                    wts.append((sv, ao_free))
                P.emit("scalar", lambda s, p=pvp: s.copy(ao_st[:, :], pv[p][0:64, :]),
                       wait=wts)
                ao_v = ss.n
                wts = [(sv, rcp_v), (sv, ones_v)]
                if rb_free:
                    wts.append((sv, rb_free))
                P.emit("tensor", lambda t: t.matmul(
                    rb[:, :], ones1[:, :], recip_sb[:, :], start=True, stop=True),
                    wait=wts, inc=[(st, 1)])
                rcp_free = st.n
                rb_v = st.n
                P.emit("vector", lambda v, _qc=qc, _hb=hb, _et=et: v.tensor_mul(
                    aoT[_hb:_hb + 64, T * _et + 512 * _qc: T * _et + 512 * (_qc + 1)],
                    ao_st[:, :], rb[:, :]),
                    wait=[(st, rb_v), (ss, ao_v)])
                ao_free = sv.n
                rb_free = sv.n
                pv_freed[pvp] = sv.n
            fil = bufs.get("filler")
            if fil is not None:
                fil()

    with contextlib.ExitStack() as octx:
        po = [octx.enter_context(nc.psum_tensor(f"C_po{i}", [128, 512], F32)) for i in range(2)]
        ptr3 = [octx.enter_context(nc.psum_tensor(f"C_ptr3{i}", [128, 128], BF16)) for i in range(2)]
        ostage = octx.enter_context(nc.sbuf_tensor("C_ost", [128, 512], BF16))
        stok = octx.enter_context(nc.sbuf_tensor("C_stok", [128, 4 * 1024], BF16))
        wobuf = octx.enter_context(nc.sbuf_tensor("C_wo", [128, 8 * 1024], BF16))
        bar_o = [(st, st.n), (ss, ss.n), (sv, sv.n)]
        wo_v = gwq[0].load(wobuf[:, :], hbm["l1oT"][:, :], wait=list(bar_o))
        ao_all = P.sems["v"].n
        po_free = [0, 0]
        ptr3_free = [0, 0]
        ost_free = 0
        stok_free = [0, 0]
        for qc in range(NQC):
            for et in range(8):
                pa = et % 2
                for dc in range(8):
                    wts = [(gwq[0].sem, wo_v), (sv, ao_all), (ss, bar_o[1][1])]
                    if dc == 0 and po_free[pa]:
                        wts.append((ss, po_free[pa]))
                    P.emit("tensor", lambda t, d=dc, e=et, p=pa, _qc=qc: t.matmul(
                        po[p][:, :],
                        wobuf[:, 1024 * d + 128 * e: 1024 * d + 128 * (e + 1)],
                        aoT[:, T * d + 512 * _qc: T * d + 512 * (_qc + 1)],
                        start=(d == 0), stop=(d == 7)),
                        wait=wts, inc=[(st, 1)] if dc == 7 else [])
                mm_v = st.n
                wts = [(st, mm_v)]
                if ost_free:
                    wts.append((st, ost_free))
                P.emit("scalar", lambda s, p=pa: s.copy(ostage[:, :], po[p][:, :]),
                       wait=wts)
                po_free[pa] = ss.n
                ost_v = ss.n
                for j in range(4):
                    tp3 = j % 2
                    wts = [(ss, ost_v), (gc.sem, CW)]
                    if ptr3_free[tp3]:
                        wts.append((sv, ptr3_free[tp3]))
                    P.emit("tensor", lambda t, _j=j, p=tp3: t.transpose(
                        ptr3[p][:, :], ostage[:, 128 * _j:128 * (_j + 1)], identb[:, :]),
                        wait=wts, inc=[(st, 1)])
                    tr_v = st.n
                    wts = [(st, tr_v)]
                    par_t = j % 2
                    if et == 0 and stok_free[par_t]:
                        wts.append((so[par_t].sem, stok_free[par_t]))
                    P.emit("vector", lambda v, _j=j, e=et, p=tp3: v.tensor_copy(
                        stok[:, 1024 * _j + 128 * e: 1024 * _j + 128 * (e + 1)], ptr3[p][:, :]),
                        wait=wts)
                    ptr3_free[tp3] = sv.n
                ost_free = st.n
            stok_v = sv.n
            for j in range(4):
                tt = 4 * qc + j
                par = j % 2
                stok_free[par] = so[par].load(
                    hbm["s1_out"][128 * tt:128 * (tt + 1), :],
                    stok[:, 1024 * j:1024 * (j + 1)], wait=[(sv, stok_v)])
        P.emit("sync", lambda s: None, wait=[(so[0].sem, so[0].wv()), (so[1].sem, so[1].wv())])
    return


def build_C(cfg):
    nc = bass.Bass("TRN2", target_bir_lowering=False)
    T = cfg["T"]
    hbm = {}
    hbm["x1"] = nc.dram_tensor("x1", [T, 1024], F32, kind="ExternalInput").ap()
    for nm in ("l1qT", "l1kT", "l1vT", "l1oT"):
        hbm[nm] = nc.dram_tensor(nm, [128, 8 * 1024], F32, kind="ExternalInput").ap()
    hbm["m01"] = nc.dram_tensor("m01", [6, 128, 512], F32, kind="ExternalInput").ap()
    hbm["ident"] = nc.dram_tensor("ident", [128, 128], F32, kind="ExternalInput").ap()
    hbm["s1_out"] = nc.dram_tensor("s1_out", [T, 1024], BF16, kind="ExternalOutput").ap()
    with contextlib.ExitStack() as ctx:
        P = Plan(nc, ctx)
        for s in ("t", "s", "v", "g"):
            P.sem(s)
        bufs = {}
        bufs["ident"] = ctx.enter_context(nc.sbuf_tensor("identb", [128, 128], BF16))
        build_phase_C(P, hbm, bufs, cfg)
        P.run()
    return nc




class WeightQuantEmitter:

    def __init__(self, P, hbm, cfg, identb):
        self.P = P
        self.bufs_ident = identb
        nc = P.nc
        I = cfg.get("I", 2816)
        ctx = P.ctx
        self.gin = [DmaStream(P, f"WQ_in{i}", "sync") for i in range(2)]
        self.sout = [DmaStream(P, f"WQ_out{i}", "sync") for i in range(2)]
        self.wtile = [ctx.enter_context(nc.sbuf_tensor(f"WQ_wt{i}", [128, 512], F32, side="right"))
                      for i in range(4)]
        WMX = max(1024, I)
        self.wq_t = [ctx.enter_context(nc.sbuf_tensor(f"WQ_q{i}", [128, WMX], BF16, side="right"))
                     for i in range(2)]

        qs = {}
        qs["t_s"] = ctx.enter_context(nc.sbuf_tensor("WQ_ts", [128, 32], F32, side="right"))
        qs["rcp"] = ctx.enter_context(nc.sbuf_tensor("WQ_rcp", [128, 32], F32, side="right"))
        for nm in ("absx", "y", "accA", "accC", "msf", "c2"):
            qs[nm] = ctx.enter_context(nc.sbuf_tensor(f"WQ_{nm}", [128, 512], F32, side="right"))
        qs["u"] = qs["absx"]
        self.qs = qs
        self._pctx = contextlib.ExitStack()
        self.jobs = []
        for nt in range(I // 128):
            self.jobs.append(("wgate", "wgq_nat", nt, 1024))
        for nt in range(I // 128):
            self.jobs.append(("wup", "wuq_nat", nt, 1024))
        for nt in range(8):
            self.jobs.append(("wdown", "wdq_nat", nt, I))
        self.hbm = hbm
        self.flat = []
        for j, (src_name, dst, nt, width) in enumerate(self.jobs):
            nch = (width + 511) // 512
            for c in range(nch):
                self.flat.append((j, c, min(512, width - 512 * c), c == nch - 1))
        self.fcur = 0
        self.nloaded = 0
        self.loadv = {}
        self.wt_free = [(0, 0)] * 4
        self.wq_free = [0, 0]
        self.done_dep = None

    def _load_upto(self, i):
        P = self.P
        ss = P.sems["s"]; sv = P.sems["v"]
        while self.nloaded < min(i, len(self.flat)):
            pos = self.nloaded
            j, c, w2, last = self.flat[pos]
            src_name, dst, nt, width = self.jobs[j]
            bi = pos % 4
            fs, fv = self.wt_free[bi]
            wts = ([(ss, fs)] if fs else []) + ([(sv, fv)] if fv else [])
            self.loadv[pos] = self.gin[j % 2].load(
                self.wtile[bi][:, 0:w2],
                self.hbm[src_name][128 * nt:128 * (nt + 1), 512 * c:512 * c + w2],
                wait=wts)
            self.nloaded += 1

    def emit_chunk(self):
        pos = self.fcur
        if pos >= len(self.flat):
            return False
        P = self.P
        j, c, w2, last = self.flat[pos]
        src_name, dst, nt, width = self.jobs[j]
        self._load_upto(pos + 4)
        bi = pos % 4
        qp = j % 2
        wts = [(self.gin[j % 2].sem, self.gin[j % 2].wv())]
        if self.wq_free[qp] and c == 0:
            wts.append((self.sout[qp].sem, self.wq_free[qp]))
        qv = emit_quant(P, self.wq_t[qp][:, 512 * c:512 * c + w2],
                        self.wtile[bi][:, 0:w2], w2, self.qs,
                        src_ready=wts)
        self.wt_free[bi] = (self.qs["last_scalar"], self.qs["src_free_v"])
        self.fcur += 1
        if last:
            self.wq_free[qp] = self.sout[qp].load(
                self.hbm[dst][128 * nt:128 * (nt + 1), 0:width],
                self.wq_t[qp][:, 0:width], wait=[qv])
            if self.fcur >= len(self.flat):
                self._pctx.close()
                self.done_dep = [(self.sout[0].sem, self.sout[0].wv()),
                                 (self.sout[1].sem, self.sout[1].wv())]
        return True

    def finish(self):
        while self.emit_chunk():
            pass
        return self.done_dep



def _hap(h, pairs):
    return bass.AP(h, 0, [[h.shape[1], 128]] + pairs)


def emit_quant(P, dst, src, width, qs, src_ready=(), extra_dst_wait=()):
    nc = P.nc
    ss = P.sems["s"]; sv = P.sems["v"]
    nb = width // 16
    src3 = bass.AP(src.tensor, src.offset,
                   [list(src.ap[0]), [16, nb], [1, 16]])
    ts = qs["t_s"]; rcp = qs["rcp"]
    P.emit("vector", lambda v: v.reduce_max(ts[:, 0:nb], src3, AX.X,
                                            apply_absolute_value=True),
           wait=list(src_ready))
    qs["src_free_v"] = sv.n
    P.emit("vector", lambda v: v.tensor_scalar(ts[:, 0:nb], ts[:, 0:nb],
                                               1.0 / 6.0, 1e-12, ALU.mult, ALU.max))
    P.emit("vector", lambda v: v.reciprocal(rcp[:, 0:nb], ts[:, 0:nb]))
    rcp_v = sv.n
    P.emit("scalar", lambda s: s.activation(qs["absx"][:, 0:width], src, AF.Abs),
           wait=list(src_ready) + ([(sv, qs["absx_free"])] if qs.get("absx_free") else []))
    absx_v = ss.n
    rcp_b = _hap(rcp, [[1, nb], [0, 16]])
    y3 = _hap(qs["y"], [[16, nb], [1, 16]])
    absx3 = _hap(qs["absx"], [[16, nb], [1, 16]])
    P.emit("vector", lambda v: v.tensor_mul(y3, absx3, rcp_b),
           wait=[(ss, absx_v)])
    qs["absx_free"] = sv.n
    y = qs["y"]; accA = qs["accA"]; accC = qs["accC"]
    u = qs["u"]; c2 = qs["c2"]
    W = width
    P.emit("vector", lambda v: v.tensor_single_scalar(
        accA[:, 0:W], y[:, 0:W], 0.25, ALU.is_gt))
    for c in (0.75, 1.25, 1.75):
        P.emit("vector", lambda v, _c=c: v.scalar_tensor_tensor(
            accA[:, 0:W], y[:, 0:W], _c, accA[:, 0:W], ALU.is_gt, ALU.add))
    P.emit("vector", lambda v: v.tensor_single_scalar(
        accC[:, 0:W], y[:, 0:W], 2.5, ALU.is_gt))
    P.emit("vector", lambda v: v.scalar_tensor_tensor(
        accC[:, 0:W], y[:, 0:W], 3.5, accC[:, 0:W], ALU.is_gt, ALU.add))
    P.emit("vector", lambda v: v.tensor_single_scalar(
        c2[:, 0:W], y[:, 0:W], 5.0, ALU.is_gt))
    P.emit("scalar", lambda s: s.activation(qs["msf"][:, 0:width], src, AF.Sign),
           wait=list(src_ready) + ([(sv, qs["msf_free"])] if qs.get("msf_free") else []))
    sgn_v = ss.n
    qs["last_scalar"] = sgn_v
    ts_b = _hap(ts, [[1, nb], [0, 16]])
    msf3 = _hap(qs["msf"], [[16, nb], [1, 16]])
    P.emit("vector", lambda v: v.tensor_mul(msf3, msf3, ts_b), wait=[(ss, sgn_v)])
    P.emit("vector", lambda v: v.scalar_tensor_tensor(
        u[:, 0:W], accA[:, 0:W], 0.5, accC[:, 0:W], ALU.mult, ALU.add))
    P.emit("vector", lambda v: v.scalar_tensor_tensor(
        u[:, 0:W], c2[:, 0:W], 2.0, u[:, 0:W], ALU.mult, ALU.add))
    P.emit("vector", lambda v: v.tensor_mul(dst, u[:, 0:W], qs["msf"][:, 0:width]),
           wait=list(extra_dst_wait))
    qs["msf_free"] = sv.n
    qs["absx_free"] = sv.n
    return (sv, sv.n)


_QS_N = [0]


def mk_quant_scratch(P, width):
    ctx = P.ctx
    nc = P.nc
    _QS_N[0] += 1
    tag = _QS_N[0]
    qs = {}
    qs["t_s"] = ctx.enter_context(nc.sbuf_tensor(f"q{tag}_ts", [128, width // 16], F32))
    qs["rcp"] = ctx.enter_context(nc.sbuf_tensor(f"q{tag}_rcp", [128, width // 16], F32))
    for nm in ("absx", "y", "accA", "accC", "msf", "c2"):
        qs[nm] = ctx.enter_context(nc.sbuf_tensor(f"q{tag}_{nm}", [128, width], F32))
    qs["u"] = qs["absx"]
    return qs


def emit_sharded_wq(P, hbm, bufs, cfg, ident_dep=None):
    nc = P.nc
    I = cfg.get("I", 2816)
    NI = I // 128
    st = P.sems["t"]; ss = P.sems["s"]; sv = P.sems["v"]
    ctx = P.ctx
    identb = bufs["ident"]
    with contextlib.ExitStack() as wctx:
        saved = P.ctx
        P.ctx = wctx
        gin = [DmaStream(P, f"SW_in{i}", "sync") for i in range(2)]
        sosc = [DmaStream(P, f"SW_out{i}", "sync") for i in range(2)]
        srs = DmaStream(P, "SW_rs", "sync")
        ccs = P.sem("SW_cc")
        WMX = max(1024, I)
        wtile = [wctx.enter_context(nc.sbuf_tensor(f"SW_wt{i}", [128, WMX], F32)) for i in range(2)]
        wq_t = wctx.enter_context(nc.sbuf_tensor("SW_q", [128, WMX], BF16))
        stage = [wctx.enter_context(nc.sbuf_tensor(f"SW_st{i}", [128, max(8, NI) * 128], BF16)) for i in range(2)]
        qs = mk_quant_scratch(P, 1024)
        ptrw = [wctx.enter_context(nc.psum_tensor(f"SW_ptr{i}", [128, 128], BF16)) for i in range(2)]
        bar_w = [(st, st.n), (ss, ss.n), (sv, sv.n)]
        wt_free = [(0, 0), (0, 0)]
        wq_free = 0
        ptrw_free = [0, 0]
        stage_free = [0, 0]

        jobs = []
        for s in range(6):
            jobs.append((hbm["wjob_a"][s, :, :], 1024 * s, 1024))
        jobs.append((hbm["wjob_b"][:, :], 6144, I))

        for ji, (srcap, orow, width) in enumerate(jobs):
            par = ji % 2
            nkt = width // 128
            fs, fv = wt_free[par]
            wts = list(bar_w) + ([(ss, fs)] if fs else []) + ([(sv, fv)] if fv else [])
            lv = gin[par].load(wtile[par][:, 0:width], srcap, wait=wts)
            for qoff in range(0, width, 1024):
                w2 = min(1024, width - qoff)
                qv = emit_quant(P, wq_t[:, qoff:qoff + w2],
                                wtile[par][:, qoff:qoff + w2], w2, qs,
                                src_ready=[(gin[par].sem, lv)],
                                extra_dst_wait=([(st, wq_free)] if (wq_free and qoff == 0) else []))
            wt_free[par] = (qs["last_scalar"], qs["src_free_v"])
            spar = ji % 2
            for kt in range(nkt):
                tp = kt % 2
                wts = [qv] + ([ident_dep] if ident_dep else [])
                if ptrw_free[tp]:
                    wts.append((sv, ptrw_free[tp]))
                P.emit("tensor", lambda t, k=kt, p=tp, _pa=par: t.transpose(
                    ptrw[p][:, :], wq_t[:, 128 * k:128 * (k + 1)], identb[:, :]),
                    wait=wts, inc=[(st, 1)])
                tr_v = st.n
                wts = [(st, tr_v)]
                if kt == 0 and stage_free[spar]:
                    wts.append((sosc[spar].sem, stage_free[spar]))
                P.emit("vector", lambda v, k=kt, p=tp, sp=spar: v.tensor_copy(
                    stage[sp][:, 128 * k:128 * (k + 1)], ptrw[p][:, :]),
                    wait=wts)
                ptrw_free[tp] = sv.n
            wq_free = st.n
            stg_v = sv.n
            dst = hbm["wq_outc"].rearrange("(r p) c -> p r c", p=128)[:, orow // 128: orow // 128 + nkt, :]
            stage_free[spar] = sosc[spar].load(
                dst, stage[spar].rearrange("p (k n) -> p k n", n=128)[:, 0:nkt, :],
                wait=[(sv, stg_v)])

        ag_wait = [(sosc[0].sem, sosc[0].wv()), (sosc[1].sem, sosc[1].wv())]
        P.emit("gpsimd", lambda g: g.collective_compute(
            "AllGather", ALU.bypass, replica_groups=[list(range(NCORES))],
            ins=[hbm["wq_outc"]], outs=[hbm["wq_gath"]]),
            wait=ag_wait, inc=[(ccs, 16)], is_dma=True)
        agv = ccs.n

        blk = (6 * 1024 + I) // 128
        gath = hbm["wq_gath"].rearrange("(r p) c -> r p c", p=128)
        for ja in range(2 * NI):
            c, s = divmod(ja, 6)
            r0 = c * blk + s * 8
            dstname = "wgq_scr" if ja < NI else "wuq_scr"
            nt = ja % NI
            srs.load(hbm[dstname][:, :, 128 * nt:128 * (nt + 1)],
                     gath[r0:r0 + 8, :, :], wait=[(ccs, agv)])
        for c in range(8):
            r0 = c * blk + 48
            srs.load(hbm["wdq_scr"][:, :, 128 * c:128 * (c + 1)],
                     gath[r0:r0 + NI, :, :], wait=[(ccs, agv)])
        dep = [(srs.sem, srs.wv())]
        P.ctx = saved
    return dep


def build_phase_D(P, hbm, bufs, cfg):
    nc = P.nc
    T = cfg["T"]; NT = T // 128
    I = cfg.get("I", 2816)
    NI = I // 128
    st = P.sems["t"]; ss = P.sems["s"]; sv = P.sems["v"]
    ctx = P.ctx
    identb = bufs["ident"]

    gcD = DmaStream(P, "D_gc", "gpsimd")
    gcD.load(identb[:, :], hbm["ident"][:, :])
    CW = gcD.wv()

    xqT = ctx.enter_context(nc.sbuf_tensor("D_xqT", [128, 8 * T], BF16))
    out_acc = ctx.enter_context(nc.sbuf_tensor("D_oacc", [128, NT * 1024], F32))

    with contextlib.ExitStack() as lctx:
        Ps = P.ctx; P.ctx = lctx
        sx = [DmaStream(P, f"D_sx{i}", "sync") for i in range(2)]
        x_t = [lctx.enter_context(nc.sbuf_tensor(f"D_x{i}", [128, 1024], F32)) for i in range(2)]
        xsq = lctx.enter_context(nc.sbuf_tensor("D_xsq", [128, 1024], F32))
        stat4 = lctx.enter_context(nc.sbuf_tensor("D_stat", [128, 8], F32))
        xn_tok = lctx.enter_context(nc.sbuf_tensor("D_xntok", [128, 1024], F32))
        xq_tok = lctx.enter_context(nc.sbuf_tensor("D_xqtok", [128, 1024], BF16))
        eps_col = lctx.enter_context(nc.sbuf_tensor("D_eps", [128, 1], F32))
        qs = mk_quant_scratch(P, 1024)
        ptr = [lctx.enter_context(nc.psum_tensor(f"D_ptr{i}", [128, 128], BF16)) for i in range(2)]
        P.emit("vector", lambda v: v.memset(eps_col[:, :], 1e-5))
        eps_v = sv.n
        x_free = [0, 0]
        xsq_free = 0
        xn_free = 0
        xq_free = 0
        ptr_free = [0, 0]
        for tt in range(NT):
            par = tt % 2
            wts = ([(sv, x_free[par])] if x_free[par] else [])
            xv = sx[par].load(x_t[par][:, :], hbm["x1"][128 * tt:128 * (tt + 1), :], wait=wts)
            xw = (sx[par].sem, xv)
            P.emit("vector", lambda v, p=par: v.reduce_sum(stat4[:, 0:1], x_t[p][:, :], AX.X),
                   wait=[xw])
            swts = [xw] + ([(sv, xsq_free)] if xsq_free else [])
            P.emit("scalar", lambda s, p=par: s.activation(xsq[:, :], x_t[p][:, :], AF.Square),
                   wait=swts)
            sq_done = ss.n
            P.emit("vector", lambda v: v.reduce_sum(stat4[:, 1:2], xsq[:, :], AX.X),
                   wait=[(ss, sq_done)])
            xsq_free = sv.n
            P.emit("vector", lambda v: v.tensor_scalar_mul(stat4[:, 2:3], stat4[:, 0:1], 1.0 / 1024.0))
            P.emit("vector", lambda v: v.tensor_scalar_mul(stat4[:, 3:4], stat4[:, 1:2], 1.0 / 1024.0))
            P.emit("vector", lambda v: v.scalar_tensor_tensor(
                stat4[:, 4:5], stat4[:, 2:3], -1.0, stat4[:, 2:3], ALU.mult, ALU.mult))
            P.emit("vector", lambda v: v.tensor_add(stat4[:, 5:6], stat4[:, 3:4], stat4[:, 4:5]))
            var_v = sv.n
            P.emit("scalar", lambda s: s.activation(stat4[:, 6:7], stat4[:, 5:6], AF.Sqrt, bias=eps_col[:, :]),
                   wait=[(sv, var_v), (sv, eps_v)])
            sd_v = ss.n
            P.emit("vector", lambda v: v.reciprocal(stat4[:, 7:8], stat4[:, 6:7]),
                   wait=[(ss, sd_v)])
            rstd_v = sv.n
            wts = [(sv, rstd_v)] + ([(ss, xn_free)] if xn_free else [])
            P.emit("vector", lambda v, p=par: v.tensor_scalar(
                xn_tok[:, :], x_t[p][:, :], stat4[:, 2:3], stat4[:, 7:8], ALU.subtract, ALU.mult),
                wait=wts)
            xn_v = sv.n
            x_free[par] = xn_v
            qv = emit_quant(P, xq_tok[:, :], xn_tok[:, :], 1024, qs,
                            src_ready=[(sv, xn_v)],
                            extra_dst_wait=([(st, xq_free)] if xq_free else []))
            xn_free = qs["last_scalar"]
            for ft in range(8):
                tp = ft % 2
                wts = [qv]
                if ptr_free[tp]:
                    wts.append((sv, ptr_free[tp]))
                wts.append((gcD.sem, CW))
                P.emit("tensor", lambda t, f=ft, p=tp: t.transpose(
                    ptr[p][:, :], xq_tok[:, 128 * f:128 * (f + 1)], identb[:, :]),
                    wait=wts, inc=[(st, 1)])
                tr_v = st.n
                P.emit("vector", lambda v, f=ft, p=tp, _tt=tt: v.tensor_copy(
                    xqT[:, T * f + 128 * _tt: T * f + 128 * (_tt + 1)], ptr[p][:, :]),
                    wait=[(st, tr_v)])
                ptr_free[tp] = sv.n
            xq_free = st.n
        P.ctx = Ps
        xq_done_v = sv.n

    if bufs.get("wq_done_dep") is not None:
        wq_all_done = bufs["wq_done_dep"]
    else:
        _wq = WeightQuantEmitter(P, hbm, cfg, identb)
        wq_all_done = _wq.finish()

    chunks = []
    off = 0
    while off < I:
        w = min(512, I - off)
        chunks.append((off, w))
        off += w
    with contextlib.ExitStack() as mctx:
        Ps = P.ctx; P.ctx = mctx
        gwd = [DmaStream(P, f"D_gwd{i}", "sync") for i in range(2)]
        wgq_sb = [mctx.enter_context(nc.sbuf_tensor(f"D_wgq{i}", [128, 8 * 512], BF16)) for i in range(2)]
        wuq_sb = [mctx.enter_context(nc.sbuf_tensor(f"D_wuq{i}", [128, 8 * 512], BF16)) for i in range(2)]
        wdq_sb = [mctx.enter_context(nc.sbuf_tensor(f"D_wdq{i}", [128, 4 * 1024], BF16)) for i in range(2)]
        sg_st = mctx.enter_context(nc.sbuf_tensor("D_sg", [128, 512], F32))
        h_st = mctx.enter_context(nc.sbuf_tensor("D_h", [128, 512], F32))
        hq_st = [mctx.enter_context(nc.sbuf_tensor(f"D_hq{i}", [128, 512], BF16)) for i in range(2)]
        hqT_st = mctx.enter_context(nc.sbuf_tensor("D_hqT", [128, 4 * 128], BF16))
        qs3 = mk_quant_scratch(P, 512)
        pg = mctx.enter_context(nc.psum_tensor("D_pg", [128, 512], F32))
        pu = mctx.enter_context(nc.psum_tensor("D_pu", [128, 512], F32))
        pd = [mctx.enter_context(nc.psum_tensor(f"D_pd{i}", [128, 1024], F32)) for i in range(2)]
        ptrh = [mctx.enter_context(nc.psum_tensor(f"D_ptrh{i}", [128, 128], BF16)) for i in range(2)]
        bar_m = [(st, st.n), (ss, ss.n), (sv, sv.n)]

        w_ready = {}
        w_free = {0: 0, 1: 0}

        def load_wchunk(ci):
            offc, wc = chunks[ci]
            par = ci % 2
            nt0 = offc // 128
            wnt = wc // 128
            wts = list(bar_m) + list(wq_all_done) + ([(st, w_free[par])] if w_free[par] else [])
            for kt in range(8):
                gwd[par].load(wgq_sb[par][:, 512 * kt:512 * kt + wc],
                              hbm["wgq_nat"][offc:offc + wc, 128 * kt:128 * (kt + 1)],
                              wait=wts, transpose=True)
                gwd[par].load(wuq_sb[par][:, 512 * kt:512 * kt + wc],
                              hbm["wuq_nat"][offc:offc + wc, 128 * kt:128 * (kt + 1)],
                              transpose=True)
            for nl in range(wnt):
                gwd[par].load(wdq_sb[par][:, 1024 * nl:1024 * (nl + 1)],
                              hbm["wdq_nat"][:, 128 * (nt0 + nl):128 * (nt0 + nl + 1)],
                              transpose=True)
            w_ready[ci] = gwd[par].wv()

        load_wchunk(0)
        if len(chunks) > 1:
            load_wchunk(1)

        pg_free = 0
        pu_free = 0
        pd_free = [0, 0]
        sg_free = 0
        h_free = 0
        hq_free = [0, 0]
        hqT_free = 0
        ptrh_free = [0, 0]
        oacc_ready = [0] * NT

        for ci, (offc, wc) in enumerate(chunks):
            par = ci % 2
            wnt = wc // 128
            wv_ = (gwd[par].sem, w_ready[ci])
            gu_vals = [0] * NT
            dmm_vals = [0] * NT

            def emit_head(tt):
                nonlocal pg_free, pu_free
                for k in range(8):
                    wts = [wv_, (sv, xq_done_v)]
                    if k == 0 and pg_free:
                        wts.append((sv, pg_free))
                    P.emit("tensor", lambda t, _k=k, _tt=tt, p=par, w=wc: t.matmul(
                        pg[:, 0:w],
                        xqT[:, T * _k + 128 * _tt: T * _k + 128 * (_tt + 1)],
                        wgq_sb[p][:, 512 * _k: 512 * _k + w],
                        start=(_k == 0), stop=(_k == 7)),
                        wait=wts, inc=[(st, 1)] if k == 7 else [])
                    wts2 = []
                    if k == 0 and pu_free:
                        wts2.append((sv, pu_free))
                    P.emit("tensor", lambda t, _k=k, _tt=tt, p=par, w=wc: t.matmul(
                        pu[:, 0:w],
                        xqT[:, T * _k + 128 * _tt: T * _k + 128 * (_tt + 1)],
                        wuq_sb[p][:, 512 * _k: 512 * _k + w],
                        start=(_k == 0), stop=(_k == 7)),
                        wait=wts2, inc=[(st, 1)] if k == 7 else [])
                gu_vals[tt] = st.n

            def emit_mid(tt):
                nonlocal sg_free, h_free, pg_free, pu_free
                hp = tt % 2
                wts = [(st, gu_vals[tt])]
                if sg_free:
                    wts.append((sv, sg_free))
                P.emit("scalar", lambda s, w=wc: s.activation(sg_st[:, 0:w], pg[:, 0:w], AF.Silu),
                       wait=wts)
                pg_free_s = ss.n
                sgv = ss.n
                wts = [(ss, sgv)]
                if h_free:
                    wts.append((ss, h_free))
                P.emit("vector", lambda v, w=wc: v.tensor_mul(h_st[:, 0:w], sg_st[:, 0:w], pu[:, 0:w]),
                       wait=wts)
                pu_free = sv.n
                pg_free = sv.n
                sg_free = sv.n
                h_v = sv.n
                qv = emit_quant(P, hq_st[hp][:, 0:wc], h_st[:, 0:wc], wc, qs3,
                                src_ready=[(sv, h_v)],
                                extra_dst_wait=([(st, hq_free[hp])] if hq_free[hp] else []))
                h_free = qs3["last_scalar"]
                return qv

            def emit_tmid(tt, qv):
                nonlocal hqT_free, hq_free
                hp = tt % 2
                pdp = tt % 2
                for nt in range(wnt):
                    tp = nt % 2
                    wts = [qv, (gcD.sem, CW)]
                    if ptrh_free[tp]:
                        wts.append((ss, ptrh_free[tp]))
                    P.emit("tensor", lambda t, _n=nt, p=tp, q=hp: t.transpose(
                        ptrh[p][:, :], hq_st[q][:, 128 * _n:128 * (_n + 1)], identb[:, :]),
                        wait=wts, inc=[(st, 1)])
                    tr_v = st.n
                    wts = [(st, tr_v)]
                    if nt == 0 and hqT_free:
                        wts.append((st, hqT_free))
                    P.emit("scalar", lambda s, _n=nt, p=tp: s.copy(
                        hqT_st[:, 128 * _n:128 * (_n + 1)], ptrh[p][:, :]),
                        wait=wts)
                    ptrh_free[tp] = ss.n
                hq_free[hp] = st.n
                hqT_v = ss.n
                for nt in range(wnt):
                    for hh in range(2):
                        wts = [(ss, hqT_v), wv_]
                        if nt == 0 and hh == 0 and pd_free[pdp]:
                            wts.append((sv, pd_free[pdp]))
                        P.emit("tensor", lambda t, _n=nt, _h=hh, p=par, q=pdp: t.matmul(
                            pd[q][:, 512 * _h:512 * (_h + 1)],
                            hqT_st[:, 128 * _n:128 * (_n + 1)],
                            wdq_sb[p][:, 1024 * _n + 512 * _h: 1024 * _n + 512 * (_h + 1)],
                            start=(_n == 0), stop=(_n == wnt - 1),
                            skip_group_check=True),
                            wait=wts, inc=[(st, 1)] if (nt == wnt - 1 and hh == 1) else [])
                hqT_free = st.n
                dmm_vals[tt] = st.n

            def emit_evac(tt):
                nonlocal pd_free
                pdp = tt % 2
                if ci == 0:
                    P.emit("vector", lambda v, _tt=tt, q=pdp: v.tensor_copy(
                        out_acc[:, 1024 * _tt:1024 * (_tt + 1)], pd[q][:, :]),
                        wait=[(st, dmm_vals[tt])])
                else:
                    P.emit("vector", lambda v, _tt=tt, q=pdp: v.tensor_add(
                        out_acc[:, 1024 * _tt:1024 * (_tt + 1)],
                        out_acc[:, 1024 * _tt:1024 * (_tt + 1)], pd[q][:, :]),
                        wait=[(st, dmm_vals[tt])])
                pd_free[pdp] = sv.n
                if ci == len(chunks) - 1:
                    oacc_ready[tt] = sv.n

            emit_head(0)
            qv_cur = emit_mid(0)
            for tt in range(NT):
                if tt + 1 < NT:
                    emit_head(tt + 1)
                emit_tmid(tt, qv_cur)
                if tt + 1 < NT:
                    qv_cur = emit_mid(tt + 1)
                if tt >= 1:
                    emit_evac(tt - 1)
            emit_evac(NT - 1)
            if ci + 2 < len(chunks):
                w_free[par] = st.n
                load_wchunk(ci + 2)

        P.ctx = Ps

    with contextlib.ExitStack() as fctx:
        Ps = P.ctx; P.ctx = fctx
        sxf = [DmaStream(P, f"D_sxf{i}", "sync") for i in range(2)]
        ssf = [DmaStream(P, f"D_ssf{i}", "sync") for i in range(2)]
        sof = [DmaStream(P, f"D_sof{i}", "sync") for i in range(2)]
        x_f = [fctx.enter_context(nc.sbuf_tensor(f"D_xf{i}", [128, 1024], F32)) for i in range(2)]
        s_f = [fctx.enter_context(nc.sbuf_tensor(f"D_sf{i}", [128, 1024], BF16)) for i in range(2)]
        g1_f = fctx.enter_context(nc.sbuf_tensor("D_g1", [128, 1024], BF16))
        o_f = [fctx.enter_context(nc.sbuf_tensor(f"D_of{i}", [128, 1024], F32)) for i in range(2)]
        bar_f = [(st, st.n), (ss, ss.n), (sv, sv.n)]
        xf_free = [0, 0]
        of_free = [0, 0]
        g1_free = 0
        for tt in range(NT):
            par = tt % 2
            wts = list(bar_f) + ([(sv, xf_free[par])] if xf_free[par] else [])
            xv = sxf[par].load(x_f[par][:, :], hbm["x1"][128 * tt:128 * (tt + 1), :], wait=wts)
            sv_ld = ssf[par].load(s_f[par][:, :], hbm["s1"][128 * tt:128 * (tt + 1), :], wait=wts)
            wts = [(sv, oacc_ready[tt])]
            if g1_free:
                wts.append((sv, g1_free))
            P.emit("scalar", lambda s, _tt=tt: s.activation(
                g1_f[:, :], out_acc[:, 1024 * _tt:1024 * (_tt + 1)], AF.Silu),
                wait=wts)
            g1_v = ss.n
            wts = [(ss, g1_v), (ssf[par].sem, sv_ld)]
            if of_free[par]:
                wts.append((sof[par].sem, of_free[par]))
            P.emit("vector", lambda v, p=par: v.tensor_mul(o_f[p][:, :], g1_f[:, :], s_f[p][:, :]),
                   wait=wts)
            g1_free = sv.n
            P.emit("vector", lambda v, p=par: v.tensor_add(o_f[p][:, :], o_f[p][:, :], x_f[p][:, :]),
                   wait=[(sxf[par].sem, xv)])
            o_v = sv.n
            xf_free[par] = o_v
            of_free[par] = sof[par].load(hbm["out"][128 * tt:128 * (tt + 1), :], o_f[par][:, :],
                                         wait=[(sv, o_v)])
        P.emit("sync", lambda s: None, wait=[(sof[0].sem, sof[0].wv()), (sof[1].sem, sof[1].wv())])
        P.ctx = Ps
    return


def build_D(cfg):
    nc = bass.Bass("TRN2", target_bir_lowering=False)
    T = cfg["T"]
    I = cfg.get("I", 2816)
    hbm = {}
    hbm["x1"] = nc.dram_tensor("x1", [T, 1024], F32, kind="ExternalInput").ap()
    hbm["s1"] = nc.dram_tensor("s1", [T, 1024], BF16, kind="ExternalInput").ap()
    hbm["wgate"] = nc.dram_tensor("wgate", [I, 1024], F32, kind="ExternalInput").ap()
    hbm["wup"] = nc.dram_tensor("wup", [I, 1024], F32, kind="ExternalInput").ap()
    hbm["wdown"] = nc.dram_tensor("wdown", [1024, I], F32, kind="ExternalInput").ap()
    hbm["ident"] = nc.dram_tensor("ident", [128, 128], F32, kind="ExternalInput").ap()
    hbm["wgq_nat"] = nc.dram_tensor("wgq_nat", [I, 1024], BF16).ap()
    hbm["wuq_nat"] = nc.dram_tensor("wuq_nat", [I, 1024], BF16).ap()
    hbm["wdq_nat"] = nc.dram_tensor("wdq_nat", [1024, I], BF16).ap()
    if cfg.get("shard_wq"):
        hbm["wjob_a"] = nc.dram_tensor("wjob_a", [6, 128, 1024], F32, kind="ExternalInput").ap()
        hbm["wjob_b"] = nc.dram_tensor("wjob_b", [128, I], F32, kind="ExternalInput").ap()
        nrow = 6 * 1024 + I
        hbm["wq_outc"] = nc.dram_tensor("wq_outc", [nrow, 128], BF16).ap()
        hbm["wq_gath"] = nc.dram_tensor("wq_gath", [8 * nrow, 128], BF16, addr_space="Shared").ap()
    hbm["out"] = nc.dram_tensor("out", [T, 1024], F32, kind="ExternalOutput").ap()
    with contextlib.ExitStack() as ctx:
        P = Plan(nc, ctx)
        for s in ("t", "s", "v", "g"):
            P.sem(s)
        bufs = {}
        bufs["ident"] = ctx.enter_context(nc.sbuf_tensor("identb", [128, 128], BF16))
        build_phase_D(P, hbm, bufs, cfg)
        P.run()
    return nc



def build_A(cfg, fused=False):
    nc = bass.Bass("TRN2", target_bir_lowering=False)
    T = cfg["T"]
    hbm = {}
    hbm["x"] = nc.dram_tensor("x", [T, 1024], F32, kind="ExternalInput").ap()
    hbm["w1all"] = nc.dram_tensor("w1all", [cfg["NE"], 1024, cfg["F"]], F32, kind="ExternalInput").ap()
    hbm["w2all"] = nc.dram_tensor("w2all", [cfg["NE"], cfg["F"], 1024], F32, kind="ExternalInput").ap()
    hbm["b1dev"] = nc.dram_tensor("b1dev", [128, (cfg["F"] // 128) * cfg["NE"]], F32, kind="ExternalInput").ap()
    hbm["b2all"] = nc.dram_tensor("b2all", [cfg["NE"], 1024], F32, kind="ExternalInput").ap()
    hbm["wgdev"] = nc.dram_tensor("wgdev", [128, 8 * (cfg["NE"] - 1)], F32, kind="ExternalInput").ap()
    hbm["ident"] = nc.dram_tensor("ident", [128, 128], F32, kind="ExternalInput").ap()
    hbm["g_out"] = nc.dram_tensor("g_out", [T, 1024], BF16, kind="ExternalOutput").ap()

    with contextlib.ExitStack() as ctx:
        P = Plan(nc, ctx)
        for s in ("sq", "gq", "t", "s", "v"):
            P.sem(s)
        bufs = {}
        bufs["xnT"] = ctx.enter_context(nc.sbuf_tensor("xnT", [128, 8 * T], BF16))
        bufs["g_acc"] = ctx.enter_context(nc.sbuf_tensor("g_acc", [128, (T // 128) * 1024], F32))
        bufs["g_tok"] = None
        bufs["ident"] = ctx.enter_context(nc.sbuf_tensor("identb", [128, 128], BF16))
        build_phase_A(P, hbm, bufs, cfg)
        P.run()
    return nc


def build_full(cfg):
    nc = bass.Bass("TRN2", target_bir_lowering=False)
    T = cfg["T"]
    I = cfg.get("I", 2816)
    h = {}
    h["x"] = nc.dram_tensor("x", [T, 1024], F32, kind="ExternalInput").ap()
    h["w1all"] = nc.dram_tensor("w1all", [cfg["NE"], 1024, cfg["F"]], F32, kind="ExternalInput").ap()
    h["w2all"] = nc.dram_tensor("w2all", [cfg["NE"], cfg["F"], 1024], F32, kind="ExternalInput").ap()
    h["b1dev"] = nc.dram_tensor("b1dev", [128, (cfg["F"] // 128) * cfg["NE"]], F32, kind="ExternalInput").ap()
    h["b2all"] = nc.dram_tensor("b2all", [cfg["NE"], 1024], F32, kind="ExternalInput").ap()
    h["wgdev"] = nc.dram_tensor("wgdev", [128, 8 * (cfg["NE"] - 1)], F32, kind="ExternalInput").ap()
    h["ident"] = nc.dram_tensor("ident", [128, 128], F32, kind="ExternalInput").ap()
    for nm in ("wqT", "wkT", "wvT", "woT", "l1qT", "l1kT", "l1vT", "l1oT"):
        h[nm] = nc.dram_tensor(nm, [128, 8 * 1024], F32, kind="ExternalInput").ap()
    h["gt"] = nc.dram_tensor("gt", [16, 128, 2 * T - 128], F32, kind="ExternalInput").ap()
    h["m01"] = nc.dram_tensor("m01", [6, 128, 512], F32, kind="ExternalInput").ap()
    h["wgate"] = nc.dram_tensor("wgate", [I, 1024], F32, kind="ExternalInput").ap()
    h["wup"] = nc.dram_tensor("wup", [I, 1024], F32, kind="ExternalInput").ap()
    h["wdown"] = nc.dram_tensor("wdown", [1024, I], F32, kind="ExternalInput").ap()
    h["g_out"] = h["g_in"] = nc.dram_tensor("g_h", [T, 1024], BF16).ap()
    h["x1_out"] = h["x1"] = nc.dram_tensor("x1_h", [T, 1024], F32).ap()
    h["s1_out"] = h["s1"] = nc.dram_tensor("s1_h", [T, 1024], BF16).ap()
    h["wgq_nat"] = nc.dram_tensor("wgq_nat", [I, 1024], BF16).ap()
    h["wuq_nat"] = nc.dram_tensor("wuq_nat", [I, 1024], BF16).ap()
    h["wdq_nat"] = nc.dram_tensor("wdq_nat", [1024, I], BF16).ap()
    if cfg.get("shard_wq"):
        h["wjob_a"] = nc.dram_tensor("wjob_a", [6, 128, 1024], F32, kind="ExternalInput").ap()
        h["wjob_b"] = nc.dram_tensor("wjob_b", [128, I], F32, kind="ExternalInput").ap()
        nrow = 6 * 1024 + I
        h["wq_outc"] = nc.dram_tensor("wq_outc", [nrow, 128], BF16).ap()
        h["wq_gath"] = nc.dram_tensor("wq_gath", [8 * nrow, 128], BF16, addr_space="Shared").ap()
    h["out"] = nc.dram_tensor("out", [T, 1024], F32, kind="ExternalOutput").ap()

    with contextlib.ExitStack() as ctx:
        P = Plan(nc, ctx)
        for s in ("t", "s", "v", "g"):
            P.sem(s)
        bufs = {}
        bufs["ident"] = ctx.enter_context(nc.sbuf_tensor("identb", [128, 128], BF16))
        bufs["g_tok"] = None
        wq = WeightQuantEmitter(P, h, cfg, bufs["ident"])
        bufs["filler"] = lambda: wq.emit_chunk()
        for builder in (build_phase_A, build_phase_B, build_phase_C, build_phase_D):
            if builder is build_phase_B:
                bufs["filler"] = None
            if builder is build_phase_D:
                bufs["wq_done_dep"] = wq.finish()
            with contextlib.ExitStack() as pctx:
                saved = P.ctx
                P.ctx = pctx
                builder(P, h, bufs, cfg)
                P.ctx = saved
            P.phase_barrier()
        P.run()
    return nc


CFG_FULL = dict(T=2048, F=4096, NG=4, NE=9, I=2816)


def kernel(**inputs):
    maps = prep_host(inputs)
    cfg = CFG_FULL
    cores = list(range(NCORES))
    ncF = build_full(cfg)
    resF = run_bass_kernel_spmd(ncF, maps, core_ids=cores)
    return np.stack([np.asarray(r["out"], dtype=np.float32) for r in resF.results], axis=0)


def kernel_multi(**inputs):
    maps = prep_host(inputs)
    cfg = CFG_FULL
    cores = list(range(NCORES))

    ncA = build_A(cfg)
    inA = [dict(x=m["x"], w1all=m["w1all"], w2all=m["w2all"], b1dev=m["b1dev"],
                b2all=m["b2all"], wgdev=m["wgdev"], ident=m["ident"]) for m in maps]
    resA = run_bass_kernel_spmd(ncA, inA, core_ids=cores)
    g_all = [r["g_out"] for r in resA.results]

    ncB = build_B(cfg)
    inB = [dict(x=m["x"], g_in=g_all[b], wqT=m["wqT"], wkT=m["wkT"], wvT=m["wvT"],
                woT=m["woT"], gt=m["gt"], ident=m["ident"]) for b, m in enumerate(maps)]
    resB = run_bass_kernel_spmd(ncB, inB, core_ids=cores)
    x1_all = [r["x1_out"] for r in resB.results]

    ncC = build_C(cfg)
    inC = [dict(x1=x1_all[b], l1qT=m["l1qT"], l1kT=m["l1kT"], l1vT=m["l1vT"],
                l1oT=m["l1oT"], m01=m["m01"], ident=m["ident"]) for b, m in enumerate(maps)]
    resC = run_bass_kernel_spmd(ncC, inC, core_ids=cores)
    s1_all = [r["s1_out"] for r in resC.results]

    ncD = build_D(cfg)
    inD = [dict(x1=x1_all[b], s1=s1_all[b], wgate=m["wgate"], wup=m["wup"],
                wdown=m["wdown"], ident=m["ident"]) for b, m in enumerate(maps)]
    resD = run_bass_kernel_spmd(ncD, inD, core_ids=cores)
    out = np.stack([np.asarray(r["out"], dtype=np.float32) for r in resD.results], axis=0)
    return out


# version 33 (final)
# speedup vs baseline: 1.2269x; 1.0262x over previous
import contextlib
import math
import numpy as np
import ml_dtypes
import concourse.bass as bass
import concourse.mybir as mybir
from concourse.bass_utils import run_bass_kernel_spmd

F32 = mybir.dt.float32
BF16 = mybir.dt.bfloat16
AF = mybir.ActivationFunctionType
ALU = mybir.AluOpType
AX = mybir.AxisListType

NCORES = 8


def _rel_bucket(rel_pos, num_buckets=32, max_distance=128):
    rel_pos = np.asarray(rel_pos, dtype=np.int64)
    nb = num_buckets // 2
    buckets = (rel_pos > 0).astype(np.int64) * nb
    rp = np.abs(rel_pos)
    max_exact = nb // 2
    is_small = rp < max_exact
    rp_f = np.maximum(rp, 1).astype(np.float64)
    large = max_exact + (
        np.log(rp_f / max_exact) / math.log(max_distance / max_exact) * (nb - max_exact)
    ).astype(np.int64)
    large = np.minimum(large, nb - 1)
    return buckets + np.where(is_small, rp, large)


def _make_gt(rel_bias, S=2048):
    d = np.arange(-(S - 1), S)
    idx = _rel_bucket(d)
    table = rel_bias[idx, :]
    W = 2 * S - 128
    i = np.arange(128)[:, None]
    c = np.arange(W)[None, :]
    t_idx = (W - 1) + i - c
    GT = table[t_idx, :]
    return np.ascontiguousarray(GT.transpose(2, 0, 1)).astype(np.float32)


def _make_m01():
    deltas = [-256, -128, 0, 128, 256, 384]
    i = np.arange(128)[None, :, None]
    j = np.arange(512)[None, None, :]
    d = np.asarray(deltas)[:, None, None]
    qk = j - (d + i)
    return ((qk >= 0) & (qk <= 255)).astype(np.float32)


def _feat_tiles(w):
    K, N = w.shape
    return np.ascontiguousarray(w.reshape(K // 128, 128, N).transpose(1, 0, 2).reshape(128, -1))


def prep_host(inputs):
    f32 = np.float32
    x_all = np.ascontiguousarray(inputs["L0_attn_hidden_states"], dtype=f32)

    wqT = _feat_tiles(np.ascontiguousarray(inputs["L0_attn_wq"].T, dtype=f32))
    wkT = _feat_tiles(np.ascontiguousarray(inputs["L0_attn_wk"].T, dtype=f32))
    wvT = _feat_tiles(np.ascontiguousarray(inputs["L0_attn_wv"].T, dtype=f32))
    woT = _feat_tiles(np.ascontiguousarray(inputs["L0_attn_wo"].T, dtype=f32))
    gt = _make_gt(np.asarray(inputs["L0_attn_rel_bias"], dtype=f32))

    W1 = np.asarray(inputs["L0_ffn_W1"], dtype=f32)
    W2 = np.asarray(inputs["L0_ffn_W2"], dtype=f32)
    Ws1 = np.asarray(inputs["L0_ffn_Ws1"], dtype=f32)[None]
    Ws2 = np.asarray(inputs["L0_ffn_Ws2"], dtype=f32)[None]
    w1all = np.ascontiguousarray(np.concatenate([W1, Ws1], 0))
    w2all = np.ascontiguousarray(np.concatenate([W2, Ws2], 0))
    b1 = np.asarray(inputs["L0_ffn_b1"], dtype=f32)
    bs1 = np.asarray(inputs["L0_ffn_bs1"], dtype=f32)[None]
    b1all = np.concatenate([b1, bs1], 0)
    b1dev = np.ascontiguousarray(b1all.reshape(9, 32, 128).transpose(2, 1, 0).reshape(128, 32 * 9))
    b2 = np.asarray(inputs["L0_ffn_b2"], dtype=f32)
    bs2 = np.asarray(inputs["L0_ffn_bs2"], dtype=f32)[None]
    b2all = np.ascontiguousarray(np.concatenate([b2, bs2], 0))
    wg = np.asarray(inputs["L0_ffn_Wg"], dtype=f32)
    wgdev = _feat_tiles(wg)

    l1qT = _feat_tiles(np.ascontiguousarray(inputs["L1_attn_Wq"].T, dtype=f32))
    l1kT = _feat_tiles(np.ascontiguousarray(inputs["L1_attn_Wk"].T, dtype=f32))
    l1vT = _feat_tiles(np.ascontiguousarray(inputs["L1_attn_Wv"].T, dtype=f32))
    l1oT = _feat_tiles(np.ascontiguousarray(inputs["L1_attn_Wo"].T, dtype=f32))
    m01 = _make_m01()

    wgate = np.ascontiguousarray(inputs["L1_ffn_w_gate"], dtype=f32)
    wup = np.ascontiguousarray(inputs["L1_ffn_w_up"], dtype=f32)
    wdown = np.ascontiguousarray(inputs["L1_ffn_w_down"], dtype=f32)

    ident = np.eye(128, dtype=f32)

    shared = dict(
        wqT=wqT, wkT=wkT, wvT=wvT, woT=woT, gt=gt,
        w1all=w1all, w2all=w2all, b1dev=b1dev, b2all=b2all, wgdev=wgdev,
        l1qT=l1qT, l1kT=l1kT, l1vT=l1vT, l1oT=l1oT, m01=m01,
        wgate=wgate, wup=wup, wdown=wdown, ident=ident,
    )
    maps = []
    for b in range(NCORES):
        m = dict(shared)
        m["x"] = np.ascontiguousarray(x_all[b])
        maps.append(m)
    return maps



ENGS = ("tensor", "scalar", "vector", "gpsimd", "sync")


class Sem:
    __slots__ = ("h", "n", "name")

    def __init__(self, h, name):
        self.h = h
        self.n = 0
        self.name = name

    def tick(self, amt=1):
        self.n += amt
        return self.n


class Plan:
    def __init__(self, nc, ctx):
        self.nc = nc
        self.ctx = ctx
        self.items = {e: [] for e in ENGS}
        self.sems = {}
        self.watermark = {}
        self._deps = []
        self._deps_todo = set()

    def phase_barrier(self):
        self._deps = [(s, s.n) for s in self.sems.values() if s.n > 0]
        self._deps_todo = set(ENGS)

    def sem(self, name):
        h = self.ctx.enter_context(self.nc.semaphore(name))
        s = Sem(h, name)
        self.sems[name] = s
        return s

    def emit(self, eng, fn, wait=(), inc=(), is_dma=False):
        wait = list(wait)
        inc = list(inc)
        if eng in self._deps_todo:
            wait = self._deps + wait
            self._deps_todo.discard(eng)
        if eng in ("vector", "scalar") or (eng == "gpsimd" and not is_dma and "g" in self.sems):
            own = self.sems["v" if eng == "vector" else ("s" if eng == "scalar" else "g")]
            if own.n:
                wait.append((own, own.n))
            if not any(s is own for s, _ in inc):
                inc.append((own, 1))
        pruned = []
        for s, v in wait:
            assert v <= s.n, f"wait on {s.name}>={v} but only {s.n} planned (deadlock)"
            key = (eng, s.name)
            if self.watermark.get(key, -1) >= v:
                continue
            self.watermark[key] = v
            pruned.append((s, v))
        incs = []
        for s, amt in inc:
            s.tick(amt)
            incs.append((s, amt, s.n))
        self.items[eng].append((pruned, fn, incs))
        return [(s, n) for s, _a, n in incs]

    def run(self):
        block = self.ctx.enter_context(self.nc.Block())
        plans = self.items

        def mk(eng_name):
            def body(proxy):
                for waits, fn, incs in plans[eng_name]:
                    for s, v in waits:
                        proxy.wait_ge(s.h, v)
                    ins = fn(proxy)
                    if incs:
                        assert ins is not None, f"inc on None instr ({eng_name})"
                        for s, amt, _n in incs:
                            ins = ins.then_inc(s.h, amt)
            return body

        block.tensor(mk("tensor"))
        block.scalar(mk("scalar"))
        block.vector(mk("vector"))
        block.gpsimd(mk("gpsimd"))
        block.sync(mk("sync"))



class DmaStream:

    def __init__(self, P, name, eng):
        self.P = P
        self.eng = eng
        self.sem = P.sem(name)

    def load(self, dst, src, wait=(), transpose=False):
        self.P.emit(self.eng, lambda g, d=dst, s=src, tr=transpose: g.dma_start(d, s, transpose=tr),
                    wait=wait, inc=[(self.sem, 16)], is_dma=True)
        return self.sem.n

    def wv(self):
        return self.sem.n



def build_phase_A(P, hbm, bufs, cfg, fused_out=False):
    nc = P.nc
    T = cfg["T"]; NT = T // 128
    NDT = 8
    F = cfg["F"]; NFT = F // 128
    NG = cfg["NG"]; FLG = NFT // NG
    NE = cfg["NE"]
    TC = 256; NTC = T // TC
    ER = NE - 1
    GL = FLG * 128

    st = P.sems["t"]; ss = P.sems["s"]; sv = P.sems["v"]
    gc = DmaStream(P, "A_gc", "gpsimd")
    sx = [DmaStream(P, f"A_sx{i}", "sync") for i in range(2)]
    gw = [DmaStream(P, f"A_gw{i}", "gpsimd") for i in range(2)]
    so = [DmaStream(P, f"A_so{i}", "sync") for i in range(2)]

    if bufs.get("xnT") is None:
        bufs_xnT = P.ctx.enter_context(nc.sbuf_tensor("A_xnT", [128, 8 * T], BF16))
        bufs_gacc = P.ctx.enter_context(nc.sbuf_tensor("A_gacc", [128, NT * 1024], F32))
    else:
        bufs_xnT = bufs["xnT"]
        bufs_gacc = bufs["g_acc"]
    xnT = bufs_xnT
    g_acc = bufs_gacc
    g_tok = bufs.get("g_tok")
    ident = bufs["ident"]

    ctx = P.ctx
    b1dev = ctx.enter_context(nc.sbuf_tensor("A_b1dev", [128, (F // 128) * NE], F32))
    cw_sb = ctx.enter_context(nc.sbuf_tensor("A_cw", [128, NT * NE], F32))
    b2bf = ctx.enter_context(nc.sbuf_tensor("A_b2bf", [NE, 1024], BF16))
    wg_sb = ctx.enter_context(nc.sbuf_tensor("A_wg", [128, NDT * ER], F32))

    gc.load(b1dev[:, :], hbm["b1dev"][:, :])
    gc.load(b2bf[:, :], hbm["b2all"][:, :])
    gc.load(wg_sb[:, :], hbm["wgdev"][:, :])
    gc.load(ident[:, :], hbm["ident"][:, :])
    identf = P.ctx.enter_context(nc.sbuf_tensor("A_identf", [128, 128], F32))
    gc.load(identf[:, :], hbm["ident"][:, :])
    CW = gc.wv()
    P.emit("vector", lambda v: v.memset(cw_sb[:, :], 0.1))
    eps_col = P.ctx.enter_context(nc.sbuf_tensor("A_eps", [128, 1], F32))
    P.emit("vector", lambda v: v.memset(eps_col[:, :], 1e-5))
    eps_v = P.sems["v"].n

    lnctx = contextlib.ExitStack()
    ptr = [lnctx.enter_context(nc.psum_tensor(f"A_ptr{i}", [128, 128], F32)) for i in range(2)]
    p_r = lnctx.enter_context(nc.psum_tensor("A_pr", [128, ER], F32))
    p_cwT = lnctx.enter_context(nc.psum_tensor("A_pcwT", [NE, 128], BF16))
    p_b2 = lnctx.enter_context(nc.psum_tensor("A_pb2", [128, 1024], F32))
    x_t = [lnctx.enter_context(nc.sbuf_tensor(f"A_x{i}", [128, 1024], F32)) for i in range(2)]
    xsq = lnctx.enter_context(nc.sbuf_tensor("A_xsq", [128, 1024], F32))
    stat4 = lnctx.enter_context(nc.sbuf_tensor("A_stat", [128, 8], F32))
    xn_tok = lnctx.enter_context(nc.sbuf_tensor("A_xntok", [128, 1024], F32))
    xnf = [lnctx.enter_context(nc.sbuf_tensor(f"A_xnf{i}", [128, 128], F32)) for i in range(2)]
    cwT = lnctx.enter_context(nc.sbuf_tensor("A_cwT", [NE, NT * 128], BF16))
    r_sc = lnctx.enter_context(nc.sbuf_tensor("A_rsc", [128, 16], F32))
    probs = lnctx.enter_context(nc.sbuf_tensor("A_probs", [128, ER], F32))
    pm = lnctx.enter_context(nc.sbuf_tensor("A_pm", [128, ER], F32))
    msk = lnctx.enter_context(nc.sbuf_tensor("A_msk", [128, ER], F32))
    cwbf = lnctx.enter_context(nc.sbuf_tensor("A_cwbf", [128, NE], BF16))
    eps = 1e-5
    inv_d = 1.0 / 1024.0
    router_done = [0] * NT
    ginit_done = [0] * NT
    x_free = [0, 0]
    xsq_free = 0
    xn_tok_free = 0
    ptr_free = [0, 0]
    pr_free = 0
    xnf_free = [0, 0]
    pcwT_free = 0
    cwbf_free = 0
    pb2_free = 0

    for tt in range(NT):
        par = tt % 2
        wts = [(sv, x_free[par])] if x_free[par] else []
        xv = sx[par].load(x_t[par][:, :], hbm["x"][128 * tt:128 * (tt + 1), :], wait=wts)
        xw = (sx[par].sem, xv)
        P.emit("vector", lambda v, p=par: v.reduce_sum(stat4[:, 0:1], x_t[p][:, :], AX.X),
               wait=[xw])
        swts = [xw] + ([(sv, xsq_free)] if xsq_free else [])
        P.emit("scalar", lambda s, p=par: s.activation(xsq[:, :], x_t[p][:, :], AF.Square),
               wait=swts)
        sq_done = ss.n
        P.emit("vector", lambda v: v.reduce_sum(stat4[:, 1:2], xsq[:, :], AX.X),
               wait=[(ss, sq_done)])
        xsq_free = sv.n
        P.emit("vector", lambda v: v.tensor_scalar_mul(stat4[:, 2:3], stat4[:, 0:1], inv_d))
        P.emit("vector", lambda v: v.tensor_scalar_mul(stat4[:, 3:4], stat4[:, 1:2], inv_d))
        P.emit("vector", lambda v: v.scalar_tensor_tensor(
            stat4[:, 4:5], stat4[:, 2:3], -1.0, stat4[:, 2:3], ALU.mult, ALU.mult))
        P.emit("vector", lambda v: v.tensor_add(stat4[:, 5:6], stat4[:, 3:4], stat4[:, 4:5]))
        var_v = sv.n
        P.emit("scalar", lambda s: s.activation(stat4[:, 6:7], stat4[:, 5:6], AF.Sqrt, bias=eps_col[:, :]),
               wait=[(sv, var_v), (sv, eps_v)])
        sd_v = ss.n
        P.emit("vector", lambda v: v.reciprocal(stat4[:, 7:8], stat4[:, 6:7]),
               wait=[(ss, sd_v)])
        rstd_v = sv.n
        wts = [(sv, rstd_v)] + ([(st, xn_tok_free)] if xn_tok_free else [])
        P.emit("vector", lambda v, p=par: v.tensor_scalar(
            xn_tok[:, :], x_t[p][:, :], stat4[:, 2:3], stat4[:, 7:8], ALU.subtract, ALU.mult),
            wait=wts)
        xn_v = sv.n
        x_free[par] = xn_v
        for ft in range(NDT):
            tp = ft % 2
            wts = [(sv, xn_v), (gc.sem, CW)]
            if ptr_free[tp]:
                wts.append((sv, ptr_free[tp]))
            P.emit("tensor", lambda t, f=ft, p=tp: t.transpose(
                ptr[p][:, :], xn_tok[:, 128 * f:128 * (f + 1)], identf[:, :]),
                wait=wts, inc=[(st, 1)])
            tr_v = st.n
            P.emit("vector", lambda v, f=ft, p=tp, _tt=tt: v.tensor_copy(
                xnT[:, T * f + 128 * _tt: T * f + 128 * (_tt + 1)], ptr[p][:, :]),
                wait=[(st, tr_v)])
            wts = [(st, tr_v)]
            if xnf_free[tp]:
                wts.append((st, xnf_free[tp]))
            P.emit("vector", lambda v, p=tp: v.tensor_copy(xnf[p][:, :], ptr[p][:, :]),
                   wait=wts)
            ptr_free[tp] = sv.n
            xnf_v = sv.n
            P.emit("tensor", lambda t, f=ft, p=tp: t.matmul(
                p_r[:, :], xnf[p][:, :], wg_sb[:, ER * f: ER * (f + 1)],
                start=(f == 0), stop=(f == NDT - 1)),
                wait=[(sv, xnf_v)] + ([(ss, pr_free)] if (ft == 0 and pr_free) else []),
                inc=[(st, 1)])
            xnf_free[tp] = st.n
        xn_tok_free = st.n
        xn_copied = sv.n

        log_v = st.n
        P.emit("vector", lambda v: v.reduce_max(r_sc[:, 0:1], p_r[:, :], AX.X, negate=True),
               wait=[(st, log_v)])
        negm_v = sv.n
        P.emit("scalar", lambda s: s.activation(probs[:, :], p_r[:, :], AF.Exp, bias=r_sc[:, 0:1]),
               wait=[(sv, negm_v)])
        punn_v = ss.n
        pr_free = punn_v
        P.emit("vector", lambda v: v.reduce_sum(r_sc[:, 1:2], probs[:, :], AX.X),
               wait=[(ss, punn_v)])
        P.emit("vector", lambda v: v.reciprocal(r_sc[:, 2:3], r_sc[:, 1:2]))
        P.emit("vector", lambda v: v.tensor_scalar_mul(probs[:, :], probs[:, :], r_sc[:, 2:3]))
        P.emit("vector", lambda v: v.reduce_max(r_sc[:, 3:4], probs[:, :], AX.X))
        P.emit("vector", lambda v: v.tensor_scalar(msk[:, :], probs[:, :], r_sc[:, 3:4], None, ALU.is_ge))
        P.emit("vector", lambda v: v.tensor_mul(pm[:, :], probs[:, :], msk[:, :]))
        P.emit("vector", lambda v: v.tensor_sub(pm[:, :], probs[:, :], pm[:, :]))
        P.emit("vector", lambda v: v.reduce_max(r_sc[:, 4:5], pm[:, :], AX.X))
        P.emit("vector", lambda v: v.tensor_scalar(r_sc[:, 8:8 + ER], pm[:, :], r_sc[:, 4:5], None, ALU.is_ge))
        P.emit("vector", lambda v: v.tensor_add(msk[:, :], msk[:, :], r_sc[:, 8:8 + ER]))
        P.emit("vector", lambda v: v.tensor_mul(pm[:, :], probs[:, :], msk[:, :]))
        P.emit("vector", lambda v: v.reduce_sum(r_sc[:, 5:6], pm[:, :], AX.X))
        P.emit("vector", lambda v: v.tensor_scalar_add(r_sc[:, 5:6], r_sc[:, 5:6], 1e-9))
        P.emit("vector", lambda v: v.reciprocal(r_sc[:, 6:7], r_sc[:, 5:6]))
        P.emit("vector", lambda v, _tt=tt: v.tensor_scalar_mul(
            cw_sb[:, NE * _tt: NE * _tt + ER], pm[:, :], r_sc[:, 6:7]))
        router_done[tt] = sv.n
        wts = [(st, cwbf_free)] if cwbf_free else []
        P.emit("vector", lambda v, _tt=tt: v.tensor_copy(cwbf[:, :], cw_sb[:, NE * _tt: NE * (_tt + 1)]),
               wait=wts)
        cwbf_v = sv.n
        wts = [(sv, cwbf_v)]
        if pcwT_free:
            wts.append((sv, pcwT_free))
        P.emit("tensor", lambda t: t.transpose(p_cwT[:, :], cwbf[:, :], ident[:, :]),
               wait=wts, inc=[(st, 1)])
        cwt_v = st.n
        cwbf_free = cwt_v
        P.emit("vector", lambda v, _tt=tt: v.tensor_copy(cwT[:, 128 * _tt:128 * (_tt + 1)], p_cwT[:, :]),
               wait=[(st, cwt_v)])
        cwT_v = sv.n
        pcwT_free = cwT_v
        for hh in range(2):
            wts = [(sv, cwT_v), (gc.sem, CW)]
            if hh == 0 and pb2_free:
                wts.append((sv, pb2_free))
            P.emit("tensor", lambda t, _tt=tt, h=hh: t.matmul(
                p_b2[:, 512 * h:512 * (h + 1)], cwT[:, 128 * _tt:128 * (_tt + 1)],
                b2bf[:, 512 * h:512 * (h + 1)], start=True, stop=True),
                wait=wts, inc=[(st, 1)] if hh == 1 else [])
        b2_v = st.n
        P.emit("vector", lambda v, _tt=tt: v.tensor_copy(
            g_acc[:, 1024 * _tt:1024 * (_tt + 1)], p_b2[:, :]),
            wait=[(st, b2_v)])
        pb2_free = sv.n
        ginit_done[tt] = sv.n

    ln_all_v = sv.n
    lnctx.close()
    psumA = [ctx.enter_context(nc.psum_tensor(f"A_pA{i}", [128, 512], F32)) for i in range(2)]
    ypsum = [ctx.enter_context(nc.psum_tensor(f"A_yp{i}", [128, 1024], F32)) for i in range(2)]
    actTg = ctx.enter_context(nc.sbuf_tensor("A_actg", [128, FLG * 512], BF16))
    w1g = [ctx.enter_context(nc.sbuf_tensor(f"A_w1g{i}", [128, 8 * GL], BF16)) for i in range(2)]
    w2g = [ctx.enter_context(nc.sbuf_tensor(f"A_w2g{i}", [128, FLG * 1024], BF16)) for i in range(2)]

    TC2 = 512
    NTC2 = T // TC2
    w_ready = {}
    w_free = {0: 0, 1: 0}
    eg_list = [(e, g) for e in range(NE) for g in range(NG)]

    def emit_wload(i):
        e, g = eg_list[i]
        par = i % 2
        w1src = hbm["w1all"].rearrange("e (hc p) f -> e p hc f", p=128)[e, :, :, GL * g:GL * (g + 1)]
        w2src = hbm["w2all"].rearrange("e (ft p) h -> e p ft h", p=128)[e, :, FLG * g:FLG * (g + 1), :]
        wts = [(st, w_free[par])] if w_free[par] else []
        gw[par].load(w1g[par].rearrange("p (hc f) -> p hc f", hc=8), w1src, wait=wts)
        gw[par].load(w2g[par].rearrange("p (ft h) -> p ft h", ft=FLG), w2src)
        w_ready[i] = gw[par].wv()

    bar_eg = [(st, st.n), (ss, ss.n), (sv, sv.n)]
    w_free[0] = -1
    w_free[1] = -1
    _orig_emit_wload = emit_wload

    def emit_wload0(i):
        e, g = eg_list[i]
        par = i % 2
        w1src = hbm["w1all"].rearrange("e (hc p) f -> e p hc f", p=128)[e, :, :, GL * g:GL * (g + 1)]
        w2src = hbm["w2all"].rearrange("e (ft p) h -> e p ft h", p=128)[e, :, FLG * g:FLG * (g + 1), :]
        gw[par].load(w1g[par].rearrange("p (hc f) -> p hc f", hc=8), w1src, wait=list(bar_eg))
        gw[par].load(w2g[par].rearrange("p (ft h) -> p ft h", ft=FLG), w2src)
        w_ready[i] = gw[par].wv()

    emit_wload0(0)
    emit_wload0(1)
    w_free[0] = 0
    w_free[1] = 0

    psA_free = [0, 0]
    act_free = 0
    y_free = [0, 0]

    for i, (e, g) in enumerate(eg_list):
        par_w = i % 2
        wv = (gw[par_w].sem, w_ready[i])
        for tc in range(NTC2):
            silu_vals = [0] * FLG
            for fl in range(FLG):
                pa = fl % 2
                for hc in range(NDT):
                    wts = [wv]
                    if hc == 0:
                        wts.append((sv, ln_all_v))
                        if psA_free[pa]:
                            wts.append((ss, psA_free[pa]))
                    P.emit("tensor", lambda t, h=hc, f=fl, p=pa, w=par_w, _tc=tc: t.matmul(
                        psumA[p][:, :],
                        w1g[w][:, GL * h + 128 * f: GL * h + 128 * (f + 1)],
                        xnT[:, T * h + TC2 * _tc: T * h + TC2 * (_tc + 1)],
                        start=(h == 0), stop=(h == NDT - 1)),
                        wait=wts, inc=[(st, 1)] if hc == NDT - 1 else [])
                mmA_v = st.n
                ft_g = FLG * g + fl
                wts = [(st, mmA_v), (gc.sem, CW)]
                if act_free:
                    wts.append((st, act_free))
                P.emit("scalar", lambda s, p=pa, f=fl, c=(ft_g * NE + e): s.activation(
                    actTg[:, 512 * f:512 * (f + 1)], psumA[p][:, :], AF.Silu, bias=b1dev[:, c:c + 1]),
                    wait=wts)
                psA_free[pa] = ss.n
                silu_vals[fl] = ss.n
            for sub in range(2):
                for ts in range(2):
                    tt = (TC2 // 128) * tc + 2 * sub + ts
                    for fl in range(FLG):
                        wts = [(ss, silu_vals[fl])]
                        if sub == 0 and ts == 0 and fl == 0 and y_free[ts]:
                            wts += [(sv, y_free[0]), (sv, y_free[1])]
                        elif fl == 0 and y_free[ts]:
                            wts.append((sv, y_free[ts]))
                        for hh in range(2):
                            P.emit("tensor", lambda t, f=fl, _s=sub, _ts=ts, h=hh, w=par_w: t.matmul(
                                ypsum[_ts][:, 512 * h:512 * (h + 1)],
                                actTg[:, 512 * f + 256 * _s + 128 * _ts: 512 * f + 256 * _s + 128 * (_ts + 1)],
                                w2g[w][:, 1024 * f + 512 * h: 1024 * f + 512 * (h + 1)],
                                start=(f == 0), stop=(f == FLG - 1)),
                                wait=wts if hh == 0 else [],
                                inc=[(st, 1)] if (fl == FLG - 1 and hh == 1) else [])
                    mmB_v = st.n
                    wts = [(st, mmB_v), (sv, ginit_done[tt])]
                    if e == NE - 1:
                        fn = lambda v, _ts=ts, _tt=tt: v.scalar_tensor_tensor(
                            g_acc[:, 1024 * _tt:1024 * (_tt + 1)], ypsum[_ts][:, :], 0.1,
                            g_acc[:, 1024 * _tt:1024 * (_tt + 1)], ALU.mult, ALU.add)
                    else:
                        fn = lambda v, _ts=ts, _tt=tt, _e=e: v.scalar_tensor_tensor(
                            g_acc[:, 1024 * _tt:1024 * (_tt + 1)], ypsum[_ts][:, :],
                            cw_sb[:, NE * _tt + _e: NE * _tt + _e + 1],
                            g_acc[:, 1024 * _tt:1024 * (_tt + 1)], ALU.mult, ALU.add)
                    P.emit("vector", fn, wait=wts)
                    y_free[ts] = sv.n
            act_free = st.n
            fil = bufs.get("filler")
            if fil is not None:
                fil()
        if i + 2 < len(eg_list):
            w_free[par_w] = st.n
            emit_wload(i + 2)

    all_evac_v = sv.n
    g_done = []
    if fused_out:
        for tt in range(NT):
            P.emit("scalar", lambda s, _tt=tt: s.activation(
                g_tok[:, 1024 * _tt:1024 * (_tt + 1)], g_acc[:, 1024 * _tt:1024 * (_tt + 1)], AF.Silu),
                wait=[(sv, all_evac_v)])
            g_done.append(ss.n)
    else:
        gst0 = ctx.enter_context(nc.sbuf_tensor("A_gst0", [128, 1024], BF16))
        gfree = 0
        for tt in range(NT):
            wts = [(sv, all_evac_v)]
            if gfree:
                wts.append((so[0].sem, gfree))
            P.emit("scalar", lambda s, _tt=tt: s.activation(
                gst0[:, :], g_acc[:, 1024 * _tt:1024 * (_tt + 1)], AF.Silu),
                wait=wts)
            sd = ss.n
            gfree = so[0].load(hbm["g_out"][128 * tt:128 * (tt + 1), :],
                               gst0[:, :], wait=[(ss, sd)])
            g_done.append(sd)
        P.emit("sync", lambda s: None, wait=[(so[0].sem, so[0].wv())])
    return g_done




def emit_ln_to_xnT(P, hbm_x, xnT, identb, cfg, pfx, deps=()):
    nc = P.nc
    T = cfg["T"]; NT = T // 128; NDT = 8
    st = P.sems["t"]; ss = P.sems["s"]; sv = P.sems["v"]
    ctx = P.ctx
    sx = [DmaStream(P, f"{pfx}_sx{i}", "sync") for i in range(2)]
    x_t = [ctx.enter_context(nc.sbuf_tensor(f"{pfx}_x{i}", [128, 1024], F32)) for i in range(2)]
    xsq = ctx.enter_context(nc.sbuf_tensor(f"{pfx}_xsq", [128, 1024], F32))
    stat4 = ctx.enter_context(nc.sbuf_tensor(f"{pfx}_stat", [128, 8], F32))
    xn_tok = ctx.enter_context(nc.sbuf_tensor(f"{pfx}_xntok", [128, 1024], BF16))
    eps_col = ctx.enter_context(nc.sbuf_tensor(f"{pfx}_eps", [128, 1], F32))
    ptr = [ctx.enter_context(nc.psum_tensor(f"{pfx}_ptr{i}", [128, 128], BF16)) for i in range(2)]
    P.emit("vector", lambda v: v.memset(eps_col[:, :], 1e-5))
    eps_v = P.sems["v"].n
    x_free = [0, 0]
    xsq_free = 0
    xn_tok_free = 0
    ptr_free = [0, 0]
    for tt in range(NT):
        par = tt % 2
        wts = ([(sv, x_free[par])] if x_free[par] else []) + list(deps)
        xv = sx[par].load(x_t[par][:, :], hbm_x[128 * tt:128 * (tt + 1), :], wait=wts)
        xw = (sx[par].sem, xv)
        P.emit("vector", lambda v, p=par: v.reduce_sum(stat4[:, 0:1], x_t[p][:, :], AX.X),
               wait=[xw])
        swts = [xw] + ([(sv, xsq_free)] if xsq_free else [])
        P.emit("scalar", lambda s, p=par: s.activation(xsq[:, :], x_t[p][:, :], AF.Square),
               wait=swts)
        sq_done = ss.n
        P.emit("vector", lambda v: v.reduce_sum(stat4[:, 1:2], xsq[:, :], AX.X),
               wait=[(ss, sq_done)])
        xsq_free = sv.n
        P.emit("vector", lambda v: v.tensor_scalar_mul(stat4[:, 2:3], stat4[:, 0:1], 1.0 / 1024.0))
        P.emit("vector", lambda v: v.tensor_scalar_mul(stat4[:, 3:4], stat4[:, 1:2], 1.0 / 1024.0))
        P.emit("vector", lambda v: v.scalar_tensor_tensor(
            stat4[:, 4:5], stat4[:, 2:3], -1.0, stat4[:, 2:3], ALU.mult, ALU.mult))
        P.emit("vector", lambda v: v.tensor_add(stat4[:, 5:6], stat4[:, 3:4], stat4[:, 4:5]))
        var_v = sv.n
        P.emit("scalar", lambda s: s.activation(stat4[:, 6:7], stat4[:, 5:6], AF.Sqrt, bias=eps_col[:, :]),
               wait=[(sv, var_v), (sv, eps_v)])
        sd_v = ss.n
        P.emit("vector", lambda v: v.reciprocal(stat4[:, 7:8], stat4[:, 6:7]),
               wait=[(ss, sd_v)])
        rstd_v = sv.n
        wts = [(sv, rstd_v)] + ([(st, xn_tok_free)] if xn_tok_free else [])
        P.emit("vector", lambda v, p=par: v.tensor_scalar(
            xn_tok[:, :], x_t[p][:, :], stat4[:, 2:3], stat4[:, 7:8], ALU.subtract, ALU.mult),
            wait=wts)
        xn_v = sv.n
        x_free[par] = xn_v
        for ft in range(NDT):
            tp = ft % 2
            wts = [(sv, xn_v)]
            if ptr_free[tp]:
                wts.append((sv, ptr_free[tp]))
            P.emit("tensor", lambda t, f=ft, p=tp: t.transpose(
                ptr[p][:, :], xn_tok[:, 128 * f:128 * (f + 1)], identb[:, :]),
                wait=wts, inc=[(st, 1)])
            tr_v = st.n
            P.emit("vector", lambda v, f=ft, p=tp, _tt=tt: v.tensor_copy(
                xnT[:, T * f + 128 * _tt: T * f + 128 * (_tt + 1)], ptr[p][:, :]),
                wait=[(st, tr_v)])
            ptr_free[tp] = sv.n
        xn_tok_free = st.n
    return P.sems["v"].n


def build_phase_B(P, hbm, bufs, cfg):
    nc = P.nc
    T = cfg["T"]; NT = T // 128; NDT = 8
    NH = 16
    QC = 512; NQC = T // QC
    NKT = T // 128
    W_GT = 2 * T - 128
    C0 = T - 128
    st = P.sems["t"]; ss = P.sems["s"]; sv = P.sems["v"]
    ctx = P.ctx

    identb = bufs["ident"]
    gc = DmaStream(P, "B_gc", "gpsimd")
    gwq = [DmaStream(P, f"B_gw{i}", "gpsimd") for i in range(2)]
    ggt = [DmaStream(P, f"B_ggt{i}", "gpsimd") for i in range(2)]
    sxc = [DmaStream(P, f"B_sxc{i}", "sync") for i in range(2)]
    sg = [DmaStream(P, f"B_sg{i}", "sync") for i in range(2)]
    so = [DmaStream(P, f"B_so{i}", "gpsimd") for i in range(2)]

    QT = ctx.enter_context(nc.sbuf_tensor("B_QT", [128, 8 * T], BF16))
    KT = ctx.enter_context(nc.sbuf_tensor("B_KT", [128, 8 * T], BF16))
    VTM = ctx.enter_context(nc.sbuf_tensor("B_VTM", [128, NKT * (NH * 65)], BF16))
    g_sb = bufs.get("g_tok")
    ones1 = ctx.enter_context(nc.sbuf_tensor("B_ones1", [1, 64], F32))

    gc.load(identb[:, :], hbm["ident"][:, :])
    CW = gc.wv()
    P.emit("vector", lambda v: v.memset(VTM[:, :], 1.0))
    vtm_init = sv.n
    P.emit("vector", lambda v: v.memset(ones1[:, :], 1.0))
    ones_v = sv.n

    with contextlib.ExitStack() as lctx:
        Pctx_saved = P.ctx
        P.ctx = lctx
        xnT = lctx.enter_context(nc.sbuf_tensor("B_xnT", [128, 8 * T], BF16))
        wbuf = [lctx.enter_context(nc.sbuf_tensor(f"B_w{i}", [128, 8 * 1024], BF16)) for i in range(2)]
        xn_done = emit_ln_to_xnT(P, hbm["x"], xnT, identb, cfg, "B_ln", deps=[(gc.sem, CW)])
        P.ctx = Pctx_saved

        names = ["wqT", "wkT", "wvT"]
        wv_ld = [0, 0, 0]
        wproj_done = [0, 0, 0]
        wv_ld[0] = gwq[0].load(wbuf[0][:, :], hbm[names[0]][:, :])
        wv_ld[1] = gwq[1].load(wbuf[1][:, :], hbm[names[1]][:, :])
        with contextlib.ExitStack() as pctx:
            pp = [pctx.enter_context(nc.psum_tensor(f"B_pp{i}", [128, 512], F32)) for i in range(2)]
            ptr2 = [pctx.enter_context(nc.psum_tensor(f"B_ptr2{i}", [128, 128], BF16)) for i in range(2)]
            vstage = pctx.enter_context(nc.sbuf_tensor("B_vstage", [128, 512], BF16))
            pp_free = [0, 0]
            ptr2_free = [0, 0]
            vstage_free = 0
            for wi, nm in enumerate(names):
                par = wi % 2
                if wi >= 2:
                    wv_ld[wi] = gwq[par].load(wbuf[par][:, :], hbm[names[wi]][:, :],
                                              wait=[(st, wproj_done[wi - 2])])
                for et in range(8):
                    for tcq in range(T // 512):
                        pa = (et * (T // 512) + tcq) % 2
                        for dc in range(8):
                            wts = [(gwq[par].sem, wv_ld[wi]), (sv, xn_done)]
                            if dc == 0 and pp_free[pa]:
                                wts.append((sv, pp_free[pa]))
                            P.emit("tensor", lambda t, d=dc, e=et, q=tcq, p=pa, w=par: t.matmul(
                                pp[p][:, :],
                                wbuf[w][:, 1024 * d + 128 * e: 1024 * d + 128 * (e + 1)],
                                xnT[:, T * d + 512 * q: T * d + 512 * (q + 1)],
                                start=(d == 0), stop=(d == 7)),
                                wait=wts, inc=[(st, 1)] if dc == 7 else [])
                        mm_v = st.n
                        if nm == "wqT":
                            P.emit("vector", lambda v, e=et, q=tcq, p=pa: v.tensor_copy(
                                QT[:, T * e + 512 * q: T * e + 512 * (q + 1)], pp[p][:, :]),
                                wait=[(st, mm_v)])
                            pp_free[pa] = sv.n
                        elif nm == "wkT":
                            P.emit("vector", lambda v, e=et, q=tcq, p=pa: v.tensor_copy(
                                KT[:, T * e + 512 * q: T * e + 512 * (q + 1)], pp[p][:, :]),
                                wait=[(st, mm_v)])
                            pp_free[pa] = sv.n
                        else:
                            wts = [(st, mm_v)]
                            if vstage_free:
                                wts.append((st, vstage_free))
                            P.emit("vector", lambda v, p=pa: v.tensor_copy(vstage[:, :], pp[p][:, :]),
                                   wait=wts)
                            pp_free[pa] = sv.n
                            vst_v = sv.n
                            for j in range(4):
                                tp2 = j % 2
                                wts = [(sv, vst_v), (gc.sem, CW)]
                                if ptr2_free[tp2]:
                                    wts.append((sv, ptr2_free[tp2]))
                                P.emit("tensor", lambda t, _j=j, p=tp2: t.transpose(
                                    ptr2[p][:, :], vstage[:, 128 * _j:128 * (_j + 1)], identb[:, :]),
                                    wait=wts, inc=[(st, 1)])
                                tr_v = st.n
                                kt_idx = tcq * 4 + j
                                for half in range(2):
                                    h_idx = 2 * et + half
                                    P.emit("vector", lambda v, k=kt_idx, h=h_idx, p=tp2, hf=half: v.tensor_copy(
                                        VTM[:, 65 * NH * k + 65 * h: 65 * NH * k + 65 * h + 64],
                                        ptr2[p][:, 64 * hf:64 * (hf + 1)]),
                                        wait=[(st, tr_v), (sv, vtm_init)])
                                ptr2_free[tp2] = sv.n
                            vstage_free = st.n
                wproj_done[wi] = st.n
        proj_evac_v = P.sems["v"].n

    aoT = ctx.enter_context(nc.sbuf_tensor("B_aoT", [128, 8 * T], BF16))

    with contextlib.ExitStack() as actx:
        gtst = [actx.enter_context(nc.sbuf_tensor(f"B_gt{i}", [128, W_GT], BF16)) for i in range(2)]
        expg = [actx.enter_context(nc.sbuf_tensor(f"B_eg{i}", [128, W_GT], BF16)) for i in range(2)]
        pstage = [actx.enter_context(nc.sbuf_tensor(f"B_pst{i}", [128, 1024], BF16)) for i in range(2)]
        neg30 = actx.enter_context(nc.sbuf_tensor("B_n30", [128, 1], F32))
        P.emit("vector", lambda v: v.memset(neg30[:, :], -30.0))
        n30_v = P.sems["v"].n
        pm_sb = [actx.enter_context(nc.sbuf_tensor(f"B_pm{i}", [128, 1024], BF16)) for i in range(3)]
        recip_sb = actx.enter_context(nc.sbuf_tensor("B_rcp", [1, 512], F32))
        lnr_sb = actx.enter_context(nc.sbuf_tensor("B_lnr", [1, 512], F32))
        ao_st = actx.enter_context(nc.sbuf_tensor("B_aost", [64, 512], BF16))
        S2 = [actx.enter_context(nc.psum_tensor(f"B_S2{i}", [128, 1024], F32)) for i in range(2)]
        pv = [actx.enter_context(nc.psum_tensor(f"B_pv{i}", [65, 512], F32)) for i in range(2)]
        rb = actx.enter_context(nc.psum_tensor("B_rb", [64, 512], F32))

        gt_ready = [0, 0]
        gt_free = [0, 0]
        bar_att = [(st, st.n), (ss, ss.n), (sv, sv.n)]

        def load_gt(h):
            par = h % 2
            wts = list(bar_att) + ([(ss, gt_free[par])] if gt_free[par] else [])
            ggt[par].load(gtst[par][:, :], hbm["gt"][h, :, :], wait=wts)
            gt_ready[par] = ggt[par].wv()

        load_gt(0)
        load_gt(1)

        expg_free = [0, 0]
        S2_free = [0, 0]
        pst_free = [0, 0]

        pm_free = [0, 0, 0]
        pv_free = 0
        pv_freed = [0, 0]
        rcp_free = 0
        ao_free = 0
        rb_free = 0

        for h in range(NH):
            par = h % 2
            wts = [(ggt[par].sem, gt_ready[par])]
            if expg_free[par]:
                wts.append((sv, expg_free[par]))
            P.emit("scalar", lambda s, p=par: s.activation(expg[p][:, :], gtst[p][:, :], AF.Exp),
                   wait=wts)
            gt_free[par] = ss.n
            expg_v = ss.n
            if h + 2 < NH:
                load_gt(h + 2)
            hb = 64 * (h % 2)
            et = h // 2
            for qc in range(NQC):
                pvp = qc % 2
                mult_vals = [0] * (NKT // 2)

                def emit_pv(ktp, _h=h, _pvp=None):
                    p_ = pvp
                    for u in range(2):
                        kt = 2 * ktp + u
                        wts = [(sv, mult_vals[ktp])]
                        if kt == 0:
                            wts.append((sv, vtm_init))
                            if pv_freed[p_]:
                                wts.append((sv, pv_freed[p_]))
                        P.emit("tensor", lambda t, k=kt, _u=u, p=p_, __h=_h, mp=ktp % 3: t.matmul(
                            pv[p][:, :],
                            VTM[:, 65 * NH * k + 65 * __h: 65 * NH * k + 65 * __h + 65],
                            pm_sb[mp][:, 512 * _u:512 * (_u + 1)],
                            start=(k == 0), stop=(k == NKT - 1)),
                            wait=wts, inc=[(st, 1)] if kt == NKT - 1 else [])
                        pm_free[ktp % 3] = st.n

                for ktp in range(NKT // 2):
                    sp = ktp % 2
                    for u in range(2):
                        kt = 2 * ktp + u
                        wts = []
                        if u == 0 and S2_free[sp]:
                            wts.append((ss, S2_free[sp]))
                        if h == 0 and qc == 0 and ktp == 0 and u == 0:
                            wts.append((st, wproj_done[2]))
                            wts.append((sv, proj_evac_v))
                        P.emit("tensor", lambda t, k=kt, _u=u, p=sp, _qc=qc, _hb=hb, _et=et: t.matmul(
                            S2[p][:, 512 * _u:512 * (_u + 1)],
                            KT[_hb:_hb + 64, T * _et + 128 * k: T * _et + 128 * (k + 1)],
                            QT[_hb:_hb + 64, T * _et + 512 * _qc: T * _et + 512 * (_qc + 1)],
                            start=True, stop=True),
                            wait=wts, inc=[(st, 1)] if u == 1 else [])
                    sc_v = st.n
                    mp = ktp % 3
                    wts = [(st, sc_v), (ss, expg_v)]
                    if pst_free[mp % 2]:
                        wts.append((sv, pst_free[mp % 2]))
                    P.emit("scalar", lambda s, p=sp, _mp=mp % 2: s.activation(
                        pstage[_mp][:, :], S2[p][:, :], AF.Exp, bias=neg30[:, :]),
                        wait=wts + [(sv, n30_v)])
                    S2_free[sp] = ss.n
                    exp_v = ss.n
                    if ktp >= 3:
                        emit_pv(ktp - 3)
                    for u in range(2):
                        kt = 2 * ktp + u
                        off = C0 + 512 * qc - 128 * kt
                        wts = [(ss, exp_v)]
                        if pm_free[mp]:
                            wts.append((st, pm_free[mp]))
                        P.emit("vector", lambda v, _u=u, p=par, o=off, _mp=mp, _ps=mp % 2: v.tensor_mul(
                            pm_sb[_mp][:, 512 * _u:512 * (_u + 1)],
                            pstage[_ps][:, 512 * _u:512 * (_u + 1)],
                            expg[p][:, o:o + 512]),
                            wait=wts)
                    pst_free[mp % 2] = sv.n
                    mult_vals[ktp] = sv.n
                for _k in range(min(3, NKT // 2), 0, -1):
                    emit_pv(NKT // 2 - _k)
                pv_v = st.n
                pvp = qc % 2
                wts = [(st, pv_v)]
                if rcp_free:
                    wts.append((st, rcp_free))
                P.emit("scalar", lambda s, p=pvp: s.activation(lnr_sb[:, :], pv[p][64:65, :], AF.Ln),
                       wait=wts)
                P.emit("scalar", lambda s: s.activation(recip_sb[:, :], lnr_sb[:, :], AF.Exp, scale=-1.0))
                rcp_v = ss.n
                wts = [(st, pv_v)]
                if ao_free:
                    wts.append((sv, ao_free))
                P.emit("scalar", lambda s, p=pvp: s.copy(ao_st[:, :], pv[p][0:64, :]),
                       wait=wts)
                ao_v = ss.n
                wts = [(ss, rcp_v), (sv, ones_v)]
                if rb_free:
                    wts.append((sv, rb_free))
                P.emit("tensor", lambda t: t.matmul(
                    rb[:, :], ones1[:, :], recip_sb[:, :], start=True, stop=True),
                    wait=wts, inc=[(st, 1)])
                rcp_free = st.n
                rb_v = st.n
                P.emit("vector", lambda v, _qc=qc, _hb=hb, _et=et: v.tensor_mul(
                    aoT[_hb:_hb + 64, T * _et + 512 * _qc: T * _et + 512 * (_qc + 1)],
                    ao_st[:, :], rb[:, :]),
                    wait=[(st, rb_v), (ss, ao_v)])
                ao_free = sv.n
                rb_free = sv.n
                pv_freed[pvp] = sv.n
            expg_free[par] = sv.n
            fil = bufs.get("filler")
            if fil is not None:
                fil()

    with contextlib.ExitStack() as octx:
        po = [octx.enter_context(nc.psum_tensor(f"B_po{i}", [128, 512], F32)) for i in range(2)]
        ptr3 = [octx.enter_context(nc.psum_tensor(f"B_ptr3{i}", [128, 128], BF16)) for i in range(2)]
        ostage = octx.enter_context(nc.sbuf_tensor("B_ost", [128, 512], BF16))
        stok = octx.enter_context(nc.sbuf_tensor("B_stok", [128, 4 * 1024], BF16))
        x_in = [octx.enter_context(nc.sbuf_tensor(f"B_xi{i}", [128, 1024], F32)) for i in range(2)]
        if g_sb is None:
            g_st = [octx.enter_context(nc.sbuf_tensor(f"B_gst{i}", [128, 1024], BF16)) for i in range(2)]
        else:
            g_st = None
        x1st = [octx.enter_context(nc.sbuf_tensor(f"B_x1{i}", [128, 1024], F32)) for i in range(2)]
        wobuf = octx.enter_context(nc.sbuf_tensor("B_wo", [128, 8 * 1024], BF16))
        bar_o = [(st, st.n), (ss, ss.n), (sv, sv.n)]
        wo_v = gwq[0].load(wobuf[:, :], hbm["woT"][:, :],
                           wait=list(bar_o))
        ao_all = P.sems["v"].n
        po_free = [0, 0]
        ptr3_free = [0, 0]
        ost_free = 0
        stok_free = 0
        xi_free = [0, 0]
        x1_free = [0, 0]
        for qc in range(NQC):
            for et in range(8):
                pa = et % 2
                for dc in range(8):
                    wts = [(gwq[0].sem, wo_v), (sv, ao_all), (ss, bar_o[1][1])]
                    if dc == 0 and po_free[pa]:
                        wts.append((ss, po_free[pa]))
                    P.emit("tensor", lambda t, d=dc, e=et, p=pa, _qc=qc: t.matmul(
                        po[p][:, :],
                        wobuf[:, 1024 * d + 128 * e: 1024 * d + 128 * (e + 1)],
                        aoT[:, T * d + 512 * _qc: T * d + 512 * (_qc + 1)],
                        start=(d == 0), stop=(d == 7)),
                        wait=wts, inc=[(st, 1)] if dc == 7 else [])
                mm_v = st.n
                wts = [(st, mm_v)]
                if ost_free:
                    wts.append((st, ost_free))
                P.emit("scalar", lambda s, p=pa: s.copy(ostage[:, :], po[p][:, :]),
                       wait=wts)
                po_free[pa] = ss.n
                ost_v = ss.n
                for j in range(4):
                    tp3 = j % 2
                    wts = [(ss, ost_v), (gc.sem, CW)]
                    if ptr3_free[tp3]:
                        wts.append((sv, ptr3_free[tp3]))
                    P.emit("tensor", lambda t, _j=j, p=tp3: t.transpose(
                        ptr3[p][:, :], ostage[:, 128 * _j:128 * (_j + 1)], identb[:, :]),
                        wait=wts, inc=[(st, 1)])
                    tr_v = st.n
                    wts = [(st, tr_v)]
                    P.emit("vector", lambda v, _j=j, e=et, p=tp3: v.tensor_copy(
                        stok[:, 1024 * _j + 128 * e: 1024 * _j + 128 * (e + 1)], ptr3[p][:, :]),
                        wait=wts)
                    ptr3_free[tp3] = sv.n
                ost_free = st.n
            stok_v = sv.n
            for j in range(4):
                tt = 4 * qc + j
                par = j % 2
                wts = list(bar_o) + ([(sv, xi_free[par])] if xi_free[par] else [])
                xv = sxc[par].load(x_in[par][:, :], hbm["x"][128 * tt:128 * (tt + 1), :], wait=wts)
                if g_st is not None:
                    gv = sg[par].load(g_st[par][:, :], hbm["g_in"][128 * tt:128 * (tt + 1), :],
                                      wait=wts)
                    g_dep = [(sg[par].sem, gv)]
                    g_ap = g_st[par]
                    g_lo = 0
                else:
                    g_dep = [(P.sems["s"], bufs["g_ready"])]
                    g_ap = g_sb
                    g_lo = 1024 * tt
                wts = [(sv, stok_v), (sxc[par].sem, xv)] + g_dep
                if x1_free[par]:
                    wts.append((so[par].sem, x1_free[par]))
                P.emit("vector", lambda v, _j=j, p=par, ga=g_ap, lo=g_lo: v.tensor_mul(
                    x1st[p][:, :], stok[:, 1024 * _j:1024 * (_j + 1)],
                    ga[:, lo:lo + 1024]),
                    wait=wts)
                P.emit("vector", lambda v, p=par: v.tensor_add(x1st[p][:, :], x1st[p][:, :], x_in[p][:, :]))
                x1_v = sv.n
                xi_free[par] = x1_v
                x1w = so[par].load(hbm["x1_out"][128 * tt:128 * (tt + 1), :], x1st[par][:, :],
                                   wait=[(sv, x1_v)])
                x1_free[par] = x1w
            stok_free = sv.n
        P.emit("sync", lambda s: None, wait=[(so[0].sem, so[0].wv()), (so[1].sem, so[1].wv())])
    return


def build_B(cfg):
    nc = bass.Bass("TRN2", target_bir_lowering=False)
    T = cfg["T"]
    hbm = {}
    hbm["x"] = nc.dram_tensor("x", [T, 1024], F32, kind="ExternalInput").ap()
    hbm["g_in"] = nc.dram_tensor("g_in", [T, 1024], BF16, kind="ExternalInput").ap()
    for nm in ("wqT", "wkT", "wvT", "woT"):
        hbm[nm] = nc.dram_tensor(nm, [128, 8 * 1024], F32, kind="ExternalInput").ap()
    hbm["gt"] = nc.dram_tensor("gt", [16, 128, 2 * T - 128], F32, kind="ExternalInput").ap()
    hbm["ident"] = nc.dram_tensor("ident", [128, 128], F32, kind="ExternalInput").ap()
    hbm["x1_out"] = nc.dram_tensor("x1_out", [T, 1024], F32, kind="ExternalOutput").ap()
    with contextlib.ExitStack() as ctx:
        P = Plan(nc, ctx)
        for s in ("t", "s", "v", "g"):
            P.sem(s)
        bufs = {}
        bufs["ident"] = ctx.enter_context(nc.sbuf_tensor("identb", [128, 128], BF16))
        build_phase_B(P, hbm, bufs, cfg)
        P.run()
    return nc




def build_phase_C(P, hbm, bufs, cfg):
    nc = P.nc
    T = cfg["T"]; NT = T // 128; NDT = 8
    NH = 16
    QC = 512; NQC = T // QC
    NKT = T // 128
    WIN_DELTAS = [-256, -128, 0, 128, 256, 384]
    st = P.sems["t"]; ss = P.sems["s"]; sv = P.sems["v"]
    ctx = P.ctx

    identb = bufs["ident"]
    gc = DmaStream(P, "C_gc", "gpsimd")
    gwq = [DmaStream(P, f"C_gw{i}", "gpsimd") for i in range(2)]
    so = [DmaStream(P, f"C_so{i}", "gpsimd") for i in range(2)]

    QT = ctx.enter_context(nc.sbuf_tensor("C_QT", [128, 8 * T], BF16))
    KT = ctx.enter_context(nc.sbuf_tensor("C_KT", [128, 8 * T], BF16))
    VTM = ctx.enter_context(nc.sbuf_tensor("C_VTM", [128, NKT * (NH * 65)], BF16))
    ones1 = ctx.enter_context(nc.sbuf_tensor("C_ones1", [1, 64], F32))

    gc.load(identb[:, :], hbm["ident"][:, :])
    CW = gc.wv()
    P.emit("vector", lambda v: v.memset(VTM[:, :], 1.0))
    vtm_init = sv.n
    P.emit("vector", lambda v: v.memset(ones1[:, :], 1.0))
    ones_v = sv.n

    with contextlib.ExitStack() as lctx:
        Pctx_saved = P.ctx
        P.ctx = lctx
        xnT = lctx.enter_context(nc.sbuf_tensor("C_xnT", [128, 8 * T], BF16))
        wbuf = [lctx.enter_context(nc.sbuf_tensor(f"C_w{i}", [128, 8 * 1024], BF16)) for i in range(2)]
        xn_done = emit_ln_to_xnT(P, hbm["x1"], xnT, identb, cfg, "C_ln", deps=[(gc.sem, CW)])
        P.ctx = Pctx_saved

        names = ["l1qT", "l1kT", "l1vT"]
        wv_ld = [0, 0, 0]
        wproj_done = [0, 0, 0]
        wv_ld[0] = gwq[0].load(wbuf[0][:, :], hbm[names[0]][:, :])
        wv_ld[1] = gwq[1].load(wbuf[1][:, :], hbm[names[1]][:, :])
        with contextlib.ExitStack() as pctx:
            pp = [pctx.enter_context(nc.psum_tensor(f"C_pp{i}", [128, 512], F32)) for i in range(2)]
            ptr2 = [pctx.enter_context(nc.psum_tensor(f"C_ptr2{i}", [128, 128], BF16)) for i in range(2)]
            vstage = pctx.enter_context(nc.sbuf_tensor("C_vstage", [128, 512], BF16))
            pp_free = [0, 0]
            ptr2_free = [0, 0]
            vstage_free = 0
            for wi, nm in enumerate(names):
                par = wi % 2
                if wi >= 2:
                    wv_ld[wi] = gwq[par].load(wbuf[par][:, :], hbm[names[wi]][:, :],
                                              wait=[(st, wproj_done[wi - 2])])
                for et in range(8):
                    for tcq in range(T // 512):
                        pa = (et * (T // 512) + tcq) % 2
                        for dc in range(8):
                            wts = [(gwq[par].sem, wv_ld[wi]), (sv, xn_done)]
                            if dc == 0 and pp_free[pa]:
                                wts.append((sv, pp_free[pa]))
                            P.emit("tensor", lambda t, d=dc, e=et, q=tcq, p=pa, w=par: t.matmul(
                                pp[p][:, :],
                                wbuf[w][:, 1024 * d + 128 * e: 1024 * d + 128 * (e + 1)],
                                xnT[:, T * d + 512 * q: T * d + 512 * (q + 1)],
                                start=(d == 0), stop=(d == 7)),
                                wait=wts, inc=[(st, 1)] if dc == 7 else [])
                        mm_v = st.n
                        if nm == "l1qT":
                            P.emit("vector", lambda v, e=et, q=tcq, p=pa: v.tensor_copy(
                                QT[:, T * e + 512 * q: T * e + 512 * (q + 1)], pp[p][:, :]),
                                wait=[(st, mm_v)])
                            pp_free[pa] = sv.n
                        elif nm == "l1kT":
                            P.emit("vector", lambda v, e=et, q=tcq, p=pa: v.tensor_copy(
                                KT[:, T * e + 512 * q: T * e + 512 * (q + 1)], pp[p][:, :]),
                                wait=[(st, mm_v)])
                            pp_free[pa] = sv.n
                        else:
                            wts = [(st, mm_v)]
                            if vstage_free:
                                wts.append((st, vstage_free))
                            P.emit("vector", lambda v, p=pa: v.tensor_copy(vstage[:, :], pp[p][:, :]),
                                   wait=wts)
                            pp_free[pa] = sv.n
                            vst_v = sv.n
                            for j in range(4):
                                tp2 = j % 2
                                wts = [(sv, vst_v), (gc.sem, CW)]
                                if ptr2_free[tp2]:
                                    wts.append((sv, ptr2_free[tp2]))
                                P.emit("tensor", lambda t, _j=j, p=tp2: t.transpose(
                                    ptr2[p][:, :], vstage[:, 128 * _j:128 * (_j + 1)], identb[:, :]),
                                    wait=wts, inc=[(st, 1)])
                                tr_v = st.n
                                kt_idx = tcq * 4 + j
                                for half in range(2):
                                    h_idx = 2 * et + half
                                    P.emit("vector", lambda v, k=kt_idx, h=h_idx, p=tp2, hf=half: v.tensor_copy(
                                        VTM[:, 65 * NH * k + 65 * h: 65 * NH * k + 65 * h + 64],
                                        ptr2[p][:, 64 * hf:64 * (hf + 1)]),
                                        wait=[(st, tr_v), (sv, vtm_init)])
                                ptr2_free[tp2] = sv.n
                            vstage_free = st.n
                wproj_done[wi] = st.n
        proj_evac_v = P.sems["v"].n

    aoT = ctx.enter_context(nc.sbuf_tensor("C_aoT", [128, 8 * T], BF16))

    with contextlib.ExitStack() as actx:
        m01 = actx.enter_context(nc.sbuf_tensor("C_m01", [128, 6 * 512], BF16))
        gc.load(m01.rearrange("p (d n) -> p d n", d=6), hbm["m01"][:, :, :].rearrange("d p n -> p d n"),
                wait=[(st, st.n), (ss, ss.n), (sv, sv.n)])
        CWM = gc.wv()
        pstage = [actx.enter_context(nc.sbuf_tensor(f"C_pst{i}", [128, 1024], BF16)) for i in range(2)]
        pm_sb = [actx.enter_context(nc.sbuf_tensor(f"C_pm{i}", [128, 1024], BF16)) for i in range(4)]
        recip_sb = actx.enter_context(nc.sbuf_tensor("C_rcp", [1, 512], F32))
        lnr_sb = actx.enter_context(nc.sbuf_tensor("C_lnr", [1, 512], F32))
        ao_st = actx.enter_context(nc.sbuf_tensor("C_aost", [64, 512], BF16))
        S2 = [actx.enter_context(nc.psum_tensor(f"C_S2{i}", [128, 1024], F32)) for i in range(2)]
        pv = [actx.enter_context(nc.psum_tensor(f"C_pv{i}", [65, 512], F32)) for i in range(2)]
        rb = actx.enter_context(nc.psum_tensor("C_rb", [64, 512], F32))
        bar_att = [(st, st.n), (ss, ss.n), (sv, sv.n)]

        S2_free = [0, 0]
        pst_free = [0, 0]
        pm_free = [0, 0, 0, 0]
        pv_freed = [0, 0]
        rcp_free = 0
        ao_free = 0
        rb_free = 0

        for h in range(NH):
            hb = 64 * (h % 2)
            et = h // 2
            for qc in range(NQC):
                kts = [(di, (512 * qc + d) // 128) for di, d in enumerate(WIN_DELTAS)
                       if 0 <= 512 * qc + d < T]
                npair = (len(kts) + 1) // 2
                pvp = qc % 2
                mult_vals = [0] * npair

                def emit_pv(pi, pairs=None):
                    pair = kts[2 * pi: 2 * pi + 2]
                    for u, (di, kt) in enumerate(pair):
                        first = (pi == 0 and u == 0)
                        last = (pi == npair - 1 and u == len(pair) - 1)
                        wts = [(sv, mult_vals[pi])]
                        if first:
                            wts.append((sv, vtm_init))
                            if pv_freed[pvp]:
                                wts.append((sv, pv_freed[pvp]))
                        P.emit("tensor", lambda t, k=kt, _u=u, p=pvp, _h=h, f=first, l2=last, mp=pi % 4: t.matmul(
                            pv[p][:, :],
                            VTM[:, 65 * NH * k + 65 * _h: 65 * NH * k + 65 * _h + 65],
                            pm_sb[mp][:, 512 * _u:512 * (_u + 1)],
                            start=f, stop=l2),
                            wait=wts, inc=[(st, 1)] if last else [])
                        pm_free[pi % 4] = st.n

                for pi in range(npair):
                    pair = kts[2 * pi: 2 * pi + 2]
                    sp = pi % 2
                    for u, (di, kt) in enumerate(pair):
                        wts = []
                        if u == 0 and S2_free[sp]:
                            wts.append((ss, S2_free[sp]))
                        if h == 0 and qc == 0 and pi == 0 and u == 0:
                            wts.append((st, wproj_done[2]))
                            wts.append((sv, proj_evac_v))
                        P.emit("tensor", lambda t, k=kt, _u=u, p=sp, _qc=qc, _hb=hb, _et=et: t.matmul(
                            S2[p][:, 512 * _u:512 * (_u + 1)],
                            KT[_hb:_hb + 64, T * _et + 128 * k: T * _et + 128 * (k + 1)],
                            QT[_hb:_hb + 64, T * _et + 512 * _qc: T * _et + 512 * (_qc + 1)],
                            start=True, stop=True),
                            wait=wts, inc=[(st, 1)] if u == len(pair) - 1 else [])
                    sc_v = st.n
                    mp = pi % 4
                    wts = [(st, sc_v), (gc.sem, CW)]
                    if pst_free[mp % 2]:
                        wts.append((sv, pst_free[mp % 2]))
                    wdt = 512 * len(pair)
                    P.emit("scalar", lambda s, p=sp, w=wdt, _mp=mp % 2: s.activation(
                        pstage[_mp][:, 0:w], S2[p][:, 0:w], AF.Exp, scale=0.125),
                        wait=wts)
                    S2_free[sp] = ss.n
                    exp_v = ss.n
                    for u, (di, kt) in enumerate(pair):
                        wts = [(ss, exp_v), (gc.sem, CWM)]
                        if pm_free[mp]:
                            wts.append((st, pm_free[mp]))
                        P.emit("vector", lambda v, _u=u, _di=di, _mp=mp, _ps=mp % 2: v.tensor_mul(
                            pm_sb[_mp][:, 512 * _u:512 * (_u + 1)],
                            pstage[_ps][:, 512 * _u:512 * (_u + 1)],
                            m01[:, 512 * _di:512 * (_di + 1)]),
                            wait=wts)
                    pst_free[mp % 2] = sv.n
                    mult_vals[pi] = sv.n
                for pi in range(npair):
                    emit_pv(pi)
                pv_v = st.n
                pvp = qc % 2
                wts = [(st, pv_v)]
                if rcp_free:
                    wts.append((st, rcp_free))
                P.emit("scalar", lambda s, p=pvp: s.activation(lnr_sb[:, :], pv[p][64:65, :], AF.Ln),
                       wait=wts)
                P.emit("scalar", lambda s: s.activation(recip_sb[:, :], lnr_sb[:, :], AF.Exp, scale=-1.0))
                rcp_v = ss.n
                wts = [(st, pv_v)]
                if ao_free:
                    wts.append((sv, ao_free))
                P.emit("scalar", lambda s, p=pvp: s.copy(ao_st[:, :], pv[p][0:64, :]),
                       wait=wts)
                ao_v = ss.n
                wts = [(ss, rcp_v), (sv, ones_v)]
                if rb_free:
                    wts.append((sv, rb_free))
                P.emit("tensor", lambda t: t.matmul(
                    rb[:, :], ones1[:, :], recip_sb[:, :], start=True, stop=True),
                    wait=wts, inc=[(st, 1)])
                rcp_free = st.n
                rb_v = st.n
                P.emit("vector", lambda v, _qc=qc, _hb=hb, _et=et: v.tensor_mul(
                    aoT[_hb:_hb + 64, T * _et + 512 * _qc: T * _et + 512 * (_qc + 1)],
                    ao_st[:, :], rb[:, :]),
                    wait=[(st, rb_v), (ss, ao_v)])
                ao_free = sv.n
                rb_free = sv.n
                pv_freed[pvp] = sv.n
            fil = bufs.get("filler")
            if fil is not None:
                fil()

    with contextlib.ExitStack() as octx:
        po = [octx.enter_context(nc.psum_tensor(f"C_po{i}", [128, 512], F32)) for i in range(2)]
        ptr3 = [octx.enter_context(nc.psum_tensor(f"C_ptr3{i}", [128, 128], BF16)) for i in range(2)]
        ostage = octx.enter_context(nc.sbuf_tensor("C_ost", [128, 512], BF16))
        stok = octx.enter_context(nc.sbuf_tensor("C_stok", [128, 4 * 1024], BF16))
        wobuf = octx.enter_context(nc.sbuf_tensor("C_wo", [128, 8 * 1024], BF16))
        bar_o = [(st, st.n), (ss, ss.n), (sv, sv.n)]
        wo_v = gwq[0].load(wobuf[:, :], hbm["l1oT"][:, :], wait=list(bar_o))
        ao_all = P.sems["v"].n
        po_free = [0, 0]
        ptr3_free = [0, 0]
        ost_free = 0
        stok_free = [0, 0]
        for qc in range(NQC):
            for et in range(8):
                pa = et % 2
                for dc in range(8):
                    wts = [(gwq[0].sem, wo_v), (sv, ao_all), (ss, bar_o[1][1])]
                    if dc == 0 and po_free[pa]:
                        wts.append((ss, po_free[pa]))
                    P.emit("tensor", lambda t, d=dc, e=et, p=pa, _qc=qc: t.matmul(
                        po[p][:, :],
                        wobuf[:, 1024 * d + 128 * e: 1024 * d + 128 * (e + 1)],
                        aoT[:, T * d + 512 * _qc: T * d + 512 * (_qc + 1)],
                        start=(d == 0), stop=(d == 7)),
                        wait=wts, inc=[(st, 1)] if dc == 7 else [])
                mm_v = st.n
                wts = [(st, mm_v)]
                if ost_free:
                    wts.append((st, ost_free))
                P.emit("scalar", lambda s, p=pa: s.copy(ostage[:, :], po[p][:, :]),
                       wait=wts)
                po_free[pa] = ss.n
                ost_v = ss.n
                for j in range(4):
                    tp3 = j % 2
                    wts = [(ss, ost_v), (gc.sem, CW)]
                    if ptr3_free[tp3]:
                        wts.append((sv, ptr3_free[tp3]))
                    P.emit("tensor", lambda t, _j=j, p=tp3: t.transpose(
                        ptr3[p][:, :], ostage[:, 128 * _j:128 * (_j + 1)], identb[:, :]),
                        wait=wts, inc=[(st, 1)])
                    tr_v = st.n
                    wts = [(st, tr_v)]
                    par_t = j % 2
                    if et == 0 and stok_free[par_t]:
                        wts.append((so[par_t].sem, stok_free[par_t]))
                    P.emit("vector", lambda v, _j=j, e=et, p=tp3: v.tensor_copy(
                        stok[:, 1024 * _j + 128 * e: 1024 * _j + 128 * (e + 1)], ptr3[p][:, :]),
                        wait=wts)
                    ptr3_free[tp3] = sv.n
                ost_free = st.n
            stok_v = sv.n
            for j in range(4):
                tt = 4 * qc + j
                par = j % 2
                stok_free[par] = so[par].load(
                    hbm["s1_out"][128 * tt:128 * (tt + 1), :],
                    stok[:, 1024 * j:1024 * (j + 1)], wait=[(sv, stok_v)])
        P.emit("sync", lambda s: None, wait=[(so[0].sem, so[0].wv()), (so[1].sem, so[1].wv())])
    return


def build_C(cfg):
    nc = bass.Bass("TRN2", target_bir_lowering=False)
    T = cfg["T"]
    hbm = {}
    hbm["x1"] = nc.dram_tensor("x1", [T, 1024], F32, kind="ExternalInput").ap()
    for nm in ("l1qT", "l1kT", "l1vT", "l1oT"):
        hbm[nm] = nc.dram_tensor(nm, [128, 8 * 1024], F32, kind="ExternalInput").ap()
    hbm["m01"] = nc.dram_tensor("m01", [6, 128, 512], F32, kind="ExternalInput").ap()
    hbm["ident"] = nc.dram_tensor("ident", [128, 128], F32, kind="ExternalInput").ap()
    hbm["s1_out"] = nc.dram_tensor("s1_out", [T, 1024], BF16, kind="ExternalOutput").ap()
    with contextlib.ExitStack() as ctx:
        P = Plan(nc, ctx)
        for s in ("t", "s", "v", "g"):
            P.sem(s)
        bufs = {}
        bufs["ident"] = ctx.enter_context(nc.sbuf_tensor("identb", [128, 128], BF16))
        build_phase_C(P, hbm, bufs, cfg)
        P.run()
    return nc




class WeightQuantEmitter:

    def __init__(self, P, hbm, cfg, identb):
        self.P = P
        self.bufs_ident = identb
        nc = P.nc
        I = cfg.get("I", 2816)
        ctx = P.ctx
        self.gin = [DmaStream(P, f"WQ_in{i}", "sync") for i in range(2)]
        self.sout = [DmaStream(P, f"WQ_out{i}", "sync") for i in range(2)]
        self.wtile = [ctx.enter_context(nc.sbuf_tensor(f"WQ_wt{i}", [128, 512], F32, side="right"))
                      for i in range(4)]
        WMX = max(1024, I)
        self.wq_t = [ctx.enter_context(nc.sbuf_tensor(f"WQ_q{i}", [128, WMX], BF16, side="right"))
                     for i in range(2)]

        qs = {}
        qs["t_s"] = ctx.enter_context(nc.sbuf_tensor("WQ_ts", [128, 32], F32, side="right"))
        qs["rcp"] = ctx.enter_context(nc.sbuf_tensor("WQ_rcp", [128, 32], F32, side="right"))
        for nm in ("absx", "y", "accA", "accC", "msf", "c2"):
            qs[nm] = ctx.enter_context(nc.sbuf_tensor(f"WQ_{nm}", [128, 512], F32, side="right"))
        qs["u"] = qs["absx"]
        self.qs = qs
        self._pctx = contextlib.ExitStack()
        self.jobs = []
        for nt in range(I // 128):
            self.jobs.append(("wgate", "wgq_nat", nt, 1024))
        for nt in range(I // 128):
            self.jobs.append(("wup", "wuq_nat", nt, 1024))
        for nt in range(8):
            self.jobs.append(("wdown", "wdq_nat", nt, I))
        self.hbm = hbm
        self.flat = []
        for j, (src_name, dst, nt, width) in enumerate(self.jobs):
            nch = (width + 511) // 512
            for c in range(nch):
                self.flat.append((j, c, min(512, width - 512 * c), c == nch - 1))
        self.fcur = 0
        self.nloaded = 0
        self.loadv = {}
        self.wt_free = [(0, 0)] * 4
        self.wq_free = [0, 0]
        self.done_dep = None

    def _load_upto(self, i):
        P = self.P
        ss = P.sems["s"]; sv = P.sems["v"]
        while self.nloaded < min(i, len(self.flat)):
            pos = self.nloaded
            j, c, w2, last = self.flat[pos]
            src_name, dst, nt, width = self.jobs[j]
            bi = pos % 4
            fs, fv = self.wt_free[bi]
            wts = ([(ss, fs)] if fs else []) + ([(sv, fv)] if fv else [])
            self.loadv[pos] = self.gin[j % 2].load(
                self.wtile[bi][:, 0:w2],
                self.hbm[src_name][128 * nt:128 * (nt + 1), 512 * c:512 * c + w2],
                wait=wts)
            self.nloaded += 1

    def emit_chunk(self):
        pos = self.fcur
        if pos >= len(self.flat):
            return False
        P = self.P
        j, c, w2, last = self.flat[pos]
        src_name, dst, nt, width = self.jobs[j]
        self._load_upto(pos + 4)
        bi = pos % 4
        qp = j % 2
        wts = [(self.gin[j % 2].sem, self.gin[j % 2].wv())]
        if self.wq_free[qp] and c == 0:
            wts.append((self.sout[qp].sem, self.wq_free[qp]))
        qv = emit_quant(P, self.wq_t[qp][:, 512 * c:512 * c + w2],
                        self.wtile[bi][:, 0:w2], w2, self.qs,
                        src_ready=wts)
        self.wt_free[bi] = (self.qs["last_scalar"], self.qs["src_free_v"])
        self.fcur += 1
        if last:
            self.wq_free[qp] = self.sout[qp].load(
                self.hbm[dst][128 * nt:128 * (nt + 1), 0:width],
                self.wq_t[qp][:, 0:width], wait=[qv])
            if self.fcur >= len(self.flat):
                self._pctx.close()
                self.done_dep = [(self.sout[0].sem, self.sout[0].wv()),
                                 (self.sout[1].sem, self.sout[1].wv())]
        return True

    def finish(self):
        while self.emit_chunk():
            pass
        return self.done_dep



def _hap(h, pairs):
    return bass.AP(h, 0, [[h.shape[1], 128]] + pairs)


def emit_quant(P, dst, src, width, qs, src_ready=(), extra_dst_wait=()):
    nc = P.nc
    ss = P.sems["s"]; sv = P.sems["v"]
    nb = width // 16
    src3 = bass.AP(src.tensor, src.offset,
                   [list(src.ap[0]), [16, nb], [1, 16]])
    ts = qs["t_s"]; rcp = qs["rcp"]
    P.emit("vector", lambda v: v.reduce_max(ts[:, 0:nb], src3, AX.X,
                                            apply_absolute_value=True),
           wait=list(src_ready))
    qs["src_free_v"] = sv.n
    P.emit("vector", lambda v: v.tensor_scalar(ts[:, 0:nb], ts[:, 0:nb],
                                               1.0 / 6.0, 1e-12, ALU.mult, ALU.max))
    P.emit("vector", lambda v: v.reciprocal(rcp[:, 0:nb], ts[:, 0:nb]))
    rcp_v = sv.n
    P.emit("scalar", lambda s: s.activation(qs["absx"][:, 0:width], src, AF.Abs),
           wait=list(src_ready) + ([(sv, qs["absx_free"])] if qs.get("absx_free") else []))
    absx_v = ss.n
    rcp_b = _hap(rcp, [[1, nb], [0, 16]])
    y3 = _hap(qs["y"], [[16, nb], [1, 16]])
    absx3 = _hap(qs["absx"], [[16, nb], [1, 16]])
    P.emit("vector", lambda v: v.tensor_mul(y3, absx3, rcp_b),
           wait=[(ss, absx_v)])
    qs["absx_free"] = sv.n
    y = qs["y"]; accA = qs["accA"]; accC = qs["accC"]
    u = qs["u"]; c2 = qs["c2"]
    W = width
    P.emit("vector", lambda v: v.tensor_single_scalar(
        accA[:, 0:W], y[:, 0:W], 0.25, ALU.is_gt))
    for c in (0.75, 1.25, 1.75):
        P.emit("vector", lambda v, _c=c: v.scalar_tensor_tensor(
            accA[:, 0:W], y[:, 0:W], _c, accA[:, 0:W], ALU.is_gt, ALU.add))
    P.emit("vector", lambda v: v.tensor_single_scalar(
        accC[:, 0:W], y[:, 0:W], 2.5, ALU.is_gt))
    P.emit("vector", lambda v: v.scalar_tensor_tensor(
        accC[:, 0:W], y[:, 0:W], 3.5, accC[:, 0:W], ALU.is_gt, ALU.add))
    P.emit("vector", lambda v: v.tensor_single_scalar(
        c2[:, 0:W], y[:, 0:W], 5.0, ALU.is_gt))
    P.emit("scalar", lambda s: s.activation(qs["msf"][:, 0:width], src, AF.Sign),
           wait=list(src_ready) + ([(sv, qs["msf_free"])] if qs.get("msf_free") else []))
    sgn_v = ss.n
    qs["last_scalar"] = sgn_v
    ts_b = _hap(ts, [[1, nb], [0, 16]])
    msf3 = _hap(qs["msf"], [[16, nb], [1, 16]])
    P.emit("vector", lambda v: v.tensor_mul(msf3, msf3, ts_b), wait=[(ss, sgn_v)])
    P.emit("vector", lambda v: v.scalar_tensor_tensor(
        u[:, 0:W], accA[:, 0:W], 0.5, accC[:, 0:W], ALU.mult, ALU.add))
    P.emit("vector", lambda v: v.scalar_tensor_tensor(
        u[:, 0:W], c2[:, 0:W], 2.0, u[:, 0:W], ALU.mult, ALU.add))
    P.emit("vector", lambda v: v.tensor_mul(dst, u[:, 0:W], qs["msf"][:, 0:width]),
           wait=list(extra_dst_wait))
    qs["msf_free"] = sv.n
    qs["absx_free"] = sv.n
    return (sv, sv.n)


_QS_N = [0]


def mk_quant_scratch(P, width):
    ctx = P.ctx
    nc = P.nc
    _QS_N[0] += 1
    tag = _QS_N[0]
    qs = {}
    qs["t_s"] = ctx.enter_context(nc.sbuf_tensor(f"q{tag}_ts", [128, width // 16], F32))
    qs["rcp"] = ctx.enter_context(nc.sbuf_tensor(f"q{tag}_rcp", [128, width // 16], F32))
    for nm in ("absx", "y", "accA", "accC", "msf", "c2"):
        qs[nm] = ctx.enter_context(nc.sbuf_tensor(f"q{tag}_{nm}", [128, width], F32))
    qs["u"] = qs["absx"]
    return qs


def emit_sharded_wq(P, hbm, bufs, cfg, ident_dep=None):
    nc = P.nc
    I = cfg.get("I", 2816)
    NI = I // 128
    st = P.sems["t"]; ss = P.sems["s"]; sv = P.sems["v"]
    ctx = P.ctx
    identb = bufs["ident"]
    with contextlib.ExitStack() as wctx:
        saved = P.ctx
        P.ctx = wctx
        gin = [DmaStream(P, f"SW_in{i}", "sync") for i in range(2)]
        sosc = [DmaStream(P, f"SW_out{i}", "sync") for i in range(2)]
        srs = DmaStream(P, "SW_rs", "sync")
        ccs = P.sem("SW_cc")
        WMX = max(1024, I)
        wtile = [wctx.enter_context(nc.sbuf_tensor(f"SW_wt{i}", [128, WMX], F32)) for i in range(2)]
        wq_t = wctx.enter_context(nc.sbuf_tensor("SW_q", [128, WMX], BF16))
        stage = [wctx.enter_context(nc.sbuf_tensor(f"SW_st{i}", [128, max(8, NI) * 128], BF16)) for i in range(2)]
        qs = mk_quant_scratch(P, 1024)
        ptrw = [wctx.enter_context(nc.psum_tensor(f"SW_ptr{i}", [128, 128], BF16)) for i in range(2)]
        bar_w = [(st, st.n), (ss, ss.n), (sv, sv.n)]
        wt_free = [(0, 0), (0, 0)]
        wq_free = 0
        ptrw_free = [0, 0]
        stage_free = [0, 0]

        jobs = []
        for s in range(6):
            jobs.append((hbm["wjob_a"][s, :, :], 1024 * s, 1024))
        jobs.append((hbm["wjob_b"][:, :], 6144, I))

        for ji, (srcap, orow, width) in enumerate(jobs):
            par = ji % 2
            nkt = width // 128
            fs, fv = wt_free[par]
            wts = list(bar_w) + ([(ss, fs)] if fs else []) + ([(sv, fv)] if fv else [])
            lv = gin[par].load(wtile[par][:, 0:width], srcap, wait=wts)
            for qoff in range(0, width, 1024):
                w2 = min(1024, width - qoff)
                qv = emit_quant(P, wq_t[:, qoff:qoff + w2],
                                wtile[par][:, qoff:qoff + w2], w2, qs,
                                src_ready=[(gin[par].sem, lv)],
                                extra_dst_wait=([(st, wq_free)] if (wq_free and qoff == 0) else []))
            wt_free[par] = (qs["last_scalar"], qs["src_free_v"])
            spar = ji % 2
            for kt in range(nkt):
                tp = kt % 2
                wts = [qv] + ([ident_dep] if ident_dep else [])
                if ptrw_free[tp]:
                    wts.append((sv, ptrw_free[tp]))
                P.emit("tensor", lambda t, k=kt, p=tp, _pa=par: t.transpose(
                    ptrw[p][:, :], wq_t[:, 128 * k:128 * (k + 1)], identb[:, :]),
                    wait=wts, inc=[(st, 1)])
                tr_v = st.n
                wts = [(st, tr_v)]
                if kt == 0 and stage_free[spar]:
                    wts.append((sosc[spar].sem, stage_free[spar]))
                P.emit("vector", lambda v, k=kt, p=tp, sp=spar: v.tensor_copy(
                    stage[sp][:, 128 * k:128 * (k + 1)], ptrw[p][:, :]),
                    wait=wts)
                ptrw_free[tp] = sv.n
            wq_free = st.n
            stg_v = sv.n
            dst = hbm["wq_outc"].rearrange("(r p) c -> p r c", p=128)[:, orow // 128: orow // 128 + nkt, :]
            stage_free[spar] = sosc[spar].load(
                dst, stage[spar].rearrange("p (k n) -> p k n", n=128)[:, 0:nkt, :],
                wait=[(sv, stg_v)])

        ag_wait = [(sosc[0].sem, sosc[0].wv()), (sosc[1].sem, sosc[1].wv())]
        P.emit("gpsimd", lambda g: g.collective_compute(
            "AllGather", ALU.bypass, replica_groups=[list(range(NCORES))],
            ins=[hbm["wq_outc"]], outs=[hbm["wq_gath"]]),
            wait=ag_wait, inc=[(ccs, 16)], is_dma=True)
        agv = ccs.n

        blk = (6 * 1024 + I) // 128
        gath = hbm["wq_gath"].rearrange("(r p) c -> r p c", p=128)
        for ja in range(2 * NI):
            c, s = divmod(ja, 6)
            r0 = c * blk + s * 8
            dstname = "wgq_scr" if ja < NI else "wuq_scr"
            nt = ja % NI
            srs.load(hbm[dstname][:, :, 128 * nt:128 * (nt + 1)],
                     gath[r0:r0 + 8, :, :], wait=[(ccs, agv)])
        for c in range(8):
            r0 = c * blk + 48
            srs.load(hbm["wdq_scr"][:, :, 128 * c:128 * (c + 1)],
                     gath[r0:r0 + NI, :, :], wait=[(ccs, agv)])
        dep = [(srs.sem, srs.wv())]
        P.ctx = saved
    return dep


def build_phase_D(P, hbm, bufs, cfg):
    nc = P.nc
    T = cfg["T"]; NT = T // 128
    I = cfg.get("I", 2816)
    NI = I // 128
    st = P.sems["t"]; ss = P.sems["s"]; sv = P.sems["v"]
    ctx = P.ctx
    identb = bufs["ident"]

    gcD = DmaStream(P, "D_gc", "gpsimd")
    gcD.load(identb[:, :], hbm["ident"][:, :])
    CW = gcD.wv()

    xqT = ctx.enter_context(nc.sbuf_tensor("D_xqT", [128, 8 * T], BF16))
    out_acc = ctx.enter_context(nc.sbuf_tensor("D_oacc", [128, NT * 1024], F32))

    with contextlib.ExitStack() as lctx:
        Ps = P.ctx; P.ctx = lctx
        sx = [DmaStream(P, f"D_sx{i}", "sync") for i in range(2)]
        x_t = [lctx.enter_context(nc.sbuf_tensor(f"D_x{i}", [128, 1024], F32)) for i in range(2)]
        xsq = lctx.enter_context(nc.sbuf_tensor("D_xsq", [128, 1024], F32))
        stat4 = lctx.enter_context(nc.sbuf_tensor("D_stat", [128, 8], F32))
        xn_tok = lctx.enter_context(nc.sbuf_tensor("D_xntok", [128, 1024], F32))
        xq_tok = lctx.enter_context(nc.sbuf_tensor("D_xqtok", [128, 1024], BF16))
        eps_col = lctx.enter_context(nc.sbuf_tensor("D_eps", [128, 1], F32))
        qs = mk_quant_scratch(P, 1024)
        ptr = [lctx.enter_context(nc.psum_tensor(f"D_ptr{i}", [128, 128], BF16)) for i in range(2)]
        P.emit("vector", lambda v: v.memset(eps_col[:, :], 1e-5))
        eps_v = sv.n
        x_free = [0, 0]
        xsq_free = 0
        xn_free = 0
        xq_free = 0
        ptr_free = [0, 0]
        for tt in range(NT):
            par = tt % 2
            wts = ([(sv, x_free[par])] if x_free[par] else [])
            xv = sx[par].load(x_t[par][:, :], hbm["x1"][128 * tt:128 * (tt + 1), :], wait=wts)
            xw = (sx[par].sem, xv)
            P.emit("vector", lambda v, p=par: v.reduce_sum(stat4[:, 0:1], x_t[p][:, :], AX.X),
                   wait=[xw])
            swts = [xw] + ([(sv, xsq_free)] if xsq_free else [])
            P.emit("scalar", lambda s, p=par: s.activation(xsq[:, :], x_t[p][:, :], AF.Square),
                   wait=swts)
            sq_done = ss.n
            P.emit("vector", lambda v: v.reduce_sum(stat4[:, 1:2], xsq[:, :], AX.X),
                   wait=[(ss, sq_done)])
            xsq_free = sv.n
            P.emit("vector", lambda v: v.tensor_scalar_mul(stat4[:, 2:3], stat4[:, 0:1], 1.0 / 1024.0))
            P.emit("vector", lambda v: v.tensor_scalar_mul(stat4[:, 3:4], stat4[:, 1:2], 1.0 / 1024.0))
            P.emit("vector", lambda v: v.scalar_tensor_tensor(
                stat4[:, 4:5], stat4[:, 2:3], -1.0, stat4[:, 2:3], ALU.mult, ALU.mult))
            P.emit("vector", lambda v: v.tensor_add(stat4[:, 5:6], stat4[:, 3:4], stat4[:, 4:5]))
            var_v = sv.n
            P.emit("scalar", lambda s: s.activation(stat4[:, 6:7], stat4[:, 5:6], AF.Sqrt, bias=eps_col[:, :]),
                   wait=[(sv, var_v), (sv, eps_v)])
            sd_v = ss.n
            P.emit("vector", lambda v: v.reciprocal(stat4[:, 7:8], stat4[:, 6:7]),
                   wait=[(ss, sd_v)])
            rstd_v = sv.n
            wts = [(sv, rstd_v)] + ([(ss, xn_free)] if xn_free else [])
            P.emit("vector", lambda v, p=par: v.tensor_scalar(
                xn_tok[:, :], x_t[p][:, :], stat4[:, 2:3], stat4[:, 7:8], ALU.subtract, ALU.mult),
                wait=wts)
            xn_v = sv.n
            x_free[par] = xn_v
            qv = emit_quant(P, xq_tok[:, :], xn_tok[:, :], 1024, qs,
                            src_ready=[(sv, xn_v)],
                            extra_dst_wait=([(st, xq_free)] if xq_free else []))
            xn_free = qs["last_scalar"]
            for ft in range(8):
                tp = ft % 2
                wts = [qv]
                if ptr_free[tp]:
                    wts.append((sv, ptr_free[tp]))
                wts.append((gcD.sem, CW))
                P.emit("tensor", lambda t, f=ft, p=tp: t.transpose(
                    ptr[p][:, :], xq_tok[:, 128 * f:128 * (f + 1)], identb[:, :]),
                    wait=wts, inc=[(st, 1)])
                tr_v = st.n
                P.emit("vector", lambda v, f=ft, p=tp, _tt=tt: v.tensor_copy(
                    xqT[:, T * f + 128 * _tt: T * f + 128 * (_tt + 1)], ptr[p][:, :]),
                    wait=[(st, tr_v)])
                ptr_free[tp] = sv.n
            xq_free = st.n
        P.ctx = Ps
        xq_done_v = sv.n

    if bufs.get("wq_done_dep") is not None:
        wq_all_done = bufs["wq_done_dep"]
    else:
        _wq = WeightQuantEmitter(P, hbm, cfg, identb)
        wq_all_done = _wq.finish()

    chunks = []
    off = 0
    while off < I:
        w = min(512, I - off)
        chunks.append((off, w))
        off += w
    with contextlib.ExitStack() as mctx:
        Ps = P.ctx; P.ctx = mctx
        gwd = [DmaStream(P, f"D_gwd{i}", "sync") for i in range(2)]
        wgq_sb = [mctx.enter_context(nc.sbuf_tensor(f"D_wgq{i}", [128, 8 * 512], BF16)) for i in range(2)]
        wuq_sb = [mctx.enter_context(nc.sbuf_tensor(f"D_wuq{i}", [128, 8 * 512], BF16)) for i in range(2)]
        wdq_sb = [mctx.enter_context(nc.sbuf_tensor(f"D_wdq{i}", [128, 4 * 1024], BF16)) for i in range(2)]
        sg_st = mctx.enter_context(nc.sbuf_tensor("D_sg", [128, 512], F32))
        h_st = mctx.enter_context(nc.sbuf_tensor("D_h", [128, 512], F32))
        hq_st = [mctx.enter_context(nc.sbuf_tensor(f"D_hq{i}", [128, 512], BF16)) for i in range(2)]
        hqT_st = mctx.enter_context(nc.sbuf_tensor("D_hqT", [128, 4 * 128], BF16))
        qs3 = mk_quant_scratch(P, 512)
        pg = mctx.enter_context(nc.psum_tensor("D_pg", [128, 512], F32))
        pu = mctx.enter_context(nc.psum_tensor("D_pu", [128, 512], F32))
        pd = [mctx.enter_context(nc.psum_tensor(f"D_pd{i}", [128, 1024], F32)) for i in range(2)]
        ptrh = [mctx.enter_context(nc.psum_tensor(f"D_ptrh{i}", [128, 128], BF16)) for i in range(2)]
        bar_m = [(st, st.n), (ss, ss.n), (sv, sv.n)]

        w_ready = {}
        w_free = {0: 0, 1: 0}

        def load_wchunk(ci):
            offc, wc = chunks[ci]
            par = ci % 2
            nt0 = offc // 128
            wnt = wc // 128
            wts = list(bar_m) + list(wq_all_done) + ([(st, w_free[par])] if w_free[par] else [])
            for kt in range(8):
                gwd[par].load(wgq_sb[par][:, 512 * kt:512 * kt + wc],
                              hbm["wgq_nat"][offc:offc + wc, 128 * kt:128 * (kt + 1)],
                              wait=wts, transpose=True)
                gwd[par].load(wuq_sb[par][:, 512 * kt:512 * kt + wc],
                              hbm["wuq_nat"][offc:offc + wc, 128 * kt:128 * (kt + 1)],
                              transpose=True)
            for nl in range(wnt):
                gwd[par].load(wdq_sb[par][:, 1024 * nl:1024 * (nl + 1)],
                              hbm["wdq_nat"][:, 128 * (nt0 + nl):128 * (nt0 + nl + 1)],
                              transpose=True)
            w_ready[ci] = gwd[par].wv()

        load_wchunk(0)
        if len(chunks) > 1:
            load_wchunk(1)

        pg_free = 0
        pu_free = 0
        pd_free = [0, 0]
        sg_free = 0
        h_free = 0
        hq_free = [0, 0]
        hqT_free = 0
        ptrh_free = [0, 0]
        oacc_ready = [0] * NT

        for ci, (offc, wc) in enumerate(chunks):
            par = ci % 2
            wnt = wc // 128
            wv_ = (gwd[par].sem, w_ready[ci])
            gu_vals = [0] * NT
            dmm_vals = [0] * NT

            def emit_head(tt):
                nonlocal pg_free, pu_free
                for k in range(8):
                    wts = [wv_, (sv, xq_done_v)]
                    if k == 0 and pg_free:
                        wts.append((sv, pg_free))
                    P.emit("tensor", lambda t, _k=k, _tt=tt, p=par, w=wc: t.matmul(
                        pg[:, 0:w],
                        xqT[:, T * _k + 128 * _tt: T * _k + 128 * (_tt + 1)],
                        wgq_sb[p][:, 512 * _k: 512 * _k + w],
                        start=(_k == 0), stop=(_k == 7)),
                        wait=wts, inc=[(st, 1)] if k == 7 else [])
                    wts2 = []
                    if k == 0 and pu_free:
                        wts2.append((sv, pu_free))
                    P.emit("tensor", lambda t, _k=k, _tt=tt, p=par, w=wc: t.matmul(
                        pu[:, 0:w],
                        xqT[:, T * _k + 128 * _tt: T * _k + 128 * (_tt + 1)],
                        wuq_sb[p][:, 512 * _k: 512 * _k + w],
                        start=(_k == 0), stop=(_k == 7)),
                        wait=wts2, inc=[(st, 1)] if k == 7 else [])
                gu_vals[tt] = st.n

            def emit_mid(tt):
                nonlocal sg_free, h_free, pg_free, pu_free
                hp = tt % 2
                wts = [(st, gu_vals[tt])]
                if sg_free:
                    wts.append((sv, sg_free))
                P.emit("scalar", lambda s, w=wc: s.activation(sg_st[:, 0:w], pg[:, 0:w], AF.Silu),
                       wait=wts)
                pg_free_s = ss.n
                sgv = ss.n
                wts = [(ss, sgv)]
                if h_free:
                    wts.append((ss, h_free))
                P.emit("vector", lambda v, w=wc: v.tensor_mul(h_st[:, 0:w], sg_st[:, 0:w], pu[:, 0:w]),
                       wait=wts)
                pu_free = sv.n
                pg_free = sv.n
                sg_free = sv.n
                h_v = sv.n
                qv = emit_quant(P, hq_st[hp][:, 0:wc], h_st[:, 0:wc], wc, qs3,
                                src_ready=[(sv, h_v)],
                                extra_dst_wait=([(st, hq_free[hp])] if hq_free[hp] else []))
                h_free = qs3["last_scalar"]
                return qv

            def emit_tmid(tt, qv):
                nonlocal hqT_free, hq_free
                hp = tt % 2
                pdp = tt % 2
                for nt in range(wnt):
                    tp = nt % 2
                    wts = [qv, (gcD.sem, CW)]
                    if ptrh_free[tp]:
                        wts.append((ss, ptrh_free[tp]))
                    P.emit("tensor", lambda t, _n=nt, p=tp, q=hp: t.transpose(
                        ptrh[p][:, :], hq_st[q][:, 128 * _n:128 * (_n + 1)], identb[:, :]),
                        wait=wts, inc=[(st, 1)])
                    tr_v = st.n
                    wts = [(st, tr_v)]
                    if nt == 0 and hqT_free:
                        wts.append((st, hqT_free))
                    P.emit("scalar", lambda s, _n=nt, p=tp: s.copy(
                        hqT_st[:, 128 * _n:128 * (_n + 1)], ptrh[p][:, :]),
                        wait=wts)
                    ptrh_free[tp] = ss.n
                hq_free[hp] = st.n
                hqT_v = ss.n
                for nt in range(wnt):
                    for hh in range(2):
                        wts = [(ss, hqT_v), wv_]
                        if nt == 0 and hh == 0 and pd_free[pdp]:
                            wts.append((sv, pd_free[pdp]))
                        P.emit("tensor", lambda t, _n=nt, _h=hh, p=par, q=pdp: t.matmul(
                            pd[q][:, 512 * _h:512 * (_h + 1)],
                            hqT_st[:, 128 * _n:128 * (_n + 1)],
                            wdq_sb[p][:, 1024 * _n + 512 * _h: 1024 * _n + 512 * (_h + 1)],
                            start=(_n == 0), stop=(_n == wnt - 1),
                            skip_group_check=True),
                            wait=wts, inc=[(st, 1)] if (nt == wnt - 1 and hh == 1) else [])
                hqT_free = st.n
                dmm_vals[tt] = st.n

            def emit_evac(tt):
                nonlocal pd_free
                pdp = tt % 2
                if ci == 0:
                    P.emit("vector", lambda v, _tt=tt, q=pdp: v.tensor_copy(
                        out_acc[:, 1024 * _tt:1024 * (_tt + 1)], pd[q][:, :]),
                        wait=[(st, dmm_vals[tt])])
                else:
                    P.emit("vector", lambda v, _tt=tt, q=pdp: v.tensor_add(
                        out_acc[:, 1024 * _tt:1024 * (_tt + 1)],
                        out_acc[:, 1024 * _tt:1024 * (_tt + 1)], pd[q][:, :]),
                        wait=[(st, dmm_vals[tt])])
                pd_free[pdp] = sv.n
                if ci == len(chunks) - 1:
                    oacc_ready[tt] = sv.n

            emit_head(0)
            qv_cur = emit_mid(0)
            for tt in range(NT):
                if tt + 1 < NT:
                    emit_head(tt + 1)
                emit_tmid(tt, qv_cur)
                if tt + 1 < NT:
                    qv_cur = emit_mid(tt + 1)
                if tt >= 1:
                    emit_evac(tt - 1)
            emit_evac(NT - 1)
            if ci + 2 < len(chunks):
                w_free[par] = st.n
                load_wchunk(ci + 2)

        P.ctx = Ps

    with contextlib.ExitStack() as fctx:
        Ps = P.ctx; P.ctx = fctx
        sxf = [DmaStream(P, f"D_sxf{i}", "sync") for i in range(2)]
        ssf = [DmaStream(P, f"D_ssf{i}", "sync") for i in range(2)]
        sof = [DmaStream(P, f"D_sof{i}", "sync") for i in range(2)]
        x_f = [fctx.enter_context(nc.sbuf_tensor(f"D_xf{i}", [128, 1024], F32)) for i in range(2)]
        s_f = [fctx.enter_context(nc.sbuf_tensor(f"D_sf{i}", [128, 1024], BF16)) for i in range(2)]
        g1_f = fctx.enter_context(nc.sbuf_tensor("D_g1", [128, 1024], BF16))
        o_f = [fctx.enter_context(nc.sbuf_tensor(f"D_of{i}", [128, 1024], F32)) for i in range(2)]
        bar_f = [(st, st.n), (ss, ss.n), (sv, sv.n)]
        xf_free = [0, 0]
        of_free = [0, 0]
        g1_free = 0
        for tt in range(NT):
            par = tt % 2
            wts = list(bar_f) + ([(sv, xf_free[par])] if xf_free[par] else [])
            xv = sxf[par].load(x_f[par][:, :], hbm["x1"][128 * tt:128 * (tt + 1), :], wait=wts)
            sv_ld = ssf[par].load(s_f[par][:, :], hbm["s1"][128 * tt:128 * (tt + 1), :], wait=wts)
            wts = [(sv, oacc_ready[tt])]
            if g1_free:
                wts.append((sv, g1_free))
            P.emit("scalar", lambda s, _tt=tt: s.activation(
                g1_f[:, :], out_acc[:, 1024 * _tt:1024 * (_tt + 1)], AF.Silu),
                wait=wts)
            g1_v = ss.n
            wts = [(ss, g1_v), (ssf[par].sem, sv_ld)]
            if of_free[par]:
                wts.append((sof[par].sem, of_free[par]))
            P.emit("vector", lambda v, p=par: v.tensor_mul(o_f[p][:, :], g1_f[:, :], s_f[p][:, :]),
                   wait=wts)
            g1_free = sv.n
            P.emit("vector", lambda v, p=par: v.tensor_add(o_f[p][:, :], o_f[p][:, :], x_f[p][:, :]),
                   wait=[(sxf[par].sem, xv)])
            o_v = sv.n
            xf_free[par] = o_v
            of_free[par] = sof[par].load(hbm["out"][128 * tt:128 * (tt + 1), :], o_f[par][:, :],
                                         wait=[(sv, o_v)])
        P.emit("sync", lambda s: None, wait=[(sof[0].sem, sof[0].wv()), (sof[1].sem, sof[1].wv())])
        P.ctx = Ps
    return


def build_D(cfg):
    nc = bass.Bass("TRN2", target_bir_lowering=False)
    T = cfg["T"]
    I = cfg.get("I", 2816)
    hbm = {}
    hbm["x1"] = nc.dram_tensor("x1", [T, 1024], F32, kind="ExternalInput").ap()
    hbm["s1"] = nc.dram_tensor("s1", [T, 1024], BF16, kind="ExternalInput").ap()
    hbm["wgate"] = nc.dram_tensor("wgate", [I, 1024], F32, kind="ExternalInput").ap()
    hbm["wup"] = nc.dram_tensor("wup", [I, 1024], F32, kind="ExternalInput").ap()
    hbm["wdown"] = nc.dram_tensor("wdown", [1024, I], F32, kind="ExternalInput").ap()
    hbm["ident"] = nc.dram_tensor("ident", [128, 128], F32, kind="ExternalInput").ap()
    hbm["wgq_nat"] = nc.dram_tensor("wgq_nat", [I, 1024], BF16).ap()
    hbm["wuq_nat"] = nc.dram_tensor("wuq_nat", [I, 1024], BF16).ap()
    hbm["wdq_nat"] = nc.dram_tensor("wdq_nat", [1024, I], BF16).ap()
    if cfg.get("shard_wq"):
        hbm["wjob_a"] = nc.dram_tensor("wjob_a", [6, 128, 1024], F32, kind="ExternalInput").ap()
        hbm["wjob_b"] = nc.dram_tensor("wjob_b", [128, I], F32, kind="ExternalInput").ap()
        nrow = 6 * 1024 + I
        hbm["wq_outc"] = nc.dram_tensor("wq_outc", [nrow, 128], BF16).ap()
        hbm["wq_gath"] = nc.dram_tensor("wq_gath", [8 * nrow, 128], BF16, addr_space="Shared").ap()
    hbm["out"] = nc.dram_tensor("out", [T, 1024], F32, kind="ExternalOutput").ap()
    with contextlib.ExitStack() as ctx:
        P = Plan(nc, ctx)
        for s in ("t", "s", "v", "g"):
            P.sem(s)
        bufs = {}
        bufs["ident"] = ctx.enter_context(nc.sbuf_tensor("identb", [128, 128], BF16))
        build_phase_D(P, hbm, bufs, cfg)
        P.run()
    return nc



def build_A(cfg, fused=False):
    nc = bass.Bass("TRN2", target_bir_lowering=False)
    T = cfg["T"]
    hbm = {}
    hbm["x"] = nc.dram_tensor("x", [T, 1024], F32, kind="ExternalInput").ap()
    hbm["w1all"] = nc.dram_tensor("w1all", [cfg["NE"], 1024, cfg["F"]], F32, kind="ExternalInput").ap()
    hbm["w2all"] = nc.dram_tensor("w2all", [cfg["NE"], cfg["F"], 1024], F32, kind="ExternalInput").ap()
    hbm["b1dev"] = nc.dram_tensor("b1dev", [128, (cfg["F"] // 128) * cfg["NE"]], F32, kind="ExternalInput").ap()
    hbm["b2all"] = nc.dram_tensor("b2all", [cfg["NE"], 1024], F32, kind="ExternalInput").ap()
    hbm["wgdev"] = nc.dram_tensor("wgdev", [128, 8 * (cfg["NE"] - 1)], F32, kind="ExternalInput").ap()
    hbm["ident"] = nc.dram_tensor("ident", [128, 128], F32, kind="ExternalInput").ap()
    hbm["g_out"] = nc.dram_tensor("g_out", [T, 1024], BF16, kind="ExternalOutput").ap()

    with contextlib.ExitStack() as ctx:
        P = Plan(nc, ctx)
        for s in ("sq", "gq", "t", "s", "v"):
            P.sem(s)
        bufs = {}
        bufs["xnT"] = ctx.enter_context(nc.sbuf_tensor("xnT", [128, 8 * T], BF16))
        bufs["g_acc"] = ctx.enter_context(nc.sbuf_tensor("g_acc", [128, (T // 128) * 1024], F32))
        bufs["g_tok"] = None
        bufs["ident"] = ctx.enter_context(nc.sbuf_tensor("identb", [128, 128], BF16))
        build_phase_A(P, hbm, bufs, cfg)
        P.run()
    return nc


def build_full(cfg):
    nc = bass.Bass("TRN2", target_bir_lowering=False)
    T = cfg["T"]
    I = cfg.get("I", 2816)
    h = {}
    h["x"] = nc.dram_tensor("x", [T, 1024], F32, kind="ExternalInput").ap()
    h["w1all"] = nc.dram_tensor("w1all", [cfg["NE"], 1024, cfg["F"]], F32, kind="ExternalInput").ap()
    h["w2all"] = nc.dram_tensor("w2all", [cfg["NE"], cfg["F"], 1024], F32, kind="ExternalInput").ap()
    h["b1dev"] = nc.dram_tensor("b1dev", [128, (cfg["F"] // 128) * cfg["NE"]], F32, kind="ExternalInput").ap()
    h["b2all"] = nc.dram_tensor("b2all", [cfg["NE"], 1024], F32, kind="ExternalInput").ap()
    h["wgdev"] = nc.dram_tensor("wgdev", [128, 8 * (cfg["NE"] - 1)], F32, kind="ExternalInput").ap()
    h["ident"] = nc.dram_tensor("ident", [128, 128], F32, kind="ExternalInput").ap()
    for nm in ("wqT", "wkT", "wvT", "woT", "l1qT", "l1kT", "l1vT", "l1oT"):
        h[nm] = nc.dram_tensor(nm, [128, 8 * 1024], F32, kind="ExternalInput").ap()
    h["gt"] = nc.dram_tensor("gt", [16, 128, 2 * T - 128], F32, kind="ExternalInput").ap()
    h["m01"] = nc.dram_tensor("m01", [6, 128, 512], F32, kind="ExternalInput").ap()
    h["wgate"] = nc.dram_tensor("wgate", [I, 1024], F32, kind="ExternalInput").ap()
    h["wup"] = nc.dram_tensor("wup", [I, 1024], F32, kind="ExternalInput").ap()
    h["wdown"] = nc.dram_tensor("wdown", [1024, I], F32, kind="ExternalInput").ap()
    h["g_out"] = h["g_in"] = nc.dram_tensor("g_h", [T, 1024], BF16).ap()
    h["x1_out"] = h["x1"] = nc.dram_tensor("x1_h", [T, 1024], F32).ap()
    h["s1_out"] = h["s1"] = nc.dram_tensor("s1_h", [T, 1024], BF16).ap()
    h["wgq_nat"] = nc.dram_tensor("wgq_nat", [I, 1024], BF16).ap()
    h["wuq_nat"] = nc.dram_tensor("wuq_nat", [I, 1024], BF16).ap()
    h["wdq_nat"] = nc.dram_tensor("wdq_nat", [1024, I], BF16).ap()
    if cfg.get("shard_wq"):
        h["wjob_a"] = nc.dram_tensor("wjob_a", [6, 128, 1024], F32, kind="ExternalInput").ap()
        h["wjob_b"] = nc.dram_tensor("wjob_b", [128, I], F32, kind="ExternalInput").ap()
        nrow = 6 * 1024 + I
        h["wq_outc"] = nc.dram_tensor("wq_outc", [nrow, 128], BF16).ap()
        h["wq_gath"] = nc.dram_tensor("wq_gath", [8 * nrow, 128], BF16, addr_space="Shared").ap()
    h["out"] = nc.dram_tensor("out", [T, 1024], F32, kind="ExternalOutput").ap()

    with contextlib.ExitStack() as ctx:
        P = Plan(nc, ctx)
        for s in ("t", "s", "v", "g"):
            P.sem(s)
        bufs = {}
        bufs["ident"] = ctx.enter_context(nc.sbuf_tensor("identb", [128, 128], BF16))
        bufs["g_tok"] = None
        wq = WeightQuantEmitter(P, h, cfg, bufs["ident"])
        bufs["filler"] = lambda: wq.emit_chunk()
        for builder in (build_phase_A, build_phase_B, build_phase_C, build_phase_D):
            if builder is build_phase_B:
                bufs["filler"] = None
            if builder is build_phase_D:
                bufs["wq_done_dep"] = wq.finish()
            with contextlib.ExitStack() as pctx:
                saved = P.ctx
                P.ctx = pctx
                builder(P, h, bufs, cfg)
                P.ctx = saved
            P.phase_barrier()
        P.run()
    return nc


CFG_FULL = dict(T=2048, F=4096, NG=4, NE=9, I=2816)


def kernel(**inputs):
    maps = prep_host(inputs)
    cfg = CFG_FULL
    cores = list(range(NCORES))
    ncF = build_full(cfg)
    resF = run_bass_kernel_spmd(ncF, maps, core_ids=cores)
    return np.stack([np.asarray(r["out"], dtype=np.float32) for r in resF.results], axis=0)


def kernel_multi(**inputs):
    maps = prep_host(inputs)
    cfg = CFG_FULL
    cores = list(range(NCORES))

    ncA = build_A(cfg)
    inA = [dict(x=m["x"], w1all=m["w1all"], w2all=m["w2all"], b1dev=m["b1dev"],
                b2all=m["b2all"], wgdev=m["wgdev"], ident=m["ident"]) for m in maps]
    resA = run_bass_kernel_spmd(ncA, inA, core_ids=cores)
    g_all = [r["g_out"] for r in resA.results]

    ncB = build_B(cfg)
    inB = [dict(x=m["x"], g_in=g_all[b], wqT=m["wqT"], wkT=m["wkT"], wvT=m["wvT"],
                woT=m["woT"], gt=m["gt"], ident=m["ident"]) for b, m in enumerate(maps)]
    resB = run_bass_kernel_spmd(ncB, inB, core_ids=cores)
    x1_all = [r["x1_out"] for r in resB.results]

    ncC = build_C(cfg)
    inC = [dict(x1=x1_all[b], l1qT=m["l1qT"], l1kT=m["l1kT"], l1vT=m["l1vT"],
                l1oT=m["l1oT"], m01=m["m01"], ident=m["ident"]) for b, m in enumerate(maps)]
    resC = run_bass_kernel_spmd(ncC, inC, core_ids=cores)
    s1_all = [r["s1_out"] for r in resC.results]

    ncD = build_D(cfg)
    inD = [dict(x1=x1_all[b], s1=s1_all[b], wgate=m["wgate"], wup=m["wup"],
                wdown=m["wdown"], ident=m["ident"]) for b, m in enumerate(maps)]
    resD = run_bass_kernel_spmd(ncD, inD, core_ids=cores)
    out = np.stack([np.asarray(r["out"], dtype=np.float32) for r in resD.results], axis=0)
    return out
